# version 09
# baseline (speedup 1.0000x reference)
import numpy as np
import concourse.bass as bass
import concourse.mybir as mybir
from concourse.bass_utils import run_bass_kernel_spmd
from concourse.bass_core_rust import IndirectOffsetOnAxis

F32 = mybir.dt.float32
BF16 = mybir.dt.bfloat16
I32 = mybir.dt.int32
AF = mybir.ActivationFunctionType
ALU = mybir.AluOpType
AX = mybir.AxisListType


class Cfg:
    D = 1024
    NCH = 8
    H = 16
    DK = 32
    DV = 64
    LR = 16
    FI = 2048
    FICH = 16
    F = 4096
    FCH = 32
    E = 8
    CAP = 640
    GATE_NORM = 16.0
    EPS = 1e-5
    T = 2048
    SEQ = 2
    NCORES = 8
    TC = 256
    GC = 64
    DKS = DK ** -0.5

    NTOK = property(lambda s: s.T * s.SEQ)
    NTILE = property(lambda s: s.NTOK // 128)
    NSLOT = property(lambda s: s.E * s.CAP)
    PERP = property(lambda s: s.NCORES * s.NTOK // 128)
    NXT = property(lambda s: s.CAP // 128)


class Seq:

    ENGS = ("sync", "scalar", "vector", "tensor", "gpsimd")
    DIMS = ("sync", "scalar", "vector", "tensor", "gpsimd",
            "sync_d", "scalar_d", "gpsimd_d")

    def __init__(self):
        self.steps = []

    def c(self, eng, fn):
        self.steps.append((eng, "c", fn))

    def dma(self, eng, fn):
        self.steps.append((eng, "d", fn))

    @staticmethod
    def key(eng, kind):
        return eng + "_d" if kind == "d" else eng

    def emit_probe(self, nc, block):
        import numpy as _np
        per = {e: [] for e in self.ENGS}
        for idx, (eng, kind, fn) in enumerate(self.steps):
            per[eng].append((idx, fn))
        info = [None] * len(self.steps)
        base_cache = {}

        def sbase(name):
            if name not in base_cache:
                base_cache[name] = nc.lookup_mloc(name).addr
            return base_cache[name]

        def ap_range(ap):
            b = getattr(ap, "bass_ap", None)
            if b is None:
                return None
            esz = _np.dtype(mybir.dt.np(b.dtype)).itemsize
            pairs = [tuple(p) for p in b.ap]
            sp = str(b.space)
            if "DRAM" in sp:
                if ap.dynamic_ap_info is not None:
                    return (("D", b.name), 0, 1 << 44)
                lo = hi = b.offset
                for st, ct in pairs:
                    d = (ct - 1) * st
                    if d < 0:
                        lo += d
                    else:
                        hi += d
                return (("D", b.name), lo * esz, (hi + 1) * esz)
            dims = pairs[1:] if len(pairs) > 1 else []
            pstride = pairs[0][0] if pairs else 1
            off = b.offset
            if pstride > 0:
                off = off % pstride
            lo = hi = off
            for st, ct in dims:
                d = (ct - 1) * st
                if d < 0:
                    lo += d
                else:
                    hi += d
            if "PSUM" in sp:
                return (("P", b.name), 0, 1 << 30)
            base = sbase(b.name)
            return (("S",), base + lo * esz, base + (hi + 1) * esz)

        def mk(ename):
            def body(eng):
                for idx, fn in per[ename]:
                    inst = fn(eng)
                    mi = inst.ins
                    rd, wr = [], []
                    for ap in (mi.ins or []):
                        r = ap_range(ap)
                        if r is not None:
                            rd.append(r)
                    for ap in (mi.outs or []):
                        r = ap_range(ap)
                        if r is not None:
                            wr.append(r)
                    info[idx] = (rd, wr)
            return body

        for e in self.ENGS:
            if per[e] or e == "sync":
                getattr(block, e)(mk(e))
        return info

    NPOOL = 24

    def compute_schedule(self, info):
        n = len(self.steps)
        NG = 8
        NSW = 8
        dims = (list(self.ENGS) + ["dp%d" % i for i in range(self.NPOOL)]
                + ["gq%d" % i for i in range(NG)]
                + ["sw%d" % i for i in range(NSW)])
        dimi = {d: i for i, d in enumerate(dims)}
        counts = {d: 0 for d in dims}
        comp = [None] * n
        clocks = [None] * n
        know = {e: [0] * len(dims) for e in self.ENGS}
        waits = [None] * n
        incs = [None] * n
        writes_map = {}
        reads_map = {}
        pool_next = 0
        gq_next = 0
        sw_next = 0
        prev_user = [None] * self.NPOOL
        prev_sw = [None] * NSW

        def overlap(a0, a1, b0, b1):
            return a0 < b1 and b0 < a1

        for j, (eng, kind, fn) in enumerate(self.steps):
            deps = set()
            if kind == "d" and eng == "gpsimd":
                p = sw_next
                sw_next = (sw_next + 1) % NSW
                dk = "sw%d" % p
                counts[dk] += 16
                comp[j] = (dimi[dk], counts[dk])
                if prev_sw[p] is not None:
                    deps.add(prev_sw[p])
                prev_sw[p] = j
                incs[j] = (dk, 16)
            elif kind == "d":
                p = pool_next
                pool_next = (pool_next + 1) % self.NPOOL
                dk = "dp%d" % p
                counts[dk] += 16
                comp[j] = (dimi[dk], counts[dk])
                if prev_user[p] is not None:
                    deps.add(prev_user[p])
                prev_user[p] = j
                incs[j] = (dk, 16)
            elif eng == "gpsimd":
                dk = "gq%d" % (gq_next % NG)
                gq_next += 1
                counts[dk] += 1
                comp[j] = (dimi[dk], counts[dk])
                incs[j] = (dk, 1)
            else:
                dk = eng
                counts[dk] += 1
                comp[j] = (dimi[dk], counts[dk])
                incs[j] = (dk, 1)
            rd, wr = info[j]
            for key, lo, hi in rd:
                wl = writes_map.get(key)
                if not wl:
                    continue
                uncov = [(lo, hi)]
                for wlo, whi, wi in reversed(wl):
                    nxt = []
                    hit = False
                    for ulo, uhi in uncov:
                        if overlap(ulo, uhi, wlo, whi):
                            hit = True
                            if ulo < wlo:
                                nxt.append((ulo, wlo))
                            if whi < uhi:
                                nxt.append((whi, uhi))
                        else:
                            nxt.append((ulo, uhi))
                    if hit:
                        deps.add(wi)
                    uncov = nxt
                    if not uncov:
                        break
            for key, lo, hi in wr:
                wl = writes_map.get(key)
                if wl:
                    uncov = [(lo, hi)]
                    for wlo, whi, wi in reversed(wl):
                        nxt = []
                        hit = False
                        for ulo, uhi in uncov:
                            if overlap(ulo, uhi, wlo, whi):
                                hit = True
                                if ulo < wlo:
                                    nxt.append((ulo, wlo))
                                if whi < uhi:
                                    nxt.append((whi, uhi))
                            else:
                                nxt.append((ulo, uhi))
                        if hit:
                            deps.add(wi)
                        uncov = nxt
                        if not uncov:
                            break
                rl = reads_map.get(key)
                if rl:
                    keep = []
                    for rlo, rhi, ri in rl:
                        if overlap(lo, hi, rlo, rhi):
                            deps.add(ri)
                            if rlo < lo or rhi > hi:
                                keep.append((rlo, rhi, ri))
                        else:
                            keep.append((rlo, rhi, ri))
                    reads_map[key] = keep
            kn = know[eng]
            wl_out = []
            for i in sorted(deps):
                ddi, dval = comp[i]
                if kn[ddi] < dval:
                    wl_out.append((dims[ddi], dval))
                ci = clocks[i]
                for z in range(len(kn)):
                    if ci[z] > kn[z]:
                        kn[z] = ci[z]
                if kn[ddi] < dval:
                    kn[ddi] = dval
            waits[j] = wl_out
            cj = list(kn)
            di, dv = comp[j]
            if cj[di] < dv:
                cj[di] = dv
            clocks[j] = cj
            for key, lo, hi in wr:
                wl = writes_map.setdefault(key, [])
                wl2 = [w for w in wl if not (lo <= w[0] and w[1] <= hi)]
                wl2.append((lo, hi, j))
                writes_map[key] = wl2
            for key, lo, hi in rd:
                reads_map.setdefault(key, []).append((lo, hi, j))
        return {"waits": waits, "incs": incs, "finals": dict(counts),
                "dims": dims}

    def emit(self, nc, block, sems, schedule=None):
        per = {e: [] for e in self.ENGS}
        tot = {k: 0 for k in sems}
        for idx, (eng, kind, fn) in enumerate(self.steps):
            per[eng].append((idx, dict(tot), kind, fn))
            if schedule is None:
                tot[self.key(eng, kind)] += 16 if kind == "d" else 1
        final = dict(tot) if schedule is None else dict(schedule["finals"])
        import os
        if os.environ.get("PRINT_TOTALS"):
            nw = (sum(len(w) for w in schedule["waits"])
                  if schedule else None)
            print("SEQ TOTALS:", final, "steps:", len(self.steps),
                  "waits:", nw)

        def mk(ename):
            def body(eng):
                seen = {k: 0 for k in sems}
                for idx, w, kind, fn in per[ename]:
                    if schedule is None:
                        for k in sems:
                            if w[k] > seen[k]:
                                eng.wait_ge(sems[k], w[k])
                                seen[k] = w[k]
                    else:
                        for k, dval in schedule["waits"][idx]:
                            if dval > seen[k]:
                                eng.wait_ge(sems[k], dval)
                                seen[k] = dval
                    inst = fn(eng)
                    if schedule is None:
                        kk = self.key(ename, kind)
                        inst.then_inc(sems[kk], 16 if kind == "d" else 1)
                    else:
                        kk, inc = schedule["incs"][idx]
                        inst.then_inc(sems[kk], inc)
                if ename == "sync":
                    for k in sems:
                        if final.get(k, 0) > seen[k]:
                            eng.wait_ge(sems[k], final[k])
            return body

        for e in self.ENGS:
            if per[e] or e == "sync":
                getattr(block, e)(mk(e))


def build_program(cfg, max_steps=None, milestones=None, debug_out=False,
                  dump_src=None, dump_shape=None, dump_dt=None,
                  phasec_probe=None, emit_mode="serial", schedule=None,
                  probe_out=None):
    nc = bass.Bass("TRN2", target_bir_lowering=False,
                   dynamic_dma_scratch_size=8192)
    q = Seq()

    D, NCH, H, DK, DV = cfg.D, cfg.NCH, cfg.H, cfg.DK, cfg.DV
    TC, GC, NTOK = cfg.TC, cfg.GC, cfg.NTOK
    NGC = TC // GC
    TPT = TC // 128
    NTILE = cfg.NTILE
    PERP = cfg.PERP
    EPS = cfg.EPS
    E = cfg.E

    def din(name, shape, dt=F32):
        return nc.dram_tensor(name, shape, dt, kind="ExternalInput")

    xT = din("xT", [D, NTOK])
    x_tm = din("x_tm", [NTOK, D])
    Wq = din("Wq", [D, H * DK]); Wk = din("Wk", [D, H * DK])
    Wv = din("Wv", [D, D]); Wg = din("Wg", [D, D]); Wo = din("Wo", [D, D])
    W1 = din("W1", [D, cfg.FI]); W2 = din("W2", [cfg.FI, D])
    gkw1 = din("gkw1", [D, cfg.LR]); gkw2 = din("gkw2", [cfg.LR, H * DK])
    ln_w = din("ln_w", [1, D]); ln_b = din("ln_b", [1, D])
    gk_b2 = din("gk_b2", [1, H * DK])
    b1i = din("b1i", [1, cfg.FI]); b2i = din("b2i", [1, D])
    rms_w = din("rms_w", [1, DV])
    gw = din("gw", [D, E]); gb = din("gb", [1, E])
    ew1 = din("ew1", [D, cfg.F]); eb1 = din("eb1", [1, cfg.F])
    ew2 = din("ew2", [cfg.F, D]); eb2 = din("eb2", [1, D])
    identf = din("identf", [128, 128]); identb = din("identb", [128, 128], BF16)
    cmask = din("cmask", [128, 64], BF16)
    sltri = din("sltri", [128, 128])
    coreid = din("coreid", [1, 1])

    dbg = dict(kind="ExternalOutput") if debug_out else {}
    x1buf = nc.dram_tensor("x1buf", [NTOK, D], F32, **dbg)
    ln2buf = nc.dram_tensor("ln2buf", [NTOK, D], BF16, **dbg)
    logitsT = nc.dram_tensor("logitsT", [E, NTOK], F32)
    logits_all = nc.dram_tensor("logits_all", [E * cfg.NCORES, NTOK], F32)
    slotall = nc.dram_tensor("slotall", [cfg.NCORES * NTILE, 128], F32)
    gpall = nc.dram_tensor("gpall", [cfg.NCORES * NTILE, 128], F32)
    slotbuf = nc.dram_tensor("slotbuf", [cfg.NSLOT + 1, D], BF16)
    rsout = nc.dram_tensor("rsout", [cfg.CAP, D], BF16)
    ag2in = nc.dram_tensor("ag2in", [cfg.CAP, D], F32)
    outbuf = nc.dram_tensor("outbuf", [cfg.NSLOT + 1, D], F32)
    srow = nc.dram_tensor("srow", [1, 64], F32)
    out = nc.dram_tensor("out", [NTOK, D], F32, kind="ExternalOutput")

    allocs = []

    def sb(name, shape, dt=F32):
        g = nc.sbuf_tensor(name, shape, dt)
        t = g.__enter__()
        allocs.append((g, name))
        return t

    def ps(name, shape, dt=F32):
        g = nc.psum_tensor(name, shape, dt)
        t = g.__enter__()
        allocs.append((g, name))
        return t

    def free_from(name):
        idx = next(i for i, (_, n) in enumerate(allocs) if n == name)
        for g, _ in reversed(allocs[idx:]):
            g.__exit__(None, None, None)
        del allocs[idx:]

    idf = sb("idf", [128, 128], F32)
    idb = sb("idb", [128, 128], BF16)
    cmk = sb("cmk", [128, 64], BF16)
    rmsr = sb("rmsr", [128, DV], F32)
    gbr = sb("gbr", [128, E], F32)
    csr = sb("csr", [128, E], F32)
    onescf = sb("onescf", [128, 1], F32)
    onesr8 = sb("onesr8", [1, 128], F32)
    lnbfm = sb("lnbfm", [128, NCH], F32)
    lnwfm = sb("lnwfm", [128, NCH], F32)
    lnwsfm = sb("lnwsfm", [128, NCH], F32)
    zrow = sb("zrow", [128, max(TC, PERP)], F32)
    zb = zrow[:, 0:256].bitcast(BF16)

    q.dma("sync", lambda s: s.dma_start(out=idf[:, :], in_=identf.ap()))
    q.dma("sync", lambda s: s.dma_start(out=idb[:, :], in_=identb.ap()))
    q.dma("sync", lambda s: s.dma_start(out=cmk[:, :], in_=cmask.ap()))
    q.dma("sync", lambda s: s.dma_start(out=rmsr[:, :], in_=rms_w.ap().broadcast_to([128, DV])))
    q.dma("sync", lambda s: s.dma_start(out=gbr[:, :], in_=gb.ap().broadcast_to([128, E])))
    q.dma("sync", lambda s: s.dma_start(
        out=lnbfm[:, :], in_=ln_b.ap().rearrange("o (c p) -> (o p) c", p=128),
        allow_slow_non_contiguous=True))
    q.dma("sync", lambda s: s.dma_start(
        out=lnwfm[:, :], in_=ln_w.ap().rearrange("o (c p) -> (o p) c", p=128),
        allow_slow_non_contiguous=True))
    q.c("vector", lambda v: v.memset(onescf[:, :], 1.0))
    q.c("vector", lambda v: v.memset(onesr8[:, :], 1.0))
    q.c("vector", lambda v: v.memset(zrow[:, :], 0.0))
    q.c("vector", lambda v: v.tensor_scalar_mul(lnwsfm[:, :], lnwfm[:, :], cfg.DKS))

    for r0 in range(0, cfg.NSLOT + 1, 128):
        rn = min(128, cfg.NSLOT + 1 - r0)
        for c0 in range(0, D, 512):
            q.dma("sync", lambda s, r0=r0, rn=rn, c0=c0: s.dma_start(
                out=slotbuf.ap()[r0:r0 + rn, c0:c0 + 512],
                in_=zrow[0:rn, 0:256].bitcast(BF16)))
    for c0 in range(0, D, 256):
        q.dma("sync", lambda s, c0=c0: s.dma_start(
            out=outbuf.ap()[cfg.NSLOT:cfg.NSLOT + 1, c0:c0 + 256],
            in_=zrow[0:1, 0:256]))

    psA = ps("psA", [128, 512], F32)
    psB = ps("psB", [128, 512], F32)
    psBig = ps("psBig", [128, 1024], F32)
    psO = ps("psO", [128, 1024], F32)
    psT = ps("psT", [128, 1024], BF16)
    psS = ps("psS", [128, 512], F32)

    xf = sb("xf", [128, NCH * TC], F32)
    xm = sb("xm", [128, NCH * TC], F32)
    xa = sb("xa", [128, NCH * TC], BF16)
    sc8 = sb("sc8", [128, NCH * TC], F32)
    qf = sb("qf", [128, 4 * TC], BF16)
    kf = sb("kf", [128, 4 * TC], BF16)
    gka = sb("gka", [128, 4 * TC], F32)
    eq = sb("eq", [128, 4 * TC], BF16)
    ek = sb("ek", [128, 4 * TC], BF16)
    bmat = sb("bmat", [128, 4 * NGC], F32)
    acar = sb("acar", [128, 4], F32)
    Ssb = sb("Ssb", [128, 4 * DV], F32)
    Sbp = sb("Sbp", [128, 4 * DV], BF16)
    Sbm = sb("Sbm", [128, 4 * DV], BF16)
    ach = sb("ach", [128, 4], F32)
    vt = sb("vt", [128, 1024], BF16)
    sg = sb("sg", [128, 1024], F32)
    pa = sb("pa", [128, 8 * 128], BF16)
    ktm = sb("ktm", [128, 512], BF16)
    o2 = sb("o2", [128, 1024], BF16)
    o2f = sb("o2f", [128, NCH * TC], BF16)
    xt1 = sb("xt1", [128, 1024], F32)
    st8 = sb("st8", [1, 4 * TC], F32)
    str2 = sb("str2", [128, 10], F32)
    r1p = sb("r1p", [2, TC], BF16)
    oneTC = sb("oneTC", [1, TC], BF16)
    r2p = sb("r2p", [2, TC], BF16)
    lgt = sb("lgt", [8, 128], F32)
    agw = sb("agw", [128, TPT * 16], F32)

    q.c("vector", lambda v: v.memset(r1p[:, :], 1.0))
    q.c("vector", lambda v: v.memset(oneTC[:, :], 1.0))
    q.c("vector", lambda v: v.memset(r2p[:, :], 1.0))

    wtq = sb("wtq", [128, NCH * 512], BF16)
    wtk = sb("wtk", [128, NCH * 512], BF16)
    wtv = sb("wtv", [128, NCH * 1024], BF16)
    wtg = sb("wtg", [128, NCH * 1024], BF16)
    wto = sb("wto", [128, NCH * 1024], BF16)
    wt1 = sb("wt1", [128, NCH * cfg.FI], BF16)
    wt2 = sb("wt2", [128, cfg.FICH * 1024], BF16)
    wgk1 = sb("wgk1", [128, NCH * cfg.LR], BF16)
    wgk2 = sb("wgk2", [cfg.LR + 1, H * DK], BF16)
    wgw = sb("wgw", [128, NCH * E], F32)
    w2b = sb("w2b", [1, D], BF16)
    r1q = sb("r1q", [2, 512], BF16)
    r1k = sb("r1k", [2, 512], BF16)
    r1v = sb("r1v", [2, 1024], BF16)
    r1g = sb("r1g", [2, 1024], BF16)
    r1gk = sb("r1gk", [2, cfg.LR], BF16)
    r1w1 = sb("r1w1", [2, cfg.FI], BF16)

    wst = sb("wst", [128, 512], F32)
    wst2 = sb("wst2", [128, 512], F32)
    _wstp = [0]
    def _stg():
        _wstp[0] ^= 1
        return (wst, "sync") if _wstp[0] else (wst2, "scalar")

    onescb = sb("onescb", [128, 1], BF16)
    lnb2 = sb("lnb2", [128, 2 * NCH], F32)
    q.c("vector", lambda v: v.memset(onescb[:, :], 1.0))
    q.c("vector", lambda v: v.memset(lnb2[:, :], 0.0))
    q.c("vector", lambda v: v.tensor_copy(
        lnb2[:, :].rearrange("p (c two) -> p c two", two=2)[:, :, 1:2],
        lnbfm[:, :].unsqueeze(2)))

    def _ureg(j):
        pd = psBig if j < 2 else psO
        return pd[0:2, (j % 2) * 512:(j % 2) * 512 + 512]

    def _ureg2(j):
        pd = psBig if j < 2 else psO
        return pd[1:2, (j % 2) * 512:(j % 2) * 512 + 512]

    def prep_scaled(Wd, outF, wt_sb, r1_sb, scale_fm, uscale=None, badd=None):
        nN = (outF + 511) // 512
        for j in range(nN):
            n0j, n1j = j * 512, min(outF, (j + 1) * 512)
            for kc in range(NCH):
                stgb, stgq = _stg()
                q.dma(stgq, lambda s, kc=kc, n0=n0j, n1=n1j, stgb=stgb: s.dma_start(
                    out=stgb[:, 0:n1 - n0],
                    in_=Wd.ap()[kc * 128:(kc + 1) * 128, n0:n1]))
                q.c("tensor", lambda t, kc=kc, n0=n0j, n1=n1j, j=j, stgb=stgb: t.matmul(
                    _ureg(j)[:, 0:n1 - n0],
                    lnb2[:, 2 * kc:2 * kc + 2], stgb[:, 0:n1 - n0],
                    start=(kc == 0), stop=(kc == NCH - 1)))
                q.c("vector", lambda v, kc=kc, n0=n0j, n1=n1j, stgb=stgb: v.tensor_scalar(
                    wt_sb[:, kc * outF + n0:kc * outF + n1], stgb[:, 0:n1 - n0],
                    scale_fm[:, kc:kc + 1], None, ALU.mult))
        for j in range(nN):
            n0, n1 = j * 512, min(outF, (j + 1) * 512)
            if badd is not None:
                q.dma("sync", lambda s, n0=n0, n1=n1: s.dma_start(
                    out=wst[0:2, 0:n1 - n0],
                    in_=badd.ap()[0:1, n0:n1].broadcast_to([2, n1 - n0])))
            if uscale is not None:
                q.c("vector", lambda v, j=j, n0=n0, n1=n1: v.tensor_scalar_mul(
                    r1_sb[0:2, n0:n1], _ureg(j)[:, 0:n1 - n0], uscale))
            elif badd is not None:
                q.c("vector", lambda v, j=j, n0=n0, n1=n1: v.tensor_add(
                    r1_sb[0:2, n0:n1], _ureg(j)[:, 0:n1 - n0], wst[0:2, 0:n1 - n0]))
            else:
                q.c("vector", lambda v, j=j, n0=n0, n1=n1: v.tensor_copy(
                    r1_sb[0:2, n0:n1], _ureg(j)[:, 0:n1 - n0]))
        for j in range(nN):
            n0, n1 = j * 512, min(outF, (j + 1) * 512)
            for kc in range(NCH):
                q.c("tensor", lambda t, kc=kc, n0=n0, n1=n1: t.matmul(
                    psB[0:1, 0:n1 - n0], onescb[:, 0:1],
                    wt_sb[:, kc * outF + n0:kc * outF + n1],
                    start=(kc == 0), stop=(kc == NCH - 1)))
            q.c("vector", lambda v, n0=n0, n1=n1: v.tensor_copy(
                r1_sb[0:1, n0:n1], psB[0:1, 0:n1 - n0]))

    prep_scaled(Wq, 512, wtq, r1q, lnwsfm, uscale=cfg.DKS)
    prep_scaled(Wk, 512, wtk, r1k, lnwfm)
    prep_scaled(Wv, 1024, wtv, r1v, lnwfm)
    prep_scaled(Wg, 1024, wtg, r1g, lnwfm)
    prep_scaled(gkw1, cfg.LR, wgk1, r1gk, lnwfm)
    prep_scaled(W1, cfg.FI, wt1, r1w1, lnwfm, badd=b1i)

    for kc in range(NCH):
        wcb = xf[:, 0:1024] if kc % 2 == 0 else xf[:, 1024:2048]
        wcq = "sync" if kc % 2 == 0 else "scalar"
        q.dma(wcq, lambda s, kc=kc, wcb=wcb: s.dma_start(
            out=wcb, in_=Wo.ap()[kc * 128:(kc + 1) * 128, :]))
        q.c("vector", lambda v, kc=kc, wcb=wcb: v.tensor_copy(
            wto[:, kc * 1024:(kc + 1) * 1024], wcb))
        q.dma("sync", lambda s, kc=kc: s.dma_start(
            out=wgw[:, kc * E:(kc + 1) * E], in_=gw.ap()[kc * 128:(kc + 1) * 128, :]))
    for kc in range(cfg.FICH):
        wcb = xf[:, 0:1024] if kc % 2 == 0 else xf[:, 1024:2048]
        wcq = "sync" if kc % 2 == 0 else "scalar"
        q.dma(wcq, lambda s, kc=kc, wcb=wcb: s.dma_start(
            out=wcb, in_=W2.ap()[kc * 128:(kc + 1) * 128, :]))
        q.c("vector", lambda v, kc=kc, wcb=wcb: v.tensor_copy(
            wt2[:, kc * 1024:(kc + 1) * 1024], wcb))
    q.dma("sync", lambda s: s.dma_start(out=wst[0:cfg.LR, 0:512], in_=gkw2.ap()))
    q.dma("sync", lambda s: s.dma_start(
        out=wst[cfg.LR:cfg.LR + 1, 0:512], in_=gk_b2.ap()))
    q.c("vector", lambda v: v.tensor_copy(
        wgk2[0:cfg.LR + 1, :], wst[0:cfg.LR + 1, 0:512]))
    q.dma("sync", lambda s: s.dma_start(out=xf[0:1, 0:1024], in_=b2i.ap()))
    q.c("vector", lambda v: v.tensor_copy(w2b[:, :], xf[0:1, 0:1024]))

    q.dma("sync", lambda s: s.dma_start(
        out=wst[:, 0:E * NCH].rearrange("p (c e) -> p c e", c=NCH),
        in_=gw.ap().rearrange("(c p) e -> p c e", p=128)))
    for kc in range(NCH):
        q.c("tensor", lambda t, kc=kc: t.matmul(
            psA[0:1, 0:E], onescf[:, 0:1],
            wst[:, 0:E * NCH].rearrange("p (c e) -> p c e", c=NCH)[:, kc, :],
            start=(kc == 0), stop=(kc == NCH - 1)))
    q.c("vector", lambda v: v.tensor_copy(wst[0:1, 0:E], psA[0:1, 0:E]))
    q.dma("sync", lambda s: s.dma_start(out=srow.ap()[0:1, 0:E], in_=wst[0:1, 0:E]))
    q.dma("sync", lambda s: s.dma_start(
        out=csr[:, :], in_=srow.ap()[0:1, 0:E].broadcast_to([128, E])))

    free_from("wst")

    t1v = sc8[:, :].bitcast(BF16)
    gsc = sc8[:, 0:4 * TC]
    assert 4 * TC >= 1024
    aot = gka
    l2t = o2
    xf3 = xf[:, :].rearrange("p (c t) -> p c t", c=NCH)
    xm3 = xm[:, :].rearrange("p (c t) -> p c t", c=NCH)
    xa3 = xa[:, :].rearrange("p (c t) -> p c t", c=NCH)
    qf3 = qf[:, :].rearrange("p (c t) -> p c t", c=4)
    kf3 = kf[:, :].rearrange("p (c t) -> p c t", c=4)
    gka4 = gka[:, :].rearrange("p (c g u) -> p c g u", c=4, g=NGC)
    gsc4 = gsc.rearrange("p (c g u) -> p c g u", c=4, g=NGC)
    eq4 = eq[:, :].rearrange("p (c g u) -> p c g u", c=4, g=NGC)
    bmat3 = bmat[:, :].rearrange("p (c g) -> p c g", c=4)
    S3 = Ssb[:, :].rearrange("p (c v) -> p c v", c=4)
    Sbp3 = Sbp[:, :].rearrange("p (c v) -> p c v", c=4)
    Sbm3 = Sbm[:, :].rearrange("p (c v) -> p c v", c=4)
    o2f3 = o2f[:, :].rearrange("p (c t) -> p c t", c=NCH)
    t1v3 = t1v[:, 0:cfg.FICH * TC].rearrange("p (c t) -> p c t", c=cfg.FICH)

    def strow(r, s=0):
        return st8[r:r + 1, s * TC:(s + 1) * TC]

    for seq_i in range(cfg.SEQ):
        q.c("vector", lambda v: v.memset(Ssb[:, :], 0.0))
        q.c("vector", lambda v: v.memset(acar[:, :], 0.0))
        for chn in range(cfg.T // TC):
            t0 = seq_i * cfg.T + chn * TC

            q.dma("sync", lambda s, t0=t0: s.dma_start(
                out=xf3, in_=xT.ap().rearrange("(c p) n -> p c n", p=128)[:, :, t0:t0 + TC]))

            q.c("vector", lambda v: v.tensor_mul(sc8[:, 0:NCH * TC], xf[:, :], xf[:, :]))
            for kc in range(NCH):
                q.c("tensor", lambda t, kc=kc: t.matmul(
                    psS[0:1, 0:TC], onescf[:, 0:1], xf3[:, kc, :],
                    start=(kc == 0), stop=(kc == NCH - 1)))
            for kc in range(NCH):
                q.c("tensor", lambda t, kc=kc: t.matmul(
                    psS[0:1, TC:2 * TC], onescf[:, 0:1],
                    sc8[:, 0:NCH * TC].rearrange("p (c t) -> p c t", c=NCH)[:, kc, :],
                    start=(kc == 0), stop=(kc == NCH - 1)))
            tm0 = gka[0:1, 0:TC]
            tm1 = gka[0:1, TC:2 * TC]
            q.c("vector", lambda v: v.tensor_scalar_mul(tm0, psS[0:1, 0:TC], 1.0 / D))
            q.c("vector", lambda v: v.tensor_scalar_mul(tm1, psS[0:1, TC:2 * TC], 1.0 / D))
            q.c("vector", lambda v: v.tensor_mul(strow(0, 3), tm0, tm0))
            q.c("vector", lambda v: v.tensor_sub(tm1, tm1, strow(0, 3)))
            q.c("vector", lambda v: v.tensor_scalar_add(strow(0, 3), tm1, EPS))
            q.c("scalar", lambda s: s.activation(strow(0, 1), strow(0, 3), AF.Sqrt))
            q.c("vector", lambda v: v.reciprocal(strow(0, 2), strow(0, 1)))
            q.c("vector", lambda v: v.reciprocal(strow(0, 1), strow(0, 3)))
            q.c("vector", lambda v: v.tensor_mul(tm1, tm1, strow(0, 1)))
            q.c("vector", lambda v: v.tensor_scalar_add(tm1, tm1, EPS))
            q.c("scalar", lambda s: s.activation(tm1, tm1, AF.Sqrt))
            q.c("vector", lambda v: v.reciprocal(tm1, tm1))
            q.c("vector", lambda v: v.tensor_mul(strow(0, 1), strow(0, 2), tm1))
            q.c("vector", lambda v: v.tensor_mul(strow(0, 3), tm0, strow(0, 2)))
            q.c("vector", lambda v: v.tensor_mul(strow(0, 0), strow(0, 1), tm0))
            q.c("vector", lambda v: v.tensor_scalar_mul(strow(0, 0), strow(0, 0), -1.0))
            q.c("vector", lambda v: v.tensor_copy(r1p[0:1, :], strow(0, 0)))

            q.c("tensor", lambda t: t.matmul(
                psBig[:, 0:TC], onesr8[0:1, :], strow(0, 1), start=True, stop=True))
            q.c("vector", lambda v: v.tensor_tensor(
                xa3, xf3, psBig[:, 0:TC].unsqueeze(1).broadcast_to([128, NCH, TC]),
                ALU.mult))

            for kc in range(NCH):
                q.c("tensor", lambda t, kc=kc: t.matmul(
                    psA[0:cfg.LR, 0:TC], wgk1[:, kc * cfg.LR:(kc + 1) * cfg.LR],
                    xa3[:, kc, :], start=(kc == 0), stop=False))
            q.c("tensor", lambda t: t.matmul(
                psA[0:cfg.LR, 0:TC], r1gk[:, :], r1p[:, :], start=False, stop=True))
            q.c("vector", lambda v: v.memset(pa[0:cfg.LR + 1, 0:TC], 1.0))
            q.c("vector", lambda v: v.tensor_copy(pa[0:cfg.LR, 0:TC], psA[0:cfg.LR, 0:TC]))
            for mc in range(4):
                q.c("tensor", lambda t, mc=mc: t.matmul(
                    psBig[:, mc * TC:(mc + 1) * TC], wgk2[:, mc * 128:(mc + 1) * 128],
                    pa[0:cfg.LR + 1, 0:TC], start=True, stop=True))
            q.c("scalar", lambda s: s.activation(gsc, psBig[:, 0:4 * TC], AF.Sigmoid))
            q.c("scalar", lambda s: s.activation(gka[:, :], gsc, AF.Ln))
            q.c("vector", lambda v: v.tensor_scalar_mul(gsc, gka[:, :], 1.0 / cfg.GATE_NORM))
            for fc in range(4):
                q.c("vector", lambda v, fc=fc: v.tensor_tensor_scan(
                    gka[:, :].rearrange("p (c t) -> p c t", c=4)[:, fc, :],
                    gsc.rearrange("p (c t) -> p c t", c=4)[:, fc, :],
                    zrow[:, 0:TC], acar[:, fc:fc + 1], ALU.add, ALU.add))
            q.c("vector", lambda v: v.tensor_copy(bmat3[:, :, 0:1], acar[:, :].unsqueeze(2)))
            for g in range(1, NGC):
                q.c("vector", lambda v, g=g: v.tensor_copy(
                    bmat3[:, :, g:g + 1],
                    gka[:, :].rearrange("p (c t) -> p c t", c=4)[:, :, g * GC - 1:g * GC]))
            q.c("vector", lambda v: v.tensor_copy(
                acar[:, :],
                gka[:, :].rearrange("p (c t) -> p c t", c=4)[:, :, TC - 1:TC].squeeze(2)))
            q.c("vector", lambda v: v.tensor_sub(
                gsc4, gka4, bmat3.unsqueeze(3).broadcast_to([128, 4, NGC, GC])))
            q.c("scalar", lambda s: s.activation(eq[:, :], gsc, AF.Exp))
            q.c("scalar", lambda s: s.activation(ek[:, :], gsc, AF.Exp, scale=-1.0))

            for dest, wt_sb, r1_sb, fac in ((qf, wtq, r1q, eq), (kf, wtk, r1k, ek)):
                for mc in range(4):
                    for kc in range(NCH):
                        q.c("tensor", lambda t, mc=mc, kc=kc, wt_sb=wt_sb: t.matmul(
                            psBig[:, mc * TC:(mc + 1) * TC],
                            wt_sb[:, kc * 512 + mc * 128:kc * 512 + (mc + 1) * 128],
                            xa3[:, kc, :], start=(kc == 0), stop=False))
                    q.c("tensor", lambda t, mc=mc, r1_sb=r1_sb: t.matmul(
                        psBig[:, mc * TC:(mc + 1) * TC],
                        r1_sb[:, mc * 128:(mc + 1) * 128], r1p[:, :],
                        start=False, stop=True))
                for mc in range(4):
                    q.c("vector", lambda v, mc=mc, dest=dest, fac=fac: v.tensor_tensor(
                        dest[:, mc * TC:(mc + 1) * TC], psBig[:, mc * TC:(mc + 1) * TC],
                        fac[:, mc * TC:(mc + 1) * TC], ALU.mult))

            for j in range(TPT):
                for kc in range(NCH):
                    q.c("tensor", lambda t, j=j, kc=kc: t.matmul(
                        psA[:, 0:E], xf3[:, kc, j * 128:(j + 1) * 128],
                        wgw[:, kc * E:(kc + 1) * E],
                        start=(kc == 0), stop=(kc == NCH - 1)))
                q.c("vector", lambda v, j=j: v.tensor_copy(
                    agw[:, j * 16:j * 16 + 8], psA[:, 0:E]))

            for j in range(TPT):
                ce, co = 2 * j, 2 * j + 1
                tcol = j * 128

                for half, wt_sb, r1_sb in ((0, wtv, r1v), (1, wtg, r1g)):
                    for nh in range(2):
                        for kc in range(NCH):
                            q.c("tensor", lambda t, kc=kc, nh=nh, tcol=tcol, wt_sb=wt_sb: t.matmul(
                                psO[:, nh * 512:(nh + 1) * 512],
                                xa3[:, kc, tcol:tcol + 128],
                                wt_sb[:, kc * 1024 + nh * 512:kc * 1024 + (nh + 1) * 512],
                                start=(kc == 0), stop=False))
                        q.c("tensor", lambda t, nh=nh, tcol=tcol, r1_sb=r1_sb: t.matmul(
                            psO[:, nh * 512:(nh + 1) * 512],
                            r1p[:, tcol:tcol + 128], r1_sb[:, nh * 512:(nh + 1) * 512],
                            start=False, stop=True))
                    if half == 0:
                        q.c("vector", lambda v: v.tensor_copy(vt[:, :], psO[:, :]))
                    else:
                        q.c("scalar", lambda s: s.activation(sg[:, :], psO[:, :], AF.Silu))

                q.c("vector", lambda v, ce=ce: v.tensor_copy(
                    ach[:, :], eq4[:, :, ce, GC - 1:GC].squeeze(2)))

                for fc in range(4):
                    q.c("tensor", lambda t, fc=fc, tcol=tcol: t.transpose(
                        psT[:, fc * 128:(fc + 1) * 128],
                        kf3[:, fc, tcol:tcol + 128], idb[:, :]))
                q.c("vector", lambda v: v.tensor_copy(ktm[:, 0:512], psT[:, 0:512]))

                q.c("vector", lambda v: v.tensor_copy(Sbp[:, :], Ssb[:, :]))
                for h in range(H):
                    rb = (h % 4) * 32
                    fch = h // 4
                    q.c("tensor", lambda t, h=h, rb=rb, fch=fch: t.matmul(
                        psA[rb:rb + 32, fch * DV:(fch + 1) * DV],
                        ktm[0:64, fch * 128 + rb:fch * 128 + rb + 32],
                        vt[0:64, h * DV:(h + 1) * DV],
                        start=True, stop=True, tile_position=(0, rb)))
                q.c("vector", lambda v: v.tensor_add(Ssb[:, :], Ssb[:, :], psA[:, 0:4 * DV]))
                q.c("vector", lambda v: v.tensor_tensor(
                    S3, S3, ach[:, :].unsqueeze(2).broadcast_to([128, 4, DV]), ALU.mult))
                q.c("vector", lambda v: v.tensor_copy(Sbm[:, :], Ssb[:, :]))
                for h in range(H):
                    rb = (h % 4) * 32
                    fch = h // 4
                    q.c("tensor", lambda t, h=h, rb=rb, fch=fch: t.matmul(
                        psA[rb:rb + 32, fch * DV:(fch + 1) * DV],
                        ktm[64:128, fch * 128 + rb:fch * 128 + rb + 32],
                        vt[64:128, h * DV:(h + 1) * DV],
                        start=True, stop=True, tile_position=(64, rb)))
                q.c("vector", lambda v, co=co: v.tensor_copy(
                    ach[:, :], eq4[:, :, co, GC - 1:GC].squeeze(2)))
                q.c("vector", lambda v: v.tensor_add(Ssb[:, :], Ssb[:, :], psA[:, 0:4 * DV]))
                q.c("vector", lambda v: v.tensor_tensor(
                    S3, S3, ach[:, :].unsqueeze(2).broadcast_to([128, 4, DV]), ALU.mult))

                for grp in range(2):
                    for hi in range(8):
                        h = grp * 8 + hi
                        rb = (h % 4) * 32
                        fch = h // 4
                        q.c("tensor", lambda t, hi=hi, rb=rb, fch=fch, ce=ce: t.matmul(
                            psBig[0:64, hi * 128:hi * 128 + 64],
                            kf3[rb:rb + 32, fch, ce * GC:(ce + 1) * GC],
                            qf3[rb:rb + 32, fch, ce * GC:(ce + 1) * GC],
                            start=True, stop=True, tile_position=(rb, 0)))
                        q.c("tensor", lambda t, hi=hi, rb=rb, fch=fch, co=co: t.matmul(
                            psBig[64:128, hi * 128 + 64:hi * 128 + 128],
                            kf3[rb:rb + 32, fch, co * GC:(co + 1) * GC],
                            qf3[rb:rb + 32, fch, co * GC:(co + 1) * GC],
                            start=True, stop=True, tile_position=(rb, 64)))
                    q.c("vector", lambda v: v.memset(
                        pa[64:128, :].rearrange("p (h t) -> p h t", h=8)[:, :, 0:64], 0.0))
                    q.c("vector", lambda v: v.memset(
                        pa[0:64, :].rearrange("p (h t) -> p h t", h=8)[:, :, 64:128], 0.0))
                    q.c("vector", lambda v: v.tensor_tensor(
                        pa[0:64, :].rearrange("p (h t) -> p h t", h=8)[:, :, 0:64],
                        psBig[0:64, :].rearrange("p (h t) -> p h t", h=8)[:, :, 0:64],
                        cmk[0:64, 0:64].unsqueeze(1).broadcast_to([64, 8, 64]), ALU.mult))
                    q.c("vector", lambda v: v.tensor_tensor(
                        pa[64:128, :].rearrange("p (h t) -> p h t", h=8)[:, :, 64:128],
                        psBig[64:128, :].rearrange("p (h t) -> p h t", h=8)[:, :, 64:128],
                        cmk[64:128, 0:64].unsqueeze(1).broadcast_to([64, 8, 64]),
                        ALU.mult))
                    for hi in range(8):
                        h = grp * 8 + hi
                        q.c("tensor", lambda t, h=h, hi=hi: t.matmul(
                            psO[:, h * DV:(h + 1) * DV],
                            pa[:, hi * 128:(hi + 1) * 128],
                            vt[:, h * DV:(h + 1) * DV], start=True, stop=True))

                for h in range(H):
                    rb = (h % 4) * 32
                    fch = h // 4
                    q.c("tensor", lambda t, h=h, rb=rb, fch=fch, ce=ce: t.matmul(
                        psBig[0:64, h * DV:(h + 1) * DV],
                        qf3[rb:rb + 32, fch, ce * GC:(ce + 1) * GC],
                        Sbp3[rb:rb + 32, fch, :],
                        start=True, stop=True, tile_position=(rb, 0)))
                    q.c("tensor", lambda t, h=h, rb=rb, fch=fch, co=co: t.matmul(
                        psBig[64:128, h * DV:(h + 1) * DV],
                        qf3[rb:rb + 32, fch, co * GC:(co + 1) * GC],
                        Sbm3[rb:rb + 32, fch, :],
                        start=True, stop=True, tile_position=(rb, 64)))
                q.c("vector", lambda v: v.tensor_copy(aot[:, 0:1024], psBig[:, :]))
                q.c("vector", lambda v: v.tensor_add(aot[:, 0:1024], aot[:, 0:1024], psO[:, :]))

                q.c("vector", lambda v: v.tensor_mul(
                    sc8[:, 1024:1024 + 1024], aot[:, 0:1024], aot[:, 0:1024]))
                q.c("vector", lambda v: v.tensor_reduce(
                    bmat[:, 0:H].unsqueeze(2),
                    sc8[:, 1024:1024 + 1024].rearrange("p (h v) -> p h v", h=H),
                    AX.X, ALU.add))
                q.c("vector", lambda v: v.tensor_scalar(
                    bmat[:, 0:H], bmat[:, 0:H], 1.0 / DV, EPS, ALU.mult, ALU.add))
                q.c("scalar", lambda s: s.activation(bmat[:, 0:H], bmat[:, 0:H], AF.Sqrt))
                q.c("vector", lambda v: v.reciprocal(bmat[:, 0:H], bmat[:, 0:H]))
                q.c("vector", lambda v: v.tensor_tensor(
                    sg[:, :].rearrange("p (h v) -> p h v", h=H),
                    sg[:, :].rearrange("p (h v) -> p h v", h=H),
                    rmsr[:, :].unsqueeze(1).broadcast_to([128, H, DV]), ALU.mult))
                q.c("vector", lambda v: v.tensor_tensor(
                    sg[:, :].rearrange("p (h v) -> p h v", h=H),
                    sg[:, :].rearrange("p (h v) -> p h v", h=H),
                    bmat[:, 0:H].unsqueeze(2).broadcast_to([128, H, DV]), ALU.mult))
                q.c("vector", lambda v: v.tensor_mul(o2[:, :], aot[:, 0:1024], sg[:, :]))
                for fc in range(NCH):
                    q.c("tensor", lambda t, fc=fc: t.transpose(
                        psT[:, fc * 128:(fc + 1) * 128], o2[:, fc * 128:(fc + 1) * 128],
                        idb[:, :]))
                q.c("vector", lambda v, tcol=tcol: v.tensor_copy(
                    o2f3[:, :, tcol:tcol + 128],
                    psT[:, :].rearrange("p (c t) -> p c t", c=NCH)[:, :, 0:128]))

            q.c("tensor", lambda t: t.matmul(
                psBig[:, 0:TC], onesr8[0:1, :], strow(0, 2), start=True, stop=True))
            q.c("vector", lambda v: v.tensor_tensor(
                xm3, xf3, psBig[:, 0:TC].unsqueeze(1).broadcast_to([128, NCH, TC]),
                ALU.mult))
            q.c("tensor", lambda t: t.matmul(
                psBig[:, 0:TC], onesr8[0:1, :], strow(0, 3), start=True, stop=True))
            q.c("vector", lambda v: v.tensor_tensor(
                xm3, xm3, psBig[:, 0:TC].unsqueeze(1).broadcast_to([128, NCH, TC]),
                ALU.subtract))
            for mc in range(NCH):
                pd = psA if mc % 2 == 0 else psB
                for kc in range(NCH):
                    q.c("tensor", lambda t, mc=mc, kc=kc, pd=pd: t.matmul(
                        pd[:, 0:TC],
                        wto[:, kc * 1024 + mc * 128:kc * 1024 + (mc + 1) * 128],
                        o2f3[:, kc, :], start=(kc == 0), stop=(kc == NCH - 1)))
                q.c("vector", lambda v, mc=mc, pd=pd: v.tensor_add(
                    xm3[:, mc, :], xm3[:, mc, :], pd[:, 0:TC]))

            q.c("vector", lambda v: v.tensor_mul(sc8[:, 0:NCH * TC], xm[:, :], xm[:, :]))
            for kc in range(NCH):
                q.c("tensor", lambda t, kc=kc: t.matmul(
                    psA[0:1, 0:TC], onescf[:, 0:1], xm3[:, kc, :],
                    start=(kc == 0), stop=(kc == NCH - 1)))
            for kc in range(NCH):
                q.c("tensor", lambda t, kc=kc: t.matmul(
                    psB[0:1, 0:TC], onescf[:, 0:1],
                    sc8[:, 0:NCH * TC].rearrange("p (c t) -> p c t", c=NCH)[:, kc, :],
                    start=(kc == 0), stop=(kc == NCH - 1)))
            q.c("vector", lambda v: v.tensor_scalar_mul(strow(0, 0), psA[0:1, 0:TC], 1.0 / D))
            q.c("vector", lambda v: v.tensor_scalar_mul(strow(0, 1), psB[0:1, 0:TC], 1.0 / D))
            q.c("vector", lambda v: v.tensor_mul(strow(0, 2), strow(0, 0), strow(0, 0)))
            q.c("vector", lambda v: v.tensor_sub(strow(0, 1), strow(0, 1), strow(0, 2)))
            q.c("vector", lambda v: v.tensor_scalar_add(strow(0, 3), strow(0, 1), EPS))
            q.c("scalar", lambda s: s.activation(strow(0, 1), strow(0, 3), AF.Sqrt))
            q.c("vector", lambda v: v.reciprocal(strow(0, 2), strow(0, 1)))
            q.c("vector", lambda v: v.tensor_mul(strow(0, 3), strow(0, 2), strow(0, 0)))
            q.c("vector", lambda v: v.tensor_scalar_mul(strow(0, 3), strow(0, 3), -1.0))
            q.c("vector", lambda v: v.tensor_copy(r2p[0:1, :], strow(0, 3)))
            q.c("tensor", lambda t: t.matmul(
                psBig[:, 0:TC], onesr8[0:1, :], strow(0, 2), start=True, stop=True))
            q.c("vector", lambda v: v.tensor_tensor(
                xa3, xm3, psBig[:, 0:TC].unsqueeze(1).broadcast_to([128, NCH, TC]),
                ALU.mult))

            for mc in range(cfg.FICH):
                pd = psA if mc % 2 == 0 else psB
                for kc in range(NCH):
                    q.c("tensor", lambda t, mc=mc, kc=kc, pd=pd: t.matmul(
                        pd[:, 0:TC],
                        wt1[:, kc * cfg.FI + mc * 128:kc * cfg.FI + (mc + 1) * 128],
                        xa3[:, kc, :], start=(kc == 0), stop=False))
                q.c("tensor", lambda t, mc=mc, pd=pd: t.matmul(
                    pd[:, 0:TC], r1w1[:, mc * 128:(mc + 1) * 128], r2p[:, :],
                    start=False, stop=True))
                q.c("scalar", lambda s, mc=mc, pd=pd: s.activation(
                    t1v3[:, mc, :], pd[:, 0:TC], AF.Gelu))
            for mc in range(NCH):
                pd = psA if mc % 2 == 0 else psB
                for kc in range(cfg.FICH):
                    q.c("tensor", lambda t, mc=mc, kc=kc, pd=pd: t.matmul(
                        pd[:, 0:TC],
                        wt2[:, kc * 1024 + mc * 128:kc * 1024 + (mc + 1) * 128],
                        t1v3[:, kc, :], start=(kc == 0), stop=False))
                q.c("tensor", lambda t, mc=mc, pd=pd: t.matmul(
                    pd[:, 0:TC], w2b[0:1, mc * 128:(mc + 1) * 128], oneTC[0:1, :],
                    start=False, stop=True))
                q.c("vector", lambda v, mc=mc, pd=pd: v.tensor_add(
                    xm3[:, mc, :], xm3[:, mc, :], pd[:, 0:TC]))

            for j in range(TPT):
                tt = t0 + j * 128
                tcol = j * 128
                for kc in range(NCH):
                    q.c("tensor", lambda t, kc=kc, tcol=tcol: t.matmul(
                        psA[:, 0:E], xm3[:, kc, tcol:tcol + 128],
                        wgw[:, kc * E:(kc + 1) * E],
                        start=(kc == 0), stop=(kc == NCH - 1)))
                q.c("vector", lambda v, j=j: v.tensor_copy(
                    agw[:, j * 16 + 8:j * 16 + 16], psA[:, 0:E]))
                for fc in range(NCH):
                    q.c("tensor", lambda t, fc=fc, tcol=tcol: t.transpose(
                        psBig[:, fc * 128:(fc + 1) * 128],
                        xm3[:, fc, tcol:tcol + 128], idf[:, :]))
                q.c("vector", lambda v: v.tensor_copy(aot[:, 0:1024], psBig[:, :]))
                q.c("vector", lambda v: v.reduce_sum(str2[:, 0:1], aot[:, 0:1024], AX.X))
                q.c("scalar", lambda s: s.activation(
                    sc8[:, 1024:2048], aot[:, 0:1024], AF.Square,
                    accum_out=str2[:, 1:2]))
                q.c("vector", lambda v: v.tensor_scalar_mul(str2[:, 0:1], str2[:, 0:1], 1.0 / D))
                q.c("vector", lambda v: v.tensor_scalar_mul(str2[:, 1:2], str2[:, 1:2], 1.0 / D))
                q.c("vector", lambda v: v.tensor_mul(str2[:, 2:3], str2[:, 0:1], str2[:, 0:1]))
                q.c("vector", lambda v: v.tensor_sub(str2[:, 1:2], str2[:, 1:2], str2[:, 2:3]))
                q.c("vector", lambda v: v.tensor_scalar_add(str2[:, 1:2], str2[:, 1:2], EPS))
                q.c("scalar", lambda s: s.activation(str2[:, 2:3], str2[:, 1:2], AF.Sqrt))
                q.c("vector", lambda v: v.reciprocal(str2[:, 3:4], str2[:, 2:3]))
                q.c("vector", lambda v: v.tensor_mul(str2[:, 4:5], str2[:, 3:4], str2[:, 0:1]))
                q.c("vector", lambda v: v.tensor_scalar_mul(str2[:, 4:5], str2[:, 4:5], -1.0))
                q.dma("sync", lambda s, tt=tt: s.dma_start(
                    out=xt1[:, :], in_=x_tm.ap()[tt:tt + 128, :]))
                q.c("vector", lambda v: v.tensor_scalar(
                    aot[:, 0:1024], aot[:, 0:1024], str2[:, 3:4], str2[:, 4:5],
                    ALU.mult, ALU.add))
                q.c("vector", lambda v: v.tensor_add(xt1[:, :], xt1[:, :], aot[:, 0:1024]))
                q.dma("sync", lambda s, tt=tt: s.dma_start(
                    out=x1buf.ap()[tt:tt + 128, :], in_=xt1[:, :]))
                q.c("vector", lambda v: v.reduce_sum(str2[:, 5:6], xt1[:, :], AX.X))
                q.c("scalar", lambda s: s.activation(
                    sc8[:, 1024:2048], xt1[:, :], AF.Square,
                    accum_out=str2[:, 6:7]))
                q.c("vector", lambda v: v.tensor_scalar_mul(str2[:, 5:6], str2[:, 5:6], 1.0 / D))
                q.c("vector", lambda v: v.tensor_scalar_mul(str2[:, 6:7], str2[:, 6:7], 1.0 / D))
                q.c("vector", lambda v: v.tensor_mul(str2[:, 7:8], str2[:, 5:6], str2[:, 5:6]))
                q.c("vector", lambda v: v.tensor_sub(str2[:, 6:7], str2[:, 6:7], str2[:, 7:8]))
                q.c("vector", lambda v: v.tensor_scalar_add(str2[:, 6:7], str2[:, 6:7], EPS))
                q.c("scalar", lambda s: s.activation(str2[:, 7:8], str2[:, 6:7], AF.Sqrt))
                q.c("vector", lambda v: v.reciprocal(str2[:, 8:9], str2[:, 7:8]))
                q.c("vector", lambda v: v.tensor_mul(str2[:, 9:10], str2[:, 8:9], str2[:, 5:6]))
                q.c("vector", lambda v: v.tensor_scalar_mul(str2[:, 9:10], str2[:, 9:10], -1.0))
                q.c("vector", lambda v: v.tensor_scalar(
                    l2t[:, :], xt1[:, :], str2[:, 8:9], str2[:, 9:10],
                    ALU.mult, ALU.add))
                q.dma("sync", lambda s, tt=tt: s.dma_start(
                    out=ln2buf.ap()[tt:tt + 128, :], in_=l2t[:, :]))
                q.c("vector", lambda v, j=j: v.tensor_scalar(
                    bmat[:, 0:E], agw[:, j * 16 + 8:j * 16 + 16],
                    str2[:, 3:4], None, ALU.mult))
                q.c("vector", lambda v, j=j: v.tensor_add(
                    bmat[:, 0:E], bmat[:, 0:E], agw[:, j * 16:j * 16 + 8]))
                q.c("vector", lambda v: v.scalar_tensor_tensor(
                    bmat[:, 8:8 + E], csr[:, :], str2[:, 4:5], bmat[:, 0:E],
                    ALU.mult, ALU.add))
                q.c("vector", lambda v: v.tensor_scalar(
                    bmat[:, 8:8 + E], bmat[:, 8:8 + E], str2[:, 8:9], None, ALU.mult))
                q.c("vector", lambda v: v.scalar_tensor_tensor(
                    bmat[:, 0:E], csr[:, :], str2[:, 9:10], bmat[:, 8:8 + E],
                    ALU.mult, ALU.add))
                q.c("vector", lambda v: v.tensor_add(bmat[:, 0:E], bmat[:, 0:E], gbr[:, :]))
                q.c("tensor", lambda t: t.transpose(psA[0:8, 0:128], bmat[:, 0:E], idf[:, :]))
                q.c("vector", lambda v: v.tensor_copy(lgt[:, :], psA[0:8, 0:128]))
                q.dma("sync", lambda s, tt=tt: s.dma_start(
                    out=logitsT.ap()[:, tt:tt + 128], in_=lgt[:, :]))

    if milestones is not None:
        milestones['phaseA_end'] = len(q.steps)
    free_from("xf")

    slotc = sb("slotc", [128, NTILE], I32)
    gpc = sb("gpc", [128, NTILE], F32)
    lgs = sb("lgs", [128, E * PERP], F32)
    rt1 = sb("rt1", [128, E * PERP], F32)
    rt2 = sb("rt2", [128, E * PERP], F32)
    rt3 = sb("rt3", [128, PERP], F32)
    rt4 = sb("rt4", [128, PERP], F32)
    rt5 = sb("rt5", [128, PERP], F32)
    emx = sb("emx", [128, PERP], F32)
    gpv = sb("gpv", [128, PERP], F32)
    tot8 = sb("tot8", [128, E], F32)
    off8 = sb("off8", [128, E], F32)
    sltr = sb("sltr", [128, 128], F32)
    idx32 = sb("idx32", [128, 1], I32)
    idxf = sb("idxf", [128, 1], F32)
    cid = sb("cid", [128, 1], F32)
    l2s = sb("l2s", [128, D], BF16)

    lgs3 = lgs[:, :].rearrange("p (e t) -> p e t", e=E)
    rt13 = rt1[:, :].rearrange("p (e t) -> p e t", e=E)
    rt23 = rt2[:, :].rearrange("p (e t) -> p e t", e=E)

    q.dma("sync", lambda s: s.dma_start(out=sltr[:, :], in_=sltri.ap()))
    if milestones is not None:
        milestones['pre_ag1'] = len(q.steps)
    q.c("gpsimd", lambda g: g.collective_compute(
        "AllGather", ALU.bypass, replica_groups=[list(range(cfg.NCORES))],
        ins=[logitsT.ap().opt()], outs=[logits_all.ap().opt()]))
    if milestones is not None:
        milestones['post_ag1'] = len(q.steps)
    for c in range(cfg.NCORES):
        q.dma("sync", lambda s, c=c: s.dma_start(
            out=lgs3[c * 16:(c + 1) * 16, :, :],
            in_=logits_all.ap()[c * E:(c + 1) * E, :].rearrange(
                "e (pp t) -> pp e t", pp=16)))
    q.c("vector", lambda v: v.tensor_copy(emx[:, :], lgs3[:, 0, :]))
    for e in range(1, E):
        q.c("vector", lambda v, e=e: v.tensor_max(emx[:, :], emx[:, :], lgs3[:, e, :]))
    q.c("vector", lambda v: v.tensor_tensor(
        rt13, lgs3, emx[:, :].unsqueeze(1).broadcast_to([128, E, PERP]), ALU.subtract))
    q.c("scalar", lambda s: s.activation(rt2[:, :], rt1[:, :], AF.Exp))
    q.c("vector", lambda v: v.tensor_copy(gpv[:, :], rt23[:, 0, :]))
    for e in range(1, E):
        q.c("vector", lambda v, e=e: v.tensor_add(gpv[:, :], gpv[:, :], rt23[:, e, :]))
    q.c("vector", lambda v: v.reciprocal(gpv[:, :], gpv[:, :]))
    q.c("vector", lambda v: v.tensor_tensor(
        rt13, lgs3, emx[:, :].unsqueeze(1).broadcast_to([128, E, PERP]), ALU.is_ge))
    q.c("vector", lambda v: v.memset(rt3[:, :], 1e6))
    for e in range(E):
        q.c("vector", lambda v, e=e: v.tensor_scalar(
            rt4[:, :], rt13[:, e, :], -1e6, 1e6 + e, ALU.mult, ALU.add))
        q.c("vector", lambda v: v.tensor_tensor(rt3[:, :], rt3[:, :], rt4[:, :], ALU.min))
    q.c("vector", lambda v: v.tensor_copy(emx[:, :], rt3[:, :]))
    for e in range(E):
        q.c("vector", lambda v, e=e: v.tensor_single_scalar(
            rt13[:, e, :], emx[:, :], float(e), ALU.is_equal))
        q.c("vector", lambda v, e=e: v.tensor_tensor_scan(
            rt23[:, e, :], rt13[:, e, :], zrow[:, 0:PERP], 0.0, ALU.add, ALU.add))
    q.c("vector", lambda v: v.tensor_copy(tot8[:, :], rt23[:, :, PERP - 1]))
    q.c("tensor", lambda t: t.matmul(
        psA[0:8, 0:128], tot8[:, :], sltr[:, :], start=True, stop=True))
    q.c("vector", lambda v: v.tensor_copy(sltr[0:8, 0:128], psA[0:8, 0:128]))
    q.c("tensor", lambda t: t.transpose(psA[:, 0:8], sltr[0:8, 0:128], idf[0:8, 0:8]))
    q.c("vector", lambda v: v.tensor_copy(off8[:, :], psA[:, 0:8]))
    q.c("vector", lambda v: v.memset(rt3[:, :], 0.0))
    q.c("vector", lambda v: v.memset(rt4[:, :], 0.0))
    for e in range(E):
        q.c("vector", lambda v, e=e: v.scalar_tensor_tensor(
            rt3[:, :], rt13[:, e, :], off8[:, e:e + 1], rt3[:, :], ALU.mult, ALU.add))
        q.c("vector", lambda v, e=e: v.tensor_mul(rt5[:, :], rt13[:, e, :], rt23[:, e, :]))
        q.c("vector", lambda v: v.tensor_add(rt4[:, :], rt4[:, :], rt5[:, :]))
    q.c("vector", lambda v: v.tensor_add(rt3[:, :], rt3[:, :], rt4[:, :]))
    q.c("vector", lambda v: v.tensor_scalar_add(rt3[:, :], rt3[:, :], -1.0))
    q.c("vector", lambda v: v.tensor_single_scalar(
        rt4[:, :], rt3[:, :], float(cfg.CAP - 1), ALU.is_le))
    q.c("vector", lambda v: v.tensor_scalar(
        rt5[:, :], emx[:, :], float(cfg.CAP), None, ALU.mult))
    q.c("vector", lambda v: v.tensor_add(rt5[:, :], rt5[:, :], rt3[:, :]))
    q.c("vector", lambda v: v.tensor_mul(rt5[:, :], rt5[:, :], rt4[:, :]))
    q.c("vector", lambda v: v.tensor_scalar(
        rt4[:, :], rt4[:, :], -float(cfg.NSLOT), float(cfg.NSLOT), ALU.mult, ALU.add))
    q.c("vector", lambda v: v.tensor_add(rt5[:, :], rt5[:, :], rt4[:, :]))

    q.dma("sync", lambda s: s.dma_start(
        out=slotall.ap().rearrange("a w -> (a w)").rearrange("(p t) -> p t", p=128),
        in_=rt5[:, :]))
    q.dma("sync", lambda s: s.dma_start(
        out=gpall.ap().rearrange("a w -> (a w)").rearrange("(p t) -> p t", p=128),
        in_=gpv[:, :]))
    q.dma("sync", lambda s: s.dma_start(
        out=cid[:, :], in_=coreid.ap().broadcast_to([128, 1])))
    q.c("gpsimd", lambda g: g.iota(idxf[:, :], pattern=[[0, 1]], base=0,
                                   channel_multiplier=1,
                                   allow_small_or_imprecise_dtypes=True))
    q.c("vector", lambda v: v.tensor_single_scalar(cid[:, :], cid[:, :], float(NTILE), ALU.mult))
    q.c("vector", lambda v: v.tensor_add(idxf[:, :], idxf[:, :], cid[:, :]))
    q.c("vector", lambda v: v.tensor_copy(idx32[:, :], idxf[:, :]))
    q.dma("gpsimd", lambda g: g.indirect_dma_start(
        sltr[0:NTILE, :], None, slotall.ap(),
        IndirectOffsetOnAxis(idx32[0:NTILE, 0:1], 0)))
    if debug_out:
        sltrdbg = nc.dram_tensor("sltrdbg", [128, 128], F32,
                                 kind="ExternalOutput")
        idxdbg = nc.dram_tensor("idxdbg", [128, 2], F32, kind="ExternalOutput")
        q.dma("sync", lambda s: s.dma_start(out=sltrdbg.ap(), in_=sltr[:, :]))
        q.c("vector", lambda v: v.tensor_copy(rt3[:, 0:1], idx32[:, :]))
        q.c("vector", lambda v: v.tensor_copy(rt3[:, 1:2], idxf[:, :]))
        q.dma("sync", lambda s: s.dma_start(out=idxdbg.ap(), in_=rt3[:, 0:2]))
    q.c("tensor", lambda t: t.transpose(
        psA[:, 0:NTILE], sltr[0:NTILE, :], idf[0:NTILE, 0:NTILE]))
    q.c("vector", lambda v: v.tensor_copy(slotc[:, :], psA[:, 0:NTILE]))
    q.dma("gpsimd", lambda g: g.indirect_dma_start(
        sltr[0:NTILE, :], None, gpall.ap(),
        IndirectOffsetOnAxis(idx32[0:NTILE, 0:1], 0)))
    q.c("tensor", lambda t: t.transpose(
        psA[:, 0:NTILE], sltr[0:NTILE, :], idf[0:NTILE, 0:NTILE]))
    q.c("vector", lambda v: v.tensor_copy(gpc[:, :], psA[:, 0:NTILE]))

    for k in range(NTILE):
        q.dma("sync", lambda s, k=k: s.dma_start(
            out=l2s[:, :], in_=ln2buf.ap()[k * 128:(k + 1) * 128, :]))
        q.dma("gpsimd", lambda g, k=k: g.indirect_dma_start(
            slotbuf.ap(), IndirectOffsetOnAxis(slotc[:, k:k + 1], 0),
            l2s[:, :], None))

    if milestones is not None:
        milestones['pre_rs'] = len(q.steps)
    q.c("gpsimd", lambda g: g.collective_compute(
        "ReduceScatter", ALU.add, replica_groups=[list(range(cfg.NCORES))],
        ins=[slotbuf.ap()[0:cfg.NSLOT, :].opt()], outs=[rsout.ap().opt()]))
    if milestones is not None:
        milestones['post_rs'] = len(q.steps)

    free_from("lgs")

    onesb = sb("onesb", [1, 1024], BF16)
    q.c("vector", lambda v: v.memset(onesb[:, :], 1.0))
    xfm = sb("xfm", [128, NCH * cfg.CAP], BF16)
    xtile = sb("xtile", [128, D], BF16)
    estg = sb("estg", [128, 1024], F32)
    estg2 = sb("estg2", [128, 1024], F32)
    _esp = [0]
    def _estg():
        _esp[0] ^= 1
        return (estg, "sync") if _esp[0] else (estg2, "scalar")

    hid = sb("hid", [128, cfg.FCH * cfg.CAP], BF16)
    b1r = sb("b1r", [1, cfg.F], BF16)
    eb2r = sb("eb2r", [128, D], F32)
    oev = sb("oev", [128, D], F32)
    xfm3 = xfm[:, :].rearrange("p (c n) -> p c n", c=NCH)
    hid3 = hid[:, :].rearrange("p (c n) -> p c n", c=cfg.FCH)

    _ch = min(1024, cfg.F)
    for j in range(cfg.F // _ch):
        q.dma("sync", lambda s, j=j, _ch=_ch: s.dma_start(
            out=estg[0:1, 0:_ch], in_=eb1.ap()[0:1, j * _ch:(j + 1) * _ch]))
        q.c("vector", lambda v, j=j, _ch=_ch: v.tensor_copy(
            b1r[0:1, j * _ch:(j + 1) * _ch], estg[0:1, 0:_ch]))
    q.dma("sync", lambda s: s.dma_start(out=eb2r[:, :], in_=eb2.ap().broadcast_to([128, D])))

    for k in range(cfg.NXT):
        q.dma("sync", lambda s, k=k: s.dma_start(
            out=xtile[:, :], in_=rsout.ap()[k * 128:(k + 1) * 128, :]))
        for fc in range(NCH):
            q.c("tensor", lambda t, fc=fc: t.transpose(
                psT[:, fc * 128:(fc + 1) * 128], xtile[:, fc * 128:(fc + 1) * 128],
                idb[:, :]))
        q.c("vector", lambda v, k=k: v.tensor_copy(
            xfm3[:, :, k * 128:(k + 1) * 128],
            psT[:, :].rearrange("p (c n) -> p c n", c=NCH)[:, :, 0:128]))

    w1s = sb("w1s", [128, NCH * cfg.F], BF16)
    for kc in range(NCH):
        for j in range(cfg.F // _ch):
            eb_, eq_ = _estg()
            q.dma(eq_, lambda s, kc=kc, j=j, _ch=_ch, eb_=eb_: s.dma_start(
                out=eb_[:, 0:_ch],
                in_=ew1.ap()[kc * 128:(kc + 1) * 128, j * _ch:(j + 1) * _ch]))
            q.c("vector", lambda v, kc=kc, j=j, _ch=_ch, eb_=eb_: v.tensor_copy(
                w1s[:, kc * cfg.F + j * _ch:kc * cfg.F + (j + 1) * _ch],
                eb_[:, 0:_ch]))
    nsplits = [(0, min(512, cfg.CAP))]
    if cfg.CAP > 512:
        nsplits.append((512, cfg.CAP))
    for mc in range(cfg.FCH):
        for si, (n0, n1) in enumerate(nsplits):
            pd = psA if si == 0 else psB
            for kc in range(NCH):
                q.c("tensor", lambda t, mc=mc, kc=kc, n0=n0, n1=n1, pd=pd: t.matmul(
                    pd[:, 0:n1 - n0],
                    w1s[:, kc * cfg.F + mc * 128:kc * cfg.F + (mc + 1) * 128],
                    xfm3[:, kc, n0:n1], start=(kc == 0), stop=False))
            q.c("tensor", lambda t, mc=mc, n0=n0, n1=n1, pd=pd: t.matmul(
                pd[:, 0:n1 - n0], b1r[0:1, mc * 128:(mc + 1) * 128],
                onesb[0:1, n0:n1], start=False, stop=True))
            q.c("scalar", lambda s, mc=mc, n0=n0, n1=n1, pd=pd: s.activation(
                hid3[:, mc, n0:n1], pd[:, 0:n1 - n0], AF.Gelu_apprx_tanh))
    free_from("w1s")

    w2s = sb("w2s", [128, cfg.FCH * D], BF16)
    for kc in range(cfg.FCH):
        eb_, eq_ = _estg()
        q.dma(eq_, lambda s, kc=kc, eb_=eb_: s.dma_start(
            out=eb_[:, :], in_=ew2.ap()[kc * 128:(kc + 1) * 128, :]))
        q.c("vector", lambda v, kc=kc, eb_=eb_: v.tensor_copy(
            w2s[:, kc * D:(kc + 1) * D], eb_[:, :]))
    for k in range(cfg.NXT):
        for n0, n1 in ((0, 512), (512, 1024)):
            for kc in range(cfg.FCH):
                q.c("tensor", lambda t, k=k, kc=kc, n0=n0, n1=n1: t.matmul(
                    psO[:, n0:n1], hid3[:, kc, k * 128:(k + 1) * 128],
                    w2s[:, kc * D + n0:kc * D + n1],
                    start=(kc == 0), stop=(kc == cfg.FCH - 1)))
        q.c("vector", lambda v: v.tensor_add(oev[:, :], psO[:, :], eb2r[:, :]))
        q.dma("sync", lambda s, k=k: s.dma_start(
            out=ag2in.ap()[k * 128:(k + 1) * 128, :], in_=oev[:, :]))

    if milestones is not None:
        milestones['pre_ag2'] = len(q.steps)
    q.c("gpsimd", lambda g: g.collective_compute(
        "AllGather", ALU.bypass, replica_groups=[list(range(cfg.NCORES))],
        ins=[ag2in.ap().opt()], outs=[outbuf.ap()[0:cfg.NSLOT, :].opt()]))
    if milestones is not None:
        milestones['post_ag2'] = len(q.steps)

    gat = sb("gat", [128, D], F32)
    xci = sb("xci", [128, D], F32)
    dm2 = sb("dm2", [128, D], F32)
    stc = sb("stc", [128, 6], F32)
    for k in range(NTILE):
        q.dma("gpsimd", lambda g, k=k: g.indirect_dma_start(
            gat[:, :], None, outbuf.ap(), IndirectOffsetOnAxis(slotc[:, k:k + 1], 0)))
        q.c("vector", lambda v, k=k: v.tensor_scalar_mul(
            gat[:, :], gat[:, :], gpc[:, k:k + 1]))
        q.c("vector", lambda v: v.reduce_sum(stc[:, 0:1], gat[:, :], AX.X))
        q.c("scalar", lambda s: s.activation(
            dm2[:, :], gat[:, :], AF.Square, accum_out=stc[:, 1:2]))
        q.c("vector", lambda v: v.tensor_scalar_mul(stc[:, 0:1], stc[:, 0:1], 1.0 / D))
        q.c("vector", lambda v: v.tensor_scalar_mul(stc[:, 1:2], stc[:, 1:2], 1.0 / D))
        q.c("vector", lambda v: v.tensor_mul(stc[:, 2:3], stc[:, 0:1], stc[:, 0:1]))
        q.c("vector", lambda v: v.tensor_sub(stc[:, 1:2], stc[:, 1:2], stc[:, 2:3]))
        q.c("vector", lambda v: v.tensor_scalar_add(stc[:, 1:2], stc[:, 1:2], EPS))
        q.c("scalar", lambda s: s.activation(stc[:, 2:3], stc[:, 1:2], AF.Sqrt))
        q.c("vector", lambda v: v.reciprocal(stc[:, 3:4], stc[:, 2:3]))
        q.c("vector", lambda v: v.tensor_mul(stc[:, 4:5], stc[:, 3:4], stc[:, 0:1]))
        q.c("vector", lambda v: v.tensor_scalar_mul(stc[:, 4:5], stc[:, 4:5], -1.0))
        q.dma("sync", lambda s, k=k: s.dma_start(
            out=xci[:, :], in_=x1buf.ap()[k * 128:(k + 1) * 128, :]))
        q.c("vector", lambda v: v.tensor_scalar(
            gat[:, :], gat[:, :], stc[:, 3:4], stc[:, 4:5], ALU.mult, ALU.add))
        q.c("vector", lambda v: v.tensor_add(gat[:, :], gat[:, :], xci[:, :]))
        q.dma("sync", lambda s, k=k: s.dma_start(
            out=out.ap()[k * 128:(k + 1) * 128, :], in_=gat[:, :]))

    if milestones is not None:
        milestones['all'] = len(q.steps)
    if max_steps is not None:
        q.steps = q.steps[:max_steps]
    if phasec_probe is not None:
        pdbg = nc.dram_tensor("pdbg", [128, D], F32, kind="ExternalOutput")
        pdbg2 = nc.dram_tensor("pdbg2", [128, cfg.NTILE + 1], F32,
                               kind="ExternalOutput")
        if phasec_probe == "iota":
            pidx = sb("pidx", [128, 2], I32)
            pidxf = pidx[:, 1:2].bitcast(F32)
            q.c("gpsimd", lambda g: g.iota(
                pidxf, pattern=[[0, 1]], base=0, channel_multiplier=1,
                allow_small_or_imprecise_dtypes=True))
            q.c("vector", lambda v: v.tensor_copy(pidx[:, 0:1], pidxf))
            q.dma("gpsimd", lambda g: g.indirect_dma_start(
                gat[:, :], None, outbuf.ap(),
                IndirectOffsetOnAxis(pidx[:, 0:1], 0)))
        else:
            q.dma("gpsimd", lambda g: g.indirect_dma_start(
                gat[:, :], None, outbuf.ap(),
                IndirectOffsetOnAxis(slotc[:, 0:1], 0)))
        q.dma("sync", lambda s: s.dma_start(out=pdbg.ap(), in_=gat[:, :]))
        q.c("vector", lambda v: v.tensor_copy(xci[:, 0:cfg.NTILE],
                                              slotc[:, :]))
        q.dma("sync", lambda s: s.dma_start(
            out=pdbg2.ap()[:, 0:cfg.NTILE], in_=xci[:, 0:cfg.NTILE]))

    if dump_src is not None:
        _dct = dict(logitsT=logitsT, logits_all=logits_all, slotall=slotall,
                    gpall=gpall, slotbuf=slotbuf, rsout=rsout, ag2in=ag2in,
                    outbuf=outbuf, x1buf=x1buf, ln2buf=ln2buf)
        for _nm, _shape, _dt in zip(dump_src, dump_shape, dump_dt):
            _src = _dct[_nm]
            dbgout = nc.dram_tensor("dbg_" + _nm, list(_shape), _dt,
                                    kind="ExternalOutput")
            for r0 in range(0, _shape[0], 512):
                r1 = min(_shape[0], r0 + 512)
                q.dma("sync", lambda s, r0=r0, r1=r1, dbgout=dbgout, _src=_src:
                      s.dma_start(out=dbgout.ap()[r0:r1, :],
                                  in_=_src.ap()[r0:r1, :]))
    if emit_mode == "probe":
        with nc.Block() as blk:
            probe_out["info"] = q.emit_probe(nc, blk)
            probe_out["seq"] = q
    elif emit_mode == "sched":
        import contextlib
        with contextlib.ExitStack() as st:
            sems = {}
            for d in schedule["dims"]:
                sems[d] = st.enter_context(nc.semaphore("sem_" + d))
            blk = st.enter_context(nc.Block())
            q.emit(nc, blk, sems, schedule=schedule)
    else:
        with nc.semaphore("s_y") as s_y, nc.semaphore("s_v") as s_v, \
             nc.semaphore("s_s") as s_s, \
             nc.semaphore("s_t") as s_t, nc.semaphore("s_g") as s_g, \
             nc.semaphore("s_yd") as s_yd, nc.semaphore("s_sd") as s_sd, \
             nc.semaphore("s_gd") as s_gd, nc.Block() as blk:
            q.emit(nc, blk, {"sync": s_y, "vector": s_v, "scalar": s_s,
                             "tensor": s_t,
                             "gpsimd": s_g, "sync_d": s_yd, "scalar_d": s_sd,
                             "gpsimd_d": s_gd})

    for g, _ in reversed(allocs):
        g.__exit__(None, None, None)
    return nc


def _consts(cfg):
    import ml_dtypes
    eye = np.eye(128, dtype=np.float32)
    g = cfg.GC
    triu = np.triu(np.ones((g, g), np.float32))
    cm = np.zeros((128, 64), np.float32)
    cm[0:g, 0:g] = triu
    cm[g:2 * g, 0:g] = triu
    return {
        "identf": eye,
        "identb": eye.astype(ml_dtypes.bfloat16),
        "cmask": cm.astype(ml_dtypes.bfloat16),
        "sltri": np.triu(np.ones((128, 128), np.float32), k=1),
    }


def make_in_maps(cfg, inputs):
    x = np.asarray(inputs["L0_attn_x"], np.float32)
    xr = np.ascontiguousarray(x.reshape(cfg.NCORES, cfg.NTOK, cfg.D))

    def f32(name):
        return np.ascontiguousarray(np.asarray(inputs[name], np.float32))

    common = {
        "Wq": f32("L0_attn_Wq"), "Wk": f32("L0_attn_Wk"),
        "Wv": f32("L0_attn_Wv"), "Wg": f32("L0_attn_Wg"),
        "Wo": f32("L0_attn_Wo"), "W1": f32("L0_attn_W1"), "W2": f32("L0_attn_W2"),
        "gkw1": f32("L0_attn_gk_w1"), "gkw2": f32("L0_attn_gk_w2"),
        "ln_w": f32("L0_attn_ln_w").reshape(1, -1),
        "ln_b": f32("L0_attn_ln_b").reshape(1, -1),
        "gk_b2": f32("L0_attn_gk_b2").reshape(1, -1),
        "b1i": f32("L0_attn_b1").reshape(1, -1),
        "b2i": f32("L0_attn_b2").reshape(1, -1),
        "rms_w": f32("L0_attn_rms_w").reshape(1, -1),
        "gw": f32("L0_ffn_gate_w"), "gb": f32("L0_ffn_gate_b").reshape(1, -1),
    }
    common.update(_consts(cfg))
    ew1 = f32("L0_ffn_w1"); eb1 = f32("L0_ffn_b1")
    ew2 = f32("L0_ffn_w2"); eb2 = f32("L0_ffn_b2")

    in_maps = []
    for c in range(cfg.NCORES):
        m = dict(common)
        m["x_tm"] = xr[c]
        m["xT"] = np.ascontiguousarray(xr[c].T)
        m["ew1"] = ew1[c]; m["eb1"] = eb1[c].reshape(1, -1)
        m["ew2"] = ew2[c]; m["eb2"] = eb2[c].reshape(1, -1)
        m["coreid"] = np.array([[c]], np.float32)
        in_maps.append(m)
    return in_maps


def build_scheduled(cfg, **kw):
    po = {}
    build_program(cfg, emit_mode="probe", probe_out=po, **kw)
    sched = po["seq"].compute_schedule(po["info"])
    return build_program(cfg, emit_mode="sched", schedule=sched, **kw)


def kernel(**inputs):
    cfg = Cfg()
    x = np.asarray(inputs["L0_attn_x"], np.float32)
    NB = x.shape[0]
    nc = build_scheduled(cfg)
    in_maps = make_in_maps(cfg, inputs)
    res = run_bass_kernel_spmd(nc, in_maps, core_ids=list(range(cfg.NCORES)))
    outs = np.stack([res.results[c]["out"] for c in range(cfg.NCORES)])
    return outs.reshape(NB, cfg.T, cfg.D)


# version 11
# speedup vs baseline: 1.0018x; 1.0018x over previous
import numpy as np
import concourse.bass as bass
import concourse.mybir as mybir
from concourse.bass_utils import run_bass_kernel_spmd
from concourse.bass_core_rust import IndirectOffsetOnAxis

F32 = mybir.dt.float32
BF16 = mybir.dt.bfloat16
I32 = mybir.dt.int32
AF = mybir.ActivationFunctionType
ALU = mybir.AluOpType
AX = mybir.AxisListType


class Cfg:
    D = 1024
    NCH = 8
    H = 16
    DK = 32
    DV = 64
    LR = 16
    FI = 2048
    FICH = 16
    F = 4096
    FCH = 32
    E = 8
    CAP = 640
    GATE_NORM = 16.0
    EPS = 1e-5
    T = 2048
    SEQ = 2
    NCORES = 8
    TC = 256
    GC = 64
    DKS = DK ** -0.5

    NTOK = property(lambda s: s.T * s.SEQ)
    NTILE = property(lambda s: s.NTOK // 128)
    NSLOT = property(lambda s: s.E * s.CAP)
    PERP = property(lambda s: s.NCORES * s.NTOK // 128)
    NXT = property(lambda s: s.CAP // 128)


class Seq:

    ENGS = ("sync", "scalar", "vector", "tensor", "gpsimd")
    DIMS = ("sync", "scalar", "vector", "tensor", "gpsimd",
            "sync_d", "scalar_d", "gpsimd_d")

    def __init__(self):
        self.steps = []

    def c(self, eng, fn):
        self.steps.append((eng, "c", fn))

    def dma(self, eng, fn):
        self.steps.append((eng, "d", fn))

    @staticmethod
    def key(eng, kind):
        return eng + "_d" if kind == "d" else eng

    def emit_probe(self, nc, block):
        import numpy as _np
        per = {e: [] for e in self.ENGS}
        for idx, (eng, kind, fn) in enumerate(self.steps):
            per[eng].append((idx, fn))
        info = [None] * len(self.steps)
        base_cache = {}

        def sbase(name):
            if name not in base_cache:
                base_cache[name] = nc.lookup_mloc(name).addr
            return base_cache[name]

        def ap_range(ap):
            b = getattr(ap, "bass_ap", None)
            if b is None:
                return None
            esz = _np.dtype(mybir.dt.np(b.dtype)).itemsize
            pairs = [tuple(p) for p in b.ap]
            sp = str(b.space)
            if "DRAM" in sp:
                if ap.dynamic_ap_info is not None:
                    return (("D", b.name), 0, 1 << 44)
                lo = hi = b.offset
                for st, ct in pairs:
                    d = (ct - 1) * st
                    if d < 0:
                        lo += d
                    else:
                        hi += d
                return (("D", b.name), lo * esz, (hi + 1) * esz)
            dims = pairs[1:] if len(pairs) > 1 else []
            pstride = pairs[0][0] if pairs else 1
            off = b.offset
            if pstride > 0:
                off = off % pstride
            lo = hi = off
            for st, ct in dims:
                d = (ct - 1) * st
                if d < 0:
                    lo += d
                else:
                    hi += d
            if "PSUM" in sp:
                return (("P", b.name), 0, 1 << 30)
            base = sbase(b.name)
            return (("S",), base + lo * esz, base + (hi + 1) * esz)

        def mk(ename):
            def body(eng):
                for idx, fn in per[ename]:
                    inst = fn(eng)
                    mi = inst.ins
                    rd, wr = [], []
                    for ap in (mi.ins or []):
                        r = ap_range(ap)
                        if r is not None:
                            rd.append(r)
                    for ap in (mi.outs or []):
                        r = ap_range(ap)
                        if r is not None:
                            wr.append(r)
                    info[idx] = (rd, wr)
            return body

        for e in self.ENGS:
            if per[e] or e == "sync":
                getattr(block, e)(mk(e))
        return info

    NPOOL = 24

    def compute_schedule(self, info):
        n = len(self.steps)
        NG = 8
        NSW = 8
        dims = (list(self.ENGS) + ["dp%d" % i for i in range(self.NPOOL)]
                + ["gq%d" % i for i in range(NG)]
                + ["sw%d" % i for i in range(NSW)])
        dimi = {d: i for i, d in enumerate(dims)}
        counts = {d: 0 for d in dims}
        comp = [None] * n
        clocks = [None] * n
        know = {e: [0] * len(dims) for e in self.ENGS}
        waits = [None] * n
        incs = [None] * n
        writes_map = {}
        reads_map = {}
        pool_next = 0
        gq_next = 0
        sw_next = 0
        prev_user = [None] * self.NPOOL
        prev_sw = [None] * NSW

        def overlap(a0, a1, b0, b1):
            return a0 < b1 and b0 < a1

        for j, (eng, kind, fn) in enumerate(self.steps):
            deps = set()
            if kind == "d" and eng == "gpsimd":
                p = sw_next
                sw_next = (sw_next + 1) % NSW
                dk = "sw%d" % p
                counts[dk] += 16
                comp[j] = (dimi[dk], counts[dk])
                if prev_sw[p] is not None:
                    deps.add(prev_sw[p])
                prev_sw[p] = j
                incs[j] = (dk, 16)
            elif kind == "d":
                p = pool_next
                pool_next = (pool_next + 1) % self.NPOOL
                dk = "dp%d" % p
                counts[dk] += 16
                comp[j] = (dimi[dk], counts[dk])
                if prev_user[p] is not None:
                    deps.add(prev_user[p])
                prev_user[p] = j
                incs[j] = (dk, 16)
            elif eng == "gpsimd":
                dk = "gq%d" % (gq_next % NG)
                gq_next += 1
                counts[dk] += 1
                comp[j] = (dimi[dk], counts[dk])
                incs[j] = (dk, 1)
            else:
                dk = eng
                counts[dk] += 1
                comp[j] = (dimi[dk], counts[dk])
                incs[j] = (dk, 1)
            rd, wr = info[j]
            for key, lo, hi in rd:
                wl = writes_map.get(key)
                if not wl:
                    continue
                uncov = [(lo, hi)]
                for wlo, whi, wi in reversed(wl):
                    nxt = []
                    hit = False
                    for ulo, uhi in uncov:
                        if overlap(ulo, uhi, wlo, whi):
                            hit = True
                            if ulo < wlo:
                                nxt.append((ulo, wlo))
                            if whi < uhi:
                                nxt.append((whi, uhi))
                        else:
                            nxt.append((ulo, uhi))
                    if hit:
                        deps.add(wi)
                    uncov = nxt
                    if not uncov:
                        break
            for key, lo, hi in wr:
                wl = writes_map.get(key)
                if wl:
                    uncov = [(lo, hi)]
                    for wlo, whi, wi in reversed(wl):
                        nxt = []
                        hit = False
                        for ulo, uhi in uncov:
                            if overlap(ulo, uhi, wlo, whi):
                                hit = True
                                if ulo < wlo:
                                    nxt.append((ulo, wlo))
                                if whi < uhi:
                                    nxt.append((whi, uhi))
                            else:
                                nxt.append((ulo, uhi))
                        if hit:
                            deps.add(wi)
                        uncov = nxt
                        if not uncov:
                            break
                rl = reads_map.get(key)
                if rl:
                    keep = []
                    for rlo, rhi, ri in rl:
                        if overlap(lo, hi, rlo, rhi):
                            deps.add(ri)
                            if rlo < lo or rhi > hi:
                                keep.append((rlo, rhi, ri))
                        else:
                            keep.append((rlo, rhi, ri))
                    reads_map[key] = keep
            kn = know[eng]
            wl_out = []
            for i in sorted(deps):
                ddi, dval = comp[i]
                if kn[ddi] < dval:
                    wl_out.append((dims[ddi], dval))
                ci = clocks[i]
                for z in range(len(kn)):
                    if ci[z] > kn[z]:
                        kn[z] = ci[z]
                if kn[ddi] < dval:
                    kn[ddi] = dval
            waits[j] = wl_out
            cj = list(kn)
            di, dv = comp[j]
            if cj[di] < dv:
                cj[di] = dv
            clocks[j] = cj
            for key, lo, hi in wr:
                wl = writes_map.setdefault(key, [])
                wl2 = [w for w in wl if not (lo <= w[0] and w[1] <= hi)]
                wl2.append((lo, hi, j))
                writes_map[key] = wl2
            for key, lo, hi in rd:
                reads_map.setdefault(key, []).append((lo, hi, j))
        return {"waits": waits, "incs": incs, "finals": dict(counts),
                "dims": dims}

    def emit(self, nc, block, sems, schedule=None):
        per = {e: [] for e in self.ENGS}
        tot = {k: 0 for k in sems}
        for idx, (eng, kind, fn) in enumerate(self.steps):
            per[eng].append((idx, dict(tot), kind, fn))
            if schedule is None:
                tot[self.key(eng, kind)] += 16 if kind == "d" else 1
        final = dict(tot) if schedule is None else dict(schedule["finals"])
        import os
        if os.environ.get("PRINT_TOTALS"):
            nw = (sum(len(w) for w in schedule["waits"])
                  if schedule else None)
            print("SEQ TOTALS:", final, "steps:", len(self.steps),
                  "waits:", nw)

        def mk(ename):
            def body(eng):
                seen = {k: 0 for k in sems}
                for idx, w, kind, fn in per[ename]:
                    if schedule is None:
                        for k in sems:
                            if w[k] > seen[k]:
                                eng.wait_ge(sems[k], w[k])
                                seen[k] = w[k]
                    else:
                        import os as _os
                        _nsw = _os.environ.get("NO_SELF_WAITS")
                        for k, dval in schedule["waits"][idx]:
                            if _nsw and k == ename:
                                continue
                            if dval > seen[k]:
                                eng.wait_ge(sems[k], dval)
                                seen[k] = dval
                    inst = fn(eng)
                    if schedule is None:
                        kk = self.key(ename, kind)
                        inst.then_inc(sems[kk], 16 if kind == "d" else 1)
                    else:
                        kk, inc = schedule["incs"][idx]
                        inst.then_inc(sems[kk], inc)
                if ename == "sync":
                    for k in sems:
                        if final.get(k, 0) > seen[k]:
                            eng.wait_ge(sems[k], final[k])
            return body

        for e in self.ENGS:
            if per[e] or e == "sync":
                getattr(block, e)(mk(e))


def build_program(cfg, max_steps=None, milestones=None, debug_out=False,
                  dump_src=None, dump_shape=None, dump_dt=None,
                  phasec_probe=None, emit_mode="serial", schedule=None,
                  probe_out=None):
    nc = bass.Bass("TRN2", target_bir_lowering=False,
                   dynamic_dma_scratch_size=8192)
    q = Seq()

    D, NCH, H, DK, DV = cfg.D, cfg.NCH, cfg.H, cfg.DK, cfg.DV
    TC, GC, NTOK = cfg.TC, cfg.GC, cfg.NTOK
    NGC = TC // GC
    TPT = TC // 128
    NTILE = cfg.NTILE
    PERP = cfg.PERP
    EPS = cfg.EPS
    E = cfg.E

    def din(name, shape, dt=F32):
        return nc.dram_tensor(name, shape, dt, kind="ExternalInput")

    xT = din("xT", [D, NTOK])
    x_tm = din("x_tm", [NTOK, D])
    Wq = din("Wq", [D, H * DK]); Wk = din("Wk", [D, H * DK])
    Wv = din("Wv", [D, D]); Wg = din("Wg", [D, D]); Wo = din("Wo", [D, D])
    W1 = din("W1", [D, cfg.FI]); W2 = din("W2", [cfg.FI, D])
    gkw1 = din("gkw1", [D, cfg.LR]); gkw2 = din("gkw2", [cfg.LR, H * DK])
    ln_w = din("ln_w", [1, D]); ln_b = din("ln_b", [1, D])
    gk_b2 = din("gk_b2", [1, H * DK])
    b1i = din("b1i", [1, cfg.FI]); b2i = din("b2i", [1, D])
    rms_w = din("rms_w", [1, DV])
    gw = din("gw", [D, E]); gb = din("gb", [1, E])
    ew1 = din("ew1", [D, cfg.F]); eb1 = din("eb1", [1, cfg.F])
    ew2 = din("ew2", [cfg.F, D]); eb2 = din("eb2", [1, D])
    identf = din("identf", [128, 128]); identb = din("identb", [128, 128], BF16)
    cmask = din("cmask", [128, 64], BF16)
    sltri = din("sltri", [128, 128])
    coreid = din("coreid", [1, 1])

    dbg = dict(kind="ExternalOutput") if debug_out else {}
    x1buf = nc.dram_tensor("x1buf", [NTOK, D], F32, **dbg)
    ln2buf = nc.dram_tensor("ln2buf", [NTOK, D], BF16, **dbg)
    logitsT = nc.dram_tensor("logitsT", [E, NTOK], F32)
    logits_all = nc.dram_tensor("logits_all", [E * cfg.NCORES, NTOK], F32)
    slotall = nc.dram_tensor("slotall", [cfg.NCORES * NTILE, 128], F32)
    gpall = nc.dram_tensor("gpall", [cfg.NCORES * NTILE, 128], F32)
    slotbuf = nc.dram_tensor("slotbuf", [cfg.NSLOT + 1, D], BF16)
    rsout = nc.dram_tensor("rsout", [cfg.CAP, D], BF16)
    ag2in = nc.dram_tensor("ag2in", [cfg.CAP, D], F32)
    outbuf = nc.dram_tensor("outbuf", [cfg.NSLOT + 1, D], F32)
    srow = nc.dram_tensor("srow", [1, 64], F32)
    out = nc.dram_tensor("out", [NTOK, D], F32, kind="ExternalOutput")

    allocs = []

    def sb(name, shape, dt=F32):
        g = nc.sbuf_tensor(name, shape, dt)
        t = g.__enter__()
        allocs.append((g, name))
        return t

    def ps(name, shape, dt=F32):
        g = nc.psum_tensor(name, shape, dt)
        t = g.__enter__()
        allocs.append((g, name))
        return t

    def free_from(name):
        idx = next(i for i, (_, n) in enumerate(allocs) if n == name)
        for g, _ in reversed(allocs[idx:]):
            g.__exit__(None, None, None)
        del allocs[idx:]

    idf = sb("idf", [128, 128], F32)
    idb = sb("idb", [128, 128], BF16)
    cmk = sb("cmk", [128, 64], BF16)
    rmsr = sb("rmsr", [128, DV], F32)
    gbr = sb("gbr", [128, E], F32)
    csr = sb("csr", [128, E], F32)
    onescf = sb("onescf", [128, 1], F32)
    onesr8 = sb("onesr8", [1, 128], F32)
    lnbfm = sb("lnbfm", [128, NCH], F32)
    lnwfm = sb("lnwfm", [128, NCH], F32)
    lnwsfm = sb("lnwsfm", [128, NCH], F32)
    zrow = sb("zrow", [128, max(TC, PERP)], F32)
    zb = zrow[:, 0:256].bitcast(BF16)

    q.dma("sync", lambda s: s.dma_start(out=idf[:, :], in_=identf.ap()))
    q.dma("sync", lambda s: s.dma_start(out=idb[:, :], in_=identb.ap()))
    q.dma("sync", lambda s: s.dma_start(out=cmk[:, :], in_=cmask.ap()))
    q.dma("sync", lambda s: s.dma_start(out=rmsr[:, :], in_=rms_w.ap().broadcast_to([128, DV])))
    q.dma("sync", lambda s: s.dma_start(out=gbr[:, :], in_=gb.ap().broadcast_to([128, E])))
    q.dma("sync", lambda s: s.dma_start(
        out=lnbfm[:, :], in_=ln_b.ap().rearrange("o (c p) -> (o p) c", p=128),
        allow_slow_non_contiguous=True))
    q.dma("sync", lambda s: s.dma_start(
        out=lnwfm[:, :], in_=ln_w.ap().rearrange("o (c p) -> (o p) c", p=128),
        allow_slow_non_contiguous=True))
    q.c("vector", lambda v: v.memset(onescf[:, :], 1.0))
    q.c("vector", lambda v: v.memset(onesr8[:, :], 1.0))
    q.c("vector", lambda v: v.memset(zrow[:, :], 0.0))
    q.c("vector", lambda v: v.tensor_scalar_mul(lnwsfm[:, :], lnwfm[:, :], cfg.DKS))

    for r0 in range(0, cfg.NSLOT + 1, 128):
        rn = min(128, cfg.NSLOT + 1 - r0)
        for c0 in range(0, D, 512):
            q.dma("sync", lambda s, r0=r0, rn=rn, c0=c0: s.dma_start(
                out=slotbuf.ap()[r0:r0 + rn, c0:c0 + 512],
                in_=zrow[0:rn, 0:256].bitcast(BF16)))
    for c0 in range(0, D, 256):
        q.dma("sync", lambda s, c0=c0: s.dma_start(
            out=outbuf.ap()[cfg.NSLOT:cfg.NSLOT + 1, c0:c0 + 256],
            in_=zrow[0:1, 0:256]))

    psA = ps("psA", [128, 512], F32)
    psB = ps("psB", [128, 512], F32)
    psBig = ps("psBig", [128, 1024], F32)
    psO = ps("psO", [128, 1024], F32)
    psT = ps("psT", [128, 1024], BF16)
    psS = ps("psS", [128, 512], F32)

    xf = sb("xf", [128, NCH * TC], F32)
    xm = sb("xm", [128, NCH * TC], F32)
    xa = sb("xa", [128, NCH * TC], BF16)
    sc8 = sb("sc8", [128, NCH * TC], F32)
    qf = sb("qf", [128, 4 * TC], BF16)
    kf = sb("kf", [128, 4 * TC], BF16)
    gka = sb("gka", [128, 4 * TC], F32)
    eq = sb("eq", [128, 4 * TC], BF16)
    ek = sb("ek", [128, 4 * TC], BF16)
    bmat = sb("bmat", [128, 4 * NGC], F32)
    acar = sb("acar", [128, 4], F32)
    Ssb = sb("Ssb", [128, 4 * DV], F32)
    Sbp = sb("Sbp", [128, 4 * DV], BF16)
    Sbm = sb("Sbm", [128, 4 * DV], BF16)
    ach = sb("ach", [128, 4], F32)
    vt = sb("vt", [128, 1024], BF16)
    sg = sb("sg", [128, 1024], F32)
    pa = sb("pa", [128, 8 * 128], BF16)
    ktm = sb("ktm", [128, 512], BF16)
    o2 = sb("o2", [128, 1024], BF16)
    o2f = sb("o2f", [128, NCH * TC], BF16)
    xt1 = sb("xt1", [128, 1024], F32)
    st8 = sb("st8", [1, 4 * TC], F32)
    str2 = sb("str2", [128, 10], F32)
    r1p = sb("r1p", [2, TC], BF16)
    oneTC = sb("oneTC", [1, TC], BF16)
    r2p = sb("r2p", [2, TC], BF16)
    lgt = sb("lgt", [8, 128], F32)
    agw = sb("agw", [128, TPT * 16], F32)

    q.c("vector", lambda v: v.memset(r1p[:, :], 1.0))
    q.c("vector", lambda v: v.memset(oneTC[:, :], 1.0))
    q.c("vector", lambda v: v.memset(r2p[:, :], 1.0))

    wtq = sb("wtq", [128, NCH * 512], BF16)
    wtk = sb("wtk", [128, NCH * 512], BF16)
    wtv = sb("wtv", [128, NCH * 1024], BF16)
    wtg = sb("wtg", [128, NCH * 1024], BF16)
    wto = sb("wto", [128, NCH * 1024], BF16)
    wt1 = sb("wt1", [128, NCH * cfg.FI], BF16)
    wt2 = sb("wt2", [128, cfg.FICH * 1024], BF16)
    wgk1 = sb("wgk1", [128, NCH * cfg.LR], BF16)
    wgk2 = sb("wgk2", [cfg.LR + 1, H * DK], BF16)
    wgw = sb("wgw", [128, NCH * E], F32)
    w2b = sb("w2b", [1, D], BF16)
    r1q = sb("r1q", [2, 512], BF16)
    r1k = sb("r1k", [2, 512], BF16)
    r1v = sb("r1v", [2, 1024], BF16)
    r1g = sb("r1g", [2, 1024], BF16)
    r1gk = sb("r1gk", [2, cfg.LR], BF16)
    r1w1 = sb("r1w1", [2, cfg.FI], BF16)

    wst = sb("wst", [128, 512], F32)
    wst2 = sb("wst2", [128, 512], F32)
    _wstp = [0]
    def _stg():
        _wstp[0] ^= 1
        return (wst, "sync") if _wstp[0] else (wst2, "scalar")

    onescb = sb("onescb", [128, 1], BF16)
    lnb2 = sb("lnb2", [128, 2 * NCH], F32)
    q.c("vector", lambda v: v.memset(onescb[:, :], 1.0))
    q.c("vector", lambda v: v.memset(lnb2[:, :], 0.0))
    q.c("vector", lambda v: v.tensor_copy(
        lnb2[:, :].rearrange("p (c two) -> p c two", two=2)[:, :, 1:2],
        lnbfm[:, :].unsqueeze(2)))

    def _ureg(j):
        pd = psBig if j < 2 else psO
        return pd[0:2, (j % 2) * 512:(j % 2) * 512 + 512]

    def _ureg2(j):
        pd = psBig if j < 2 else psO
        return pd[1:2, (j % 2) * 512:(j % 2) * 512 + 512]

    def prep_scaled(Wd, outF, wt_sb, r1_sb, scale_fm, uscale=None, badd=None):
        nN = (outF + 511) // 512
        for j in range(nN):
            n0j, n1j = j * 512, min(outF, (j + 1) * 512)
            for kc in range(NCH):
                stgb, stgq = _stg()
                q.dma(stgq, lambda s, kc=kc, n0=n0j, n1=n1j, stgb=stgb: s.dma_start(
                    out=stgb[:, 0:n1 - n0],
                    in_=Wd.ap()[kc * 128:(kc + 1) * 128, n0:n1]))
                q.c("tensor", lambda t, kc=kc, n0=n0j, n1=n1j, j=j, stgb=stgb: t.matmul(
                    _ureg(j)[:, 0:n1 - n0],
                    lnb2[:, 2 * kc:2 * kc + 2], stgb[:, 0:n1 - n0],
                    start=(kc == 0), stop=(kc == NCH - 1)))
                q.c("vector", lambda v, kc=kc, n0=n0j, n1=n1j, stgb=stgb: v.tensor_scalar(
                    wt_sb[:, kc * outF + n0:kc * outF + n1], stgb[:, 0:n1 - n0],
                    scale_fm[:, kc:kc + 1], None, ALU.mult))
        for j in range(nN):
            n0, n1 = j * 512, min(outF, (j + 1) * 512)
            if badd is not None:
                q.dma("sync", lambda s, n0=n0, n1=n1: s.dma_start(
                    out=wst[0:2, 0:n1 - n0],
                    in_=badd.ap()[0:1, n0:n1].broadcast_to([2, n1 - n0])))
            if uscale is not None:
                q.c("vector", lambda v, j=j, n0=n0, n1=n1: v.tensor_scalar_mul(
                    r1_sb[0:2, n0:n1], _ureg(j)[:, 0:n1 - n0], uscale))
            elif badd is not None:
                q.c("vector", lambda v, j=j, n0=n0, n1=n1: v.tensor_add(
                    r1_sb[0:2, n0:n1], _ureg(j)[:, 0:n1 - n0], wst[0:2, 0:n1 - n0]))
            else:
                q.c("vector", lambda v, j=j, n0=n0, n1=n1: v.tensor_copy(
                    r1_sb[0:2, n0:n1], _ureg(j)[:, 0:n1 - n0]))
        for j in range(nN):
            n0, n1 = j * 512, min(outF, (j + 1) * 512)
            for kc in range(NCH):
                q.c("tensor", lambda t, kc=kc, n0=n0, n1=n1: t.matmul(
                    psB[0:1, 0:n1 - n0], onescb[:, 0:1],
                    wt_sb[:, kc * outF + n0:kc * outF + n1],
                    start=(kc == 0), stop=(kc == NCH - 1)))
            q.c("vector", lambda v, n0=n0, n1=n1: v.tensor_copy(
                r1_sb[0:1, n0:n1], psB[0:1, 0:n1 - n0]))

    prep_scaled(Wq, 512, wtq, r1q, lnwsfm, uscale=cfg.DKS)
    prep_scaled(Wk, 512, wtk, r1k, lnwfm)
    prep_scaled(Wv, 1024, wtv, r1v, lnwfm)
    prep_scaled(Wg, 1024, wtg, r1g, lnwfm)
    prep_scaled(gkw1, cfg.LR, wgk1, r1gk, lnwfm)
    prep_scaled(W1, cfg.FI, wt1, r1w1, lnwfm, badd=b1i)

    for kc in range(NCH):
        wcb = xf[:, 0:1024] if kc % 2 == 0 else xf[:, 1024:2048]
        wcq = "sync" if kc % 2 == 0 else "scalar"
        q.dma(wcq, lambda s, kc=kc, wcb=wcb: s.dma_start(
            out=wcb, in_=Wo.ap()[kc * 128:(kc + 1) * 128, :]))
        q.c("vector", lambda v, kc=kc, wcb=wcb: v.tensor_copy(
            wto[:, kc * 1024:(kc + 1) * 1024], wcb))
        q.dma("sync", lambda s, kc=kc: s.dma_start(
            out=wgw[:, kc * E:(kc + 1) * E], in_=gw.ap()[kc * 128:(kc + 1) * 128, :]))
    for kc in range(cfg.FICH):
        wcb = xf[:, 0:1024] if kc % 2 == 0 else xf[:, 1024:2048]
        wcq = "sync" if kc % 2 == 0 else "scalar"
        q.dma(wcq, lambda s, kc=kc, wcb=wcb: s.dma_start(
            out=wcb, in_=W2.ap()[kc * 128:(kc + 1) * 128, :]))
        q.c("vector", lambda v, kc=kc, wcb=wcb: v.tensor_copy(
            wt2[:, kc * 1024:(kc + 1) * 1024], wcb))
    q.dma("sync", lambda s: s.dma_start(out=wst[0:cfg.LR, 0:512], in_=gkw2.ap()))
    q.dma("sync", lambda s: s.dma_start(
        out=wst[cfg.LR:cfg.LR + 1, 0:512], in_=gk_b2.ap()))
    q.c("vector", lambda v: v.tensor_copy(
        wgk2[0:cfg.LR + 1, :], wst[0:cfg.LR + 1, 0:512]))
    q.dma("sync", lambda s: s.dma_start(out=xf[0:1, 0:1024], in_=b2i.ap()))
    q.c("vector", lambda v: v.tensor_copy(w2b[:, :], xf[0:1, 0:1024]))

    q.dma("sync", lambda s: s.dma_start(
        out=wst[:, 0:E * NCH].rearrange("p (c e) -> p c e", c=NCH),
        in_=gw.ap().rearrange("(c p) e -> p c e", p=128)))
    for kc in range(NCH):
        q.c("tensor", lambda t, kc=kc: t.matmul(
            psA[0:1, 0:E], onescf[:, 0:1],
            wst[:, 0:E * NCH].rearrange("p (c e) -> p c e", c=NCH)[:, kc, :],
            start=(kc == 0), stop=(kc == NCH - 1)))
    q.c("vector", lambda v: v.tensor_copy(wst[0:1, 0:E], psA[0:1, 0:E]))
    q.dma("sync", lambda s: s.dma_start(out=srow.ap()[0:1, 0:E], in_=wst[0:1, 0:E]))
    q.dma("sync", lambda s: s.dma_start(
        out=csr[:, :], in_=srow.ap()[0:1, 0:E].broadcast_to([128, E])))

    free_from("wst")

    t1v = sc8[:, :].bitcast(BF16)
    gsc = sc8[:, 0:4 * TC]
    assert 4 * TC >= 1024
    aot = gka
    l2t = o2
    xf3 = xf[:, :].rearrange("p (c t) -> p c t", c=NCH)
    xm3 = xm[:, :].rearrange("p (c t) -> p c t", c=NCH)
    xa3 = xa[:, :].rearrange("p (c t) -> p c t", c=NCH)
    qf3 = qf[:, :].rearrange("p (c t) -> p c t", c=4)
    kf3 = kf[:, :].rearrange("p (c t) -> p c t", c=4)
    gka4 = gka[:, :].rearrange("p (c g u) -> p c g u", c=4, g=NGC)
    gsc4 = gsc.rearrange("p (c g u) -> p c g u", c=4, g=NGC)
    eq4 = eq[:, :].rearrange("p (c g u) -> p c g u", c=4, g=NGC)
    bmat3 = bmat[:, :].rearrange("p (c g) -> p c g", c=4)
    S3 = Ssb[:, :].rearrange("p (c v) -> p c v", c=4)
    Sbp3 = Sbp[:, :].rearrange("p (c v) -> p c v", c=4)
    Sbm3 = Sbm[:, :].rearrange("p (c v) -> p c v", c=4)
    o2f3 = o2f[:, :].rearrange("p (c t) -> p c t", c=NCH)
    t1v3 = t1v[:, 0:cfg.FICH * TC].rearrange("p (c t) -> p c t", c=cfg.FICH)

    def strow(r, s=0):
        return st8[r:r + 1, s * TC:(s + 1) * TC]

    chunk_marks = []
    for seq_i in range(cfg.SEQ):
        q.c("vector", lambda v: v.memset(Ssb[:, :], 0.0))
        q.c("vector", lambda v: v.memset(acar[:, :], 0.0))
        for chn in range(cfg.T // TC):
            t0 = seq_i * cfg.T + chn * TC
            _mk = {"cs": len(q.steps)}
            chunk_marks.append(_mk)

            q.dma("sync", lambda s, t0=t0: s.dma_start(
                out=xf3, in_=xT.ap().rearrange("(c p) n -> p c n", p=128)[:, :, t0:t0 + TC]))

            q.c("vector", lambda v: v.tensor_mul(sc8[:, 0:NCH * TC], xf[:, :], xf[:, :]))
            for kc in range(NCH):
                q.c("tensor", lambda t, kc=kc: t.matmul(
                    psS[0:1, 0:TC], onescf[:, 0:1], xf3[:, kc, :],
                    start=(kc == 0), stop=(kc == NCH - 1)))
            for kc in range(NCH):
                q.c("tensor", lambda t, kc=kc: t.matmul(
                    psS[0:1, TC:2 * TC], onescf[:, 0:1],
                    sc8[:, 0:NCH * TC].rearrange("p (c t) -> p c t", c=NCH)[:, kc, :],
                    start=(kc == 0), stop=(kc == NCH - 1)))
            q.c("vector", lambda v: v.tensor_scalar_mul(
                strow(0, 3), psS[0:1, 0:TC], 1.0 / D))
            q.c("vector", lambda v: v.tensor_scalar_mul(
                strow(0, 1), psS[0:1, TC:2 * TC], 1.0 / D))
            q.c("vector", lambda v: v.tensor_mul(strow(0, 0), strow(0, 3), strow(0, 3)))
            q.c("vector", lambda v: v.tensor_sub(strow(0, 1), strow(0, 1), strow(0, 0)))
            q.c("vector", lambda v: v.tensor_scalar_add(strow(0, 0), strow(0, 1), EPS))
            q.c("scalar", lambda s: s.activation(strow(0, 2), strow(0, 0), AF.Sqrt))
            q.c("vector", lambda v: v.reciprocal(strow(0, 2), strow(0, 2)))
            q.c("vector", lambda v: v.reciprocal(strow(0, 0), strow(0, 0)))
            q.c("vector", lambda v: v.tensor_mul(strow(0, 1), strow(0, 1), strow(0, 0)))
            q.c("vector", lambda v: v.tensor_scalar_add(strow(0, 1), strow(0, 1), EPS))
            q.c("scalar", lambda s: s.activation(strow(0, 1), strow(0, 1), AF.Sqrt))
            q.c("vector", lambda v: v.reciprocal(strow(0, 1), strow(0, 1)))
            q.c("vector", lambda v: v.tensor_mul(strow(0, 1), strow(0, 1), strow(0, 2)))
            q.c("vector", lambda v: v.tensor_mul(strow(0, 0), strow(0, 1), strow(0, 3)))
            q.c("vector", lambda v: v.tensor_mul(strow(0, 3), strow(0, 3), strow(0, 2)))
            q.c("vector", lambda v: v.tensor_scalar_mul(strow(0, 0), strow(0, 0), -1.0))
            q.c("vector", lambda v: v.tensor_copy(r1p[0:1, :], strow(0, 0)))

            q.c("tensor", lambda t: t.matmul(
                psBig[:, 0:TC], onesr8[0:1, :], strow(0, 1), start=True, stop=True))
            q.c("vector", lambda v: v.tensor_tensor(
                xa3, xf3, psBig[:, 0:TC].unsqueeze(1).broadcast_to([128, NCH, TC]),
                ALU.mult))
            _mk["se"] = len(q.steps)

            for kc in range(NCH):
                q.c("tensor", lambda t, kc=kc: t.matmul(
                    psA[0:cfg.LR, 0:TC], wgk1[:, kc * cfg.LR:(kc + 1) * cfg.LR],
                    xa3[:, kc, :], start=(kc == 0), stop=False))
            q.c("tensor", lambda t: t.matmul(
                psA[0:cfg.LR, 0:TC], r1gk[:, :], r1p[:, :], start=False, stop=True))
            q.c("vector", lambda v: v.memset(pa[0:cfg.LR + 1, 0:TC], 1.0))
            q.c("vector", lambda v: v.tensor_copy(pa[0:cfg.LR, 0:TC], psA[0:cfg.LR, 0:TC]))
            for mc in range(4):
                q.c("tensor", lambda t, mc=mc: t.matmul(
                    psBig[:, mc * TC:(mc + 1) * TC], wgk2[:, mc * 128:(mc + 1) * 128],
                    pa[0:cfg.LR + 1, 0:TC], start=True, stop=True))
            q.c("scalar", lambda s: s.activation(gsc, psBig[:, 0:4 * TC], AF.Sigmoid))
            q.c("scalar", lambda s: s.activation(gka[:, :], gsc, AF.Ln))
            q.c("vector", lambda v: v.tensor_scalar_mul(gsc, gka[:, :], 1.0 / cfg.GATE_NORM))
            for fc in range(4):
                q.c("vector", lambda v, fc=fc: v.tensor_tensor_scan(
                    gka[:, :].rearrange("p (c t) -> p c t", c=4)[:, fc, :],
                    gsc.rearrange("p (c t) -> p c t", c=4)[:, fc, :],
                    zrow[:, 0:TC], acar[:, fc:fc + 1], ALU.add, ALU.add))
            q.c("vector", lambda v: v.tensor_copy(bmat3[:, :, 0:1], acar[:, :].unsqueeze(2)))
            for g in range(1, NGC):
                q.c("vector", lambda v, g=g: v.tensor_copy(
                    bmat3[:, :, g:g + 1],
                    gka[:, :].rearrange("p (c t) -> p c t", c=4)[:, :, g * GC - 1:g * GC]))
            q.c("vector", lambda v: v.tensor_copy(
                acar[:, :],
                gka[:, :].rearrange("p (c t) -> p c t", c=4)[:, :, TC - 1:TC].squeeze(2)))
            q.c("vector", lambda v: v.tensor_sub(
                gsc4, gka4, bmat3.unsqueeze(3).broadcast_to([128, 4, NGC, GC])))
            q.c("scalar", lambda s: s.activation(eq[:, :], gsc, AF.Exp))
            q.c("scalar", lambda s: s.activation(ek[:, :], gsc, AF.Exp, scale=-1.0))

            for dest, wt_sb, r1_sb, fac in ((qf, wtq, r1q, eq), (kf, wtk, r1k, ek)):
                for mc in range(4):
                    for kc in range(NCH):
                        q.c("tensor", lambda t, mc=mc, kc=kc, wt_sb=wt_sb: t.matmul(
                            psBig[:, mc * TC:(mc + 1) * TC],
                            wt_sb[:, kc * 512 + mc * 128:kc * 512 + (mc + 1) * 128],
                            xa3[:, kc, :], start=(kc == 0), stop=False))
                    q.c("tensor", lambda t, mc=mc, r1_sb=r1_sb: t.matmul(
                        psBig[:, mc * TC:(mc + 1) * TC],
                        r1_sb[:, mc * 128:(mc + 1) * 128], r1p[:, :],
                        start=False, stop=True))
                for mc in range(4):
                    q.c("vector", lambda v, mc=mc, dest=dest, fac=fac: v.tensor_tensor(
                        dest[:, mc * TC:(mc + 1) * TC], psBig[:, mc * TC:(mc + 1) * TC],
                        fac[:, mc * TC:(mc + 1) * TC], ALU.mult))

            for j in range(TPT):
                for kc in range(NCH):
                    q.c("tensor", lambda t, j=j, kc=kc: t.matmul(
                        psA[:, 0:E], xf3[:, kc, j * 128:(j + 1) * 128],
                        wgw[:, kc * E:(kc + 1) * E],
                        start=(kc == 0), stop=(kc == NCH - 1)))
                q.c("vector", lambda v, j=j: v.tensor_copy(
                    agw[:, j * 16:j * 16 + 8], psA[:, 0:E]))

            for j in range(TPT):
                ce, co = 2 * j, 2 * j + 1
                tcol = j * 128

                for half, wt_sb, r1_sb in ((0, wtv, r1v), (1, wtg, r1g)):
                    for nh in range(2):
                        for kc in range(NCH):
                            q.c("tensor", lambda t, kc=kc, nh=nh, tcol=tcol, wt_sb=wt_sb: t.matmul(
                                psO[:, nh * 512:(nh + 1) * 512],
                                xa3[:, kc, tcol:tcol + 128],
                                wt_sb[:, kc * 1024 + nh * 512:kc * 1024 + (nh + 1) * 512],
                                start=(kc == 0), stop=False))
                        q.c("tensor", lambda t, nh=nh, tcol=tcol, r1_sb=r1_sb: t.matmul(
                            psO[:, nh * 512:(nh + 1) * 512],
                            r1p[:, tcol:tcol + 128], r1_sb[:, nh * 512:(nh + 1) * 512],
                            start=False, stop=True))
                    if half == 0:
                        q.c("vector", lambda v: v.tensor_copy(vt[:, :], psO[:, :]))
                    else:
                        q.c("scalar", lambda s: s.activation(sg[:, :], psO[:, :], AF.Silu))

                q.c("vector", lambda v, ce=ce: v.tensor_copy(
                    ach[:, :], eq4[:, :, ce, GC - 1:GC].squeeze(2)))

                for fc in range(4):
                    q.c("tensor", lambda t, fc=fc, tcol=tcol: t.transpose(
                        psT[:, fc * 128:(fc + 1) * 128],
                        kf3[:, fc, tcol:tcol + 128], idb[:, :]))
                q.c("vector", lambda v: v.tensor_copy(ktm[:, 0:512], psT[:, 0:512]))

                q.c("vector", lambda v: v.tensor_copy(Sbp[:, :], Ssb[:, :]))
                for h in range(H):
                    rb = (h % 4) * 32
                    fch = h // 4
                    q.c("tensor", lambda t, h=h, rb=rb, fch=fch: t.matmul(
                        psA[rb:rb + 32, fch * DV:(fch + 1) * DV],
                        ktm[0:64, fch * 128 + rb:fch * 128 + rb + 32],
                        vt[0:64, h * DV:(h + 1) * DV],
                        start=True, stop=True, tile_position=(0, rb)))
                q.c("vector", lambda v: v.tensor_add(Ssb[:, :], Ssb[:, :], psA[:, 0:4 * DV]))
                q.c("vector", lambda v: v.tensor_tensor(
                    S3, S3, ach[:, :].unsqueeze(2).broadcast_to([128, 4, DV]), ALU.mult))
                q.c("vector", lambda v: v.tensor_copy(Sbm[:, :], Ssb[:, :]))
                for h in range(H):
                    rb = (h % 4) * 32
                    fch = h // 4
                    q.c("tensor", lambda t, h=h, rb=rb, fch=fch: t.matmul(
                        psA[rb:rb + 32, fch * DV:(fch + 1) * DV],
                        ktm[64:128, fch * 128 + rb:fch * 128 + rb + 32],
                        vt[64:128, h * DV:(h + 1) * DV],
                        start=True, stop=True, tile_position=(64, rb)))
                q.c("vector", lambda v, co=co: v.tensor_copy(
                    ach[:, :], eq4[:, :, co, GC - 1:GC].squeeze(2)))
                q.c("vector", lambda v: v.tensor_add(Ssb[:, :], Ssb[:, :], psA[:, 0:4 * DV]))
                q.c("vector", lambda v: v.tensor_tensor(
                    S3, S3, ach[:, :].unsqueeze(2).broadcast_to([128, 4, DV]), ALU.mult))

                for grp in range(2):
                    for hi in range(8):
                        h = grp * 8 + hi
                        rb = (h % 4) * 32
                        fch = h // 4
                        q.c("tensor", lambda t, hi=hi, rb=rb, fch=fch, ce=ce: t.matmul(
                            psBig[0:64, hi * 128:hi * 128 + 64],
                            kf3[rb:rb + 32, fch, ce * GC:(ce + 1) * GC],
                            qf3[rb:rb + 32, fch, ce * GC:(ce + 1) * GC],
                            start=True, stop=True, tile_position=(rb, 0)))
                        q.c("tensor", lambda t, hi=hi, rb=rb, fch=fch, co=co: t.matmul(
                            psBig[64:128, hi * 128 + 64:hi * 128 + 128],
                            kf3[rb:rb + 32, fch, co * GC:(co + 1) * GC],
                            qf3[rb:rb + 32, fch, co * GC:(co + 1) * GC],
                            start=True, stop=True, tile_position=(rb, 64)))
                    q.c("vector", lambda v: v.memset(
                        pa[64:128, :].rearrange("p (h t) -> p h t", h=8)[:, :, 0:64], 0.0))
                    q.c("vector", lambda v: v.memset(
                        pa[0:64, :].rearrange("p (h t) -> p h t", h=8)[:, :, 64:128], 0.0))
                    q.c("vector", lambda v: v.tensor_tensor(
                        pa[0:64, :].rearrange("p (h t) -> p h t", h=8)[:, :, 0:64],
                        psBig[0:64, :].rearrange("p (h t) -> p h t", h=8)[:, :, 0:64],
                        cmk[0:64, 0:64].unsqueeze(1).broadcast_to([64, 8, 64]), ALU.mult))
                    q.c("vector", lambda v: v.tensor_tensor(
                        pa[64:128, :].rearrange("p (h t) -> p h t", h=8)[:, :, 64:128],
                        psBig[64:128, :].rearrange("p (h t) -> p h t", h=8)[:, :, 64:128],
                        cmk[64:128, 0:64].unsqueeze(1).broadcast_to([64, 8, 64]),
                        ALU.mult))
                    for hi in range(8):
                        h = grp * 8 + hi
                        q.c("tensor", lambda t, h=h, hi=hi: t.matmul(
                            psO[:, h * DV:(h + 1) * DV],
                            pa[:, hi * 128:(hi + 1) * 128],
                            vt[:, h * DV:(h + 1) * DV], start=True, stop=True))

                for h in range(H):
                    rb = (h % 4) * 32
                    fch = h // 4
                    q.c("tensor", lambda t, h=h, rb=rb, fch=fch, ce=ce: t.matmul(
                        psBig[0:64, h * DV:(h + 1) * DV],
                        qf3[rb:rb + 32, fch, ce * GC:(ce + 1) * GC],
                        Sbp3[rb:rb + 32, fch, :],
                        start=True, stop=True, tile_position=(rb, 0)))
                    q.c("tensor", lambda t, h=h, rb=rb, fch=fch, co=co: t.matmul(
                        psBig[64:128, h * DV:(h + 1) * DV],
                        qf3[rb:rb + 32, fch, co * GC:(co + 1) * GC],
                        Sbm3[rb:rb + 32, fch, :],
                        start=True, stop=True, tile_position=(rb, 64)))
                q.c("vector", lambda v: v.tensor_copy(aot[:, 0:1024], psBig[:, :]))
                q.c("vector", lambda v: v.tensor_add(aot[:, 0:1024], aot[:, 0:1024], psO[:, :]))

                q.c("vector", lambda v: v.tensor_mul(
                    sc8[:, 1024:1024 + 1024], aot[:, 0:1024], aot[:, 0:1024]))
                q.c("vector", lambda v: v.tensor_reduce(
                    bmat[:, 0:H].unsqueeze(2),
                    sc8[:, 1024:1024 + 1024].rearrange("p (h v) -> p h v", h=H),
                    AX.X, ALU.add))
                q.c("vector", lambda v: v.tensor_scalar(
                    bmat[:, 0:H], bmat[:, 0:H], 1.0 / DV, EPS, ALU.mult, ALU.add))
                q.c("scalar", lambda s: s.activation(bmat[:, 0:H], bmat[:, 0:H], AF.Sqrt))
                q.c("vector", lambda v: v.reciprocal(bmat[:, 0:H], bmat[:, 0:H]))
                q.c("vector", lambda v: v.tensor_tensor(
                    sg[:, :].rearrange("p (h v) -> p h v", h=H),
                    sg[:, :].rearrange("p (h v) -> p h v", h=H),
                    rmsr[:, :].unsqueeze(1).broadcast_to([128, H, DV]), ALU.mult))
                q.c("vector", lambda v: v.tensor_tensor(
                    sg[:, :].rearrange("p (h v) -> p h v", h=H),
                    sg[:, :].rearrange("p (h v) -> p h v", h=H),
                    bmat[:, 0:H].unsqueeze(2).broadcast_to([128, H, DV]), ALU.mult))
                q.c("vector", lambda v: v.tensor_mul(o2[:, :], aot[:, 0:1024], sg[:, :]))
                for fc in range(NCH):
                    q.c("tensor", lambda t, fc=fc: t.transpose(
                        psT[:, fc * 128:(fc + 1) * 128], o2[:, fc * 128:(fc + 1) * 128],
                        idb[:, :]))
                q.c("vector", lambda v, tcol=tcol: v.tensor_copy(
                    o2f3[:, :, tcol:tcol + 128],
                    psT[:, :].rearrange("p (c t) -> p c t", c=NCH)[:, :, 0:128]))

            q.c("tensor", lambda t: t.matmul(
                psBig[:, 0:TC], onesr8[0:1, :], strow(0, 2), start=True, stop=True))
            q.c("vector", lambda v: v.tensor_tensor(
                xm3, xf3, psBig[:, 0:TC].unsqueeze(1).broadcast_to([128, NCH, TC]),
                ALU.mult))
            q.c("tensor", lambda t: t.matmul(
                psBig[:, 0:TC], onesr8[0:1, :], strow(0, 3), start=True, stop=True))
            q.c("vector", lambda v: v.tensor_tensor(
                xm3, xm3, psBig[:, 0:TC].unsqueeze(1).broadcast_to([128, NCH, TC]),
                ALU.subtract))
            for mc in range(NCH):
                pd = psA if mc % 2 == 0 else psB
                for kc in range(NCH):
                    q.c("tensor", lambda t, mc=mc, kc=kc, pd=pd: t.matmul(
                        pd[:, 0:TC],
                        wto[:, kc * 1024 + mc * 128:kc * 1024 + (mc + 1) * 128],
                        o2f3[:, kc, :], start=(kc == 0), stop=(kc == NCH - 1)))
                q.c("vector", lambda v, mc=mc, pd=pd: v.tensor_add(
                    xm3[:, mc, :], xm3[:, mc, :], pd[:, 0:TC]))

            q.c("vector", lambda v: v.tensor_mul(sc8[:, 0:NCH * TC], xm[:, :], xm[:, :]))
            for kc in range(NCH):
                q.c("tensor", lambda t, kc=kc: t.matmul(
                    psA[0:1, 0:TC], onescf[:, 0:1], xm3[:, kc, :],
                    start=(kc == 0), stop=(kc == NCH - 1)))
            for kc in range(NCH):
                q.c("tensor", lambda t, kc=kc: t.matmul(
                    psB[0:1, 0:TC], onescf[:, 0:1],
                    sc8[:, 0:NCH * TC].rearrange("p (c t) -> p c t", c=NCH)[:, kc, :],
                    start=(kc == 0), stop=(kc == NCH - 1)))
            q.c("vector", lambda v: v.tensor_scalar_mul(strow(0, 0), psA[0:1, 0:TC], 1.0 / D))
            q.c("vector", lambda v: v.tensor_scalar_mul(strow(0, 1), psB[0:1, 0:TC], 1.0 / D))
            q.c("vector", lambda v: v.tensor_mul(strow(0, 2), strow(0, 0), strow(0, 0)))
            q.c("vector", lambda v: v.tensor_sub(strow(0, 1), strow(0, 1), strow(0, 2)))
            q.c("vector", lambda v: v.tensor_scalar_add(strow(0, 3), strow(0, 1), EPS))
            q.c("scalar", lambda s: s.activation(strow(0, 1), strow(0, 3), AF.Sqrt))
            q.c("vector", lambda v: v.reciprocal(strow(0, 2), strow(0, 1)))
            q.c("vector", lambda v: v.tensor_mul(strow(0, 3), strow(0, 2), strow(0, 0)))
            q.c("vector", lambda v: v.tensor_scalar_mul(strow(0, 3), strow(0, 3), -1.0))
            q.c("vector", lambda v: v.tensor_copy(r2p[0:1, :], strow(0, 3)))
            q.c("tensor", lambda t: t.matmul(
                psBig[:, 0:TC], onesr8[0:1, :], strow(0, 2), start=True, stop=True))
            q.c("vector", lambda v: v.tensor_tensor(
                xa3, xm3, psBig[:, 0:TC].unsqueeze(1).broadcast_to([128, NCH, TC]),
                ALU.mult))

            for mc in range(cfg.FICH):
                pd = psA if mc % 2 == 0 else psB
                for kc in range(NCH):
                    q.c("tensor", lambda t, mc=mc, kc=kc, pd=pd: t.matmul(
                        pd[:, 0:TC],
                        wt1[:, kc * cfg.FI + mc * 128:kc * cfg.FI + (mc + 1) * 128],
                        xa3[:, kc, :], start=(kc == 0), stop=False))
                q.c("tensor", lambda t, mc=mc, pd=pd: t.matmul(
                    pd[:, 0:TC], r1w1[:, mc * 128:(mc + 1) * 128], r2p[:, :],
                    start=False, stop=True))
                q.c("scalar", lambda s, mc=mc, pd=pd: s.activation(
                    t1v3[:, mc, :], pd[:, 0:TC], AF.Gelu))
            for mc in range(NCH):
                pd = psA if mc % 2 == 0 else psB
                for kc in range(cfg.FICH):
                    q.c("tensor", lambda t, mc=mc, kc=kc, pd=pd: t.matmul(
                        pd[:, 0:TC],
                        wt2[:, kc * 1024 + mc * 128:kc * 1024 + (mc + 1) * 128],
                        t1v3[:, kc, :], start=(kc == 0), stop=False))
                q.c("tensor", lambda t, mc=mc, pd=pd: t.matmul(
                    pd[:, 0:TC], w2b[0:1, mc * 128:(mc + 1) * 128], oneTC[0:1, :],
                    start=False, stop=True))
                q.c("vector", lambda v, mc=mc, pd=pd: v.tensor_add(
                    xm3[:, mc, :], xm3[:, mc, :], pd[:, 0:TC]))

            _mk["ts"] = len(q.steps)
            for j in range(TPT):
                tt = t0 + j * 128
                tcol = j * 128
                for kc in range(NCH):
                    q.c("tensor", lambda t, kc=kc, tcol=tcol: t.matmul(
                        psA[:, 0:E], xm3[:, kc, tcol:tcol + 128],
                        wgw[:, kc * E:(kc + 1) * E],
                        start=(kc == 0), stop=(kc == NCH - 1)))
                q.c("vector", lambda v, j=j: v.tensor_copy(
                    agw[:, j * 16 + 8:j * 16 + 16], psA[:, 0:E]))
                for fc in range(NCH):
                    q.c("tensor", lambda t, fc=fc, tcol=tcol: t.transpose(
                        psBig[:, fc * 128:(fc + 1) * 128],
                        xm3[:, fc, tcol:tcol + 128], idf[:, :]))
                q.c("vector", lambda v: v.tensor_copy(aot[:, 0:1024], psBig[:, :]))
                q.c("vector", lambda v: v.reduce_sum(str2[:, 0:1], aot[:, 0:1024], AX.X))
                q.c("scalar", lambda s: s.activation(
                    sc8[:, 1024:2048], aot[:, 0:1024], AF.Square,
                    accum_out=str2[:, 1:2]))
                q.c("vector", lambda v: v.tensor_scalar_mul(str2[:, 0:1], str2[:, 0:1], 1.0 / D))
                q.c("vector", lambda v: v.tensor_scalar_mul(str2[:, 1:2], str2[:, 1:2], 1.0 / D))
                q.c("vector", lambda v: v.tensor_mul(str2[:, 2:3], str2[:, 0:1], str2[:, 0:1]))
                q.c("vector", lambda v: v.tensor_sub(str2[:, 1:2], str2[:, 1:2], str2[:, 2:3]))
                q.c("vector", lambda v: v.tensor_scalar_add(str2[:, 1:2], str2[:, 1:2], EPS))
                q.c("scalar", lambda s: s.activation(str2[:, 2:3], str2[:, 1:2], AF.Sqrt))
                q.c("vector", lambda v: v.reciprocal(str2[:, 3:4], str2[:, 2:3]))
                q.c("vector", lambda v: v.tensor_mul(str2[:, 4:5], str2[:, 3:4], str2[:, 0:1]))
                q.c("vector", lambda v: v.tensor_scalar_mul(str2[:, 4:5], str2[:, 4:5], -1.0))
                q.dma("sync", lambda s, tt=tt: s.dma_start(
                    out=xt1[:, :], in_=x_tm.ap()[tt:tt + 128, :]))
                q.c("vector", lambda v: v.tensor_scalar(
                    aot[:, 0:1024], aot[:, 0:1024], str2[:, 3:4], str2[:, 4:5],
                    ALU.mult, ALU.add))
                q.c("vector", lambda v: v.tensor_add(xt1[:, :], xt1[:, :], aot[:, 0:1024]))
                q.dma("sync", lambda s, tt=tt: s.dma_start(
                    out=x1buf.ap()[tt:tt + 128, :], in_=xt1[:, :]))
                q.c("vector", lambda v: v.reduce_sum(str2[:, 5:6], xt1[:, :], AX.X))
                q.c("scalar", lambda s: s.activation(
                    sc8[:, 1024:2048], xt1[:, :], AF.Square,
                    accum_out=str2[:, 6:7]))
                q.c("vector", lambda v: v.tensor_scalar_mul(str2[:, 5:6], str2[:, 5:6], 1.0 / D))
                q.c("vector", lambda v: v.tensor_scalar_mul(str2[:, 6:7], str2[:, 6:7], 1.0 / D))
                q.c("vector", lambda v: v.tensor_mul(str2[:, 7:8], str2[:, 5:6], str2[:, 5:6]))
                q.c("vector", lambda v: v.tensor_sub(str2[:, 6:7], str2[:, 6:7], str2[:, 7:8]))
                q.c("vector", lambda v: v.tensor_scalar_add(str2[:, 6:7], str2[:, 6:7], EPS))
                q.c("scalar", lambda s: s.activation(str2[:, 7:8], str2[:, 6:7], AF.Sqrt))
                q.c("vector", lambda v: v.reciprocal(str2[:, 8:9], str2[:, 7:8]))
                q.c("vector", lambda v: v.tensor_mul(str2[:, 9:10], str2[:, 8:9], str2[:, 5:6]))
                q.c("vector", lambda v: v.tensor_scalar_mul(str2[:, 9:10], str2[:, 9:10], -1.0))
                q.c("vector", lambda v: v.tensor_scalar(
                    l2t[:, :], xt1[:, :], str2[:, 8:9], str2[:, 9:10],
                    ALU.mult, ALU.add))
                q.dma("sync", lambda s, tt=tt: s.dma_start(
                    out=ln2buf.ap()[tt:tt + 128, :], in_=l2t[:, :]))
                q.c("vector", lambda v, j=j: v.tensor_scalar(
                    bmat[:, 0:E], agw[:, j * 16 + 8:j * 16 + 16],
                    str2[:, 3:4], None, ALU.mult))
                q.c("vector", lambda v, j=j: v.tensor_add(
                    bmat[:, 0:E], bmat[:, 0:E], agw[:, j * 16:j * 16 + 8]))
                q.c("vector", lambda v: v.scalar_tensor_tensor(
                    bmat[:, 8:8 + E], csr[:, :], str2[:, 4:5], bmat[:, 0:E],
                    ALU.mult, ALU.add))
                q.c("vector", lambda v: v.tensor_scalar(
                    bmat[:, 8:8 + E], bmat[:, 8:8 + E], str2[:, 8:9], None, ALU.mult))
                q.c("vector", lambda v: v.scalar_tensor_tensor(
                    bmat[:, 0:E], csr[:, :], str2[:, 9:10], bmat[:, 8:8 + E],
                    ALU.mult, ALU.add))
                q.c("vector", lambda v: v.tensor_add(bmat[:, 0:E], bmat[:, 0:E], gbr[:, :]))
                q.c("tensor", lambda t: t.transpose(psA[0:8, 0:128], bmat[:, 0:E], idf[:, :]))
                q.c("vector", lambda v: v.tensor_copy(lgt[:, :], psA[0:8, 0:128]))
                q.dma("sync", lambda s, tt=tt: s.dma_start(
                    out=logitsT.ap()[:, tt:tt + 128], in_=lgt[:, :]))

    if milestones is not None:
        milestones['phaseA_end'] = len(q.steps)
    free_from("xf")

    slotc = sb("slotc", [128, NTILE], I32)
    gpc = sb("gpc", [128, NTILE], F32)
    lgs = sb("lgs", [128, E * PERP], F32)
    rt1 = sb("rt1", [128, E * PERP], F32)
    rt2 = sb("rt2", [128, E * PERP], F32)
    rt3 = sb("rt3", [128, PERP], F32)
    rt4 = sb("rt4", [128, PERP], F32)
    rt5 = sb("rt5", [128, PERP], F32)
    emx = sb("emx", [128, PERP], F32)
    gpv = sb("gpv", [128, PERP], F32)
    tot8 = sb("tot8", [128, E], F32)
    off8 = sb("off8", [128, E], F32)
    sltr = sb("sltr", [128, 128], F32)
    idx32 = sb("idx32", [128, 1], I32)
    idxf = sb("idxf", [128, 1], F32)
    cid = sb("cid", [128, 1], F32)
    l2s = sb("l2s", [128, D], BF16)

    lgs3 = lgs[:, :].rearrange("p (e t) -> p e t", e=E)
    rt13 = rt1[:, :].rearrange("p (e t) -> p e t", e=E)
    rt23 = rt2[:, :].rearrange("p (e t) -> p e t", e=E)

    q.dma("sync", lambda s: s.dma_start(out=sltr[:, :], in_=sltri.ap()))
    if milestones is not None:
        milestones['pre_ag1'] = len(q.steps)
    q.c("gpsimd", lambda g: g.collective_compute(
        "AllGather", ALU.bypass, replica_groups=[list(range(cfg.NCORES))],
        ins=[logitsT.ap().opt()], outs=[logits_all.ap().opt()]))
    if milestones is not None:
        milestones['post_ag1'] = len(q.steps)
    for c in range(cfg.NCORES):
        q.dma("sync", lambda s, c=c: s.dma_start(
            out=lgs3[c * 16:(c + 1) * 16, :, :],
            in_=logits_all.ap()[c * E:(c + 1) * E, :].rearrange(
                "e (pp t) -> pp e t", pp=16)))
    q.c("vector", lambda v: v.tensor_copy(emx[:, :], lgs3[:, 0, :]))
    for e in range(1, E):
        q.c("vector", lambda v, e=e: v.tensor_max(emx[:, :], emx[:, :], lgs3[:, e, :]))
    q.c("vector", lambda v: v.tensor_tensor(
        rt13, lgs3, emx[:, :].unsqueeze(1).broadcast_to([128, E, PERP]), ALU.subtract))
    q.c("scalar", lambda s: s.activation(rt2[:, :], rt1[:, :], AF.Exp))
    q.c("vector", lambda v: v.tensor_copy(gpv[:, :], rt23[:, 0, :]))
    for e in range(1, E):
        q.c("vector", lambda v, e=e: v.tensor_add(gpv[:, :], gpv[:, :], rt23[:, e, :]))
    q.c("vector", lambda v: v.reciprocal(gpv[:, :], gpv[:, :]))
    q.c("vector", lambda v: v.tensor_tensor(
        rt13, lgs3, emx[:, :].unsqueeze(1).broadcast_to([128, E, PERP]), ALU.is_ge))
    q.c("vector", lambda v: v.memset(rt3[:, :], 1e6))
    for e in range(E):
        q.c("vector", lambda v, e=e: v.tensor_scalar(
            rt4[:, :], rt13[:, e, :], -1e6, 1e6 + e, ALU.mult, ALU.add))
        q.c("vector", lambda v: v.tensor_tensor(rt3[:, :], rt3[:, :], rt4[:, :], ALU.min))
    q.c("vector", lambda v: v.tensor_copy(emx[:, :], rt3[:, :]))
    for e in range(E):
        q.c("vector", lambda v, e=e: v.tensor_single_scalar(
            rt13[:, e, :], emx[:, :], float(e), ALU.is_equal))
        q.c("vector", lambda v, e=e: v.tensor_tensor_scan(
            rt23[:, e, :], rt13[:, e, :], zrow[:, 0:PERP], 0.0, ALU.add, ALU.add))
    q.c("vector", lambda v: v.tensor_copy(tot8[:, :], rt23[:, :, PERP - 1]))
    q.c("tensor", lambda t: t.matmul(
        psA[0:8, 0:128], tot8[:, :], sltr[:, :], start=True, stop=True))
    q.c("vector", lambda v: v.tensor_copy(sltr[0:8, 0:128], psA[0:8, 0:128]))
    q.c("tensor", lambda t: t.transpose(psA[:, 0:8], sltr[0:8, 0:128], idf[0:8, 0:8]))
    q.c("vector", lambda v: v.tensor_copy(off8[:, :], psA[:, 0:8]))
    q.c("vector", lambda v: v.memset(rt3[:, :], 0.0))
    q.c("vector", lambda v: v.memset(rt4[:, :], 0.0))
    for e in range(E):
        q.c("vector", lambda v, e=e: v.scalar_tensor_tensor(
            rt3[:, :], rt13[:, e, :], off8[:, e:e + 1], rt3[:, :], ALU.mult, ALU.add))
        q.c("vector", lambda v, e=e: v.tensor_mul(rt5[:, :], rt13[:, e, :], rt23[:, e, :]))
        q.c("vector", lambda v: v.tensor_add(rt4[:, :], rt4[:, :], rt5[:, :]))
    q.c("vector", lambda v: v.tensor_add(rt3[:, :], rt3[:, :], rt4[:, :]))
    q.c("vector", lambda v: v.tensor_scalar_add(rt3[:, :], rt3[:, :], -1.0))
    q.c("vector", lambda v: v.tensor_single_scalar(
        rt4[:, :], rt3[:, :], float(cfg.CAP - 1), ALU.is_le))
    q.c("vector", lambda v: v.tensor_scalar(
        rt5[:, :], emx[:, :], float(cfg.CAP), None, ALU.mult))
    q.c("vector", lambda v: v.tensor_add(rt5[:, :], rt5[:, :], rt3[:, :]))
    q.c("vector", lambda v: v.tensor_mul(rt5[:, :], rt5[:, :], rt4[:, :]))
    q.c("vector", lambda v: v.tensor_scalar(
        rt4[:, :], rt4[:, :], -float(cfg.NSLOT), float(cfg.NSLOT), ALU.mult, ALU.add))
    q.c("vector", lambda v: v.tensor_add(rt5[:, :], rt5[:, :], rt4[:, :]))

    q.dma("sync", lambda s: s.dma_start(
        out=slotall.ap().rearrange("a w -> (a w)").rearrange("(p t) -> p t", p=128),
        in_=rt5[:, :]))
    q.dma("sync", lambda s: s.dma_start(
        out=gpall.ap().rearrange("a w -> (a w)").rearrange("(p t) -> p t", p=128),
        in_=gpv[:, :]))
    q.dma("sync", lambda s: s.dma_start(
        out=cid[:, :], in_=coreid.ap().broadcast_to([128, 1])))
    q.c("gpsimd", lambda g: g.iota(idxf[:, :], pattern=[[0, 1]], base=0,
                                   channel_multiplier=1,
                                   allow_small_or_imprecise_dtypes=True))
    q.c("vector", lambda v: v.tensor_single_scalar(cid[:, :], cid[:, :], float(NTILE), ALU.mult))
    q.c("vector", lambda v: v.tensor_add(idxf[:, :], idxf[:, :], cid[:, :]))
    q.c("vector", lambda v: v.tensor_copy(idx32[:, :], idxf[:, :]))
    q.dma("gpsimd", lambda g: g.indirect_dma_start(
        sltr[0:NTILE, :], None, slotall.ap(),
        IndirectOffsetOnAxis(idx32[0:NTILE, 0:1], 0)))
    if debug_out:
        sltrdbg = nc.dram_tensor("sltrdbg", [128, 128], F32,
                                 kind="ExternalOutput")
        idxdbg = nc.dram_tensor("idxdbg", [128, 2], F32, kind="ExternalOutput")
        q.dma("sync", lambda s: s.dma_start(out=sltrdbg.ap(), in_=sltr[:, :]))
        q.c("vector", lambda v: v.tensor_copy(rt3[:, 0:1], idx32[:, :]))
        q.c("vector", lambda v: v.tensor_copy(rt3[:, 1:2], idxf[:, :]))
        q.dma("sync", lambda s: s.dma_start(out=idxdbg.ap(), in_=rt3[:, 0:2]))
    q.c("tensor", lambda t: t.transpose(
        psA[:, 0:NTILE], sltr[0:NTILE, :], idf[0:NTILE, 0:NTILE]))
    q.c("vector", lambda v: v.tensor_copy(slotc[:, :], psA[:, 0:NTILE]))
    q.dma("gpsimd", lambda g: g.indirect_dma_start(
        sltr[0:NTILE, :], None, gpall.ap(),
        IndirectOffsetOnAxis(idx32[0:NTILE, 0:1], 0)))
    q.c("tensor", lambda t: t.transpose(
        psA[:, 0:NTILE], sltr[0:NTILE, :], idf[0:NTILE, 0:NTILE]))
    q.c("vector", lambda v: v.tensor_copy(gpc[:, :], psA[:, 0:NTILE]))

    for k in range(NTILE):
        q.dma("sync", lambda s, k=k: s.dma_start(
            out=l2s[:, :], in_=ln2buf.ap()[k * 128:(k + 1) * 128, :]))
        q.dma("gpsimd", lambda g, k=k: g.indirect_dma_start(
            slotbuf.ap(), IndirectOffsetOnAxis(slotc[:, k:k + 1], 0),
            l2s[:, :], None))

    if milestones is not None:
        milestones['pre_rs'] = len(q.steps)
    q.c("gpsimd", lambda g: g.collective_compute(
        "ReduceScatter", ALU.add, replica_groups=[list(range(cfg.NCORES))],
        ins=[slotbuf.ap()[0:cfg.NSLOT, :].opt()], outs=[rsout.ap().opt()]))
    if milestones is not None:
        milestones['post_rs'] = len(q.steps)

    free_from("lgs")

    onesb = sb("onesb", [1, 1024], BF16)
    q.c("vector", lambda v: v.memset(onesb[:, :], 1.0))
    xfm = sb("xfm", [128, NCH * cfg.CAP], BF16)
    xtile = sb("xtile", [128, D], BF16)
    estg = sb("estg", [128, 1024], F32)
    estg2 = sb("estg2", [128, 1024], F32)
    _esp = [0]
    def _estg():
        _esp[0] ^= 1
        return (estg, "sync") if _esp[0] else (estg2, "scalar")

    hid = sb("hid", [128, cfg.FCH * cfg.CAP], BF16)
    b1r = sb("b1r", [1, cfg.F], BF16)
    eb2r = sb("eb2r", [128, D], F32)
    oev = sb("oev", [128, D], F32)
    xfm3 = xfm[:, :].rearrange("p (c n) -> p c n", c=NCH)
    hid3 = hid[:, :].rearrange("p (c n) -> p c n", c=cfg.FCH)

    _ch = min(1024, cfg.F)
    for j in range(cfg.F // _ch):
        q.dma("sync", lambda s, j=j, _ch=_ch: s.dma_start(
            out=estg[0:1, 0:_ch], in_=eb1.ap()[0:1, j * _ch:(j + 1) * _ch]))
        q.c("vector", lambda v, j=j, _ch=_ch: v.tensor_copy(
            b1r[0:1, j * _ch:(j + 1) * _ch], estg[0:1, 0:_ch]))
    q.dma("sync", lambda s: s.dma_start(out=eb2r[:, :], in_=eb2.ap().broadcast_to([128, D])))

    for k in range(cfg.NXT):
        q.dma("sync", lambda s, k=k: s.dma_start(
            out=xtile[:, :], in_=rsout.ap()[k * 128:(k + 1) * 128, :]))
        for fc in range(NCH):
            q.c("tensor", lambda t, fc=fc: t.transpose(
                psT[:, fc * 128:(fc + 1) * 128], xtile[:, fc * 128:(fc + 1) * 128],
                idb[:, :]))
        q.c("vector", lambda v, k=k: v.tensor_copy(
            xfm3[:, :, k * 128:(k + 1) * 128],
            psT[:, :].rearrange("p (c n) -> p c n", c=NCH)[:, :, 0:128]))

    w1s = sb("w1s", [128, NCH * cfg.F], BF16)
    for kc in range(NCH):
        for j in range(cfg.F // _ch):
            eb_, eq_ = _estg()
            q.dma(eq_, lambda s, kc=kc, j=j, _ch=_ch, eb_=eb_: s.dma_start(
                out=eb_[:, 0:_ch],
                in_=ew1.ap()[kc * 128:(kc + 1) * 128, j * _ch:(j + 1) * _ch]))
            q.c("vector", lambda v, kc=kc, j=j, _ch=_ch, eb_=eb_: v.tensor_copy(
                w1s[:, kc * cfg.F + j * _ch:kc * cfg.F + (j + 1) * _ch],
                eb_[:, 0:_ch]))
    nsplits = [(0, min(512, cfg.CAP))]
    if cfg.CAP > 512:
        nsplits.append((512, cfg.CAP))
    for mc in range(cfg.FCH):
        for si, (n0, n1) in enumerate(nsplits):
            pd = psA if si == 0 else psB
            for kc in range(NCH):
                q.c("tensor", lambda t, mc=mc, kc=kc, n0=n0, n1=n1, pd=pd: t.matmul(
                    pd[:, 0:n1 - n0],
                    w1s[:, kc * cfg.F + mc * 128:kc * cfg.F + (mc + 1) * 128],
                    xfm3[:, kc, n0:n1], start=(kc == 0), stop=False))
            q.c("tensor", lambda t, mc=mc, n0=n0, n1=n1, pd=pd: t.matmul(
                pd[:, 0:n1 - n0], b1r[0:1, mc * 128:(mc + 1) * 128],
                onesb[0:1, n0:n1], start=False, stop=True))
            q.c("scalar", lambda s, mc=mc, n0=n0, n1=n1, pd=pd: s.activation(
                hid3[:, mc, n0:n1], pd[:, 0:n1 - n0], AF.Gelu_apprx_tanh))
    free_from("w1s")

    w2s = sb("w2s", [128, cfg.FCH * D], BF16)
    for kc in range(cfg.FCH):
        eb_, eq_ = _estg()
        q.dma(eq_, lambda s, kc=kc, eb_=eb_: s.dma_start(
            out=eb_[:, :], in_=ew2.ap()[kc * 128:(kc + 1) * 128, :]))
        q.c("vector", lambda v, kc=kc, eb_=eb_: v.tensor_copy(
            w2s[:, kc * D:(kc + 1) * D], eb_[:, :]))
    for k in range(cfg.NXT):
        for n0, n1 in ((0, 512), (512, 1024)):
            for kc in range(cfg.FCH):
                q.c("tensor", lambda t, k=k, kc=kc, n0=n0, n1=n1: t.matmul(
                    psO[:, n0:n1], hid3[:, kc, k * 128:(k + 1) * 128],
                    w2s[:, kc * D + n0:kc * D + n1],
                    start=(kc == 0), stop=(kc == cfg.FCH - 1)))
        q.c("vector", lambda v: v.tensor_add(oev[:, :], psO[:, :], eb2r[:, :]))
        q.dma("sync", lambda s, k=k: s.dma_start(
            out=ag2in.ap()[k * 128:(k + 1) * 128, :], in_=oev[:, :]))

    if milestones is not None:
        milestones['pre_ag2'] = len(q.steps)
    q.c("gpsimd", lambda g: g.collective_compute(
        "AllGather", ALU.bypass, replica_groups=[list(range(cfg.NCORES))],
        ins=[ag2in.ap().opt()], outs=[outbuf.ap()[0:cfg.NSLOT, :].opt()]))
    if milestones is not None:
        milestones['post_ag2'] = len(q.steps)

    gat = sb("gat", [128, D], F32)
    xci = sb("xci", [128, D], F32)
    dm2 = sb("dm2", [128, D], F32)
    stc = sb("stc", [128, 6], F32)
    for k in range(NTILE):
        q.dma("gpsimd", lambda g, k=k: g.indirect_dma_start(
            gat[:, :], None, outbuf.ap(), IndirectOffsetOnAxis(slotc[:, k:k + 1], 0)))
        q.c("vector", lambda v, k=k: v.tensor_scalar_mul(
            gat[:, :], gat[:, :], gpc[:, k:k + 1]))
        q.c("vector", lambda v: v.reduce_sum(stc[:, 0:1], gat[:, :], AX.X))
        q.c("scalar", lambda s: s.activation(
            dm2[:, :], gat[:, :], AF.Square, accum_out=stc[:, 1:2]))
        q.c("vector", lambda v: v.tensor_scalar_mul(stc[:, 0:1], stc[:, 0:1], 1.0 / D))
        q.c("vector", lambda v: v.tensor_scalar_mul(stc[:, 1:2], stc[:, 1:2], 1.0 / D))
        q.c("vector", lambda v: v.tensor_mul(stc[:, 2:3], stc[:, 0:1], stc[:, 0:1]))
        q.c("vector", lambda v: v.tensor_sub(stc[:, 1:2], stc[:, 1:2], stc[:, 2:3]))
        q.c("vector", lambda v: v.tensor_scalar_add(stc[:, 1:2], stc[:, 1:2], EPS))
        q.c("scalar", lambda s: s.activation(stc[:, 2:3], stc[:, 1:2], AF.Sqrt))
        q.c("vector", lambda v: v.reciprocal(stc[:, 3:4], stc[:, 2:3]))
        q.c("vector", lambda v: v.tensor_mul(stc[:, 4:5], stc[:, 3:4], stc[:, 0:1]))
        q.c("vector", lambda v: v.tensor_scalar_mul(stc[:, 4:5], stc[:, 4:5], -1.0))
        q.dma("sync", lambda s, k=k: s.dma_start(
            out=xci[:, :], in_=x1buf.ap()[k * 128:(k + 1) * 128, :]))
        q.c("vector", lambda v: v.tensor_scalar(
            gat[:, :], gat[:, :], stc[:, 3:4], stc[:, 4:5], ALU.mult, ALU.add))
        q.c("vector", lambda v: v.tensor_add(gat[:, :], gat[:, :], xci[:, :]))
        q.dma("sync", lambda s, k=k: s.dma_start(
            out=out.ap()[k * 128:(k + 1) * 128, :], in_=gat[:, :]))

    if milestones is not None:
        milestones['all'] = len(q.steps)
    if max_steps is not None:
        q.steps = q.steps[:max_steps]
    if phasec_probe is not None:
        pdbg = nc.dram_tensor("pdbg", [128, D], F32, kind="ExternalOutput")
        pdbg2 = nc.dram_tensor("pdbg2", [128, cfg.NTILE + 1], F32,
                               kind="ExternalOutput")
        if phasec_probe == "iota":
            pidx = sb("pidx", [128, 2], I32)
            pidxf = pidx[:, 1:2].bitcast(F32)
            q.c("gpsimd", lambda g: g.iota(
                pidxf, pattern=[[0, 1]], base=0, channel_multiplier=1,
                allow_small_or_imprecise_dtypes=True))
            q.c("vector", lambda v: v.tensor_copy(pidx[:, 0:1], pidxf))
            q.dma("gpsimd", lambda g: g.indirect_dma_start(
                gat[:, :], None, outbuf.ap(),
                IndirectOffsetOnAxis(pidx[:, 0:1], 0)))
        else:
            q.dma("gpsimd", lambda g: g.indirect_dma_start(
                gat[:, :], None, outbuf.ap(),
                IndirectOffsetOnAxis(slotc[:, 0:1], 0)))
        q.dma("sync", lambda s: s.dma_start(out=pdbg.ap(), in_=gat[:, :]))
        q.c("vector", lambda v: v.tensor_copy(xci[:, 0:cfg.NTILE],
                                              slotc[:, :]))
        q.dma("sync", lambda s: s.dma_start(
            out=pdbg2.ap()[:, 0:cfg.NTILE], in_=xci[:, 0:cfg.NTILE]))

    if dump_src is not None:
        _dct = dict(logitsT=logitsT, logits_all=logits_all, slotall=slotall,
                    gpall=gpall, slotbuf=slotbuf, rsout=rsout, ag2in=ag2in,
                    outbuf=outbuf, x1buf=x1buf, ln2buf=ln2buf)
        for _nm, _shape, _dt in zip(dump_src, dump_shape, dump_dt):
            _src = _dct[_nm]
            dbgout = nc.dram_tensor("dbg_" + _nm, list(_shape), _dt,
                                    kind="ExternalOutput")
            for r0 in range(0, _shape[0], 512):
                r1 = min(_shape[0], r0 + 512)
                q.dma("sync", lambda s, r0=r0, r1=r1, dbgout=dbgout, _src=_src:
                      s.dma_start(out=dbgout.ap()[r0:r1, :],
                                  in_=_src.ap()[r0:r1, :]))
    if emit_mode == "probe":
        with nc.Block() as blk:
            probe_out["info"] = q.emit_probe(nc, blk)
            probe_out["seq"] = q
    elif emit_mode == "sched":
        import contextlib
        with contextlib.ExitStack() as st:
            sems = {}
            for d in schedule["dims"]:
                sems[d] = st.enter_context(nc.semaphore("sem_" + d))
            blk = st.enter_context(nc.Block())
            q.emit(nc, blk, sems, schedule=schedule)
    else:
        with nc.semaphore("s_y") as s_y, nc.semaphore("s_v") as s_v, \
             nc.semaphore("s_s") as s_s, \
             nc.semaphore("s_t") as s_t, nc.semaphore("s_g") as s_g, \
             nc.semaphore("s_yd") as s_yd, nc.semaphore("s_sd") as s_sd, \
             nc.semaphore("s_gd") as s_gd, nc.Block() as blk:
            q.emit(nc, blk, {"sync": s_y, "vector": s_v, "scalar": s_s,
                             "tensor": s_t,
                             "gpsimd": s_g, "sync_d": s_yd, "scalar_d": s_sd,
                             "gpsimd_d": s_gd})

    for g, _ in reversed(allocs):
        g.__exit__(None, None, None)
    return nc


def _consts(cfg):
    import ml_dtypes
    eye = np.eye(128, dtype=np.float32)
    g = cfg.GC
    triu = np.triu(np.ones((g, g), np.float32))
    cm = np.zeros((128, 64), np.float32)
    cm[0:g, 0:g] = triu
    cm[g:2 * g, 0:g] = triu
    return {
        "identf": eye,
        "identb": eye.astype(ml_dtypes.bfloat16),
        "cmask": cm.astype(ml_dtypes.bfloat16),
        "sltri": np.triu(np.ones((128, 128), np.float32), k=1),
    }


def make_in_maps(cfg, inputs):
    x = np.asarray(inputs["L0_attn_x"], np.float32)
    xr = np.ascontiguousarray(x.reshape(cfg.NCORES, cfg.NTOK, cfg.D))

    def f32(name):
        return np.ascontiguousarray(np.asarray(inputs[name], np.float32))

    common = {
        "Wq": f32("L0_attn_Wq"), "Wk": f32("L0_attn_Wk"),
        "Wv": f32("L0_attn_Wv"), "Wg": f32("L0_attn_Wg"),
        "Wo": f32("L0_attn_Wo"), "W1": f32("L0_attn_W1"), "W2": f32("L0_attn_W2"),
        "gkw1": f32("L0_attn_gk_w1"), "gkw2": f32("L0_attn_gk_w2"),
        "ln_w": f32("L0_attn_ln_w").reshape(1, -1),
        "ln_b": f32("L0_attn_ln_b").reshape(1, -1),
        "gk_b2": f32("L0_attn_gk_b2").reshape(1, -1),
        "b1i": f32("L0_attn_b1").reshape(1, -1),
        "b2i": f32("L0_attn_b2").reshape(1, -1),
        "rms_w": f32("L0_attn_rms_w").reshape(1, -1),
        "gw": f32("L0_ffn_gate_w"), "gb": f32("L0_ffn_gate_b").reshape(1, -1),
    }
    common.update(_consts(cfg))
    ew1 = f32("L0_ffn_w1"); eb1 = f32("L0_ffn_b1")
    ew2 = f32("L0_ffn_w2"); eb2 = f32("L0_ffn_b2")

    in_maps = []
    for c in range(cfg.NCORES):
        m = dict(common)
        m["x_tm"] = xr[c]
        m["xT"] = np.ascontiguousarray(xr[c].T)
        m["ew1"] = ew1[c]; m["eb1"] = eb1[c].reshape(1, -1)
        m["ew2"] = ew2[c]; m["eb2"] = eb2[c].reshape(1, -1)
        m["coreid"] = np.array([[c]], np.float32)
        in_maps.append(m)
    return in_maps


def build_scheduled(cfg, **kw):
    po = {}
    build_program(cfg, emit_mode="probe", probe_out=po, **kw)
    sched = po["seq"].compute_schedule(po["info"])
    return build_program(cfg, emit_mode="sched", schedule=sched, **kw)


def kernel(**inputs):
    cfg = Cfg()
    x = np.asarray(inputs["L0_attn_x"], np.float32)
    NB = x.shape[0]
    nc = build_scheduled(cfg)
    in_maps = make_in_maps(cfg, inputs)
    res = run_bass_kernel_spmd(nc, in_maps, core_ids=list(range(cfg.NCORES)))
    outs = np.stack([res.results[c]["out"] for c in range(cfg.NCORES)])
    return outs.reshape(NB, cfg.T, cfg.D)


# version 13
# speedup vs baseline: 1.0079x; 1.0061x over previous
import numpy as np
import concourse.bass as bass
import concourse.mybir as mybir
from concourse.bass_utils import run_bass_kernel_spmd
from concourse.bass_core_rust import IndirectOffsetOnAxis

F32 = mybir.dt.float32
BF16 = mybir.dt.bfloat16
I32 = mybir.dt.int32
AF = mybir.ActivationFunctionType
ALU = mybir.AluOpType
AX = mybir.AxisListType


class Cfg:
    D = 1024
    NCH = 8
    H = 16
    DK = 32
    DV = 64
    LR = 16
    FI = 2048
    FICH = 16
    F = 4096
    FCH = 32
    E = 8
    CAP = 640
    GATE_NORM = 16.0
    EPS = 1e-5
    T = 2048
    SEQ = 2
    NCORES = 8
    TC = 256
    GC = 64
    DKS = DK ** -0.5

    NTOK = property(lambda s: s.T * s.SEQ)
    NTILE = property(lambda s: s.NTOK // 128)
    NSLOT = property(lambda s: s.E * s.CAP)
    PERP = property(lambda s: s.NCORES * s.NTOK // 128)
    NXT = property(lambda s: s.CAP // 128)


class Seq:

    ENGS = ("sync", "scalar", "vector", "tensor", "gpsimd")
    DIMS = ("sync", "scalar", "vector", "tensor", "gpsimd",
            "sync_d", "scalar_d", "gpsimd_d")

    def __init__(self):
        self.steps = []

    def c(self, eng, fn):
        self.steps.append((eng, "c", fn))

    def dma(self, eng, fn):
        self.steps.append((eng, "d", fn))

    @staticmethod
    def key(eng, kind):
        return eng + "_d" if kind == "d" else eng

    def emit_probe(self, nc, block):
        import numpy as _np
        per = {e: [] for e in self.ENGS}
        for idx, (eng, kind, fn) in enumerate(self.steps):
            per[eng].append((idx, fn))
        info = [None] * len(self.steps)
        base_cache = {}

        def sbase(name):
            if name not in base_cache:
                base_cache[name] = nc.lookup_mloc(name).addr
            return base_cache[name]

        def ap_range(ap):
            b = getattr(ap, "bass_ap", None)
            if b is None:
                return None
            esz = _np.dtype(mybir.dt.np(b.dtype)).itemsize
            pairs = [tuple(p) for p in b.ap]
            sp = str(b.space)
            if "DRAM" in sp:
                if ap.dynamic_ap_info is not None:
                    return (("D", b.name), 0, 1 << 44)
                lo = hi = b.offset
                for st, ct in pairs:
                    d = (ct - 1) * st
                    if d < 0:
                        lo += d
                    else:
                        hi += d
                return (("D", b.name), lo * esz, (hi + 1) * esz)
            dims = pairs[1:] if len(pairs) > 1 else []
            pstride = pairs[0][0] if pairs else 1
            off = b.offset
            if pstride > 0:
                off = off % pstride
            lo = hi = off
            for st, ct in dims:
                d = (ct - 1) * st
                if d < 0:
                    lo += d
                else:
                    hi += d
            if "PSUM" in sp:
                return (("P", b.name), 0, 1 << 30)
            base = sbase(b.name)
            return (("S",), base + lo * esz, base + (hi + 1) * esz)

        def mk(ename):
            def body(eng):
                for idx, fn in per[ename]:
                    inst = fn(eng)
                    mi = inst.ins
                    rd, wr = [], []
                    for ap in (mi.ins or []):
                        r = ap_range(ap)
                        if r is not None:
                            rd.append(r)
                    for ap in (mi.outs or []):
                        r = ap_range(ap)
                        if r is not None:
                            wr.append(r)
                    info[idx] = (rd, wr)
            return body

        for e in self.ENGS:
            if per[e] or e == "sync":
                getattr(block, e)(mk(e))
        return info

    NPOOL = 24

    def compute_schedule(self, info):
        n = len(self.steps)
        NG = 8
        NSW = 8
        dims = (list(self.ENGS) + ["dp%d" % i for i in range(self.NPOOL)]
                + ["gq%d" % i for i in range(NG)]
                + ["sw%d" % i for i in range(NSW)])
        dimi = {d: i for i, d in enumerate(dims)}
        counts = {d: 0 for d in dims}
        comp = [None] * n
        clocks = [None] * n
        know = {e: [0] * len(dims) for e in self.ENGS}
        waits = [None] * n
        incs = [None] * n
        writes_map = {}
        reads_map = {}
        pool_next = 0
        gq_next = 0
        sw_next = 0
        prev_user = [None] * self.NPOOL
        prev_sw = [None] * NSW

        def overlap(a0, a1, b0, b1):
            return a0 < b1 and b0 < a1

        for j, (eng, kind, fn) in enumerate(self.steps):
            deps = set()
            if kind == "d" and eng == "gpsimd":
                p = sw_next
                sw_next = (sw_next + 1) % NSW
                dk = "sw%d" % p
                counts[dk] += 16
                comp[j] = (dimi[dk], counts[dk])
                if prev_sw[p] is not None:
                    deps.add(prev_sw[p])
                prev_sw[p] = j
                incs[j] = (dk, 16)
            elif kind == "d":
                p = pool_next
                pool_next = (pool_next + 1) % self.NPOOL
                dk = "dp%d" % p
                counts[dk] += 16
                comp[j] = (dimi[dk], counts[dk])
                if prev_user[p] is not None:
                    deps.add(prev_user[p])
                prev_user[p] = j
                incs[j] = (dk, 16)
            elif eng == "gpsimd":
                dk = "gq%d" % (gq_next % NG)
                gq_next += 1
                counts[dk] += 1
                comp[j] = (dimi[dk], counts[dk])
                incs[j] = (dk, 1)
            else:
                dk = eng
                counts[dk] += 1
                comp[j] = (dimi[dk], counts[dk])
                incs[j] = (dk, 1)
            rd, wr = info[j]
            for key, lo, hi in rd:
                wl = writes_map.get(key)
                if not wl:
                    continue
                uncov = [(lo, hi)]
                for wlo, whi, wi in reversed(wl):
                    nxt = []
                    hit = False
                    for ulo, uhi in uncov:
                        if overlap(ulo, uhi, wlo, whi):
                            hit = True
                            if ulo < wlo:
                                nxt.append((ulo, wlo))
                            if whi < uhi:
                                nxt.append((whi, uhi))
                        else:
                            nxt.append((ulo, uhi))
                    if hit:
                        deps.add(wi)
                    uncov = nxt
                    if not uncov:
                        break
            for key, lo, hi in wr:
                wl = writes_map.get(key)
                if wl:
                    uncov = [(lo, hi)]
                    for wlo, whi, wi in reversed(wl):
                        nxt = []
                        hit = False
                        for ulo, uhi in uncov:
                            if overlap(ulo, uhi, wlo, whi):
                                hit = True
                                if ulo < wlo:
                                    nxt.append((ulo, wlo))
                                if whi < uhi:
                                    nxt.append((whi, uhi))
                            else:
                                nxt.append((ulo, uhi))
                        if hit:
                            deps.add(wi)
                        uncov = nxt
                        if not uncov:
                            break
                rl = reads_map.get(key)
                if rl:
                    keep = []
                    for rlo, rhi, ri in rl:
                        if overlap(lo, hi, rlo, rhi):
                            deps.add(ri)
                            if rlo < lo or rhi > hi:
                                keep.append((rlo, rhi, ri))
                        else:
                            keep.append((rlo, rhi, ri))
                    reads_map[key] = keep
            kn = know[eng]
            wl_out = []
            for i in sorted(deps):
                ddi, dval = comp[i]
                if kn[ddi] < dval:
                    wl_out.append((dims[ddi], dval))
                ci = clocks[i]
                for z in range(len(kn)):
                    if ci[z] > kn[z]:
                        kn[z] = ci[z]
                if kn[ddi] < dval:
                    kn[ddi] = dval
            waits[j] = wl_out
            cj = list(kn)
            di, dv = comp[j]
            if cj[di] < dv:
                cj[di] = dv
            clocks[j] = cj
            for key, lo, hi in wr:
                wl = writes_map.setdefault(key, [])
                wl2 = [w for w in wl if not (lo <= w[0] and w[1] <= hi)]
                wl2.append((lo, hi, j))
                writes_map[key] = wl2
            for key, lo, hi in rd:
                reads_map.setdefault(key, []).append((lo, hi, j))
        return {"waits": waits, "incs": incs, "finals": dict(counts),
                "dims": dims}

    def emit(self, nc, block, sems, schedule=None):
        per = {e: [] for e in self.ENGS}
        tot = {k: 0 for k in sems}
        for idx, (eng, kind, fn) in enumerate(self.steps):
            per[eng].append((idx, dict(tot), kind, fn))
            if schedule is None:
                tot[self.key(eng, kind)] += 16 if kind == "d" else 1
        final = dict(tot) if schedule is None else dict(schedule["finals"])
        import os
        if os.environ.get("PRINT_TOTALS"):
            nw = (sum(len(w) for w in schedule["waits"])
                  if schedule else None)
            print("SEQ TOTALS:", final, "steps:", len(self.steps),
                  "waits:", nw)

        def mk(ename):
            def body(eng):
                seen = {k: 0 for k in sems}
                for idx, w, kind, fn in per[ename]:
                    if schedule is None:
                        for k in sems:
                            if w[k] > seen[k]:
                                eng.wait_ge(sems[k], w[k])
                                seen[k] = w[k]
                    else:
                        import os as _os
                        _nsw = _os.environ.get("NO_SELF_WAITS")
                        for k, dval in schedule["waits"][idx]:
                            if _nsw and k == ename:
                                continue
                            if dval > seen[k]:
                                eng.wait_ge(sems[k], dval)
                                seen[k] = dval
                    inst = fn(eng)
                    if schedule is None:
                        kk = self.key(ename, kind)
                        inst.then_inc(sems[kk], 16 if kind == "d" else 1)
                    else:
                        kk, inc = schedule["incs"][idx]
                        inst.then_inc(sems[kk], inc)
                if ename == "sync":
                    for k in sems:
                        if final.get(k, 0) > seen[k]:
                            eng.wait_ge(sems[k], final[k])
            return body

        for e in self.ENGS:
            if per[e] or e == "sync":
                getattr(block, e)(mk(e))


def build_program(cfg, max_steps=None, milestones=None, debug_out=False,
                  dump_src=None, dump_shape=None, dump_dt=None,
                  phasec_probe=None, emit_mode="serial", schedule=None,
                  probe_out=None):
    nc = bass.Bass("TRN2", target_bir_lowering=False,
                   dynamic_dma_scratch_size=8192)
    q = Seq()

    D, NCH, H, DK, DV = cfg.D, cfg.NCH, cfg.H, cfg.DK, cfg.DV
    TC, GC, NTOK = cfg.TC, cfg.GC, cfg.NTOK
    NGC = TC // GC
    TPT = TC // 128
    NTILE = cfg.NTILE
    PERP = cfg.PERP
    EPS = cfg.EPS
    E = cfg.E

    def din(name, shape, dt=F32):
        return nc.dram_tensor(name, shape, dt, kind="ExternalInput")

    xT = din("xT", [D, NTOK])
    x_tm = din("x_tm", [NTOK, D])
    Wq = din("Wq", [D, H * DK]); Wk = din("Wk", [D, H * DK])
    Wv = din("Wv", [D, D]); Wg = din("Wg", [D, D]); Wo = din("Wo", [D, D])
    W1 = din("W1", [D, cfg.FI]); W2 = din("W2", [cfg.FI, D])
    gkw1 = din("gkw1", [D, cfg.LR]); gkw2 = din("gkw2", [cfg.LR, H * DK])
    ln_w = din("ln_w", [1, D]); ln_b = din("ln_b", [1, D])
    gk_b2 = din("gk_b2", [1, H * DK])
    b1i = din("b1i", [1, cfg.FI]); b2i = din("b2i", [1, D])
    rms_w = din("rms_w", [1, DV])
    gw = din("gw", [D, E]); gb = din("gb", [1, E])
    ew1 = din("ew1", [D, cfg.F]); eb1 = din("eb1", [1, cfg.F])
    ew2 = din("ew2", [cfg.F, D]); eb2 = din("eb2", [1, D])
    identf = din("identf", [128, 128]); identb = din("identb", [128, 128], BF16)
    cmask = din("cmask", [128, 64], BF16)
    sltri = din("sltri", [128, 128])
    coreid = din("coreid", [1, 1])

    dbg = dict(kind="ExternalOutput") if debug_out else {}
    x1buf = nc.dram_tensor("x1buf", [NTOK, D], F32, **dbg)
    ln2buf = nc.dram_tensor("ln2buf", [NTOK, D], BF16, **dbg)
    logitsT = nc.dram_tensor("logitsT", [E, NTOK], F32)
    logits_all = nc.dram_tensor("logits_all", [E * cfg.NCORES, NTOK], F32)
    slotall = nc.dram_tensor("slotall", [cfg.NCORES * NTILE, 128], F32)
    gpall = nc.dram_tensor("gpall", [cfg.NCORES * NTILE, 128], F32)
    slotbuf = nc.dram_tensor("slotbuf", [cfg.NSLOT + 1, D], BF16)
    rsout = nc.dram_tensor("rsout", [cfg.CAP, D], BF16)
    ag2in = nc.dram_tensor("ag2in", [cfg.CAP, D], F32)
    outbuf = nc.dram_tensor("outbuf", [cfg.NSLOT + 1, D], F32)
    srow = nc.dram_tensor("srow", [1, 64], F32)
    out = nc.dram_tensor("out", [NTOK, D], F32, kind="ExternalOutput")

    allocs = []

    def sb(name, shape, dt=F32):
        g = nc.sbuf_tensor(name, shape, dt)
        t = g.__enter__()
        allocs.append((g, name))
        return t

    def ps(name, shape, dt=F32):
        g = nc.psum_tensor(name, shape, dt)
        t = g.__enter__()
        allocs.append((g, name))
        return t

    def free_from(name):
        idx = next(i for i, (_, n) in enumerate(allocs) if n == name)
        for g, _ in reversed(allocs[idx:]):
            g.__exit__(None, None, None)
        del allocs[idx:]

    idf = sb("idf", [128, 128], F32)
    idb = sb("idb", [128, 128], BF16)
    cmk = sb("cmk", [128, 64], BF16)
    rmsr = sb("rmsr", [128, DV], F32)
    gbr = sb("gbr", [128, E], F32)
    csr = sb("csr", [128, E], F32)
    onescf = sb("onescf", [128, 1], F32)
    onesr8 = sb("onesr8", [1, 128], F32)
    lnbfm = sb("lnbfm", [128, NCH], F32)
    lnwfm = sb("lnwfm", [128, NCH], F32)
    lnwsfm = sb("lnwsfm", [128, NCH], F32)
    zrow = sb("zrow", [128, max(TC, PERP)], F32)
    zb = zrow[:, 0:256].bitcast(BF16)

    q.dma("sync", lambda s: s.dma_start(out=idf[:, :], in_=identf.ap()))
    q.dma("sync", lambda s: s.dma_start(out=idb[:, :], in_=identb.ap()))
    q.dma("sync", lambda s: s.dma_start(out=cmk[:, :], in_=cmask.ap()))
    q.dma("sync", lambda s: s.dma_start(out=rmsr[:, :], in_=rms_w.ap().broadcast_to([128, DV])))
    q.dma("sync", lambda s: s.dma_start(out=gbr[:, :], in_=gb.ap().broadcast_to([128, E])))
    q.dma("sync", lambda s: s.dma_start(
        out=lnbfm[:, :], in_=ln_b.ap().rearrange("o (c p) -> (o p) c", p=128),
        allow_slow_non_contiguous=True))
    q.dma("sync", lambda s: s.dma_start(
        out=lnwfm[:, :], in_=ln_w.ap().rearrange("o (c p) -> (o p) c", p=128),
        allow_slow_non_contiguous=True))
    q.c("vector", lambda v: v.memset(onescf[:, :], 1.0))
    q.c("vector", lambda v: v.memset(onesr8[:, :], 1.0))
    q.c("vector", lambda v: v.memset(zrow[:, :], 0.0))
    q.c("vector", lambda v: v.tensor_scalar_mul(lnwsfm[:, :], lnwfm[:, :], cfg.DKS))

    for r0 in range(0, cfg.NSLOT + 1, 128):
        rn = min(128, cfg.NSLOT + 1 - r0)
        for c0 in range(0, D, 512):
            q.dma("sync", lambda s, r0=r0, rn=rn, c0=c0: s.dma_start(
                out=slotbuf.ap()[r0:r0 + rn, c0:c0 + 512],
                in_=zrow[0:rn, 0:256].bitcast(BF16)))
    for c0 in range(0, D, 256):
        q.dma("sync", lambda s, c0=c0: s.dma_start(
            out=outbuf.ap()[cfg.NSLOT:cfg.NSLOT + 1, c0:c0 + 256],
            in_=zrow[0:1, 0:256]))

    psA = ps("psA", [128, 512], F32)
    psB = ps("psB", [128, 512], F32)
    psBig = ps("psBig", [128, 1024], F32)
    psO = ps("psO", [128, 1024], F32)
    psT = ps("psT", [128, 1024], BF16)
    psS = ps("psS", [128, 512], F32)

    xf = sb("xf", [128, NCH * TC], F32)
    xm = sb("xm", [128, NCH * TC], F32)
    xa = sb("xa", [128, NCH * TC], BF16)
    sc8 = sb("sc8", [128, NCH * TC], F32)
    qf = sb("qf", [128, 4 * TC], BF16)
    kf = sb("kf", [128, 4 * TC], BF16)
    gka = sb("gka", [128, 4 * TC], F32)
    eq = sb("eq", [128, 4 * TC], BF16)
    ek = sb("ek", [128, 4 * TC], BF16)
    bmat = sb("bmat", [128, 4 * NGC], F32)
    acar = sb("acar", [128, 4], F32)
    Ssb = sb("Ssb", [128, 4 * DV], F32)
    Sbp = sb("Sbp", [128, 4 * DV], BF16)
    Sbm = sb("Sbm", [128, 4 * DV], BF16)
    ach = sb("ach", [128, 4], F32)
    vt = sb("vt", [128, 1024], BF16)
    sg = sb("sg", [128, 1024], F32)
    pa = sb("pa", [128, 8 * 128], BF16)
    ktm = sb("ktm", [128, 512], BF16)
    o2 = sb("o2", [128, 1024], BF16)
    o2f = sb("o2f", [128, NCH * TC], BF16)
    xt1 = sb("xt1", [128, 1024], F32)
    st8 = sb("st8", [1, 4 * TC], F32)
    str2 = sb("str2", [128, 10], F32)
    tbm = sb("tbm", [128, 16], F32)
    r1p = sb("r1p", [2, TC], BF16)
    r2p = sb("r2p", [2, TC], BF16)
    lgt = sb("lgt", [8, 128], F32)
    agw = sb("agw", [128, 2 * TPT * 16], F32)

    q.c("vector", lambda v: v.memset(r1p[:, :], 1.0))
    q.c("vector", lambda v: v.memset(r2p[:, :], 1.0))

    wtq = sb("wtq", [128, NCH * 512], BF16)
    wtk = sb("wtk", [128, NCH * 512], BF16)
    wtv = sb("wtv", [128, NCH * 1024], BF16)
    wtg = sb("wtg", [128, NCH * 1024], BF16)
    wto = sb("wto", [128, NCH * 1024], BF16)
    wt1 = sb("wt1", [128, NCH * cfg.FI], BF16)
    wt2 = sb("wt2", [128, cfg.FICH * 1024], BF16)
    wgk1 = sb("wgk1", [128, NCH * cfg.LR], BF16)
    wgk2 = sb("wgk2", [cfg.LR + 1, H * DK], BF16)
    wgw = sb("wgw", [128, NCH * E], F32)
    w2bf = sb("w2bf", [128, NCH], F32)
    r1q = sb("r1q", [2, 512], BF16)
    r1k = sb("r1k", [2, 512], BF16)
    r1v = sb("r1v", [2, 1024], BF16)
    r1g = sb("r1g", [2, 1024], BF16)
    r1gk = sb("r1gk", [2, cfg.LR], BF16)
    r1w1 = sb("r1w1", [2, cfg.FI], BF16)

    wst = sb("wst", [128, 512], F32)
    wst2 = sb("wst2", [128, 512], F32)
    _wstp = [0]
    def _stg():
        _wstp[0] ^= 1
        return (wst, "sync") if _wstp[0] else (wst2, "scalar")

    onescb = sb("onescb", [128, 1], BF16)
    lnb2 = sb("lnb2", [128, 2 * NCH], F32)
    q.c("vector", lambda v: v.memset(onescb[:, :], 1.0))
    q.c("vector", lambda v: v.memset(lnb2[:, :], 0.0))
    q.c("vector", lambda v: v.tensor_copy(
        lnb2[:, :].rearrange("p (c two) -> p c two", two=2)[:, :, 1:2],
        lnbfm[:, :].unsqueeze(2)))

    def _ureg(j):
        pd = psBig if j < 2 else psO
        return pd[0:2, (j % 2) * 512:(j % 2) * 512 + 512]

    def _ureg2(j):
        pd = psBig if j < 2 else psO
        return pd[1:2, (j % 2) * 512:(j % 2) * 512 + 512]

    def prep_scaled(Wd, outF, wt_sb, r1_sb, scale_fm, uscale=None, badd=None):
        nN = (outF + 511) // 512
        for j in range(nN):
            n0j, n1j = j * 512, min(outF, (j + 1) * 512)
            for kc in range(NCH):
                stgb, stgq = _stg()
                q.dma(stgq, lambda s, kc=kc, n0=n0j, n1=n1j, stgb=stgb: s.dma_start(
                    out=stgb[:, 0:n1 - n0],
                    in_=Wd.ap()[kc * 128:(kc + 1) * 128, n0:n1]))
                q.c("tensor", lambda t, kc=kc, n0=n0j, n1=n1j, j=j, stgb=stgb: t.matmul(
                    _ureg(j)[:, 0:n1 - n0],
                    lnb2[:, 2 * kc:2 * kc + 2], stgb[:, 0:n1 - n0],
                    start=(kc == 0), stop=(kc == NCH - 1)))
                q.c("vector", lambda v, kc=kc, n0=n0j, n1=n1j, stgb=stgb: v.tensor_scalar(
                    wt_sb[:, kc * outF + n0:kc * outF + n1], stgb[:, 0:n1 - n0],
                    scale_fm[:, kc:kc + 1], None, ALU.mult))
        for j in range(nN):
            n0, n1 = j * 512, min(outF, (j + 1) * 512)
            if badd is not None:
                q.dma("sync", lambda s, n0=n0, n1=n1: s.dma_start(
                    out=wst[0:2, 0:n1 - n0],
                    in_=badd.ap()[0:1, n0:n1].broadcast_to([2, n1 - n0])))
            if uscale is not None:
                q.c("vector", lambda v, j=j, n0=n0, n1=n1: v.tensor_scalar_mul(
                    r1_sb[0:2, n0:n1], _ureg(j)[:, 0:n1 - n0], uscale))
            elif badd is not None:
                q.c("vector", lambda v, j=j, n0=n0, n1=n1: v.tensor_add(
                    r1_sb[0:2, n0:n1], _ureg(j)[:, 0:n1 - n0], wst[0:2, 0:n1 - n0]))
            else:
                q.c("vector", lambda v, j=j, n0=n0, n1=n1: v.tensor_copy(
                    r1_sb[0:2, n0:n1], _ureg(j)[:, 0:n1 - n0]))
        for j in range(nN):
            n0, n1 = j * 512, min(outF, (j + 1) * 512)
            for kc in range(NCH):
                q.c("tensor", lambda t, kc=kc, n0=n0, n1=n1: t.matmul(
                    psB[0:1, 0:n1 - n0], onescb[:, 0:1],
                    wt_sb[:, kc * outF + n0:kc * outF + n1],
                    start=(kc == 0), stop=(kc == NCH - 1)))
            q.c("vector", lambda v, n0=n0, n1=n1: v.tensor_copy(
                r1_sb[0:1, n0:n1], psB[0:1, 0:n1 - n0]))

    prep_scaled(Wq, 512, wtq, r1q, lnwsfm, uscale=cfg.DKS)
    prep_scaled(Wk, 512, wtk, r1k, lnwfm)
    prep_scaled(Wv, 1024, wtv, r1v, lnwfm)
    prep_scaled(Wg, 1024, wtg, r1g, lnwfm)
    prep_scaled(gkw1, cfg.LR, wgk1, r1gk, lnwfm)
    prep_scaled(W1, cfg.FI, wt1, r1w1, lnwfm, badd=b1i)

    for kc in range(NCH):
        wcb = xf[:, 0:1024] if kc % 2 == 0 else xf[:, 1024:2048]
        wcq = "sync" if kc % 2 == 0 else "scalar"
        q.dma(wcq, lambda s, kc=kc, wcb=wcb: s.dma_start(
            out=wcb, in_=Wo.ap()[kc * 128:(kc + 1) * 128, :]))
        q.c("vector", lambda v, kc=kc, wcb=wcb: v.tensor_copy(
            wto[:, kc * 1024:(kc + 1) * 1024], wcb))
        q.dma("sync", lambda s, kc=kc: s.dma_start(
            out=wgw[:, kc * E:(kc + 1) * E], in_=gw.ap()[kc * 128:(kc + 1) * 128, :]))
    for kc in range(cfg.FICH):
        wcb = xf[:, 0:1024] if kc % 2 == 0 else xf[:, 1024:2048]
        wcq = "sync" if kc % 2 == 0 else "scalar"
        q.dma(wcq, lambda s, kc=kc, wcb=wcb: s.dma_start(
            out=wcb, in_=W2.ap()[kc * 128:(kc + 1) * 128, :]))
        q.c("vector", lambda v, kc=kc, wcb=wcb: v.tensor_copy(
            wt2[:, kc * 1024:(kc + 1) * 1024], wcb))
    q.dma("sync", lambda s: s.dma_start(out=wst[0:cfg.LR, 0:512], in_=gkw2.ap()))
    q.dma("sync", lambda s: s.dma_start(
        out=wst[cfg.LR:cfg.LR + 1, 0:512], in_=gk_b2.ap()))
    q.c("vector", lambda v: v.tensor_copy(
        wgk2[0:cfg.LR + 1, :], wst[0:cfg.LR + 1, 0:512]))
    q.dma("sync", lambda s: s.dma_start(
        out=w2bf[:, :], in_=b2i.ap().rearrange("o (c p) -> (o p) c", p=128),
        allow_slow_non_contiguous=True))

    q.dma("sync", lambda s: s.dma_start(
        out=wst[:, 0:E * NCH].rearrange("p (c e) -> p c e", c=NCH),
        in_=gw.ap().rearrange("(c p) e -> p c e", p=128)))
    for kc in range(NCH):
        q.c("tensor", lambda t, kc=kc: t.matmul(
            psA[0:1, 0:E], onescf[:, 0:1],
            wst[:, 0:E * NCH].rearrange("p (c e) -> p c e", c=NCH)[:, kc, :],
            start=(kc == 0), stop=(kc == NCH - 1)))
    q.c("vector", lambda v: v.tensor_copy(wst[0:1, 0:E], psA[0:1, 0:E]))
    q.dma("sync", lambda s: s.dma_start(out=srow.ap()[0:1, 0:E], in_=wst[0:1, 0:E]))
    q.dma("sync", lambda s: s.dma_start(
        out=csr[:, :], in_=srow.ap()[0:1, 0:E].broadcast_to([128, E])))

    free_from("wst")

    t1v = sc8[:, :].bitcast(BF16)
    gsc = sc8[:, 0:4 * TC]
    assert 4 * TC >= 1024
    aot = gka
    l2t = o2
    xf3 = xf[:, :].rearrange("p (c t) -> p c t", c=NCH)
    xm3 = xm[:, :].rearrange("p (c t) -> p c t", c=NCH)
    xa3 = xa[:, :].rearrange("p (c t) -> p c t", c=NCH)
    qf3 = qf[:, :].rearrange("p (c t) -> p c t", c=4)
    kf3 = kf[:, :].rearrange("p (c t) -> p c t", c=4)
    gka4 = gka[:, :].rearrange("p (c g u) -> p c g u", c=4, g=NGC)
    gsc4 = gsc.rearrange("p (c g u) -> p c g u", c=4, g=NGC)
    eq4 = eq[:, :].rearrange("p (c g u) -> p c g u", c=4, g=NGC)
    bmat3 = bmat[:, :].rearrange("p (c g) -> p c g", c=4)
    S3 = Ssb[:, :].rearrange("p (c v) -> p c v", c=4)
    Sbp3 = Sbp[:, :].rearrange("p (c v) -> p c v", c=4)
    Sbm3 = Sbm[:, :].rearrange("p (c v) -> p c v", c=4)
    o2f3 = o2f[:, :].rearrange("p (c t) -> p c t", c=NCH)
    t1v3 = t1v[:, 0:cfg.FICH * TC].rearrange("p (c t) -> p c t", c=cfg.FICH)

    def strow(r, s=0):
        return st8[r:r + 1, s * TC:(s + 1) * TC]

    chunk_marks = []
    for seq_i in range(cfg.SEQ):
        q.c("vector", lambda v: v.memset(Ssb[:, :], 0.0))
        q.c("vector", lambda v: v.memset(acar[:, :], 0.0))
        for chn in range(cfg.T // TC):
            t0 = seq_i * cfg.T + chn * TC
            _mk = {"cs": len(q.steps)}
            chunk_marks.append(_mk)
            agb = (chn % 2) * TPT * 16

            q.dma("sync", lambda s, t0=t0: s.dma_start(
                out=xf3, in_=xT.ap().rearrange("(c p) n -> p c n", p=128)[:, :, t0:t0 + TC]))

            q.c("vector", lambda v: v.tensor_mul(sc8[:, 0:NCH * TC], xf[:, :], xf[:, :]))
            for kc in range(NCH):
                q.c("tensor", lambda t, kc=kc: t.matmul(
                    psS[0:1, 0:TC], onescf[:, 0:1], xf3[:, kc, :],
                    start=(kc == 0), stop=(kc == NCH - 1)))
            for kc in range(NCH):
                q.c("tensor", lambda t, kc=kc: t.matmul(
                    psS[0:1, TC:2 * TC], onescf[:, 0:1],
                    sc8[:, 0:NCH * TC].rearrange("p (c t) -> p c t", c=NCH)[:, kc, :],
                    start=(kc == 0), stop=(kc == NCH - 1)))
            q.c("vector", lambda v: v.tensor_scalar_mul(
                strow(0, 3), psS[0:1, 0:TC], 1.0 / D))
            q.c("vector", lambda v: v.tensor_scalar_mul(
                strow(0, 1), psS[0:1, TC:2 * TC], 1.0 / D))
            q.c("vector", lambda v: v.tensor_mul(strow(0, 0), strow(0, 3), strow(0, 3)))
            q.c("vector", lambda v: v.tensor_sub(strow(0, 1), strow(0, 1), strow(0, 0)))
            q.c("vector", lambda v: v.tensor_scalar_add(strow(0, 0), strow(0, 1), EPS))
            q.c("scalar", lambda s: s.activation(strow(0, 2), strow(0, 0), AF.Sqrt))
            q.c("vector", lambda v: v.reciprocal(strow(0, 2), strow(0, 2)))
            q.c("vector", lambda v: v.reciprocal(strow(0, 0), strow(0, 0)))
            q.c("vector", lambda v: v.tensor_mul(strow(0, 1), strow(0, 1), strow(0, 0)))
            q.c("vector", lambda v: v.tensor_scalar_add(strow(0, 1), strow(0, 1), EPS))
            q.c("scalar", lambda s: s.activation(strow(0, 1), strow(0, 1), AF.Sqrt))
            q.c("vector", lambda v: v.reciprocal(strow(0, 1), strow(0, 1)))
            q.c("vector", lambda v: v.tensor_mul(strow(0, 1), strow(0, 1), strow(0, 2)))
            q.c("vector", lambda v: v.tensor_mul(strow(0, 0), strow(0, 1), strow(0, 3)))
            q.c("vector", lambda v: v.tensor_mul(strow(0, 3), strow(0, 3), strow(0, 2)))
            q.c("vector", lambda v: v.tensor_scalar_mul(strow(0, 0), strow(0, 0), -1.0))
            q.c("vector", lambda v: v.tensor_copy(r1p[0:1, :], strow(0, 0)))

            q.c("tensor", lambda t: t.matmul(
                psBig[:, 0:TC], onesr8[0:1, :], strow(0, 1), start=True, stop=True))
            q.c("vector", lambda v: v.tensor_tensor(
                xa3, xf3, psBig[:, 0:TC].unsqueeze(1).broadcast_to([128, NCH, TC]),
                ALU.mult))
            _mk["se"] = len(q.steps)

            for kc in range(NCH):
                q.c("tensor", lambda t, kc=kc: t.matmul(
                    psA[0:cfg.LR, 0:TC], wgk1[:, kc * cfg.LR:(kc + 1) * cfg.LR],
                    xa3[:, kc, :], start=(kc == 0), stop=False))
            q.c("tensor", lambda t: t.matmul(
                psA[0:cfg.LR, 0:TC], r1gk[:, :], r1p[:, :], start=False, stop=True))
            q.c("vector", lambda v: v.memset(pa[0:cfg.LR + 1, 0:TC], 1.0))
            q.c("vector", lambda v: v.tensor_copy(pa[0:cfg.LR, 0:TC], psA[0:cfg.LR, 0:TC]))
            for mc in range(4):
                q.c("tensor", lambda t, mc=mc: t.matmul(
                    psBig[:, mc * TC:(mc + 1) * TC], wgk2[:, mc * 128:(mc + 1) * 128],
                    pa[0:cfg.LR + 1, 0:TC], start=True, stop=True))
            q.c("scalar", lambda s: s.activation(gsc, psBig[:, 0:4 * TC], AF.Sigmoid))
            q.c("scalar", lambda s: s.activation(gka[:, :], gsc, AF.Ln))
            q.c("vector", lambda v: v.tensor_scalar_mul(gsc, gka[:, :], 1.0 / cfg.GATE_NORM))
            for fc in range(4):
                q.c("vector", lambda v, fc=fc: v.tensor_tensor_scan(
                    gka[:, :].rearrange("p (c t) -> p c t", c=4)[:, fc, :],
                    gsc.rearrange("p (c t) -> p c t", c=4)[:, fc, :],
                    zrow[:, 0:TC], acar[:, fc:fc + 1], ALU.add, ALU.add))
            q.c("vector", lambda v: v.tensor_copy(bmat3[:, :, 0:1], acar[:, :].unsqueeze(2)))
            for g in range(1, NGC):
                q.c("vector", lambda v, g=g: v.tensor_copy(
                    bmat3[:, :, g:g + 1],
                    gka[:, :].rearrange("p (c t) -> p c t", c=4)[:, :, g * GC - 1:g * GC]))
            q.c("vector", lambda v: v.tensor_copy(
                acar[:, :],
                gka[:, :].rearrange("p (c t) -> p c t", c=4)[:, :, TC - 1:TC].squeeze(2)))
            q.c("vector", lambda v: v.tensor_sub(
                gsc4, gka4, bmat3.unsqueeze(3).broadcast_to([128, 4, NGC, GC])))
            q.c("scalar", lambda s: s.activation(eq[:, :], gsc, AF.Exp))
            q.c("scalar", lambda s: s.activation(ek[:, :], gsc, AF.Exp, scale=-1.0))
            _mk["ge"] = len(q.steps)

            for dest, wt_sb, r1_sb, fac in ((qf, wtq, r1q, eq), (kf, wtk, r1k, ek)):
                for mc in range(4):
                    for kc in range(NCH):
                        q.c("tensor", lambda t, mc=mc, kc=kc, wt_sb=wt_sb: t.matmul(
                            psBig[:, mc * TC:(mc + 1) * TC],
                            wt_sb[:, kc * 512 + mc * 128:kc * 512 + (mc + 1) * 128],
                            xa3[:, kc, :], start=(kc == 0), stop=False))
                    q.c("tensor", lambda t, mc=mc, r1_sb=r1_sb: t.matmul(
                        psBig[:, mc * TC:(mc + 1) * TC],
                        r1_sb[:, mc * 128:(mc + 1) * 128], r1p[:, :],
                        start=False, stop=True))
                for mc in range(4):
                    q.c("vector", lambda v, mc=mc, dest=dest, fac=fac: v.tensor_tensor(
                        dest[:, mc * TC:(mc + 1) * TC], psBig[:, mc * TC:(mc + 1) * TC],
                        fac[:, mc * TC:(mc + 1) * TC], ALU.mult))

            for j in range(TPT):
                for kc in range(NCH):
                    q.c("tensor", lambda t, j=j, kc=kc: t.matmul(
                        psA[:, 0:E], xf3[:, kc, j * 128:(j + 1) * 128],
                        wgw[:, kc * E:(kc + 1) * E],
                        start=(kc == 0), stop=(kc == NCH - 1)))
                q.c("vector", lambda v, j=j, agb=agb: v.tensor_copy(
                    agw[:, agb + j * 16:agb + j * 16 + 8], psA[:, 0:E]))

            _mk["xe"] = len(q.steps)
            for j in range(TPT):
                ce, co = 2 * j, 2 * j + 1
                tcol = j * 128

                for half, wt_sb, r1_sb in ((0, wtv, r1v), (1, wtg, r1g)):
                    for nh in range(2):
                        for kc in range(NCH):
                            q.c("tensor", lambda t, kc=kc, nh=nh, tcol=tcol, wt_sb=wt_sb: t.matmul(
                                psO[:, nh * 512:(nh + 1) * 512],
                                xa3[:, kc, tcol:tcol + 128],
                                wt_sb[:, kc * 1024 + nh * 512:kc * 1024 + (nh + 1) * 512],
                                start=(kc == 0), stop=False))
                        q.c("tensor", lambda t, nh=nh, tcol=tcol, r1_sb=r1_sb: t.matmul(
                            psO[:, nh * 512:(nh + 1) * 512],
                            r1p[:, tcol:tcol + 128], r1_sb[:, nh * 512:(nh + 1) * 512],
                            start=False, stop=True))
                    if half == 0:
                        q.c("vector", lambda v: v.tensor_copy(vt[:, :], psO[:, :]))
                    else:
                        q.c("scalar", lambda s: s.activation(sg[:, :], psO[:, :], AF.Silu))

                q.c("vector", lambda v, ce=ce: v.tensor_copy(
                    ach[:, :], eq4[:, :, ce, GC - 1:GC].squeeze(2)))

                for fc in range(4):
                    q.c("tensor", lambda t, fc=fc, tcol=tcol: t.transpose(
                        psT[:, fc * 128:(fc + 1) * 128],
                        kf3[:, fc, tcol:tcol + 128], idb[:, :]))
                q.c("vector", lambda v: v.tensor_copy(ktm[:, 0:512], psT[:, 0:512]))

                q.c("vector", lambda v: v.tensor_copy(Sbp[:, :], Ssb[:, :]))
                for h in range(H):
                    rb = (h % 4) * 32
                    fch = h // 4
                    q.c("tensor", lambda t, h=h, rb=rb, fch=fch: t.matmul(
                        psA[rb:rb + 32, fch * DV:(fch + 1) * DV],
                        ktm[0:64, fch * 128 + rb:fch * 128 + rb + 32],
                        vt[0:64, h * DV:(h + 1) * DV],
                        start=True, stop=True, tile_position=(0, rb)))
                q.c("vector", lambda v: v.tensor_add(Ssb[:, :], Ssb[:, :], psA[:, 0:4 * DV]))
                q.c("vector", lambda v: v.tensor_tensor(
                    S3, S3, ach[:, :].unsqueeze(2).broadcast_to([128, 4, DV]), ALU.mult))
                q.c("vector", lambda v: v.tensor_copy(Sbm[:, :], Ssb[:, :]))
                for h in range(H):
                    rb = (h % 4) * 32
                    fch = h // 4
                    q.c("tensor", lambda t, h=h, rb=rb, fch=fch: t.matmul(
                        psA[rb:rb + 32, fch * DV:(fch + 1) * DV],
                        ktm[64:128, fch * 128 + rb:fch * 128 + rb + 32],
                        vt[64:128, h * DV:(h + 1) * DV],
                        start=True, stop=True, tile_position=(64, rb)))
                q.c("vector", lambda v, co=co: v.tensor_copy(
                    ach[:, :], eq4[:, :, co, GC - 1:GC].squeeze(2)))
                q.c("vector", lambda v: v.tensor_add(Ssb[:, :], Ssb[:, :], psA[:, 0:4 * DV]))
                q.c("vector", lambda v: v.tensor_tensor(
                    S3, S3, ach[:, :].unsqueeze(2).broadcast_to([128, 4, DV]), ALU.mult))

                for grp in range(2):
                    for hi in range(8):
                        h = grp * 8 + hi
                        rb = (h % 4) * 32
                        fch = h // 4
                        q.c("tensor", lambda t, hi=hi, rb=rb, fch=fch, ce=ce: t.matmul(
                            psBig[0:64, hi * 128:hi * 128 + 64],
                            kf3[rb:rb + 32, fch, ce * GC:(ce + 1) * GC],
                            qf3[rb:rb + 32, fch, ce * GC:(ce + 1) * GC],
                            start=True, stop=True, tile_position=(rb, 0)))
                        q.c("tensor", lambda t, hi=hi, rb=rb, fch=fch, co=co: t.matmul(
                            psBig[64:128, hi * 128 + 64:hi * 128 + 128],
                            kf3[rb:rb + 32, fch, co * GC:(co + 1) * GC],
                            qf3[rb:rb + 32, fch, co * GC:(co + 1) * GC],
                            start=True, stop=True, tile_position=(rb, 64)))
                    q.c("vector", lambda v: v.memset(
                        pa[64:128, :].rearrange("p (h t) -> p h t", h=8)[:, :, 0:64], 0.0))
                    q.c("vector", lambda v: v.memset(
                        pa[0:64, :].rearrange("p (h t) -> p h t", h=8)[:, :, 64:128], 0.0))
                    q.c("vector", lambda v: v.tensor_tensor(
                        pa[0:64, :].rearrange("p (h t) -> p h t", h=8)[:, :, 0:64],
                        psBig[0:64, :].rearrange("p (h t) -> p h t", h=8)[:, :, 0:64],
                        cmk[0:64, 0:64].unsqueeze(1).broadcast_to([64, 8, 64]), ALU.mult))
                    q.c("vector", lambda v: v.tensor_tensor(
                        pa[64:128, :].rearrange("p (h t) -> p h t", h=8)[:, :, 64:128],
                        psBig[64:128, :].rearrange("p (h t) -> p h t", h=8)[:, :, 64:128],
                        cmk[64:128, 0:64].unsqueeze(1).broadcast_to([64, 8, 64]),
                        ALU.mult))
                    for hi in range(8):
                        h = grp * 8 + hi
                        q.c("tensor", lambda t, h=h, hi=hi: t.matmul(
                            psO[:, h * DV:(h + 1) * DV],
                            pa[:, hi * 128:(hi + 1) * 128],
                            vt[:, h * DV:(h + 1) * DV], start=True, stop=True))

                for h in range(H):
                    rb = (h % 4) * 32
                    fch = h // 4
                    q.c("tensor", lambda t, h=h, rb=rb, fch=fch, ce=ce: t.matmul(
                        psBig[0:64, h * DV:(h + 1) * DV],
                        qf3[rb:rb + 32, fch, ce * GC:(ce + 1) * GC],
                        Sbp3[rb:rb + 32, fch, :],
                        start=True, stop=True, tile_position=(rb, 0)))
                    q.c("tensor", lambda t, h=h, rb=rb, fch=fch, co=co: t.matmul(
                        psBig[64:128, h * DV:(h + 1) * DV],
                        qf3[rb:rb + 32, fch, co * GC:(co + 1) * GC],
                        Sbm3[rb:rb + 32, fch, :],
                        start=True, stop=True, tile_position=(rb, 64)))
                q.c("vector", lambda v: v.tensor_copy(aot[:, 0:1024], psBig[:, :]))
                q.c("vector", lambda v: v.tensor_add(aot[:, 0:1024], aot[:, 0:1024], psO[:, :]))

                q.c("vector", lambda v: v.tensor_mul(
                    sc8[:, 1024:1024 + 1024], aot[:, 0:1024], aot[:, 0:1024]))
                q.c("vector", lambda v: v.tensor_reduce(
                    bmat[:, 0:H].unsqueeze(2),
                    sc8[:, 1024:1024 + 1024].rearrange("p (h v) -> p h v", h=H),
                    AX.X, ALU.add))
                q.c("vector", lambda v: v.tensor_scalar(
                    bmat[:, 0:H], bmat[:, 0:H], 1.0 / DV, EPS, ALU.mult, ALU.add))
                q.c("scalar", lambda s: s.activation(bmat[:, 0:H], bmat[:, 0:H], AF.Sqrt))
                q.c("vector", lambda v: v.reciprocal(bmat[:, 0:H], bmat[:, 0:H]))
                q.c("vector", lambda v: v.tensor_tensor(
                    sg[:, :].rearrange("p (h v) -> p h v", h=H),
                    sg[:, :].rearrange("p (h v) -> p h v", h=H),
                    rmsr[:, :].unsqueeze(1).broadcast_to([128, H, DV]), ALU.mult))
                q.c("vector", lambda v: v.tensor_tensor(
                    sg[:, :].rearrange("p (h v) -> p h v", h=H),
                    sg[:, :].rearrange("p (h v) -> p h v", h=H),
                    bmat[:, 0:H].unsqueeze(2).broadcast_to([128, H, DV]), ALU.mult))
                q.c("vector", lambda v: v.tensor_mul(o2[:, :], aot[:, 0:1024], sg[:, :]))
                for fc in range(NCH):
                    q.c("tensor", lambda t, fc=fc: t.transpose(
                        psT[:, fc * 128:(fc + 1) * 128], o2[:, fc * 128:(fc + 1) * 128],
                        idb[:, :]))
                q.c("vector", lambda v, tcol=tcol: v.tensor_copy(
                    o2f3[:, :, tcol:tcol + 128],
                    psT[:, :].rearrange("p (c t) -> p c t", c=NCH)[:, :, 0:128]))

            q.c("tensor", lambda t: t.matmul(
                psBig[:, 0:TC], onesr8[0:1, :], strow(0, 2), start=True, stop=True))
            q.c("vector", lambda v: v.tensor_tensor(
                xm3, xf3, psBig[:, 0:TC].unsqueeze(1).broadcast_to([128, NCH, TC]),
                ALU.mult))
            q.c("tensor", lambda t: t.matmul(
                psBig[:, 0:TC], onesr8[0:1, :], strow(0, 3), start=True, stop=True))
            q.c("vector", lambda v: v.tensor_tensor(
                xm3, xm3, psBig[:, 0:TC].unsqueeze(1).broadcast_to([128, NCH, TC]),
                ALU.subtract))
            for mc in range(NCH):
                pd = psA if mc % 2 == 0 else psB
                for kc in range(NCH):
                    q.c("tensor", lambda t, mc=mc, kc=kc, pd=pd: t.matmul(
                        pd[:, 0:TC],
                        wto[:, kc * 1024 + mc * 128:kc * 1024 + (mc + 1) * 128],
                        o2f3[:, kc, :], start=(kc == 0), stop=(kc == NCH - 1)))
                q.c("vector", lambda v, mc=mc, pd=pd: v.tensor_add(
                    xm3[:, mc, :], xm3[:, mc, :], pd[:, 0:TC]))

            q.c("vector", lambda v: v.tensor_mul(sc8[:, 0:NCH * TC], xm[:, :], xm[:, :]))
            for kc in range(NCH):
                q.c("tensor", lambda t, kc=kc: t.matmul(
                    psA[0:1, 0:TC], onescf[:, 0:1], xm3[:, kc, :],
                    start=(kc == 0), stop=(kc == NCH - 1)))
            for kc in range(NCH):
                q.c("tensor", lambda t, kc=kc: t.matmul(
                    psB[0:1, 0:TC], onescf[:, 0:1],
                    sc8[:, 0:NCH * TC].rearrange("p (c t) -> p c t", c=NCH)[:, kc, :],
                    start=(kc == 0), stop=(kc == NCH - 1)))
            q.c("vector", lambda v: v.tensor_scalar_mul(strow(0, 0), psA[0:1, 0:TC], 1.0 / D))
            q.c("vector", lambda v: v.tensor_scalar_mul(strow(0, 1), psB[0:1, 0:TC], 1.0 / D))
            q.c("vector", lambda v: v.tensor_mul(strow(0, 2), strow(0, 0), strow(0, 0)))
            q.c("vector", lambda v: v.tensor_sub(strow(0, 1), strow(0, 1), strow(0, 2)))
            q.c("vector", lambda v: v.tensor_scalar_add(strow(0, 3), strow(0, 1), EPS))
            q.c("scalar", lambda s: s.activation(strow(0, 1), strow(0, 3), AF.Sqrt))
            q.c("vector", lambda v: v.reciprocal(strow(0, 2), strow(0, 1)))
            q.c("vector", lambda v: v.tensor_mul(strow(0, 3), strow(0, 2), strow(0, 0)))
            q.c("vector", lambda v: v.tensor_scalar_mul(strow(0, 3), strow(0, 3), -1.0))
            q.c("vector", lambda v: v.tensor_copy(r2p[0:1, :], strow(0, 3)))
            q.c("tensor", lambda t: t.matmul(
                psBig[:, 0:TC], onesr8[0:1, :], strow(0, 2), start=True, stop=True))
            q.c("vector", lambda v: v.tensor_tensor(
                xa3, xm3, psBig[:, 0:TC].unsqueeze(1).broadcast_to([128, NCH, TC]),
                ALU.mult))

            for mc in range(cfg.FICH):
                pd = psA if mc % 2 == 0 else psB
                for kc in range(NCH):
                    q.c("tensor", lambda t, mc=mc, kc=kc, pd=pd: t.matmul(
                        pd[:, 0:TC],
                        wt1[:, kc * cfg.FI + mc * 128:kc * cfg.FI + (mc + 1) * 128],
                        xa3[:, kc, :], start=(kc == 0), stop=False))
                q.c("tensor", lambda t, mc=mc, pd=pd: t.matmul(
                    pd[:, 0:TC], r1w1[:, mc * 128:(mc + 1) * 128], r2p[:, :],
                    start=False, stop=True))
                q.c("scalar", lambda s, mc=mc, pd=pd: s.activation(
                    t1v3[:, mc, :], pd[:, 0:TC], AF.Gelu))
            for mc in range(NCH):
                pd = psA if mc % 2 == 0 else psB
                for kc in range(cfg.FICH):
                    q.c("tensor", lambda t, mc=mc, kc=kc, pd=pd: t.matmul(
                        pd[:, 0:TC],
                        wt2[:, kc * 1024 + mc * 128:kc * 1024 + (mc + 1) * 128],
                        t1v3[:, kc, :], start=(kc == 0),
                        stop=(kc == cfg.FICH - 1)))
                q.c("vector", lambda v, mc=mc, pd=pd: v.scalar_tensor_tensor(
                    xm3[:, mc, :], pd[:, 0:TC], w2bf[:, mc:mc + 1],
                    xm3[:, mc, :], ALU.add, ALU.add))

            _mk["ts"] = len(q.steps)
            _mk["tj"] = []
            att = sc8[:, 1024:2048]
            for j in range(TPT):
                _tj = {"a": len(q.steps)}
                _mk["tj"].append(_tj)
                tt = t0 + j * 128
                tcol = j * 128
                for kc in range(NCH):
                    q.c("tensor", lambda t, kc=kc, tcol=tcol: t.matmul(
                        psA[:, 0:E], xm3[:, kc, tcol:tcol + 128],
                        wgw[:, kc * E:(kc + 1) * E],
                        start=(kc == 0), stop=(kc == NCH - 1)))
                q.c("vector", lambda v, j=j, agb=agb: v.tensor_copy(
                    agw[:, agb + j * 16 + 8:agb + j * 16 + 16], psA[:, 0:E]))
                for fc in range(NCH):
                    q.c("tensor", lambda t, fc=fc, tcol=tcol: t.transpose(
                        psBig[:, fc * 128:(fc + 1) * 128],
                        xm3[:, fc, tcol:tcol + 128], idf[:, :]))
                q.c("vector", lambda v: v.tensor_copy(att, psBig[:, :]))
                _tj["m"] = len(q.steps)
                q.c("vector", lambda v: v.reduce_sum(str2[:, 0:1], att, AX.X))
                q.c("scalar", lambda s: s.activation(
                    psO[:, :], att, AF.Square,
                    accum_out=str2[:, 1:2]))
                q.c("vector", lambda v: v.tensor_scalar_mul(str2[:, 0:1], str2[:, 0:1], 1.0 / D))
                q.c("vector", lambda v: v.tensor_scalar_mul(str2[:, 1:2], str2[:, 1:2], 1.0 / D))
                q.c("vector", lambda v: v.tensor_mul(str2[:, 2:3], str2[:, 0:1], str2[:, 0:1]))
                q.c("vector", lambda v: v.tensor_sub(str2[:, 1:2], str2[:, 1:2], str2[:, 2:3]))
                q.c("vector", lambda v: v.tensor_scalar_add(str2[:, 1:2], str2[:, 1:2], EPS))
                q.c("scalar", lambda s: s.activation(str2[:, 2:3], str2[:, 1:2], AF.Sqrt))
                q.c("vector", lambda v: v.reciprocal(str2[:, 3:4], str2[:, 2:3]))
                q.c("vector", lambda v: v.tensor_mul(str2[:, 4:5], str2[:, 3:4], str2[:, 0:1]))
                q.c("vector", lambda v: v.tensor_scalar_mul(str2[:, 4:5], str2[:, 4:5], -1.0))
                q.dma("sync", lambda s, tt=tt: s.dma_start(
                    out=xt1[:, :], in_=x_tm.ap()[tt:tt + 128, :]))
                q.c("vector", lambda v: v.tensor_scalar(
                    att, att, str2[:, 3:4], str2[:, 4:5],
                    ALU.mult, ALU.add))
                q.c("vector", lambda v: v.tensor_add(xt1[:, :], xt1[:, :], att))
                q.dma("sync", lambda s, tt=tt: s.dma_start(
                    out=x1buf.ap()[tt:tt + 128, :], in_=xt1[:, :]))
                q.c("vector", lambda v: v.reduce_sum(str2[:, 5:6], xt1[:, :], AX.X))
                q.c("scalar", lambda s: s.activation(
                    psO[:, :], xt1[:, :], AF.Square,
                    accum_out=str2[:, 6:7]))
                q.c("vector", lambda v: v.tensor_scalar_mul(str2[:, 5:6], str2[:, 5:6], 1.0 / D))
                q.c("vector", lambda v: v.tensor_scalar_mul(str2[:, 6:7], str2[:, 6:7], 1.0 / D))
                q.c("vector", lambda v: v.tensor_mul(str2[:, 7:8], str2[:, 5:6], str2[:, 5:6]))
                q.c("vector", lambda v: v.tensor_sub(str2[:, 6:7], str2[:, 6:7], str2[:, 7:8]))
                q.c("vector", lambda v: v.tensor_scalar_add(str2[:, 6:7], str2[:, 6:7], EPS))
                q.c("scalar", lambda s: s.activation(str2[:, 7:8], str2[:, 6:7], AF.Sqrt))
                q.c("vector", lambda v: v.reciprocal(str2[:, 8:9], str2[:, 7:8]))
                q.c("vector", lambda v: v.tensor_mul(str2[:, 9:10], str2[:, 8:9], str2[:, 5:6]))
                q.c("vector", lambda v: v.tensor_scalar_mul(str2[:, 9:10], str2[:, 9:10], -1.0))
                q.c("vector", lambda v: v.tensor_scalar(
                    l2t[:, :], xt1[:, :], str2[:, 8:9], str2[:, 9:10],
                    ALU.mult, ALU.add))
                q.dma("sync", lambda s, tt=tt: s.dma_start(
                    out=ln2buf.ap()[tt:tt + 128, :], in_=l2t[:, :]))
                q.c("vector", lambda v, j=j, agb=agb: v.tensor_scalar(
                    tbm[:, 0:E], agw[:, agb + j * 16 + 8:agb + j * 16 + 16],
                    str2[:, 3:4], None, ALU.mult))
                q.c("vector", lambda v, j=j, agb=agb: v.tensor_add(
                    tbm[:, 0:E], tbm[:, 0:E], agw[:, agb + j * 16:agb + j * 16 + 8]))
                q.c("vector", lambda v: v.scalar_tensor_tensor(
                    tbm[:, 8:8 + E], csr[:, :], str2[:, 4:5], tbm[:, 0:E],
                    ALU.mult, ALU.add))
                q.c("vector", lambda v: v.tensor_scalar(
                    tbm[:, 8:8 + E], tbm[:, 8:8 + E], str2[:, 8:9], None, ALU.mult))
                q.c("vector", lambda v: v.scalar_tensor_tensor(
                    tbm[:, 0:E], csr[:, :], str2[:, 9:10], tbm[:, 8:8 + E],
                    ALU.mult, ALU.add))
                q.c("vector", lambda v: v.tensor_add(tbm[:, 0:E], tbm[:, 0:E], gbr[:, :]))
                q.c("tensor", lambda t: t.transpose(psA[0:8, 0:128], tbm[:, 0:E], idf[:, :]))
                q.c("vector", lambda v: v.tensor_copy(lgt[:, :], psA[0:8, 0:128]))
                q.dma("sync", lambda s, tt=tt: s.dma_start(
                    out=logitsT.ap()[:, tt:tt + 128], in_=lgt[:, :]))
                _tj["e"] = len(q.steps)
            _mk["ce"] = len(q.steps)

    _new = q.steps[:chunk_marks[0]["cs"]]
    _pend = None
    _prev = chunk_marks[0]["cs"]
    for _m in chunk_marks:
        _new += q.steps[_prev:_m["cs"]]
        _new += q.steps[_m["cs"]:_m["se"]]
        if _pend:
            _new += _pend[0]
        _new += q.steps[_m["se"]:_m["ge"]]
        if _pend:
            _new += _pend[1]
            _new += _pend[2]
        _new += q.steps[_m["ge"]:_m["xe"]]
        if _pend:
            _new += _pend[3]
        _new += q.steps[_m["xe"]:_m["ts"]]
        tj0, tj1 = _m["tj"]
        _pend = (q.steps[tj0["a"]:tj0["m"]], q.steps[tj0["m"]:tj0["e"]],
                 q.steps[tj1["a"]:tj1["m"]], q.steps[tj1["m"]:tj1["e"]])
        _prev = _m["ce"]
    for _p in _pend:
        _new += _p
    _new += q.steps[_prev:]
    assert len(_new) == len(q.steps), (len(_new), len(q.steps))
    q.steps = _new

    if milestones is not None:
        milestones['phaseA_end'] = len(q.steps)
    free_from("xf")

    slotc = sb("slotc", [128, NTILE], I32)
    gpc = sb("gpc", [128, NTILE], F32)
    lgs = sb("lgs", [128, E * PERP], F32)
    rt1 = sb("rt1", [128, E * PERP], F32)
    rt2 = sb("rt2", [128, E * PERP], F32)
    rt3 = sb("rt3", [128, PERP], F32)
    rt4 = sb("rt4", [128, PERP], F32)
    rt5 = sb("rt5", [128, PERP], F32)
    emx = sb("emx", [128, PERP], F32)
    gpv = sb("gpv", [128, PERP], F32)
    tot8 = sb("tot8", [128, E], F32)
    off8 = sb("off8", [128, E], F32)
    sltr = sb("sltr", [128, 128], F32)
    idx32 = sb("idx32", [128, 1], I32)
    idxf = sb("idxf", [128, 1], F32)
    cid = sb("cid", [128, 1], F32)
    l2s = sb("l2s", [128, D], BF16)

    lgs3 = lgs[:, :].rearrange("p (e t) -> p e t", e=E)
    rt13 = rt1[:, :].rearrange("p (e t) -> p e t", e=E)
    rt23 = rt2[:, :].rearrange("p (e t) -> p e t", e=E)

    q.dma("sync", lambda s: s.dma_start(out=sltr[:, :], in_=sltri.ap()))
    if milestones is not None:
        milestones['pre_ag1'] = len(q.steps)
    q.c("gpsimd", lambda g: g.collective_compute(
        "AllGather", ALU.bypass, replica_groups=[list(range(cfg.NCORES))],
        ins=[logitsT.ap().opt()], outs=[logits_all.ap().opt()]))
    if milestones is not None:
        milestones['post_ag1'] = len(q.steps)
    for c in range(cfg.NCORES):
        q.dma("sync", lambda s, c=c: s.dma_start(
            out=lgs3[c * 16:(c + 1) * 16, :, :],
            in_=logits_all.ap()[c * E:(c + 1) * E, :].rearrange(
                "e (pp t) -> pp e t", pp=16)))
    q.c("vector", lambda v: v.tensor_copy(emx[:, :], lgs3[:, 0, :]))
    for e in range(1, E):
        q.c("vector", lambda v, e=e: v.tensor_max(emx[:, :], emx[:, :], lgs3[:, e, :]))
    q.c("vector", lambda v: v.tensor_tensor(
        rt13, lgs3, emx[:, :].unsqueeze(1).broadcast_to([128, E, PERP]), ALU.subtract))
    q.c("scalar", lambda s: s.activation(rt2[:, :], rt1[:, :], AF.Exp))
    q.c("vector", lambda v: v.tensor_copy(gpv[:, :], rt23[:, 0, :]))
    for e in range(1, E):
        q.c("vector", lambda v, e=e: v.tensor_add(gpv[:, :], gpv[:, :], rt23[:, e, :]))
    q.c("vector", lambda v: v.reciprocal(gpv[:, :], gpv[:, :]))
    q.c("vector", lambda v: v.tensor_tensor(
        rt13, lgs3, emx[:, :].unsqueeze(1).broadcast_to([128, E, PERP]), ALU.is_ge))
    q.c("vector", lambda v: v.memset(rt3[:, :], 1e6))
    for e in range(E):
        q.c("vector", lambda v, e=e: v.tensor_scalar(
            rt4[:, :], rt13[:, e, :], -1e6, 1e6 + e, ALU.mult, ALU.add))
        q.c("vector", lambda v: v.tensor_tensor(rt3[:, :], rt3[:, :], rt4[:, :], ALU.min))
    q.c("vector", lambda v: v.tensor_copy(emx[:, :], rt3[:, :]))
    for e in range(E):
        q.c("vector", lambda v, e=e: v.tensor_single_scalar(
            rt13[:, e, :], emx[:, :], float(e), ALU.is_equal))
        q.c("vector", lambda v, e=e: v.tensor_tensor_scan(
            rt23[:, e, :], rt13[:, e, :], zrow[:, 0:PERP], 0.0, ALU.add, ALU.add))
    q.c("vector", lambda v: v.tensor_copy(tot8[:, :], rt23[:, :, PERP - 1]))
    q.c("tensor", lambda t: t.matmul(
        psA[0:8, 0:128], tot8[:, :], sltr[:, :], start=True, stop=True))
    q.c("vector", lambda v: v.tensor_copy(sltr[0:8, 0:128], psA[0:8, 0:128]))
    q.c("tensor", lambda t: t.transpose(psA[:, 0:8], sltr[0:8, 0:128], idf[0:8, 0:8]))
    q.c("vector", lambda v: v.tensor_copy(off8[:, :], psA[:, 0:8]))
    q.c("vector", lambda v: v.memset(rt3[:, :], 0.0))
    q.c("vector", lambda v: v.memset(rt4[:, :], 0.0))
    for e in range(E):
        q.c("vector", lambda v, e=e: v.scalar_tensor_tensor(
            rt3[:, :], rt13[:, e, :], off8[:, e:e + 1], rt3[:, :], ALU.mult, ALU.add))
        q.c("vector", lambda v, e=e: v.tensor_mul(rt5[:, :], rt13[:, e, :], rt23[:, e, :]))
        q.c("vector", lambda v: v.tensor_add(rt4[:, :], rt4[:, :], rt5[:, :]))
    q.c("vector", lambda v: v.tensor_add(rt3[:, :], rt3[:, :], rt4[:, :]))
    q.c("vector", lambda v: v.tensor_scalar_add(rt3[:, :], rt3[:, :], -1.0))
    q.c("vector", lambda v: v.tensor_single_scalar(
        rt4[:, :], rt3[:, :], float(cfg.CAP - 1), ALU.is_le))
    q.c("vector", lambda v: v.tensor_scalar(
        rt5[:, :], emx[:, :], float(cfg.CAP), None, ALU.mult))
    q.c("vector", lambda v: v.tensor_add(rt5[:, :], rt5[:, :], rt3[:, :]))
    q.c("vector", lambda v: v.tensor_mul(rt5[:, :], rt5[:, :], rt4[:, :]))
    q.c("vector", lambda v: v.tensor_scalar(
        rt4[:, :], rt4[:, :], -float(cfg.NSLOT), float(cfg.NSLOT), ALU.mult, ALU.add))
    q.c("vector", lambda v: v.tensor_add(rt5[:, :], rt5[:, :], rt4[:, :]))

    q.dma("sync", lambda s: s.dma_start(
        out=slotall.ap().rearrange("a w -> (a w)").rearrange("(p t) -> p t", p=128),
        in_=rt5[:, :]))
    q.dma("sync", lambda s: s.dma_start(
        out=gpall.ap().rearrange("a w -> (a w)").rearrange("(p t) -> p t", p=128),
        in_=gpv[:, :]))
    q.dma("sync", lambda s: s.dma_start(
        out=cid[:, :], in_=coreid.ap().broadcast_to([128, 1])))
    q.c("gpsimd", lambda g: g.iota(idxf[:, :], pattern=[[0, 1]], base=0,
                                   channel_multiplier=1,
                                   allow_small_or_imprecise_dtypes=True))
    q.c("vector", lambda v: v.tensor_single_scalar(cid[:, :], cid[:, :], float(NTILE), ALU.mult))
    q.c("vector", lambda v: v.tensor_add(idxf[:, :], idxf[:, :], cid[:, :]))
    q.c("vector", lambda v: v.tensor_copy(idx32[:, :], idxf[:, :]))
    q.dma("gpsimd", lambda g: g.indirect_dma_start(
        sltr[0:NTILE, :], None, slotall.ap(),
        IndirectOffsetOnAxis(idx32[0:NTILE, 0:1], 0)))
    if debug_out:
        sltrdbg = nc.dram_tensor("sltrdbg", [128, 128], F32,
                                 kind="ExternalOutput")
        idxdbg = nc.dram_tensor("idxdbg", [128, 2], F32, kind="ExternalOutput")
        q.dma("sync", lambda s: s.dma_start(out=sltrdbg.ap(), in_=sltr[:, :]))
        q.c("vector", lambda v: v.tensor_copy(rt3[:, 0:1], idx32[:, :]))
        q.c("vector", lambda v: v.tensor_copy(rt3[:, 1:2], idxf[:, :]))
        q.dma("sync", lambda s: s.dma_start(out=idxdbg.ap(), in_=rt3[:, 0:2]))
    q.c("tensor", lambda t: t.transpose(
        psA[:, 0:NTILE], sltr[0:NTILE, :], idf[0:NTILE, 0:NTILE]))
    q.c("vector", lambda v: v.tensor_copy(slotc[:, :], psA[:, 0:NTILE]))
    q.dma("gpsimd", lambda g: g.indirect_dma_start(
        sltr[0:NTILE, :], None, gpall.ap(),
        IndirectOffsetOnAxis(idx32[0:NTILE, 0:1], 0)))
    q.c("tensor", lambda t: t.transpose(
        psA[:, 0:NTILE], sltr[0:NTILE, :], idf[0:NTILE, 0:NTILE]))
    q.c("vector", lambda v: v.tensor_copy(gpc[:, :], psA[:, 0:NTILE]))

    for k in range(NTILE):
        q.dma("sync", lambda s, k=k: s.dma_start(
            out=l2s[:, :], in_=ln2buf.ap()[k * 128:(k + 1) * 128, :]))
        q.dma("gpsimd", lambda g, k=k: g.indirect_dma_start(
            slotbuf.ap(), IndirectOffsetOnAxis(slotc[:, k:k + 1], 0),
            l2s[:, :], None))

    if milestones is not None:
        milestones['pre_rs'] = len(q.steps)
    q.c("gpsimd", lambda g: g.collective_compute(
        "ReduceScatter", ALU.add, replica_groups=[list(range(cfg.NCORES))],
        ins=[slotbuf.ap()[0:cfg.NSLOT, :].opt()], outs=[rsout.ap().opt()]))
    if milestones is not None:
        milestones['post_rs'] = len(q.steps)

    free_from("lgs")

    onesb = sb("onesb", [1, 1024], BF16)
    q.c("vector", lambda v: v.memset(onesb[:, :], 1.0))
    xfm = sb("xfm", [128, NCH * cfg.CAP], BF16)
    xtile = sb("xtile", [128, D], BF16)
    estg = sb("estg", [128, 1024], F32)
    estg2 = sb("estg2", [128, 1024], F32)
    _esp = [0]
    def _estg():
        _esp[0] ^= 1
        return (estg, "sync") if _esp[0] else (estg2, "scalar")

    hid = sb("hid", [128, cfg.FCH * cfg.CAP], BF16)
    b1r = sb("b1r", [1, cfg.F], BF16)
    eb2r = sb("eb2r", [128, D], F32)
    oev = sb("oev", [128, D], F32)
    xfm3 = xfm[:, :].rearrange("p (c n) -> p c n", c=NCH)
    hid3 = hid[:, :].rearrange("p (c n) -> p c n", c=cfg.FCH)

    _ch = min(1024, cfg.F)
    for j in range(cfg.F // _ch):
        q.dma("sync", lambda s, j=j, _ch=_ch: s.dma_start(
            out=estg[0:1, 0:_ch], in_=eb1.ap()[0:1, j * _ch:(j + 1) * _ch]))
        q.c("vector", lambda v, j=j, _ch=_ch: v.tensor_copy(
            b1r[0:1, j * _ch:(j + 1) * _ch], estg[0:1, 0:_ch]))
    q.dma("sync", lambda s: s.dma_start(out=eb2r[:, :], in_=eb2.ap().broadcast_to([128, D])))

    for k in range(cfg.NXT):
        q.dma("sync", lambda s, k=k: s.dma_start(
            out=xtile[:, :], in_=rsout.ap()[k * 128:(k + 1) * 128, :]))
        for fc in range(NCH):
            q.c("tensor", lambda t, fc=fc: t.transpose(
                psT[:, fc * 128:(fc + 1) * 128], xtile[:, fc * 128:(fc + 1) * 128],
                idb[:, :]))
        q.c("vector", lambda v, k=k: v.tensor_copy(
            xfm3[:, :, k * 128:(k + 1) * 128],
            psT[:, :].rearrange("p (c n) -> p c n", c=NCH)[:, :, 0:128]))

    w1s = sb("w1s", [128, NCH * cfg.F], BF16)
    for kc in range(NCH):
        for j in range(cfg.F // _ch):
            eb_, eq_ = _estg()
            q.dma(eq_, lambda s, kc=kc, j=j, _ch=_ch, eb_=eb_: s.dma_start(
                out=eb_[:, 0:_ch],
                in_=ew1.ap()[kc * 128:(kc + 1) * 128, j * _ch:(j + 1) * _ch]))
            q.c("vector", lambda v, kc=kc, j=j, _ch=_ch, eb_=eb_: v.tensor_copy(
                w1s[:, kc * cfg.F + j * _ch:kc * cfg.F + (j + 1) * _ch],
                eb_[:, 0:_ch]))
    nsplits = [(0, min(512, cfg.CAP))]
    if cfg.CAP > 512:
        nsplits.append((512, cfg.CAP))
    for mc in range(cfg.FCH):
        for si, (n0, n1) in enumerate(nsplits):
            pd = psA if si == 0 else psB
            for kc in range(NCH):
                q.c("tensor", lambda t, mc=mc, kc=kc, n0=n0, n1=n1, pd=pd: t.matmul(
                    pd[:, 0:n1 - n0],
                    w1s[:, kc * cfg.F + mc * 128:kc * cfg.F + (mc + 1) * 128],
                    xfm3[:, kc, n0:n1], start=(kc == 0), stop=False))
            q.c("tensor", lambda t, mc=mc, n0=n0, n1=n1, pd=pd: t.matmul(
                pd[:, 0:n1 - n0], b1r[0:1, mc * 128:(mc + 1) * 128],
                onesb[0:1, n0:n1], start=False, stop=True))
            q.c("scalar", lambda s, mc=mc, n0=n0, n1=n1, pd=pd: s.activation(
                hid3[:, mc, n0:n1], pd[:, 0:n1 - n0], AF.Gelu_apprx_tanh))
    free_from("w1s")

    w2s = sb("w2s", [128, cfg.FCH * D], BF16)
    for kc in range(cfg.FCH):
        eb_, eq_ = _estg()
        q.dma(eq_, lambda s, kc=kc, eb_=eb_: s.dma_start(
            out=eb_[:, :], in_=ew2.ap()[kc * 128:(kc + 1) * 128, :]))
        q.c("vector", lambda v, kc=kc, eb_=eb_: v.tensor_copy(
            w2s[:, kc * D:(kc + 1) * D], eb_[:, :]))
    for k in range(cfg.NXT):
        for n0, n1 in ((0, 512), (512, 1024)):
            for kc in range(cfg.FCH):
                q.c("tensor", lambda t, k=k, kc=kc, n0=n0, n1=n1: t.matmul(
                    psO[:, n0:n1], hid3[:, kc, k * 128:(k + 1) * 128],
                    w2s[:, kc * D + n0:kc * D + n1],
                    start=(kc == 0), stop=(kc == cfg.FCH - 1)))
        q.c("vector", lambda v: v.tensor_add(oev[:, :], psO[:, :], eb2r[:, :]))
        q.dma("sync", lambda s, k=k: s.dma_start(
            out=ag2in.ap()[k * 128:(k + 1) * 128, :], in_=oev[:, :]))

    if milestones is not None:
        milestones['pre_ag2'] = len(q.steps)
    q.c("gpsimd", lambda g: g.collective_compute(
        "AllGather", ALU.bypass, replica_groups=[list(range(cfg.NCORES))],
        ins=[ag2in.ap().opt()], outs=[outbuf.ap()[0:cfg.NSLOT, :].opt()]))
    if milestones is not None:
        milestones['post_ag2'] = len(q.steps)

    gat = sb("gat", [128, D], F32)
    xci = sb("xci", [128, D], F32)
    dm2 = sb("dm2", [128, D], F32)
    stc = sb("stc", [128, 6], F32)
    for k in range(NTILE):
        q.dma("gpsimd", lambda g, k=k: g.indirect_dma_start(
            gat[:, :], None, outbuf.ap(), IndirectOffsetOnAxis(slotc[:, k:k + 1], 0)))
        q.c("vector", lambda v, k=k: v.tensor_scalar_mul(
            gat[:, :], gat[:, :], gpc[:, k:k + 1]))
        q.c("vector", lambda v: v.reduce_sum(stc[:, 0:1], gat[:, :], AX.X))
        q.c("scalar", lambda s: s.activation(
            dm2[:, :], gat[:, :], AF.Square, accum_out=stc[:, 1:2]))
        q.c("vector", lambda v: v.tensor_scalar_mul(stc[:, 0:1], stc[:, 0:1], 1.0 / D))
        q.c("vector", lambda v: v.tensor_scalar_mul(stc[:, 1:2], stc[:, 1:2], 1.0 / D))
        q.c("vector", lambda v: v.tensor_mul(stc[:, 2:3], stc[:, 0:1], stc[:, 0:1]))
        q.c("vector", lambda v: v.tensor_sub(stc[:, 1:2], stc[:, 1:2], stc[:, 2:3]))
        q.c("vector", lambda v: v.tensor_scalar_add(stc[:, 1:2], stc[:, 1:2], EPS))
        q.c("scalar", lambda s: s.activation(stc[:, 2:3], stc[:, 1:2], AF.Sqrt))
        q.c("vector", lambda v: v.reciprocal(stc[:, 3:4], stc[:, 2:3]))
        q.c("vector", lambda v: v.tensor_mul(stc[:, 4:5], stc[:, 3:4], stc[:, 0:1]))
        q.c("vector", lambda v: v.tensor_scalar_mul(stc[:, 4:5], stc[:, 4:5], -1.0))
        q.dma("sync", lambda s, k=k: s.dma_start(
            out=xci[:, :], in_=x1buf.ap()[k * 128:(k + 1) * 128, :]))
        q.c("vector", lambda v: v.tensor_scalar(
            gat[:, :], gat[:, :], stc[:, 3:4], stc[:, 4:5], ALU.mult, ALU.add))
        q.c("vector", lambda v: v.tensor_add(gat[:, :], gat[:, :], xci[:, :]))
        q.dma("sync", lambda s, k=k: s.dma_start(
            out=out.ap()[k * 128:(k + 1) * 128, :], in_=gat[:, :]))

    if milestones is not None:
        milestones['all'] = len(q.steps)
    if max_steps is not None:
        q.steps = q.steps[:max_steps]
    if phasec_probe is not None:
        pdbg = nc.dram_tensor("pdbg", [128, D], F32, kind="ExternalOutput")
        pdbg2 = nc.dram_tensor("pdbg2", [128, cfg.NTILE + 1], F32,
                               kind="ExternalOutput")
        if phasec_probe == "iota":
            pidx = sb("pidx", [128, 2], I32)
            pidxf = pidx[:, 1:2].bitcast(F32)
            q.c("gpsimd", lambda g: g.iota(
                pidxf, pattern=[[0, 1]], base=0, channel_multiplier=1,
                allow_small_or_imprecise_dtypes=True))
            q.c("vector", lambda v: v.tensor_copy(pidx[:, 0:1], pidxf))
            q.dma("gpsimd", lambda g: g.indirect_dma_start(
                gat[:, :], None, outbuf.ap(),
                IndirectOffsetOnAxis(pidx[:, 0:1], 0)))
        else:
            q.dma("gpsimd", lambda g: g.indirect_dma_start(
                gat[:, :], None, outbuf.ap(),
                IndirectOffsetOnAxis(slotc[:, 0:1], 0)))
        q.dma("sync", lambda s: s.dma_start(out=pdbg.ap(), in_=gat[:, :]))
        q.c("vector", lambda v: v.tensor_copy(xci[:, 0:cfg.NTILE],
                                              slotc[:, :]))
        q.dma("sync", lambda s: s.dma_start(
            out=pdbg2.ap()[:, 0:cfg.NTILE], in_=xci[:, 0:cfg.NTILE]))

    if dump_src is not None:
        _dct = dict(logitsT=logitsT, logits_all=logits_all, slotall=slotall,
                    gpall=gpall, slotbuf=slotbuf, rsout=rsout, ag2in=ag2in,
                    outbuf=outbuf, x1buf=x1buf, ln2buf=ln2buf)
        for _nm, _shape, _dt in zip(dump_src, dump_shape, dump_dt):
            _src = _dct[_nm]
            dbgout = nc.dram_tensor("dbg_" + _nm, list(_shape), _dt,
                                    kind="ExternalOutput")
            for r0 in range(0, _shape[0], 512):
                r1 = min(_shape[0], r0 + 512)
                q.dma("sync", lambda s, r0=r0, r1=r1, dbgout=dbgout, _src=_src:
                      s.dma_start(out=dbgout.ap()[r0:r1, :],
                                  in_=_src.ap()[r0:r1, :]))
    if emit_mode == "probe":
        with nc.Block() as blk:
            probe_out["info"] = q.emit_probe(nc, blk)
            probe_out["seq"] = q
    elif emit_mode == "sched":
        import contextlib
        with contextlib.ExitStack() as st:
            sems = {}
            for d in schedule["dims"]:
                sems[d] = st.enter_context(nc.semaphore("sem_" + d))
            blk = st.enter_context(nc.Block())
            q.emit(nc, blk, sems, schedule=schedule)
    else:
        with nc.semaphore("s_y") as s_y, nc.semaphore("s_v") as s_v, \
             nc.semaphore("s_s") as s_s, \
             nc.semaphore("s_t") as s_t, nc.semaphore("s_g") as s_g, \
             nc.semaphore("s_yd") as s_yd, nc.semaphore("s_sd") as s_sd, \
             nc.semaphore("s_gd") as s_gd, nc.Block() as blk:
            q.emit(nc, blk, {"sync": s_y, "vector": s_v, "scalar": s_s,
                             "tensor": s_t,
                             "gpsimd": s_g, "sync_d": s_yd, "scalar_d": s_sd,
                             "gpsimd_d": s_gd})

    for g, _ in reversed(allocs):
        g.__exit__(None, None, None)
    return nc


def _consts(cfg):
    import ml_dtypes
    eye = np.eye(128, dtype=np.float32)
    g = cfg.GC
    triu = np.triu(np.ones((g, g), np.float32))
    cm = np.zeros((128, 64), np.float32)
    cm[0:g, 0:g] = triu
    cm[g:2 * g, 0:g] = triu
    return {
        "identf": eye,
        "identb": eye.astype(ml_dtypes.bfloat16),
        "cmask": cm.astype(ml_dtypes.bfloat16),
        "sltri": np.triu(np.ones((128, 128), np.float32), k=1),
    }


def make_in_maps(cfg, inputs):
    x = np.asarray(inputs["L0_attn_x"], np.float32)
    xr = np.ascontiguousarray(x.reshape(cfg.NCORES, cfg.NTOK, cfg.D))

    def f32(name):
        return np.ascontiguousarray(np.asarray(inputs[name], np.float32))

    common = {
        "Wq": f32("L0_attn_Wq"), "Wk": f32("L0_attn_Wk"),
        "Wv": f32("L0_attn_Wv"), "Wg": f32("L0_attn_Wg"),
        "Wo": f32("L0_attn_Wo"), "W1": f32("L0_attn_W1"), "W2": f32("L0_attn_W2"),
        "gkw1": f32("L0_attn_gk_w1"), "gkw2": f32("L0_attn_gk_w2"),
        "ln_w": f32("L0_attn_ln_w").reshape(1, -1),
        "ln_b": f32("L0_attn_ln_b").reshape(1, -1),
        "gk_b2": f32("L0_attn_gk_b2").reshape(1, -1),
        "b1i": f32("L0_attn_b1").reshape(1, -1),
        "b2i": f32("L0_attn_b2").reshape(1, -1),
        "rms_w": f32("L0_attn_rms_w").reshape(1, -1),
        "gw": f32("L0_ffn_gate_w"), "gb": f32("L0_ffn_gate_b").reshape(1, -1),
    }
    common.update(_consts(cfg))
    ew1 = f32("L0_ffn_w1"); eb1 = f32("L0_ffn_b1")
    ew2 = f32("L0_ffn_w2"); eb2 = f32("L0_ffn_b2")

    in_maps = []
    for c in range(cfg.NCORES):
        m = dict(common)
        m["x_tm"] = xr[c]
        m["xT"] = np.ascontiguousarray(xr[c].T)
        m["ew1"] = ew1[c]; m["eb1"] = eb1[c].reshape(1, -1)
        m["ew2"] = ew2[c]; m["eb2"] = eb2[c].reshape(1, -1)
        m["coreid"] = np.array([[c]], np.float32)
        in_maps.append(m)
    return in_maps


def build_scheduled(cfg, **kw):
    po = {}
    build_program(cfg, emit_mode="probe", probe_out=po, **kw)
    sched = po["seq"].compute_schedule(po["info"])
    return build_program(cfg, emit_mode="sched", schedule=sched, **kw)


def kernel(**inputs):
    cfg = Cfg()
    x = np.asarray(inputs["L0_attn_x"], np.float32)
    NB = x.shape[0]
    nc = build_scheduled(cfg)
    in_maps = make_in_maps(cfg, inputs)
    res = run_bass_kernel_spmd(nc, in_maps, core_ids=list(range(cfg.NCORES)))
    outs = np.stack([res.results[c]["out"] for c in range(cfg.NCORES)])
    return outs.reshape(NB, cfg.T, cfg.D)


# version 15
# speedup vs baseline: 1.0099x; 1.0020x over previous
import numpy as np
import concourse.bass as bass
import concourse.mybir as mybir
from concourse.bass_utils import run_bass_kernel_spmd
from concourse.bass_core_rust import IndirectOffsetOnAxis

F32 = mybir.dt.float32
BF16 = mybir.dt.bfloat16
I32 = mybir.dt.int32
AF = mybir.ActivationFunctionType
ALU = mybir.AluOpType
AX = mybir.AxisListType


class Cfg:
    D = 1024
    NCH = 8
    H = 16
    DK = 32
    DV = 64
    LR = 16
    FI = 2048
    FICH = 16
    F = 4096
    FCH = 32
    E = 8
    CAP = 640
    GATE_NORM = 16.0
    EPS = 1e-5
    T = 2048
    SEQ = 2
    NCORES = 8
    TC = 256
    GC = 64
    DKS = DK ** -0.5

    NTOK = property(lambda s: s.T * s.SEQ)
    NTILE = property(lambda s: s.NTOK // 128)
    NSLOT = property(lambda s: s.E * s.CAP)
    PERP = property(lambda s: s.NCORES * s.NTOK // 128)
    NXT = property(lambda s: s.CAP // 128)


class Seq:

    ENGS = ("sync", "scalar", "vector", "tensor", "gpsimd")
    DIMS = ("sync", "scalar", "vector", "tensor", "gpsimd",
            "sync_d", "scalar_d", "gpsimd_d")

    def __init__(self):
        self.steps = []

    def c(self, eng, fn):
        self.steps.append((eng, "c", fn))

    def dma(self, eng, fn):
        self.steps.append((eng, "d", fn))

    @staticmethod
    def key(eng, kind):
        return eng + "_d" if kind == "d" else eng

    def emit_probe(self, nc, block):
        import numpy as _np
        per = {e: [] for e in self.ENGS}
        for idx, (eng, kind, fn) in enumerate(self.steps):
            per[eng].append((idx, fn))
        info = [None] * len(self.steps)
        base_cache = {}

        def sbase(name):
            if name not in base_cache:
                base_cache[name] = nc.lookup_mloc(name).addr
            return base_cache[name]

        def ap_range(ap):
            b = getattr(ap, "bass_ap", None)
            if b is None:
                return None
            esz = _np.dtype(mybir.dt.np(b.dtype)).itemsize
            pairs = [tuple(p) for p in b.ap]
            sp = str(b.space)
            if "DRAM" in sp:
                if ap.dynamic_ap_info is not None:
                    return (("D", b.name), 0, 1 << 44)
                lo = hi = b.offset
                for st, ct in pairs:
                    d = (ct - 1) * st
                    if d < 0:
                        lo += d
                    else:
                        hi += d
                return (("D", b.name), lo * esz, (hi + 1) * esz)
            dims = pairs[1:] if len(pairs) > 1 else []
            pstride = pairs[0][0] if pairs else 1
            off = b.offset
            if pstride > 0:
                off = off % pstride
            lo = hi = off
            for st, ct in dims:
                d = (ct - 1) * st
                if d < 0:
                    lo += d
                else:
                    hi += d
            if "PSUM" in sp:
                return (("P", b.name), 0, 1 << 30)
            base = sbase(b.name)
            return (("S",), base + lo * esz, base + (hi + 1) * esz)

        def mk(ename):
            def body(eng):
                for idx, fn in per[ename]:
                    inst = fn(eng)
                    mi = inst.ins
                    rd, wr = [], []
                    for ap in (mi.ins or []):
                        r = ap_range(ap)
                        if r is not None:
                            rd.append(r)
                    for ap in (mi.outs or []):
                        r = ap_range(ap)
                        if r is not None:
                            wr.append(r)
                    info[idx] = (rd, wr)
            return body

        for e in self.ENGS:
            if per[e] or e == "sync":
                getattr(block, e)(mk(e))
        return info

    NPOOL = 24

    def compute_schedule(self, info):
        n = len(self.steps)
        NG = 8
        NSW = 8
        dims = (list(self.ENGS) + ["dp%d" % i for i in range(self.NPOOL)]
                + ["gq%d" % i for i in range(NG)]
                + ["sw%d" % i for i in range(NSW)])
        dimi = {d: i for i, d in enumerate(dims)}
        counts = {d: 0 for d in dims}
        comp = [None] * n
        clocks = [None] * n
        know = {e: [0] * len(dims) for e in self.ENGS}
        waits = [None] * n
        incs = [None] * n
        writes_map = {}
        reads_map = {}
        pool_next = 0
        gq_next = 0
        sw_next = 0
        prev_user = [None] * self.NPOOL
        prev_sw = [None] * NSW

        def overlap(a0, a1, b0, b1):
            return a0 < b1 and b0 < a1

        for j, (eng, kind, fn) in enumerate(self.steps):
            deps = set()
            if kind == "d" and eng == "gpsimd":
                p = sw_next
                sw_next = (sw_next + 1) % NSW
                dk = "sw%d" % p
                counts[dk] += 16
                comp[j] = (dimi[dk], counts[dk])
                if prev_sw[p] is not None:
                    deps.add(prev_sw[p])
                prev_sw[p] = j
                incs[j] = (dk, 16)
            elif kind == "d":
                p = pool_next
                pool_next = (pool_next + 1) % self.NPOOL
                dk = "dp%d" % p
                counts[dk] += 16
                comp[j] = (dimi[dk], counts[dk])
                if prev_user[p] is not None:
                    deps.add(prev_user[p])
                prev_user[p] = j
                incs[j] = (dk, 16)
            elif eng == "gpsimd":
                dk = "gq%d" % (gq_next % NG)
                gq_next += 1
                counts[dk] += 1
                comp[j] = (dimi[dk], counts[dk])
                incs[j] = (dk, 1)
            else:
                dk = eng
                counts[dk] += 1
                comp[j] = (dimi[dk], counts[dk])
                incs[j] = (dk, 1)
            rd, wr = info[j]
            for key, lo, hi in rd:
                wl = writes_map.get(key)
                if not wl:
                    continue
                uncov = [(lo, hi)]
                for wlo, whi, wi in reversed(wl):
                    nxt = []
                    hit = False
                    for ulo, uhi in uncov:
                        if overlap(ulo, uhi, wlo, whi):
                            hit = True
                            if ulo < wlo:
                                nxt.append((ulo, wlo))
                            if whi < uhi:
                                nxt.append((whi, uhi))
                        else:
                            nxt.append((ulo, uhi))
                    if hit:
                        deps.add(wi)
                    uncov = nxt
                    if not uncov:
                        break
            for key, lo, hi in wr:
                wl = writes_map.get(key)
                if wl:
                    uncov = [(lo, hi)]
                    for wlo, whi, wi in reversed(wl):
                        nxt = []
                        hit = False
                        for ulo, uhi in uncov:
                            if overlap(ulo, uhi, wlo, whi):
                                hit = True
                                if ulo < wlo:
                                    nxt.append((ulo, wlo))
                                if whi < uhi:
                                    nxt.append((whi, uhi))
                            else:
                                nxt.append((ulo, uhi))
                        if hit:
                            deps.add(wi)
                        uncov = nxt
                        if not uncov:
                            break
                rl = reads_map.get(key)
                if rl:
                    keep = []
                    for rlo, rhi, ri in rl:
                        if overlap(lo, hi, rlo, rhi):
                            deps.add(ri)
                            if rlo < lo or rhi > hi:
                                keep.append((rlo, rhi, ri))
                        else:
                            keep.append((rlo, rhi, ri))
                    reads_map[key] = keep
            kn = know[eng]
            wl_out = []
            for i in sorted(deps):
                ddi, dval = comp[i]
                if kn[ddi] < dval:
                    wl_out.append((dims[ddi], dval))
                ci = clocks[i]
                for z in range(len(kn)):
                    if ci[z] > kn[z]:
                        kn[z] = ci[z]
                if kn[ddi] < dval:
                    kn[ddi] = dval
            waits[j] = wl_out
            cj = list(kn)
            di, dv = comp[j]
            if cj[di] < dv:
                cj[di] = dv
            clocks[j] = cj
            for key, lo, hi in wr:
                wl = writes_map.setdefault(key, [])
                wl2 = [w for w in wl if not (lo <= w[0] and w[1] <= hi)]
                wl2.append((lo, hi, j))
                writes_map[key] = wl2
            for key, lo, hi in rd:
                reads_map.setdefault(key, []).append((lo, hi, j))
        return {"waits": waits, "incs": incs, "finals": dict(counts),
                "dims": dims}

    def emit(self, nc, block, sems, schedule=None):
        per = {e: [] for e in self.ENGS}
        tot = {k: 0 for k in sems}
        for idx, (eng, kind, fn) in enumerate(self.steps):
            per[eng].append((idx, dict(tot), kind, fn))
            if schedule is None:
                tot[self.key(eng, kind)] += 16 if kind == "d" else 1
        final = dict(tot) if schedule is None else dict(schedule["finals"])
        import os
        if os.environ.get("PRINT_TOTALS"):
            nw = (sum(len(w) for w in schedule["waits"])
                  if schedule else None)
            print("SEQ TOTALS:", final, "steps:", len(self.steps),
                  "waits:", nw)

        def mk(ename):
            def body(eng):
                seen = {k: 0 for k in sems}
                for idx, w, kind, fn in per[ename]:
                    if schedule is None:
                        for k in sems:
                            if w[k] > seen[k]:
                                eng.wait_ge(sems[k], w[k])
                                seen[k] = w[k]
                    else:
                        import os as _os
                        _nsw = _os.environ.get("NO_SELF_WAITS")
                        for k, dval in schedule["waits"][idx]:
                            if _nsw and k == ename:
                                continue
                            if dval > seen[k]:
                                eng.wait_ge(sems[k], dval)
                                seen[k] = dval
                    inst = fn(eng)
                    if schedule is None:
                        kk = self.key(ename, kind)
                        inst.then_inc(sems[kk], 16 if kind == "d" else 1)
                    else:
                        kk, inc = schedule["incs"][idx]
                        inst.then_inc(sems[kk], inc)
                if ename == "sync":
                    for k in sems:
                        if final.get(k, 0) > seen[k]:
                            eng.wait_ge(sems[k], final[k])
            return body

        for e in self.ENGS:
            if per[e] or e == "sync":
                getattr(block, e)(mk(e))


def build_program(cfg, max_steps=None, milestones=None, debug_out=False,
                  dump_src=None, dump_shape=None, dump_dt=None,
                  phasec_probe=None, emit_mode="serial", schedule=None,
                  probe_out=None):
    nc = bass.Bass("TRN2", target_bir_lowering=False,
                   dynamic_dma_scratch_size=8192)
    q = Seq()

    D, NCH, H, DK, DV = cfg.D, cfg.NCH, cfg.H, cfg.DK, cfg.DV
    TC, GC, NTOK = cfg.TC, cfg.GC, cfg.NTOK
    NGC = TC // GC
    TPT = TC // 128
    NTILE = cfg.NTILE
    PERP = cfg.PERP
    EPS = cfg.EPS
    E = cfg.E

    def din(name, shape, dt=F32):
        return nc.dram_tensor(name, shape, dt, kind="ExternalInput")

    xT = din("xT", [D, NTOK])
    x_tm = din("x_tm", [NTOK, D])
    Wq = din("Wq", [D, H * DK]); Wk = din("Wk", [D, H * DK])
    Wv = din("Wv", [D, D]); Wg = din("Wg", [D, D]); Wo = din("Wo", [D, D])
    W1 = din("W1", [D, cfg.FI]); W2 = din("W2", [cfg.FI, D])
    gkw1 = din("gkw1", [D, cfg.LR]); gkw2 = din("gkw2", [cfg.LR, H * DK])
    ln_w = din("ln_w", [1, D]); ln_b = din("ln_b", [1, D])
    gk_b2 = din("gk_b2", [1, H * DK])
    b1i = din("b1i", [1, cfg.FI]); b2i = din("b2i", [1, D])
    rms_w = din("rms_w", [1, DV])
    gw = din("gw", [D, E]); gb = din("gb", [1, E])
    ew1 = din("ew1", [D, cfg.F]); eb1 = din("eb1", [1, cfg.F])
    ew2 = din("ew2", [cfg.F, D]); eb2 = din("eb2", [1, D])
    identf = din("identf", [128, 128]); identb = din("identb", [128, 128], BF16)
    cmask = din("cmask", [128, 64], BF16)
    sltri = din("sltri", [128, 128])
    coreid = din("coreid", [1, 1])

    dbg = dict(kind="ExternalOutput") if debug_out else {}
    x1buf = nc.dram_tensor("x1buf", [NTOK, D], F32, **dbg)
    ln2buf = nc.dram_tensor("ln2buf", [NTOK, D], BF16, **dbg)
    logitsT = nc.dram_tensor("logitsT", [E, NTOK], F32)
    logits_all = nc.dram_tensor("logits_all", [E * cfg.NCORES, NTOK], F32)
    slotall = nc.dram_tensor("slotall", [cfg.NCORES * NTILE, 128], F32)
    gpall = nc.dram_tensor("gpall", [cfg.NCORES * NTILE, 128], F32)
    slotbuf = nc.dram_tensor("slotbuf", [cfg.NSLOT + 1, D], BF16)
    rsout = nc.dram_tensor("rsout", [cfg.CAP, D], BF16)
    ag2in = nc.dram_tensor("ag2in", [cfg.CAP, D], F32)
    outbuf = nc.dram_tensor("outbuf", [cfg.NSLOT + 1, D], F32)
    srow = nc.dram_tensor("srow", [1, 64], F32)
    out = nc.dram_tensor("out", [NTOK, D], F32, kind="ExternalOutput")

    allocs = []

    def sb(name, shape, dt=F32):
        g = nc.sbuf_tensor(name, shape, dt)
        t = g.__enter__()
        allocs.append((g, name))
        return t

    def ps(name, shape, dt=F32):
        g = nc.psum_tensor(name, shape, dt)
        t = g.__enter__()
        allocs.append((g, name))
        return t

    def free_from(name):
        idx = next(i for i, (_, n) in enumerate(allocs) if n == name)
        for g, _ in reversed(allocs[idx:]):
            g.__exit__(None, None, None)
        del allocs[idx:]

    idf = sb("idf", [128, 128], F32)
    idb = sb("idb", [128, 128], BF16)
    cmk = sb("cmk", [128, 64], BF16)
    rmsr = sb("rmsr", [128, DV], F32)
    gbr = sb("gbr", [128, E], F32)
    csr = sb("csr", [128, E], F32)
    onescf = sb("onescf", [128, 1], F32)
    onesr8 = sb("onesr8", [1, 128], F32)
    lnbfm = sb("lnbfm", [128, NCH], F32)
    lnwfm = sb("lnwfm", [128, NCH], F32)
    lnwsfm = sb("lnwsfm", [128, NCH], F32)
    zrow = sb("zrow", [128, max(TC, PERP)], F32)
    zb = zrow[:, 0:256].bitcast(BF16)

    q.dma("sync", lambda s: s.dma_start(out=idf[:, :], in_=identf.ap()))
    q.dma("sync", lambda s: s.dma_start(out=idb[:, :], in_=identb.ap()))
    q.dma("sync", lambda s: s.dma_start(out=cmk[:, :], in_=cmask.ap()))
    q.dma("sync", lambda s: s.dma_start(out=rmsr[:, :], in_=rms_w.ap().broadcast_to([128, DV])))
    q.dma("sync", lambda s: s.dma_start(out=gbr[:, :], in_=gb.ap().broadcast_to([128, E])))
    q.dma("sync", lambda s: s.dma_start(
        out=lnbfm[:, :], in_=ln_b.ap().rearrange("o (c p) -> (o p) c", p=128),
        allow_slow_non_contiguous=True))
    q.dma("sync", lambda s: s.dma_start(
        out=lnwfm[:, :], in_=ln_w.ap().rearrange("o (c p) -> (o p) c", p=128),
        allow_slow_non_contiguous=True))
    q.c("vector", lambda v: v.memset(onescf[:, :], 1.0))
    q.c("vector", lambda v: v.memset(onesr8[:, :], 1.0))
    q.c("vector", lambda v: v.memset(zrow[:, :], 0.0))
    q.c("vector", lambda v: v.tensor_scalar_mul(lnwsfm[:, :], lnwfm[:, :], cfg.DKS))

    for r0 in range(0, cfg.NSLOT + 1, 128):
        rn = min(128, cfg.NSLOT + 1 - r0)
        for c0 in range(0, D, 512):
            q.dma("sync", lambda s, r0=r0, rn=rn, c0=c0: s.dma_start(
                out=slotbuf.ap()[r0:r0 + rn, c0:c0 + 512],
                in_=zrow[0:rn, 0:256].bitcast(BF16)))
    for c0 in range(0, D, 256):
        q.dma("sync", lambda s, c0=c0: s.dma_start(
            out=outbuf.ap()[cfg.NSLOT:cfg.NSLOT + 1, c0:c0 + 256],
            in_=zrow[0:1, 0:256]))

    psA = ps("psA", [128, 512], F32)
    psB = ps("psB", [128, 512], F32)
    psBig = ps("psBig", [128, 1024], F32)
    psO = ps("psO", [128, 1024], F32)
    psT = ps("psT", [128, 1024], BF16)
    psS = ps("psS", [128, 512], F32)

    xf = sb("xf", [128, NCH * TC], F32)
    xm = sb("xm", [128, NCH * TC], F32)
    xa = sb("xa", [128, NCH * TC], BF16)
    sc8 = sb("sc8", [128, NCH * TC], F32)
    qf = sb("qf", [128, 4 * TC], BF16)
    kf = sb("kf", [128, 4 * TC], BF16)
    gka = sb("gka", [128, 4 * TC], F32)
    eq = sb("eq", [128, 4 * TC], BF16)
    ek = sb("ek", [128, 4 * TC], BF16)
    bmat = sb("bmat", [128, 4 * NGC], F32)
    acar = sb("acar", [128, 4], F32)
    Ssb = sb("Ssb", [128, 4 * DV], F32)
    Sbp = sb("Sbp", [128, 4 * DV], BF16)
    Sbm = sb("Sbm", [128, 4 * DV], BF16)
    ach = sb("ach", [128, 4], F32)
    vt = sb("vt", [128, 1024], BF16)
    sg = sb("sg", [128, 1024], F32)
    pa = sb("pa", [128, 8 * 128], BF16)
    ktm = sb("ktm", [128, 512], BF16)
    o2 = sb("o2", [128, 1024], BF16)
    o2f = sb("o2f", [128, NCH * TC], BF16)
    xt1 = sb("xt1", [128, 1024], F32)
    st8 = sb("st8", [1, 4 * TC], F32)
    str2 = sb("str2", [128, 10], F32)
    tbm = sb("tbm", [128, 16], F32)
    r1p = sb("r1p", [2, TC], BF16)
    r2p = sb("r2p", [2, TC], BF16)
    lgt = sb("lgt", [8, 128], F32)
    agw = sb("agw", [128, 2 * TPT * 16], F32)

    q.c("vector", lambda v: v.memset(r1p[:, :], 1.0))
    q.c("vector", lambda v: v.memset(r2p[:, :], 1.0))

    wtq = sb("wtq", [128, NCH * 512], BF16)
    wtk = sb("wtk", [128, NCH * 512], BF16)
    wtv = sb("wtv", [128, NCH * 1024], BF16)
    wtg = sb("wtg", [128, NCH * 1024], BF16)
    wto = sb("wto", [128, NCH * 1024], BF16)
    wt1 = sb("wt1", [128, NCH * cfg.FI], BF16)
    wt2 = sb("wt2", [128, cfg.FICH * 1024], BF16)
    wgk1 = sb("wgk1", [128, NCH * cfg.LR], BF16)
    wgk2 = sb("wgk2", [cfg.LR + 1, H * DK], BF16)
    wgw = sb("wgw", [128, NCH * E], F32)
    w2bf = sb("w2bf", [128, NCH], F32)
    r1q = sb("r1q", [2, 512], BF16)
    r1k = sb("r1k", [2, 512], BF16)
    r1v = sb("r1v", [2, 1024], BF16)
    r1g = sb("r1g", [2, 1024], BF16)
    r1gk = sb("r1gk", [2, cfg.LR], BF16)
    r1w1 = sb("r1w1", [2, cfg.FI], BF16)

    wst = sb("wst", [128, 512], F32)
    wst2 = sb("wst2", [128, 512], F32)
    _wstp = [0]
    def _stg():
        _wstp[0] ^= 1
        return (wst, "sync") if _wstp[0] else (wst2, "scalar")

    onescb = sb("onescb", [128, 1], BF16)
    lnb2 = sb("lnb2", [128, 2 * NCH], F32)
    q.c("vector", lambda v: v.memset(onescb[:, :], 1.0))
    q.c("vector", lambda v: v.memset(lnb2[:, :], 0.0))
    q.c("vector", lambda v: v.tensor_copy(
        lnb2[:, :].rearrange("p (c two) -> p c two", two=2)[:, :, 1:2],
        lnbfm[:, :].unsqueeze(2)))

    def _ureg(j):
        pd = psBig if j < 2 else psO
        return pd[0:2, (j % 2) * 512:(j % 2) * 512 + 512]

    def _ureg2(j):
        pd = psBig if j < 2 else psO
        return pd[1:2, (j % 2) * 512:(j % 2) * 512 + 512]

    def prep_scaled(Wd, outF, wt_sb, r1_sb, scale_fm, uscale=None, badd=None):
        nN = (outF + 511) // 512
        for j in range(nN):
            n0j, n1j = j * 512, min(outF, (j + 1) * 512)
            for kc in range(NCH):
                stgb, stgq = _stg()
                q.dma(stgq, lambda s, kc=kc, n0=n0j, n1=n1j, stgb=stgb: s.dma_start(
                    out=stgb[:, 0:n1 - n0],
                    in_=Wd.ap()[kc * 128:(kc + 1) * 128, n0:n1]))
                q.c("tensor", lambda t, kc=kc, n0=n0j, n1=n1j, j=j, stgb=stgb: t.matmul(
                    _ureg(j)[:, 0:n1 - n0],
                    lnb2[:, 2 * kc:2 * kc + 2], stgb[:, 0:n1 - n0],
                    start=(kc == 0), stop=(kc == NCH - 1)))
                q.c("vector", lambda v, kc=kc, n0=n0j, n1=n1j, stgb=stgb: v.tensor_scalar(
                    wt_sb[:, kc * outF + n0:kc * outF + n1], stgb[:, 0:n1 - n0],
                    scale_fm[:, kc:kc + 1], None, ALU.mult))
        for j in range(nN):
            n0, n1 = j * 512, min(outF, (j + 1) * 512)
            if badd is not None:
                q.dma("sync", lambda s, n0=n0, n1=n1: s.dma_start(
                    out=wst[0:2, 0:n1 - n0],
                    in_=badd.ap()[0:1, n0:n1].broadcast_to([2, n1 - n0])))
            if uscale is not None:
                q.c("vector", lambda v, j=j, n0=n0, n1=n1: v.tensor_scalar_mul(
                    r1_sb[0:2, n0:n1], _ureg(j)[:, 0:n1 - n0], uscale))
            elif badd is not None:
                q.c("vector", lambda v, j=j, n0=n0, n1=n1: v.tensor_add(
                    r1_sb[0:2, n0:n1], _ureg(j)[:, 0:n1 - n0], wst[0:2, 0:n1 - n0]))
            else:
                q.c("vector", lambda v, j=j, n0=n0, n1=n1: v.tensor_copy(
                    r1_sb[0:2, n0:n1], _ureg(j)[:, 0:n1 - n0]))
        for j in range(nN):
            n0, n1 = j * 512, min(outF, (j + 1) * 512)
            for kc in range(NCH):
                q.c("tensor", lambda t, kc=kc, n0=n0, n1=n1: t.matmul(
                    psB[0:1, 0:n1 - n0], onescb[:, 0:1],
                    wt_sb[:, kc * outF + n0:kc * outF + n1],
                    start=(kc == 0), stop=(kc == NCH - 1)))
            q.c("vector", lambda v, n0=n0, n1=n1: v.tensor_copy(
                r1_sb[0:1, n0:n1], psB[0:1, 0:n1 - n0]))

    prep_scaled(Wq, 512, wtq, r1q, lnwsfm, uscale=cfg.DKS)
    prep_scaled(Wk, 512, wtk, r1k, lnwfm)
    prep_scaled(Wv, 1024, wtv, r1v, lnwfm)
    prep_scaled(Wg, 1024, wtg, r1g, lnwfm)
    prep_scaled(gkw1, cfg.LR, wgk1, r1gk, lnwfm)
    prep_scaled(W1, cfg.FI, wt1, r1w1, lnwfm, badd=b1i)

    for kc in range(NCH):
        wcb = xf[:, 0:1024] if kc % 2 == 0 else xf[:, 1024:2048]
        wcq = "sync" if kc % 2 == 0 else "scalar"
        q.dma(wcq, lambda s, kc=kc, wcb=wcb: s.dma_start(
            out=wcb, in_=Wo.ap()[kc * 128:(kc + 1) * 128, :]))
        q.c("vector", lambda v, kc=kc, wcb=wcb: v.tensor_copy(
            wto[:, kc * 1024:(kc + 1) * 1024], wcb))
        q.dma("sync", lambda s, kc=kc: s.dma_start(
            out=wgw[:, kc * E:(kc + 1) * E], in_=gw.ap()[kc * 128:(kc + 1) * 128, :]))
    for kc in range(cfg.FICH):
        wcb = xf[:, 0:1024] if kc % 2 == 0 else xf[:, 1024:2048]
        wcq = "sync" if kc % 2 == 0 else "scalar"
        q.dma(wcq, lambda s, kc=kc, wcb=wcb: s.dma_start(
            out=wcb, in_=W2.ap()[kc * 128:(kc + 1) * 128, :]))
        q.c("vector", lambda v, kc=kc, wcb=wcb: v.tensor_copy(
            wt2[:, kc * 1024:(kc + 1) * 1024], wcb))
    q.dma("sync", lambda s: s.dma_start(out=wst[0:cfg.LR, 0:512], in_=gkw2.ap()))
    q.dma("sync", lambda s: s.dma_start(
        out=wst[cfg.LR:cfg.LR + 1, 0:512], in_=gk_b2.ap()))
    q.c("vector", lambda v: v.tensor_copy(
        wgk2[0:cfg.LR + 1, :], wst[0:cfg.LR + 1, 0:512]))
    q.dma("sync", lambda s: s.dma_start(
        out=w2bf[:, :], in_=b2i.ap().rearrange("o (c p) -> (o p) c", p=128),
        allow_slow_non_contiguous=True))

    q.dma("sync", lambda s: s.dma_start(
        out=wst[:, 0:E * NCH].rearrange("p (c e) -> p c e", c=NCH),
        in_=gw.ap().rearrange("(c p) e -> p c e", p=128)))
    for kc in range(NCH):
        q.c("tensor", lambda t, kc=kc: t.matmul(
            psA[0:1, 0:E], onescf[:, 0:1],
            wst[:, 0:E * NCH].rearrange("p (c e) -> p c e", c=NCH)[:, kc, :],
            start=(kc == 0), stop=(kc == NCH - 1)))
    q.c("vector", lambda v: v.tensor_copy(wst[0:1, 0:E], psA[0:1, 0:E]))
    q.dma("sync", lambda s: s.dma_start(out=srow.ap()[0:1, 0:E], in_=wst[0:1, 0:E]))
    q.dma("sync", lambda s: s.dma_start(
        out=csr[:, :], in_=srow.ap()[0:1, 0:E].broadcast_to([128, E])))

    free_from("wst")

    t1v = sc8[:, :].bitcast(BF16)
    gsc = sc8[:, 0:4 * TC]
    assert 4 * TC >= 1024
    aot = gka
    l2t = o2
    xf3 = xf[:, :].rearrange("p (c t) -> p c t", c=NCH)
    xm3 = xm[:, :].rearrange("p (c t) -> p c t", c=NCH)
    xa3 = xa[:, :].rearrange("p (c t) -> p c t", c=NCH)
    qf3 = qf[:, :].rearrange("p (c t) -> p c t", c=4)
    kf3 = kf[:, :].rearrange("p (c t) -> p c t", c=4)
    gka4 = gka[:, :].rearrange("p (c g u) -> p c g u", c=4, g=NGC)
    gsc4 = gsc.rearrange("p (c g u) -> p c g u", c=4, g=NGC)
    eq4 = eq[:, :].rearrange("p (c g u) -> p c g u", c=4, g=NGC)
    bmat3 = bmat[:, :].rearrange("p (c g) -> p c g", c=4)
    S3 = Ssb[:, :].rearrange("p (c v) -> p c v", c=4)
    Sbp3 = Sbp[:, :].rearrange("p (c v) -> p c v", c=4)
    Sbm3 = Sbm[:, :].rearrange("p (c v) -> p c v", c=4)
    o2f3 = o2f[:, :].rearrange("p (c t) -> p c t", c=NCH)
    t1v3 = t1v[:, 0:cfg.FICH * TC].rearrange("p (c t) -> p c t", c=cfg.FICH)

    def strow(r, s=0):
        return st8[r:r + 1, s * TC:(s + 1) * TC]

    chunk_marks = []
    for seq_i in range(cfg.SEQ):
        q.c("vector", lambda v: v.memset(Ssb[:, :], 0.0))
        q.c("vector", lambda v: v.memset(acar[:, :], 0.0))
        for chn in range(cfg.T // TC):
            t0 = seq_i * cfg.T + chn * TC
            _mk = {"cs": len(q.steps)}
            chunk_marks.append(_mk)
            agb = (chn % 2) * TPT * 16

            q.dma("sync", lambda s, t0=t0: s.dma_start(
                out=xf3, in_=xT.ap().rearrange("(c p) n -> p c n", p=128)[:, :, t0:t0 + TC]))

            q.c("vector", lambda v: v.tensor_mul(sc8[:, 0:NCH * TC], xf[:, :], xf[:, :]))
            for kc in range(NCH):
                q.c("tensor", lambda t, kc=kc: t.matmul(
                    psS[0:1, 0:TC], onescf[:, 0:1], xf3[:, kc, :],
                    start=(kc == 0), stop=(kc == NCH - 1)))
            for kc in range(NCH):
                q.c("tensor", lambda t, kc=kc: t.matmul(
                    psS[0:1, TC:2 * TC], onescf[:, 0:1],
                    sc8[:, 0:NCH * TC].rearrange("p (c t) -> p c t", c=NCH)[:, kc, :],
                    start=(kc == 0), stop=(kc == NCH - 1)))
            q.c("vector", lambda v: v.tensor_scalar_mul(
                strow(0, 3), psS[0:1, 0:TC], 1.0 / D))
            q.c("vector", lambda v: v.tensor_scalar_mul(
                strow(0, 1), psS[0:1, TC:2 * TC], 1.0 / D))
            q.c("vector", lambda v: v.tensor_mul(strow(0, 0), strow(0, 3), strow(0, 3)))
            q.c("vector", lambda v: v.tensor_sub(strow(0, 1), strow(0, 1), strow(0, 0)))
            q.c("vector", lambda v: v.tensor_scalar_add(strow(0, 0), strow(0, 1), EPS))
            q.c("scalar", lambda s: s.activation(strow(0, 2), strow(0, 0), AF.Sqrt))
            q.c("vector", lambda v: v.reciprocal(strow(0, 2), strow(0, 2)))
            q.c("vector", lambda v: v.reciprocal(strow(0, 0), strow(0, 0)))
            q.c("vector", lambda v: v.tensor_mul(strow(0, 1), strow(0, 1), strow(0, 0)))
            q.c("vector", lambda v: v.tensor_scalar_add(strow(0, 1), strow(0, 1), EPS))
            q.c("scalar", lambda s: s.activation(strow(0, 1), strow(0, 1), AF.Sqrt))
            q.c("vector", lambda v: v.reciprocal(strow(0, 1), strow(0, 1)))
            q.c("vector", lambda v: v.tensor_mul(strow(0, 1), strow(0, 1), strow(0, 2)))
            q.c("vector", lambda v: v.tensor_mul(strow(0, 0), strow(0, 1), strow(0, 3)))
            q.c("vector", lambda v: v.tensor_mul(strow(0, 3), strow(0, 3), strow(0, 2)))
            q.c("vector", lambda v: v.tensor_scalar_mul(strow(0, 0), strow(0, 0), -1.0))
            q.c("vector", lambda v: v.tensor_copy(r1p[0:1, :], strow(0, 0)))

            q.c("tensor", lambda t: t.matmul(
                psBig[:, 0:TC], onesr8[0:1, :], strow(0, 1), start=True, stop=True))
            q.c("vector", lambda v: v.tensor_tensor(
                xa3, xf3, psBig[:, 0:TC].unsqueeze(1).broadcast_to([128, NCH, TC]),
                ALU.mult))
            _mk["se"] = len(q.steps)

            for kc in range(NCH):
                q.c("tensor", lambda t, kc=kc: t.matmul(
                    psA[0:cfg.LR, 0:TC], wgk1[:, kc * cfg.LR:(kc + 1) * cfg.LR],
                    xa3[:, kc, :], start=(kc == 0), stop=False))
            q.c("tensor", lambda t: t.matmul(
                psA[0:cfg.LR, 0:TC], r1gk[:, :], r1p[:, :], start=False, stop=True))
            q.c("vector", lambda v: v.memset(pa[0:cfg.LR + 1, 0:TC], 1.0))
            q.c("vector", lambda v: v.tensor_copy(pa[0:cfg.LR, 0:TC], psA[0:cfg.LR, 0:TC]))
            for mc in range(4):
                q.c("tensor", lambda t, mc=mc: t.matmul(
                    psBig[:, mc * TC:(mc + 1) * TC], wgk2[:, mc * 128:(mc + 1) * 128],
                    pa[0:cfg.LR + 1, 0:TC], start=True, stop=True))
            q.c("scalar", lambda s: s.activation(gsc, psBig[:, 0:4 * TC], AF.Sigmoid))
            q.c("scalar", lambda s: s.activation(gka[:, :], gsc, AF.Ln))
            q.c("vector", lambda v: v.tensor_scalar_mul(gsc, gka[:, :], 1.0 / cfg.GATE_NORM))
            for fc in range(4):
                q.c("vector", lambda v, fc=fc: v.tensor_tensor_scan(
                    gka[:, :].rearrange("p (c t) -> p c t", c=4)[:, fc, :],
                    gsc.rearrange("p (c t) -> p c t", c=4)[:, fc, :],
                    zrow[:, 0:TC], acar[:, fc:fc + 1], ALU.add, ALU.add))
            q.c("vector", lambda v: v.tensor_copy(bmat3[:, :, 0:1], acar[:, :].unsqueeze(2)))
            for g in range(1, NGC):
                q.c("vector", lambda v, g=g: v.tensor_copy(
                    bmat3[:, :, g:g + 1],
                    gka[:, :].rearrange("p (c t) -> p c t", c=4)[:, :, g * GC - 1:g * GC]))
            q.c("vector", lambda v: v.tensor_copy(
                acar[:, :],
                gka[:, :].rearrange("p (c t) -> p c t", c=4)[:, :, TC - 1:TC].squeeze(2)))
            q.c("vector", lambda v: v.tensor_sub(
                gsc4, gka4, bmat3.unsqueeze(3).broadcast_to([128, 4, NGC, GC])))
            q.c("scalar", lambda s: s.activation(eq[:, :], gsc, AF.Exp))
            q.c("scalar", lambda s: s.activation(ek[:, :], gsc, AF.Exp, scale=-1.0))
            _mk["ge"] = len(q.steps)

            for dest, wt_sb, r1_sb, fac in ((qf, wtq, r1q, eq), (kf, wtk, r1k, ek)):
                for mc in range(4):
                    for kc in range(NCH):
                        q.c("tensor", lambda t, mc=mc, kc=kc, wt_sb=wt_sb: t.matmul(
                            psBig[:, mc * TC:(mc + 1) * TC],
                            wt_sb[:, kc * 512 + mc * 128:kc * 512 + (mc + 1) * 128],
                            xa3[:, kc, :], start=(kc == 0), stop=False))
                    q.c("tensor", lambda t, mc=mc, r1_sb=r1_sb: t.matmul(
                        psBig[:, mc * TC:(mc + 1) * TC],
                        r1_sb[:, mc * 128:(mc + 1) * 128], r1p[:, :],
                        start=False, stop=True))
                for mc in range(4):
                    q.c("vector", lambda v, mc=mc, dest=dest, fac=fac: v.tensor_tensor(
                        dest[:, mc * TC:(mc + 1) * TC], psBig[:, mc * TC:(mc + 1) * TC],
                        fac[:, mc * TC:(mc + 1) * TC], ALU.mult))

            for j in range(TPT):
                for kc in range(NCH):
                    q.c("tensor", lambda t, j=j, kc=kc: t.matmul(
                        psA[:, 0:E], xf3[:, kc, j * 128:(j + 1) * 128],
                        wgw[:, kc * E:(kc + 1) * E],
                        start=(kc == 0), stop=(kc == NCH - 1)))
                q.c("vector", lambda v, j=j, agb=agb: v.tensor_copy(
                    agw[:, agb + j * 16:agb + j * 16 + 8], psA[:, 0:E]))

            _mk["xe"] = len(q.steps)
            for j in range(TPT):
                ce, co = 2 * j, 2 * j + 1
                tcol = j * 128

                for half, wt_sb, r1_sb in ((0, wtv, r1v), (1, wtg, r1g)):
                    for nh in range(2):
                        for kc in range(NCH):
                            q.c("tensor", lambda t, kc=kc, nh=nh, tcol=tcol, wt_sb=wt_sb: t.matmul(
                                psO[:, nh * 512:(nh + 1) * 512],
                                xa3[:, kc, tcol:tcol + 128],
                                wt_sb[:, kc * 1024 + nh * 512:kc * 1024 + (nh + 1) * 512],
                                start=(kc == 0), stop=False))
                        q.c("tensor", lambda t, nh=nh, tcol=tcol, r1_sb=r1_sb: t.matmul(
                            psO[:, nh * 512:(nh + 1) * 512],
                            r1p[:, tcol:tcol + 128], r1_sb[:, nh * 512:(nh + 1) * 512],
                            start=False, stop=True))
                    if half == 0:
                        q.c("vector", lambda v: v.tensor_copy(vt[:, :], psO[:, :]))
                    else:
                        q.c("scalar", lambda s: s.activation(sg[:, :], psO[:, :], AF.Silu))

                q.c("vector", lambda v, ce=ce: v.tensor_copy(
                    ach[:, :], eq4[:, :, ce, GC - 1:GC].squeeze(2)))

                for fc in range(4):
                    q.c("tensor", lambda t, fc=fc, tcol=tcol: t.transpose(
                        psT[:, fc * 128:(fc + 1) * 128],
                        kf3[:, fc, tcol:tcol + 128], idb[:, :]))
                q.c("vector", lambda v: v.tensor_copy(ktm[:, 0:512], psT[:, 0:512]))

                q.c("vector", lambda v: v.tensor_copy(Sbp[:, :], Ssb[:, :]))
                for h in range(H):
                    rb = (h % 4) * 32
                    fch = h // 4
                    q.c("tensor", lambda t, h=h, rb=rb, fch=fch: t.matmul(
                        psA[rb:rb + 32, fch * DV:(fch + 1) * DV],
                        ktm[0:64, fch * 128 + rb:fch * 128 + rb + 32],
                        vt[0:64, h * DV:(h + 1) * DV],
                        start=True, stop=True, tile_position=(0, rb)))
                q.c("vector", lambda v: v.tensor_add(Ssb[:, :], Ssb[:, :], psA[:, 0:4 * DV]))
                q.c("vector", lambda v: v.tensor_tensor(
                    S3, S3, ach[:, :].unsqueeze(2).broadcast_to([128, 4, DV]), ALU.mult))
                q.c("vector", lambda v: v.tensor_copy(Sbm[:, :], Ssb[:, :]))
                for h in range(H):
                    rb = (h % 4) * 32
                    fch = h // 4
                    q.c("tensor", lambda t, h=h, rb=rb, fch=fch: t.matmul(
                        psA[rb:rb + 32, fch * DV:(fch + 1) * DV],
                        ktm[64:128, fch * 128 + rb:fch * 128 + rb + 32],
                        vt[64:128, h * DV:(h + 1) * DV],
                        start=True, stop=True, tile_position=(64, rb)))
                q.c("vector", lambda v, co=co: v.tensor_copy(
                    ach[:, :], eq4[:, :, co, GC - 1:GC].squeeze(2)))
                q.c("vector", lambda v: v.tensor_add(Ssb[:, :], Ssb[:, :], psA[:, 0:4 * DV]))
                q.c("vector", lambda v: v.tensor_tensor(
                    S3, S3, ach[:, :].unsqueeze(2).broadcast_to([128, 4, DV]), ALU.mult))

                for grp in range(2):
                    for hi in range(8):
                        h = grp * 8 + hi
                        rb = (h % 4) * 32
                        fch = h // 4
                        q.c("tensor", lambda t, hi=hi, rb=rb, fch=fch, ce=ce: t.matmul(
                            psBig[0:64, hi * 128:hi * 128 + 64],
                            kf3[rb:rb + 32, fch, ce * GC:(ce + 1) * GC],
                            qf3[rb:rb + 32, fch, ce * GC:(ce + 1) * GC],
                            start=True, stop=True, tile_position=(rb, 0)))
                        q.c("tensor", lambda t, hi=hi, rb=rb, fch=fch, co=co: t.matmul(
                            psBig[64:128, hi * 128 + 64:hi * 128 + 128],
                            kf3[rb:rb + 32, fch, co * GC:(co + 1) * GC],
                            qf3[rb:rb + 32, fch, co * GC:(co + 1) * GC],
                            start=True, stop=True, tile_position=(rb, 64)))
                    q.c("vector", lambda v: v.memset(
                        pa[64:128, :].rearrange("p (h t) -> p h t", h=8)[:, :, 0:64], 0.0))
                    q.c("vector", lambda v: v.memset(
                        pa[0:64, :].rearrange("p (h t) -> p h t", h=8)[:, :, 64:128], 0.0))
                    q.c("vector", lambda v: v.tensor_tensor(
                        pa[0:64, :].rearrange("p (h t) -> p h t", h=8)[:, :, 0:64],
                        psBig[0:64, :].rearrange("p (h t) -> p h t", h=8)[:, :, 0:64],
                        cmk[0:64, 0:64].unsqueeze(1).broadcast_to([64, 8, 64]), ALU.mult))
                    q.c("vector", lambda v: v.tensor_tensor(
                        pa[64:128, :].rearrange("p (h t) -> p h t", h=8)[:, :, 64:128],
                        psBig[64:128, :].rearrange("p (h t) -> p h t", h=8)[:, :, 64:128],
                        cmk[64:128, 0:64].unsqueeze(1).broadcast_to([64, 8, 64]),
                        ALU.mult))
                    for hi in range(8):
                        h = grp * 8 + hi
                        q.c("tensor", lambda t, h=h, hi=hi: t.matmul(
                            psO[:, h * DV:(h + 1) * DV],
                            pa[:, hi * 128:(hi + 1) * 128],
                            vt[:, h * DV:(h + 1) * DV], start=True, stop=True))

                for h in range(H):
                    rb = (h % 4) * 32
                    fch = h // 4
                    q.c("tensor", lambda t, h=h, rb=rb, fch=fch, ce=ce: t.matmul(
                        psBig[0:64, h * DV:(h + 1) * DV],
                        qf3[rb:rb + 32, fch, ce * GC:(ce + 1) * GC],
                        Sbp3[rb:rb + 32, fch, :],
                        start=True, stop=True, tile_position=(rb, 0)))
                    q.c("tensor", lambda t, h=h, rb=rb, fch=fch, co=co: t.matmul(
                        psBig[64:128, h * DV:(h + 1) * DV],
                        qf3[rb:rb + 32, fch, co * GC:(co + 1) * GC],
                        Sbm3[rb:rb + 32, fch, :],
                        start=True, stop=True, tile_position=(rb, 64)))
                q.c("vector", lambda v: v.tensor_copy(aot[:, 0:1024], psBig[:, :]))
                q.c("vector", lambda v: v.tensor_add(aot[:, 0:1024], aot[:, 0:1024], psO[:, :]))

                q.c("vector", lambda v: v.tensor_mul(
                    sc8[:, 1024:1024 + 1024], aot[:, 0:1024], aot[:, 0:1024]))
                q.c("vector", lambda v: v.tensor_reduce(
                    bmat[:, 0:H].unsqueeze(2),
                    sc8[:, 1024:1024 + 1024].rearrange("p (h v) -> p h v", h=H),
                    AX.X, ALU.add))
                q.c("vector", lambda v: v.tensor_scalar(
                    bmat[:, 0:H], bmat[:, 0:H], 1.0 / DV, EPS, ALU.mult, ALU.add))
                q.c("scalar", lambda s: s.activation(bmat[:, 0:H], bmat[:, 0:H], AF.Sqrt))
                q.c("vector", lambda v: v.reciprocal(bmat[:, 0:H], bmat[:, 0:H]))
                q.c("vector", lambda v: v.tensor_tensor(
                    sg[:, :].rearrange("p (h v) -> p h v", h=H),
                    sg[:, :].rearrange("p (h v) -> p h v", h=H),
                    rmsr[:, :].unsqueeze(1).broadcast_to([128, H, DV]), ALU.mult))
                q.c("vector", lambda v: v.tensor_tensor(
                    sg[:, :].rearrange("p (h v) -> p h v", h=H),
                    sg[:, :].rearrange("p (h v) -> p h v", h=H),
                    bmat[:, 0:H].unsqueeze(2).broadcast_to([128, H, DV]), ALU.mult))
                q.c("vector", lambda v: v.tensor_mul(o2[:, :], aot[:, 0:1024], sg[:, :]))
                for fc in range(NCH):
                    q.c("tensor", lambda t, fc=fc: t.transpose(
                        psT[:, fc * 128:(fc + 1) * 128], o2[:, fc * 128:(fc + 1) * 128],
                        idb[:, :]))
                q.c("vector", lambda v, tcol=tcol: v.tensor_copy(
                    o2f3[:, :, tcol:tcol + 128],
                    psT[:, :].rearrange("p (c t) -> p c t", c=NCH)[:, :, 0:128]))

            q.c("tensor", lambda t: t.matmul(
                psBig[:, 0:TC], onesr8[0:1, :], strow(0, 2), start=True, stop=True))
            q.c("vector", lambda v: v.tensor_tensor(
                xm3, xf3, psBig[:, 0:TC].unsqueeze(1).broadcast_to([128, NCH, TC]),
                ALU.mult))
            q.c("tensor", lambda t: t.matmul(
                psBig[:, 0:TC], onesr8[0:1, :], strow(0, 3), start=True, stop=True))
            q.c("vector", lambda v: v.tensor_tensor(
                xm3, xm3, psBig[:, 0:TC].unsqueeze(1).broadcast_to([128, NCH, TC]),
                ALU.subtract))
            for mc in range(NCH):
                pd = psA if mc % 2 == 0 else psB
                for kc in range(NCH):
                    q.c("tensor", lambda t, mc=mc, kc=kc, pd=pd: t.matmul(
                        pd[:, 0:TC],
                        wto[:, kc * 1024 + mc * 128:kc * 1024 + (mc + 1) * 128],
                        o2f3[:, kc, :], start=(kc == 0), stop=(kc == NCH - 1)))
                q.c("vector", lambda v, mc=mc, pd=pd: v.tensor_add(
                    xm3[:, mc, :], xm3[:, mc, :], pd[:, 0:TC]))

            q.c("vector", lambda v: v.tensor_mul(sc8[:, 0:NCH * TC], xm[:, :], xm[:, :]))
            for kc in range(NCH):
                q.c("tensor", lambda t, kc=kc: t.matmul(
                    psA[0:1, 0:TC], onescf[:, 0:1], xm3[:, kc, :],
                    start=(kc == 0), stop=(kc == NCH - 1)))
            for kc in range(NCH):
                q.c("tensor", lambda t, kc=kc: t.matmul(
                    psB[0:1, 0:TC], onescf[:, 0:1],
                    sc8[:, 0:NCH * TC].rearrange("p (c t) -> p c t", c=NCH)[:, kc, :],
                    start=(kc == 0), stop=(kc == NCH - 1)))
            q.c("vector", lambda v: v.tensor_scalar_mul(strow(0, 0), psA[0:1, 0:TC], 1.0 / D))
            q.c("vector", lambda v: v.tensor_scalar_mul(strow(0, 1), psB[0:1, 0:TC], 1.0 / D))
            q.c("vector", lambda v: v.tensor_mul(strow(0, 2), strow(0, 0), strow(0, 0)))
            q.c("vector", lambda v: v.tensor_sub(strow(0, 1), strow(0, 1), strow(0, 2)))
            q.c("vector", lambda v: v.tensor_scalar_add(strow(0, 3), strow(0, 1), EPS))
            q.c("scalar", lambda s: s.activation(strow(0, 1), strow(0, 3), AF.Sqrt))
            q.c("vector", lambda v: v.reciprocal(strow(0, 2), strow(0, 1)))
            q.c("vector", lambda v: v.tensor_mul(strow(0, 3), strow(0, 2), strow(0, 0)))
            q.c("vector", lambda v: v.tensor_scalar_mul(strow(0, 3), strow(0, 3), -1.0))
            q.c("vector", lambda v: v.tensor_copy(r2p[0:1, :], strow(0, 3)))
            q.c("tensor", lambda t: t.matmul(
                psBig[:, 0:TC], onesr8[0:1, :], strow(0, 2), start=True, stop=True))
            q.c("vector", lambda v: v.tensor_tensor(
                xa3, xm3, psBig[:, 0:TC].unsqueeze(1).broadcast_to([128, NCH, TC]),
                ALU.mult))

            for mc in range(cfg.FICH):
                pd = psA if mc % 2 == 0 else psB
                for kc in range(NCH):
                    q.c("tensor", lambda t, mc=mc, kc=kc, pd=pd: t.matmul(
                        pd[:, 0:TC],
                        wt1[:, kc * cfg.FI + mc * 128:kc * cfg.FI + (mc + 1) * 128],
                        xa3[:, kc, :], start=(kc == 0), stop=False))
                q.c("tensor", lambda t, mc=mc, pd=pd: t.matmul(
                    pd[:, 0:TC], r1w1[:, mc * 128:(mc + 1) * 128], r2p[:, :],
                    start=False, stop=True))
                q.c("scalar", lambda s, mc=mc, pd=pd: s.activation(
                    t1v3[:, mc, :], pd[:, 0:TC], AF.Gelu))
            for mc in range(NCH):
                pd = psA if mc % 2 == 0 else psB
                for kc in range(cfg.FICH):
                    q.c("tensor", lambda t, mc=mc, kc=kc, pd=pd: t.matmul(
                        pd[:, 0:TC],
                        wt2[:, kc * 1024 + mc * 128:kc * 1024 + (mc + 1) * 128],
                        t1v3[:, kc, :], start=(kc == 0),
                        stop=(kc == cfg.FICH - 1)))
                q.c("vector", lambda v, mc=mc, pd=pd: v.scalar_tensor_tensor(
                    xm3[:, mc, :], pd[:, 0:TC], w2bf[:, mc:mc + 1],
                    xm3[:, mc, :], ALU.add, ALU.add))

            _mk["ts"] = len(q.steps)
            _mk["tj"] = []
            att = sc8[:, 1024:2048]
            for j in range(TPT):
                _tj = {"a": len(q.steps)}
                _mk["tj"].append(_tj)
                tt = t0 + j * 128
                tcol = j * 128
                for kc in range(NCH):
                    q.c("tensor", lambda t, kc=kc, tcol=tcol: t.matmul(
                        psA[:, 0:E], xm3[:, kc, tcol:tcol + 128],
                        wgw[:, kc * E:(kc + 1) * E],
                        start=(kc == 0), stop=(kc == NCH - 1)))
                q.c("vector", lambda v, j=j, agb=agb: v.tensor_copy(
                    agw[:, agb + j * 16 + 8:agb + j * 16 + 16], psA[:, 0:E]))
                for fc in range(NCH):
                    q.c("tensor", lambda t, fc=fc, tcol=tcol: t.transpose(
                        psBig[:, fc * 128:(fc + 1) * 128],
                        xm3[:, fc, tcol:tcol + 128], idf[:, :]))
                q.c("vector", lambda v: v.tensor_copy(att, psBig[:, :]))
                _tj["m"] = len(q.steps)
                q.c("vector", lambda v: v.reduce_sum(str2[:, 0:1], att, AX.X))
                q.c("scalar", lambda s: s.activation(
                    psO[:, :], att, AF.Square,
                    accum_out=str2[:, 1:2]))
                q.c("vector", lambda v: v.tensor_scalar_mul(str2[:, 0:1], str2[:, 0:1], 1.0 / D))
                q.c("vector", lambda v: v.tensor_scalar_mul(str2[:, 1:2], str2[:, 1:2], 1.0 / D))
                q.c("vector", lambda v: v.tensor_mul(str2[:, 2:3], str2[:, 0:1], str2[:, 0:1]))
                q.c("vector", lambda v: v.tensor_sub(str2[:, 1:2], str2[:, 1:2], str2[:, 2:3]))
                q.c("vector", lambda v: v.tensor_scalar_add(str2[:, 1:2], str2[:, 1:2], EPS))
                q.c("scalar", lambda s: s.activation(str2[:, 2:3], str2[:, 1:2], AF.Sqrt))
                q.c("vector", lambda v: v.reciprocal(str2[:, 3:4], str2[:, 2:3]))
                q.c("vector", lambda v: v.tensor_mul(str2[:, 4:5], str2[:, 3:4], str2[:, 0:1]))
                q.c("vector", lambda v: v.tensor_scalar_mul(str2[:, 4:5], str2[:, 4:5], -1.0))
                q.dma("sync", lambda s, tt=tt: s.dma_start(
                    out=xt1[:, :], in_=x_tm.ap()[tt:tt + 128, :]))
                q.c("vector", lambda v: v.tensor_scalar(
                    att, att, str2[:, 3:4], str2[:, 4:5],
                    ALU.mult, ALU.add))
                q.c("vector", lambda v: v.tensor_add(xt1[:, :], xt1[:, :], att))
                q.dma("sync", lambda s, tt=tt: s.dma_start(
                    out=x1buf.ap()[tt:tt + 128, :], in_=xt1[:, :]))
                q.c("vector", lambda v: v.reduce_sum(str2[:, 5:6], xt1[:, :], AX.X))
                q.c("scalar", lambda s: s.activation(
                    psO[:, :], xt1[:, :], AF.Square,
                    accum_out=str2[:, 6:7]))
                q.c("vector", lambda v: v.tensor_scalar_mul(str2[:, 5:6], str2[:, 5:6], 1.0 / D))
                q.c("vector", lambda v: v.tensor_scalar_mul(str2[:, 6:7], str2[:, 6:7], 1.0 / D))
                q.c("vector", lambda v: v.tensor_mul(str2[:, 7:8], str2[:, 5:6], str2[:, 5:6]))
                q.c("vector", lambda v: v.tensor_sub(str2[:, 6:7], str2[:, 6:7], str2[:, 7:8]))
                q.c("vector", lambda v: v.tensor_scalar_add(str2[:, 6:7], str2[:, 6:7], EPS))
                q.c("scalar", lambda s: s.activation(str2[:, 7:8], str2[:, 6:7], AF.Sqrt))
                q.c("vector", lambda v: v.reciprocal(str2[:, 8:9], str2[:, 7:8]))
                q.c("vector", lambda v: v.tensor_mul(str2[:, 9:10], str2[:, 8:9], str2[:, 5:6]))
                q.c("vector", lambda v: v.tensor_scalar_mul(str2[:, 9:10], str2[:, 9:10], -1.0))
                q.c("vector", lambda v: v.tensor_scalar(
                    l2t[:, :], xt1[:, :], str2[:, 8:9], str2[:, 9:10],
                    ALU.mult, ALU.add))
                q.dma("sync", lambda s, tt=tt: s.dma_start(
                    out=ln2buf.ap()[tt:tt + 128, :], in_=l2t[:, :]))
                q.c("vector", lambda v, j=j, agb=agb: v.tensor_scalar(
                    tbm[:, 0:E], agw[:, agb + j * 16 + 8:agb + j * 16 + 16],
                    str2[:, 3:4], None, ALU.mult))
                q.c("vector", lambda v, j=j, agb=agb: v.tensor_add(
                    tbm[:, 0:E], tbm[:, 0:E], agw[:, agb + j * 16:agb + j * 16 + 8]))
                q.c("vector", lambda v: v.scalar_tensor_tensor(
                    tbm[:, 8:8 + E], csr[:, :], str2[:, 4:5], tbm[:, 0:E],
                    ALU.mult, ALU.add))
                q.c("vector", lambda v: v.tensor_scalar(
                    tbm[:, 8:8 + E], tbm[:, 8:8 + E], str2[:, 8:9], None, ALU.mult))
                q.c("vector", lambda v: v.scalar_tensor_tensor(
                    tbm[:, 0:E], csr[:, :], str2[:, 9:10], tbm[:, 8:8 + E],
                    ALU.mult, ALU.add))
                q.c("vector", lambda v: v.tensor_add(tbm[:, 0:E], tbm[:, 0:E], gbr[:, :]))
                q.c("tensor", lambda t: t.transpose(psA[0:8, 0:128], tbm[:, 0:E], idf[:, :]))
                q.c("vector", lambda v: v.tensor_copy(lgt[:, :], psA[0:8, 0:128]))
                q.dma("sync", lambda s, tt=tt: s.dma_start(
                    out=logitsT.ap()[:, tt:tt + 128], in_=lgt[:, :]))
                _tj["e"] = len(q.steps)
            _mk["ce"] = len(q.steps)

    _new = q.steps[:chunk_marks[0]["cs"]]
    _pend = None
    _prev = chunk_marks[0]["cs"]
    for _m in chunk_marks:
        _new += q.steps[_prev:_m["cs"]]
        _new += q.steps[_m["cs"]:_m["se"]]
        if _pend:
            _new += _pend[0]
        _new += q.steps[_m["se"]:_m["ge"]]
        if _pend:
            _new += _pend[1]
            _new += _pend[2]
        _new += q.steps[_m["ge"]:_m["xe"]]
        if _pend:
            _new += _pend[3]
        _new += q.steps[_m["xe"]:_m["ts"]]
        tj0, tj1 = _m["tj"]
        _pend = (q.steps[tj0["a"]:tj0["m"]], q.steps[tj0["m"]:tj0["e"]],
                 q.steps[tj1["a"]:tj1["m"]], q.steps[tj1["m"]:tj1["e"]])
        _prev = _m["ce"]
    for _p in _pend:
        _new += _p
    _new += q.steps[_prev:]
    assert len(_new) == len(q.steps), (len(_new), len(q.steps))
    q.steps = _new

    if milestones is not None:
        milestones['phaseA_end'] = len(q.steps)
    free_from("xf")

    slotc = sb("slotc", [128, NTILE], I32)
    gpc = sb("gpc", [128, NTILE], F32)
    w1s = sb("w1s", [128, NCH * cfg.F], BF16)
    w2s = sb("w2s", [128, cfg.FCH * D], BF16)
    lgs = sb("lgs", [128, E * PERP], F32)
    rt1 = sb("rt1", [128, E * PERP], F32)
    rt2 = sb("rt2", [128, E * PERP], F32)
    rt3 = sb("rt3", [128, PERP], F32)
    rt4 = sb("rt4", [128, PERP], F32)
    rt5 = sb("rt5", [128, PERP], F32)
    emx = sb("emx", [128, PERP], F32)
    gpv = sb("gpv", [128, PERP], F32)
    tot8 = sb("tot8", [128, E], F32)
    off8 = sb("off8", [128, E], F32)
    sltr = sb("sltr", [128, 128], F32)
    idx32 = sb("idx32", [128, 1], I32)
    idxf = sb("idxf", [128, 1], F32)
    cid = sb("cid", [128, 1], F32)
    l2s = sb("l2s", [128, D], BF16)

    lgs3 = lgs[:, :].rearrange("p (e t) -> p e t", e=E)
    rt13 = rt1[:, :].rearrange("p (e t) -> p e t", e=E)
    rt23 = rt2[:, :].rearrange("p (e t) -> p e t", e=E)

    q.dma("sync", lambda s: s.dma_start(out=sltr[:, :], in_=sltri.ap()))
    if milestones is not None:
        milestones['pre_ag1'] = len(q.steps)
    q.c("gpsimd", lambda g: g.collective_compute(
        "AllGather", ALU.bypass, replica_groups=[list(range(cfg.NCORES))],
        ins=[logitsT.ap().opt()], outs=[logits_all.ap().opt()]))
    if milestones is not None:
        milestones['post_ag1'] = len(q.steps)
    for c in range(cfg.NCORES):
        q.dma("sync", lambda s, c=c: s.dma_start(
            out=lgs3[c * 16:(c + 1) * 16, :, :],
            in_=logits_all.ap()[c * E:(c + 1) * E, :].rearrange(
                "e (pp t) -> pp e t", pp=16)))
    q.c("vector", lambda v: v.tensor_copy(emx[:, :], lgs3[:, 0, :]))
    for e in range(1, E):
        q.c("vector", lambda v, e=e: v.tensor_max(emx[:, :], emx[:, :], lgs3[:, e, :]))
    q.c("vector", lambda v: v.tensor_tensor(
        rt13, lgs3, emx[:, :].unsqueeze(1).broadcast_to([128, E, PERP]), ALU.subtract))
    q.c("scalar", lambda s: s.activation(rt2[:, :], rt1[:, :], AF.Exp))
    q.c("vector", lambda v: v.tensor_copy(gpv[:, :], rt23[:, 0, :]))
    for e in range(1, E):
        q.c("vector", lambda v, e=e: v.tensor_add(gpv[:, :], gpv[:, :], rt23[:, e, :]))
    q.c("vector", lambda v: v.reciprocal(gpv[:, :], gpv[:, :]))
    q.c("vector", lambda v: v.tensor_tensor(
        rt13, lgs3, emx[:, :].unsqueeze(1).broadcast_to([128, E, PERP]), ALU.is_ge))
    q.c("vector", lambda v: v.memset(rt3[:, :], 1e6))
    for e in range(E):
        q.c("vector", lambda v, e=e: v.tensor_scalar(
            rt4[:, :], rt13[:, e, :], -1e6, 1e6 + e, ALU.mult, ALU.add))
        q.c("vector", lambda v: v.tensor_tensor(rt3[:, :], rt3[:, :], rt4[:, :], ALU.min))
    q.c("vector", lambda v: v.tensor_copy(emx[:, :], rt3[:, :]))
    for e in range(E):
        q.c("vector", lambda v, e=e: v.tensor_single_scalar(
            rt13[:, e, :], emx[:, :], float(e), ALU.is_equal))
        q.c("vector", lambda v, e=e: v.tensor_tensor_scan(
            rt23[:, e, :], rt13[:, e, :], zrow[:, 0:PERP], 0.0, ALU.add, ALU.add))
    q.c("vector", lambda v: v.tensor_copy(tot8[:, :], rt23[:, :, PERP - 1]))
    q.c("tensor", lambda t: t.matmul(
        psA[0:8, 0:128], tot8[:, :], sltr[:, :], start=True, stop=True))
    q.c("vector", lambda v: v.tensor_copy(sltr[0:8, 0:128], psA[0:8, 0:128]))
    q.c("tensor", lambda t: t.transpose(psA[:, 0:8], sltr[0:8, 0:128], idf[0:8, 0:8]))
    q.c("vector", lambda v: v.tensor_copy(off8[:, :], psA[:, 0:8]))
    q.c("vector", lambda v: v.memset(rt3[:, :], 0.0))
    q.c("vector", lambda v: v.memset(rt4[:, :], 0.0))
    for e in range(E):
        q.c("vector", lambda v, e=e: v.scalar_tensor_tensor(
            rt3[:, :], rt13[:, e, :], off8[:, e:e + 1], rt3[:, :], ALU.mult, ALU.add))
        q.c("vector", lambda v, e=e: v.tensor_mul(rt5[:, :], rt13[:, e, :], rt23[:, e, :]))
        q.c("vector", lambda v: v.tensor_add(rt4[:, :], rt4[:, :], rt5[:, :]))
    q.c("vector", lambda v: v.tensor_add(rt3[:, :], rt3[:, :], rt4[:, :]))
    q.c("vector", lambda v: v.tensor_scalar_add(rt3[:, :], rt3[:, :], -1.0))
    q.c("vector", lambda v: v.tensor_single_scalar(
        rt4[:, :], rt3[:, :], float(cfg.CAP - 1), ALU.is_le))
    q.c("vector", lambda v: v.tensor_scalar(
        rt5[:, :], emx[:, :], float(cfg.CAP), None, ALU.mult))
    q.c("vector", lambda v: v.tensor_add(rt5[:, :], rt5[:, :], rt3[:, :]))
    q.c("vector", lambda v: v.tensor_mul(rt5[:, :], rt5[:, :], rt4[:, :]))
    q.c("vector", lambda v: v.tensor_scalar(
        rt4[:, :], rt4[:, :], -float(cfg.NSLOT), float(cfg.NSLOT), ALU.mult, ALU.add))
    q.c("vector", lambda v: v.tensor_add(rt5[:, :], rt5[:, :], rt4[:, :]))

    q.dma("sync", lambda s: s.dma_start(
        out=slotall.ap().rearrange("a w -> (a w)").rearrange("(p t) -> p t", p=128),
        in_=rt5[:, :]))
    q.dma("sync", lambda s: s.dma_start(
        out=gpall.ap().rearrange("a w -> (a w)").rearrange("(p t) -> p t", p=128),
        in_=gpv[:, :]))
    q.dma("sync", lambda s: s.dma_start(
        out=cid[:, :], in_=coreid.ap().broadcast_to([128, 1])))
    q.c("gpsimd", lambda g: g.iota(idxf[:, :], pattern=[[0, 1]], base=0,
                                   channel_multiplier=1,
                                   allow_small_or_imprecise_dtypes=True))
    q.c("vector", lambda v: v.tensor_single_scalar(cid[:, :], cid[:, :], float(NTILE), ALU.mult))
    q.c("vector", lambda v: v.tensor_add(idxf[:, :], idxf[:, :], cid[:, :]))
    q.c("vector", lambda v: v.tensor_copy(idx32[:, :], idxf[:, :]))
    q.dma("gpsimd", lambda g: g.indirect_dma_start(
        sltr[0:NTILE, :], None, slotall.ap(),
        IndirectOffsetOnAxis(idx32[0:NTILE, 0:1], 0)))
    if debug_out:
        sltrdbg = nc.dram_tensor("sltrdbg", [128, 128], F32,
                                 kind="ExternalOutput")
        idxdbg = nc.dram_tensor("idxdbg", [128, 2], F32, kind="ExternalOutput")
        q.dma("sync", lambda s: s.dma_start(out=sltrdbg.ap(), in_=sltr[:, :]))
        q.c("vector", lambda v: v.tensor_copy(rt3[:, 0:1], idx32[:, :]))
        q.c("vector", lambda v: v.tensor_copy(rt3[:, 1:2], idxf[:, :]))
        q.dma("sync", lambda s: s.dma_start(out=idxdbg.ap(), in_=rt3[:, 0:2]))
    q.c("tensor", lambda t: t.transpose(
        psA[:, 0:NTILE], sltr[0:NTILE, :], idf[0:NTILE, 0:NTILE]))
    q.c("vector", lambda v: v.tensor_copy(slotc[:, :], psA[:, 0:NTILE]))
    q.dma("gpsimd", lambda g: g.indirect_dma_start(
        sltr[0:NTILE, :], None, gpall.ap(),
        IndirectOffsetOnAxis(idx32[0:NTILE, 0:1], 0)))
    q.c("tensor", lambda t: t.transpose(
        psA[:, 0:NTILE], sltr[0:NTILE, :], idf[0:NTILE, 0:NTILE]))
    q.c("vector", lambda v: v.tensor_copy(gpc[:, :], psA[:, 0:NTILE]))

    for k in range(NTILE):
        q.dma("sync", lambda s, k=k: s.dma_start(
            out=l2s[:, :], in_=ln2buf.ap()[k * 128:(k + 1) * 128, :]))
        q.dma("gpsimd", lambda g, k=k: g.indirect_dma_start(
            slotbuf.ap(), IndirectOffsetOnAxis(slotc[:, k:k + 1], 0),
            l2s[:, :], None))

    if milestones is not None:
        milestones['pre_rs'] = len(q.steps)
    q.c("gpsimd", lambda g: g.collective_compute(
        "ReduceScatter", ALU.add, replica_groups=[list(range(cfg.NCORES))],
        ins=[slotbuf.ap()[0:cfg.NSLOT, :].opt()], outs=[rsout.ap().opt()]))
    if milestones is not None:
        milestones['post_rs'] = len(q.steps)

    free_from("lgs")

    onesb = sb("onesb", [1, 1024], BF16)
    q.c("vector", lambda v: v.memset(onesb[:, :], 1.0))
    xfm = sb("xfm", [128, NCH * cfg.CAP], BF16)
    xtile = sb("xtile", [128, D], BF16)
    estg = sb("estg", [128, 1024], F32)
    estg2 = sb("estg2", [128, 1024], F32)
    _esp = [0]
    def _estg():
        _esp[0] ^= 1
        return (estg, "sync") if _esp[0] else (estg2, "scalar")

    hid = sb("hid", [128, cfg.FCH * cfg.CAP], BF16)
    b1r = sb("b1r", [1, cfg.F], BF16)
    eb2r = sb("eb2r", [128, D], F32)
    oev = sb("oev", [128, D], F32)
    xfm3 = xfm[:, :].rearrange("p (c n) -> p c n", c=NCH)
    hid3 = hid[:, :].rearrange("p (c n) -> p c n", c=cfg.FCH)

    _ch = min(1024, cfg.F)
    for j in range(cfg.F // _ch):
        q.dma("sync", lambda s, j=j, _ch=_ch: s.dma_start(
            out=estg[0:1, 0:_ch], in_=eb1.ap()[0:1, j * _ch:(j + 1) * _ch]))
        q.c("vector", lambda v, j=j, _ch=_ch: v.tensor_copy(
            b1r[0:1, j * _ch:(j + 1) * _ch], estg[0:1, 0:_ch]))
    q.dma("sync", lambda s: s.dma_start(out=eb2r[:, :], in_=eb2.ap().broadcast_to([128, D])))

    for k in range(cfg.NXT):
        q.dma("sync", lambda s, k=k: s.dma_start(
            out=xtile[:, :], in_=rsout.ap()[k * 128:(k + 1) * 128, :]))
        for fc in range(NCH):
            q.c("tensor", lambda t, fc=fc: t.transpose(
                psT[:, fc * 128:(fc + 1) * 128], xtile[:, fc * 128:(fc + 1) * 128],
                idb[:, :]))
        q.c("vector", lambda v, k=k: v.tensor_copy(
            xfm3[:, :, k * 128:(k + 1) * 128],
            psT[:, :].rearrange("p (c n) -> p c n", c=NCH)[:, :, 0:128]))

    for kc in range(NCH):
        for j in range(cfg.F // _ch):
            eb_, eq_ = _estg()
            q.dma(eq_, lambda s, kc=kc, j=j, _ch=_ch, eb_=eb_: s.dma_start(
                out=eb_[:, 0:_ch],
                in_=ew1.ap()[kc * 128:(kc + 1) * 128, j * _ch:(j + 1) * _ch]))
            q.c("vector", lambda v, kc=kc, j=j, _ch=_ch, eb_=eb_: v.tensor_copy(
                w1s[:, kc * cfg.F + j * _ch:kc * cfg.F + (j + 1) * _ch],
                eb_[:, 0:_ch]))
    nsplits = [(0, min(512, cfg.CAP))]
    if cfg.CAP > 512:
        nsplits.append((512, cfg.CAP))
    for mc in range(cfg.FCH):
        for si, (n0, n1) in enumerate(nsplits):
            pd = psA if si == 0 else psB
            for kc in range(NCH):
                q.c("tensor", lambda t, mc=mc, kc=kc, n0=n0, n1=n1, pd=pd: t.matmul(
                    pd[:, 0:n1 - n0],
                    w1s[:, kc * cfg.F + mc * 128:kc * cfg.F + (mc + 1) * 128],
                    xfm3[:, kc, n0:n1], start=(kc == 0), stop=False))
            q.c("tensor", lambda t, mc=mc, n0=n0, n1=n1, pd=pd: t.matmul(
                pd[:, 0:n1 - n0], b1r[0:1, mc * 128:(mc + 1) * 128],
                onesb[0:1, n0:n1], start=False, stop=True))
            q.c("scalar", lambda s, mc=mc, n0=n0, n1=n1, pd=pd: s.activation(
                hid3[:, mc, n0:n1], pd[:, 0:n1 - n0], AF.Gelu_apprx_tanh))
    for kc in range(cfg.FCH):
        eb_, eq_ = _estg()
        q.dma(eq_, lambda s, kc=kc, eb_=eb_: s.dma_start(
            out=eb_[:, :], in_=ew2.ap()[kc * 128:(kc + 1) * 128, :]))
        q.c("vector", lambda v, kc=kc, eb_=eb_: v.tensor_copy(
            w2s[:, kc * D:(kc + 1) * D], eb_[:, :]))
    for k in range(cfg.NXT):
        for n0, n1 in ((0, 512), (512, 1024)):
            for kc in range(cfg.FCH):
                q.c("tensor", lambda t, k=k, kc=kc, n0=n0, n1=n1: t.matmul(
                    psO[:, n0:n1], hid3[:, kc, k * 128:(k + 1) * 128],
                    w2s[:, kc * D + n0:kc * D + n1],
                    start=(kc == 0), stop=(kc == cfg.FCH - 1)))
        q.c("vector", lambda v: v.tensor_add(oev[:, :], psO[:, :], eb2r[:, :]))
        q.dma("sync", lambda s, k=k: s.dma_start(
            out=ag2in.ap()[k * 128:(k + 1) * 128, :], in_=oev[:, :]))

    if milestones is not None:
        milestones['pre_ag2'] = len(q.steps)
    q.c("gpsimd", lambda g: g.collective_compute(
        "AllGather", ALU.bypass, replica_groups=[list(range(cfg.NCORES))],
        ins=[ag2in.ap().opt()], outs=[outbuf.ap()[0:cfg.NSLOT, :].opt()]))
    if milestones is not None:
        milestones['post_ag2'] = len(q.steps)

    free_from("w1s")
    gat = sb("gat", [128, D], F32)
    xci = sb("xci", [128, D], F32)
    dm2 = sb("dm2", [128, D], F32)
    stc = sb("stc", [128, 6], F32)
    for k in range(NTILE):
        q.dma("gpsimd", lambda g, k=k: g.indirect_dma_start(
            gat[:, :], None, outbuf.ap(), IndirectOffsetOnAxis(slotc[:, k:k + 1], 0)))
        q.c("vector", lambda v, k=k: v.tensor_scalar_mul(
            gat[:, :], gat[:, :], gpc[:, k:k + 1]))
        q.c("vector", lambda v: v.reduce_sum(stc[:, 0:1], gat[:, :], AX.X))
        q.c("scalar", lambda s: s.activation(
            dm2[:, :], gat[:, :], AF.Square, accum_out=stc[:, 1:2]))
        q.c("vector", lambda v: v.tensor_scalar_mul(stc[:, 0:1], stc[:, 0:1], 1.0 / D))
        q.c("vector", lambda v: v.tensor_scalar_mul(stc[:, 1:2], stc[:, 1:2], 1.0 / D))
        q.c("vector", lambda v: v.tensor_mul(stc[:, 2:3], stc[:, 0:1], stc[:, 0:1]))
        q.c("vector", lambda v: v.tensor_sub(stc[:, 1:2], stc[:, 1:2], stc[:, 2:3]))
        q.c("vector", lambda v: v.tensor_scalar_add(stc[:, 1:2], stc[:, 1:2], EPS))
        q.c("scalar", lambda s: s.activation(stc[:, 2:3], stc[:, 1:2], AF.Sqrt))
        q.c("vector", lambda v: v.reciprocal(stc[:, 3:4], stc[:, 2:3]))
        q.c("vector", lambda v: v.tensor_mul(stc[:, 4:5], stc[:, 3:4], stc[:, 0:1]))
        q.c("vector", lambda v: v.tensor_scalar_mul(stc[:, 4:5], stc[:, 4:5], -1.0))
        q.dma("sync", lambda s, k=k: s.dma_start(
            out=xci[:, :], in_=x1buf.ap()[k * 128:(k + 1) * 128, :]))
        q.c("vector", lambda v: v.tensor_scalar(
            gat[:, :], gat[:, :], stc[:, 3:4], stc[:, 4:5], ALU.mult, ALU.add))
        q.c("vector", lambda v: v.tensor_add(gat[:, :], gat[:, :], xci[:, :]))
        q.dma("sync", lambda s, k=k: s.dma_start(
            out=out.ap()[k * 128:(k + 1) * 128, :], in_=gat[:, :]))

    if milestones is not None:
        milestones['all'] = len(q.steps)
    if max_steps is not None:
        q.steps = q.steps[:max_steps]
    if phasec_probe is not None:
        pdbg = nc.dram_tensor("pdbg", [128, D], F32, kind="ExternalOutput")
        pdbg2 = nc.dram_tensor("pdbg2", [128, cfg.NTILE + 1], F32,
                               kind="ExternalOutput")
        if phasec_probe == "iota":
            pidx = sb("pidx", [128, 2], I32)
            pidxf = pidx[:, 1:2].bitcast(F32)
            q.c("gpsimd", lambda g: g.iota(
                pidxf, pattern=[[0, 1]], base=0, channel_multiplier=1,
                allow_small_or_imprecise_dtypes=True))
            q.c("vector", lambda v: v.tensor_copy(pidx[:, 0:1], pidxf))
            q.dma("gpsimd", lambda g: g.indirect_dma_start(
                gat[:, :], None, outbuf.ap(),
                IndirectOffsetOnAxis(pidx[:, 0:1], 0)))
        else:
            q.dma("gpsimd", lambda g: g.indirect_dma_start(
                gat[:, :], None, outbuf.ap(),
                IndirectOffsetOnAxis(slotc[:, 0:1], 0)))
        q.dma("sync", lambda s: s.dma_start(out=pdbg.ap(), in_=gat[:, :]))
        q.c("vector", lambda v: v.tensor_copy(xci[:, 0:cfg.NTILE],
                                              slotc[:, :]))
        q.dma("sync", lambda s: s.dma_start(
            out=pdbg2.ap()[:, 0:cfg.NTILE], in_=xci[:, 0:cfg.NTILE]))

    if dump_src is not None:
        _dct = dict(logitsT=logitsT, logits_all=logits_all, slotall=slotall,
                    gpall=gpall, slotbuf=slotbuf, rsout=rsout, ag2in=ag2in,
                    outbuf=outbuf, x1buf=x1buf, ln2buf=ln2buf)
        for _nm, _shape, _dt in zip(dump_src, dump_shape, dump_dt):
            _src = _dct[_nm]
            dbgout = nc.dram_tensor("dbg_" + _nm, list(_shape), _dt,
                                    kind="ExternalOutput")
            for r0 in range(0, _shape[0], 512):
                r1 = min(_shape[0], r0 + 512)
                q.dma("sync", lambda s, r0=r0, r1=r1, dbgout=dbgout, _src=_src:
                      s.dma_start(out=dbgout.ap()[r0:r1, :],
                                  in_=_src.ap()[r0:r1, :]))
    if emit_mode == "probe":
        with nc.Block() as blk:
            probe_out["info"] = q.emit_probe(nc, blk)
            probe_out["seq"] = q
    elif emit_mode == "sched":
        import contextlib
        with contextlib.ExitStack() as st:
            sems = {}
            for d in schedule["dims"]:
                sems[d] = st.enter_context(nc.semaphore("sem_" + d))
            blk = st.enter_context(nc.Block())
            q.emit(nc, blk, sems, schedule=schedule)
    else:
        with nc.semaphore("s_y") as s_y, nc.semaphore("s_v") as s_v, \
             nc.semaphore("s_s") as s_s, \
             nc.semaphore("s_t") as s_t, nc.semaphore("s_g") as s_g, \
             nc.semaphore("s_yd") as s_yd, nc.semaphore("s_sd") as s_sd, \
             nc.semaphore("s_gd") as s_gd, nc.Block() as blk:
            q.emit(nc, blk, {"sync": s_y, "vector": s_v, "scalar": s_s,
                             "tensor": s_t,
                             "gpsimd": s_g, "sync_d": s_yd, "scalar_d": s_sd,
                             "gpsimd_d": s_gd})

    for g, _ in reversed(allocs):
        g.__exit__(None, None, None)
    return nc


def _consts(cfg):
    import ml_dtypes
    eye = np.eye(128, dtype=np.float32)
    g = cfg.GC
    triu = np.triu(np.ones((g, g), np.float32))
    cm = np.zeros((128, 64), np.float32)
    cm[0:g, 0:g] = triu
    cm[g:2 * g, 0:g] = triu
    return {
        "identf": eye,
        "identb": eye.astype(ml_dtypes.bfloat16),
        "cmask": cm.astype(ml_dtypes.bfloat16),
        "sltri": np.triu(np.ones((128, 128), np.float32), k=1),
    }


def make_in_maps(cfg, inputs):
    x = np.asarray(inputs["L0_attn_x"], np.float32)
    xr = np.ascontiguousarray(x.reshape(cfg.NCORES, cfg.NTOK, cfg.D))

    def f32(name):
        return np.ascontiguousarray(np.asarray(inputs[name], np.float32))

    common = {
        "Wq": f32("L0_attn_Wq"), "Wk": f32("L0_attn_Wk"),
        "Wv": f32("L0_attn_Wv"), "Wg": f32("L0_attn_Wg"),
        "Wo": f32("L0_attn_Wo"), "W1": f32("L0_attn_W1"), "W2": f32("L0_attn_W2"),
        "gkw1": f32("L0_attn_gk_w1"), "gkw2": f32("L0_attn_gk_w2"),
        "ln_w": f32("L0_attn_ln_w").reshape(1, -1),
        "ln_b": f32("L0_attn_ln_b").reshape(1, -1),
        "gk_b2": f32("L0_attn_gk_b2").reshape(1, -1),
        "b1i": f32("L0_attn_b1").reshape(1, -1),
        "b2i": f32("L0_attn_b2").reshape(1, -1),
        "rms_w": f32("L0_attn_rms_w").reshape(1, -1),
        "gw": f32("L0_ffn_gate_w"), "gb": f32("L0_ffn_gate_b").reshape(1, -1),
    }
    common.update(_consts(cfg))
    ew1 = f32("L0_ffn_w1"); eb1 = f32("L0_ffn_b1")
    ew2 = f32("L0_ffn_w2"); eb2 = f32("L0_ffn_b2")

    in_maps = []
    for c in range(cfg.NCORES):
        m = dict(common)
        m["x_tm"] = xr[c]
        m["xT"] = np.ascontiguousarray(xr[c].T)
        m["ew1"] = ew1[c]; m["eb1"] = eb1[c].reshape(1, -1)
        m["ew2"] = ew2[c]; m["eb2"] = eb2[c].reshape(1, -1)
        m["coreid"] = np.array([[c]], np.float32)
        in_maps.append(m)
    return in_maps


def build_scheduled(cfg, **kw):
    po = {}
    build_program(cfg, emit_mode="probe", probe_out=po, **kw)
    sched = po["seq"].compute_schedule(po["info"])
    return build_program(cfg, emit_mode="sched", schedule=sched, **kw)


def kernel(**inputs):
    cfg = Cfg()
    x = np.asarray(inputs["L0_attn_x"], np.float32)
    NB = x.shape[0]
    nc = build_scheduled(cfg)
    in_maps = make_in_maps(cfg, inputs)
    res = run_bass_kernel_spmd(nc, in_maps, core_ids=list(range(cfg.NCORES)))
    outs = np.stack([res.results[c]["out"] for c in range(cfg.NCORES)])
    return outs.reshape(NB, cfg.T, cfg.D)


# version 16 (final)
# speedup vs baseline: 1.0331x; 1.0230x over previous
import numpy as np
import concourse.bass as bass
import concourse.mybir as mybir
from concourse.bass_utils import run_bass_kernel_spmd
from concourse.bass_core_rust import IndirectOffsetOnAxis

F32 = mybir.dt.float32
BF16 = mybir.dt.bfloat16
I32 = mybir.dt.int32
AF = mybir.ActivationFunctionType
ALU = mybir.AluOpType
AX = mybir.AxisListType


class Cfg:
    D = 1024
    NCH = 8
    H = 16
    DK = 32
    DV = 64
    LR = 16
    FI = 2048
    FICH = 16
    F = 4096
    FCH = 32
    E = 8
    CAP = 640
    GATE_NORM = 16.0
    EPS = 1e-5
    T = 2048
    SEQ = 2
    NCORES = 8
    TC = 256
    GC = 64
    DKS = DK ** -0.5

    NTOK = property(lambda s: s.T * s.SEQ)
    NTILE = property(lambda s: s.NTOK // 128)
    NSLOT = property(lambda s: s.E * s.CAP)
    PERP = property(lambda s: s.NCORES * s.NTOK // 128)
    NXT = property(lambda s: s.CAP // 128)


class Seq:

    ENGS = ("sync", "scalar", "vector", "tensor", "gpsimd")
    DIMS = ("sync", "scalar", "vector", "tensor", "gpsimd",
            "sync_d", "scalar_d", "gpsimd_d")

    def __init__(self):
        self.steps = []

    def c(self, eng, fn):
        self.steps.append((eng, "c", fn))

    def dma(self, eng, fn):
        self.steps.append((eng, "d", fn))

    @staticmethod
    def key(eng, kind):
        return eng + "_d" if kind == "d" else eng

    def emit_probe(self, nc, block):
        import numpy as _np
        per = {e: [] for e in self.ENGS}
        for idx, (eng, kind, fn) in enumerate(self.steps):
            per[eng].append((idx, fn))
        info = [None] * len(self.steps)
        base_cache = {}

        def sbase(name):
            if name not in base_cache:
                base_cache[name] = nc.lookup_mloc(name).addr
            return base_cache[name]

        def ap_range(ap):
            b = getattr(ap, "bass_ap", None)
            if b is None:
                return None
            esz = _np.dtype(mybir.dt.np(b.dtype)).itemsize
            pairs = [tuple(p) for p in b.ap]
            sp = str(b.space)
            if "DRAM" in sp:
                if ap.dynamic_ap_info is not None:
                    return (("D", b.name), 0, 1 << 44)
                lo = hi = b.offset
                for st, ct in pairs:
                    d = (ct - 1) * st
                    if d < 0:
                        lo += d
                    else:
                        hi += d
                return (("D", b.name), lo * esz, (hi + 1) * esz)
            dims = pairs[1:] if len(pairs) > 1 else []
            pstride = pairs[0][0] if pairs else 1
            off = b.offset
            if pstride > 0:
                off = off % pstride
            lo = hi = off
            for st, ct in dims:
                d = (ct - 1) * st
                if d < 0:
                    lo += d
                else:
                    hi += d
            if "PSUM" in sp:
                return (("P", b.name), 0, 1 << 30)
            base = sbase(b.name)
            return (("S",), base + lo * esz, base + (hi + 1) * esz)

        def mk(ename):
            def body(eng):
                for idx, fn in per[ename]:
                    inst = fn(eng)
                    mi = inst.ins
                    rd, wr = [], []
                    for ap in (mi.ins or []):
                        r = ap_range(ap)
                        if r is not None:
                            rd.append(r)
                    for ap in (mi.outs or []):
                        r = ap_range(ap)
                        if r is not None:
                            wr.append(r)
                    info[idx] = (rd, wr)
            return body

        for e in self.ENGS:
            if per[e] or e == "sync":
                getattr(block, e)(mk(e))
        return info

    NPOOL = 24

    def compute_schedule(self, info):
        n = len(self.steps)
        NG = 8
        NSW = 8
        dims = (list(self.ENGS) + ["dp%d" % i for i in range(self.NPOOL)]
                + ["gq%d" % i for i in range(NG)]
                + ["sw%d" % i for i in range(NSW)])
        dimi = {d: i for i, d in enumerate(dims)}
        counts = {d: 0 for d in dims}
        comp = [None] * n
        clocks = [None] * n
        know = {e: [0] * len(dims) for e in self.ENGS}
        waits = [None] * n
        incs = [None] * n
        writes_map = {}
        reads_map = {}
        pool_next = 0
        gq_next = 0
        sw_next = 0
        prev_user = [None] * self.NPOOL
        prev_sw = [None] * NSW

        def overlap(a0, a1, b0, b1):
            return a0 < b1 and b0 < a1

        for j, (eng, kind, fn) in enumerate(self.steps):
            deps = set()
            if kind == "d" and eng == "gpsimd":
                p = sw_next
                sw_next = (sw_next + 1) % NSW
                dk = "sw%d" % p
                counts[dk] += 16
                comp[j] = (dimi[dk], counts[dk])
                if prev_sw[p] is not None:
                    deps.add(prev_sw[p])
                prev_sw[p] = j
                incs[j] = (dk, 16)
            elif kind == "d":
                p = pool_next
                pool_next = (pool_next + 1) % self.NPOOL
                dk = "dp%d" % p
                counts[dk] += 16
                comp[j] = (dimi[dk], counts[dk])
                if prev_user[p] is not None:
                    deps.add(prev_user[p])
                prev_user[p] = j
                incs[j] = (dk, 16)
            elif eng == "gpsimd":
                dk = "gq%d" % (gq_next % NG)
                gq_next += 1
                counts[dk] += 1
                comp[j] = (dimi[dk], counts[dk])
                incs[j] = (dk, 1)
            else:
                dk = eng
                counts[dk] += 1
                comp[j] = (dimi[dk], counts[dk])
                incs[j] = (dk, 1)
            rd, wr = info[j]
            for key, lo, hi in rd:
                wl = writes_map.get(key)
                if not wl:
                    continue
                uncov = [(lo, hi)]
                for wlo, whi, wi in reversed(wl):
                    nxt = []
                    hit = False
                    for ulo, uhi in uncov:
                        if overlap(ulo, uhi, wlo, whi):
                            hit = True
                            if ulo < wlo:
                                nxt.append((ulo, wlo))
                            if whi < uhi:
                                nxt.append((whi, uhi))
                        else:
                            nxt.append((ulo, uhi))
                    if hit:
                        deps.add(wi)
                    uncov = nxt
                    if not uncov:
                        break
            for key, lo, hi in wr:
                wl = writes_map.get(key)
                if wl:
                    uncov = [(lo, hi)]
                    for wlo, whi, wi in reversed(wl):
                        nxt = []
                        hit = False
                        for ulo, uhi in uncov:
                            if overlap(ulo, uhi, wlo, whi):
                                hit = True
                                if ulo < wlo:
                                    nxt.append((ulo, wlo))
                                if whi < uhi:
                                    nxt.append((whi, uhi))
                            else:
                                nxt.append((ulo, uhi))
                        if hit:
                            deps.add(wi)
                        uncov = nxt
                        if not uncov:
                            break
                rl = reads_map.get(key)
                if rl:
                    keep = []
                    for rlo, rhi, ri in rl:
                        if overlap(lo, hi, rlo, rhi):
                            deps.add(ri)
                            if rlo < lo or rhi > hi:
                                keep.append((rlo, rhi, ri))
                        else:
                            keep.append((rlo, rhi, ri))
                    reads_map[key] = keep
            kn = know[eng]
            wl_out = []
            for i in sorted(deps):
                ddi, dval = comp[i]
                if kn[ddi] < dval:
                    wl_out.append((dims[ddi], dval))
                ci = clocks[i]
                for z in range(len(kn)):
                    if ci[z] > kn[z]:
                        kn[z] = ci[z]
                if kn[ddi] < dval:
                    kn[ddi] = dval
            waits[j] = wl_out
            cj = list(kn)
            di, dv = comp[j]
            if cj[di] < dv:
                cj[di] = dv
            clocks[j] = cj
            for key, lo, hi in wr:
                wl = writes_map.setdefault(key, [])
                wl2 = [w for w in wl if not (lo <= w[0] and w[1] <= hi)]
                wl2.append((lo, hi, j))
                writes_map[key] = wl2
            for key, lo, hi in rd:
                reads_map.setdefault(key, []).append((lo, hi, j))
        return {"waits": waits, "incs": incs, "finals": dict(counts),
                "dims": dims}

    def emit(self, nc, block, sems, schedule=None):
        per = {e: [] for e in self.ENGS}
        tot = {k: 0 for k in sems}
        for idx, (eng, kind, fn) in enumerate(self.steps):
            per[eng].append((idx, dict(tot), kind, fn))
            if schedule is None:
                tot[self.key(eng, kind)] += 16 if kind == "d" else 1
        final = dict(tot) if schedule is None else dict(schedule["finals"])
        import os
        if os.environ.get("PRINT_TOTALS"):
            nw = (sum(len(w) for w in schedule["waits"])
                  if schedule else None)
            print("SEQ TOTALS:", final, "steps:", len(self.steps),
                  "waits:", nw)

        def mk(ename):
            def body(eng):
                seen = {k: 0 for k in sems}
                for idx, w, kind, fn in per[ename]:
                    if schedule is None:
                        for k in sems:
                            if w[k] > seen[k]:
                                eng.wait_ge(sems[k], w[k])
                                seen[k] = w[k]
                    else:
                        import os as _os
                        _nsw = _os.environ.get("NO_SELF_WAITS")
                        for k, dval in schedule["waits"][idx]:
                            if _nsw and k == ename:
                                continue
                            if dval > seen[k]:
                                eng.wait_ge(sems[k], dval)
                                seen[k] = dval
                    inst = fn(eng)
                    if schedule is None:
                        kk = self.key(ename, kind)
                        inst.then_inc(sems[kk], 16 if kind == "d" else 1)
                    else:
                        kk, inc = schedule["incs"][idx]
                        inst.then_inc(sems[kk], inc)
                if ename == "sync":
                    for k in sems:
                        if final.get(k, 0) > seen[k]:
                            eng.wait_ge(sems[k], final[k])
            return body

        for e in self.ENGS:
            if per[e] or e == "sync":
                getattr(block, e)(mk(e))


def build_program(cfg, max_steps=None, milestones=None, debug_out=False,
                  dump_src=None, dump_shape=None, dump_dt=None,
                  phasec_probe=None, emit_mode="serial", schedule=None,
                  probe_out=None):
    nc = bass.Bass("TRN2", target_bir_lowering=False,
                   dynamic_dma_scratch_size=8192)
    q = Seq()

    D, NCH, H, DK, DV = cfg.D, cfg.NCH, cfg.H, cfg.DK, cfg.DV
    TC, GC, NTOK = cfg.TC, cfg.GC, cfg.NTOK
    NGC = TC // GC
    TPT = TC // 128
    NTILE = cfg.NTILE
    PERP = cfg.PERP
    EPS = cfg.EPS
    E = cfg.E

    def din(name, shape, dt=F32):
        return nc.dram_tensor(name, shape, dt, kind="ExternalInput")

    xT = din("xT", [D, NTOK])
    x_tm = din("x_tm", [NTOK, D])
    Wq = din("Wq", [D, H * DK]); Wk = din("Wk", [D, H * DK])
    Wv = din("Wv", [D, D]); Wg = din("Wg", [D, D]); Wo = din("Wo", [D, D])
    W1 = din("W1", [D, cfg.FI]); W2 = din("W2", [cfg.FI, D])
    gkw1 = din("gkw1", [D, cfg.LR]); gkw2 = din("gkw2", [cfg.LR, H * DK])
    ln_w = din("ln_w", [1, D]); ln_b = din("ln_b", [1, D])
    gk_b2 = din("gk_b2", [1, H * DK])
    b1i = din("b1i", [1, cfg.FI]); b2i = din("b2i", [1, D])
    rms_w = din("rms_w", [1, DV])
    gw = din("gw", [D, E]); gb = din("gb", [1, E])
    ew1 = din("ew1", [D, cfg.F]); eb1 = din("eb1", [1, cfg.F])
    ew2 = din("ew2", [cfg.F, D]); eb2 = din("eb2", [1, D])
    identf = din("identf", [128, 128]); identb = din("identb", [128, 128], BF16)
    cmask = din("cmask", [128, 64], BF16)
    sltri = din("sltri", [128, 128])
    coreid = din("coreid", [1, 1])

    dbg = dict(kind="ExternalOutput") if debug_out else {}
    x1buf = nc.dram_tensor("x1buf", [NTOK, D], F32, **dbg)
    ln2buf = nc.dram_tensor("ln2buf", [NTOK, D], BF16, **dbg)
    logitsT = nc.dram_tensor("logitsT", [E, NTOK], F32)
    logits_all = nc.dram_tensor("logits_all", [E * cfg.NCORES, NTOK], F32)
    slotall = nc.dram_tensor("slotall", [cfg.NCORES * NTILE, 128], F32)
    gpall = nc.dram_tensor("gpall", [cfg.NCORES * NTILE, 128], F32)
    slotbuf = nc.dram_tensor("slotbuf", [cfg.NSLOT + 1, D], BF16)
    rsout = nc.dram_tensor("rsout", [cfg.CAP, D], BF16)
    ag2in = nc.dram_tensor("ag2in", [cfg.CAP, D], F32)
    outbuf = nc.dram_tensor("outbuf", [cfg.NSLOT + 1, D], F32)
    srow = nc.dram_tensor("srow", [1, 64], F32)
    out = nc.dram_tensor("out", [NTOK, D], F32, kind="ExternalOutput")

    allocs = []

    def sb(name, shape, dt=F32):
        g = nc.sbuf_tensor(name, shape, dt)
        t = g.__enter__()
        allocs.append((g, name))
        return t

    def ps(name, shape, dt=F32):
        g = nc.psum_tensor(name, shape, dt)
        t = g.__enter__()
        allocs.append((g, name))
        return t

    def free_from(name):
        idx = next(i for i, (_, n) in enumerate(allocs) if n == name)
        for g, _ in reversed(allocs[idx:]):
            g.__exit__(None, None, None)
        del allocs[idx:]

    idf = sb("idf", [128, 128], F32)
    idb = sb("idb", [128, 128], BF16)
    cmk = sb("cmk", [128, 64], BF16)
    rmsr = sb("rmsr", [128, DV], F32)
    gbr = sb("gbr", [128, E], F32)
    csr = sb("csr", [128, E], F32)
    onescf = sb("onescf", [128, 1], F32)
    onesr8 = sb("onesr8", [1, 128], F32)
    lnbfm = sb("lnbfm", [128, NCH], F32)
    lnwfm = sb("lnwfm", [128, NCH], F32)
    lnwsfm = sb("lnwsfm", [128, NCH], F32)
    zrow = sb("zrow", [128, max(TC, PERP)], F32)
    zb = zrow[:, 0:256].bitcast(BF16)

    q.dma("sync", lambda s: s.dma_start(out=idf[:, :], in_=identf.ap()))
    q.dma("sync", lambda s: s.dma_start(out=idb[:, :], in_=identb.ap()))
    q.dma("sync", lambda s: s.dma_start(out=cmk[:, :], in_=cmask.ap()))
    q.dma("sync", lambda s: s.dma_start(out=rmsr[:, :], in_=rms_w.ap().broadcast_to([128, DV])))
    q.dma("sync", lambda s: s.dma_start(out=gbr[:, :], in_=gb.ap().broadcast_to([128, E])))
    q.dma("sync", lambda s: s.dma_start(
        out=lnbfm[:, :], in_=ln_b.ap().rearrange("o (c p) -> (o p) c", p=128),
        allow_slow_non_contiguous=True))
    q.dma("sync", lambda s: s.dma_start(
        out=lnwfm[:, :], in_=ln_w.ap().rearrange("o (c p) -> (o p) c", p=128),
        allow_slow_non_contiguous=True))
    q.c("vector", lambda v: v.memset(onescf[:, :], 1.0))
    q.c("vector", lambda v: v.memset(onesr8[:, :], 1.0))
    q.c("vector", lambda v: v.memset(zrow[:, :], 0.0))
    q.c("vector", lambda v: v.tensor_scalar_mul(lnwsfm[:, :], lnwfm[:, :], cfg.DKS))

    for r0 in range(0, cfg.NSLOT + 1, 128):
        rn = min(128, cfg.NSLOT + 1 - r0)
        for c0 in range(0, D, 512):
            q.dma("sync", lambda s, r0=r0, rn=rn, c0=c0: s.dma_start(
                out=slotbuf.ap()[r0:r0 + rn, c0:c0 + 512],
                in_=zrow[0:rn, 0:256].bitcast(BF16)))
    for c0 in range(0, D, 256):
        q.dma("sync", lambda s, c0=c0: s.dma_start(
            out=outbuf.ap()[cfg.NSLOT:cfg.NSLOT + 1, c0:c0 + 256],
            in_=zrow[0:1, 0:256]))

    psA = ps("psA", [128, 512], F32)
    psB = ps("psB", [128, 512], F32)
    psBig = ps("psBig", [128, 1024], F32)
    psO = ps("psO", [128, 1024], F32)
    psT = ps("psT", [128, 1024], BF16)
    psS = ps("psS", [128, 512], F32)

    xf = sb("xf", [128, NCH * TC], F32)
    xm = sb("xm", [128, NCH * TC], F32)
    xa = sb("xa", [128, NCH * TC], BF16)
    sc8 = sb("sc8", [128, NCH * TC], F32)
    qf = sb("qf", [128, 4 * TC], BF16)
    kf = sb("kf", [128, 4 * TC], BF16)
    gka = sb("gka", [128, 4 * TC], F32)
    eq = sb("eq", [128, 4 * TC], BF16)
    ek = sb("ek", [128, 4 * TC], BF16)
    bmat = sb("bmat", [128, 4 * NGC], F32)
    acar = sb("acar", [128, 4], F32)
    Ssb = sb("Ssb", [128, 4 * DV], F32)
    Sbp = sb("Sbp", [128, 4 * DV], BF16)
    Sbm = sb("Sbm", [128, 4 * DV], BF16)
    ach = sb("ach", [128, 4], F32)
    vt = sb("vt", [128, 1024], BF16)
    sg = sb("sg", [128, 1024], F32)
    pa = sb("pa", [128, 8 * 128], BF16)
    ktm = sb("ktm", [128, 512], BF16)
    o2 = sb("o2", [128, 1024], BF16)
    o2f = sb("o2f", [128, NCH * TC], BF16)
    xt1 = sb("xt1", [128, 1024], F32)
    st8 = sb("st8", [1, 4 * TC], F32)
    str2 = sb("str2", [128, 10], F32)
    tbm = sb("tbm", [128, 16], F32)
    r1p = sb("r1p", [2, TC], BF16)
    r2p = sb("r2p", [2, TC], BF16)
    lgt = sb("lgt", [8, 128], F32)
    agw = sb("agw", [128, 2 * TPT * 16], F32)

    q.c("vector", lambda v: v.memset(r1p[:, :], 1.0))
    q.c("vector", lambda v: v.memset(r2p[:, :], 1.0))

    wtq = sb("wtq", [128, NCH * 512], BF16)
    wtk = sb("wtk", [128, NCH * 512], BF16)
    wtv = sb("wtv", [128, NCH * 1024], BF16)
    wtg = sb("wtg", [128, NCH * 1024], BF16)
    wto = sb("wto", [128, NCH * 1024], BF16)
    wt1 = sb("wt1", [128, NCH * cfg.FI], BF16)
    wt2 = sb("wt2", [128, cfg.FICH * 1024], BF16)
    wgk1 = sb("wgk1", [128, NCH * cfg.LR], BF16)
    wgk2 = sb("wgk2", [cfg.LR + 1, H * DK], BF16)
    wgw = sb("wgw", [128, NCH * E], F32)
    w2bf = sb("w2bf", [128, NCH], F32)
    r1q = sb("r1q", [2, 512], BF16)
    r1k = sb("r1k", [2, 512], BF16)
    r1v = sb("r1v", [2, 1024], BF16)
    r1g = sb("r1g", [2, 1024], BF16)
    r1gk = sb("r1gk", [2, cfg.LR], BF16)
    r1w1 = sb("r1w1", [2, cfg.FI], BF16)

    wst = sb("wst", [128, 512], F32)
    wst2 = sb("wst2", [128, 512], F32)
    _wstp = [0]
    def _stg():
        _wstp[0] ^= 1
        return (wst, "sync") if _wstp[0] else (wst2, "scalar")

    onescb = sb("onescb", [128, 1], BF16)
    lnb2 = sb("lnb2", [128, 2 * NCH], F32)
    q.c("vector", lambda v: v.memset(onescb[:, :], 1.0))
    q.c("vector", lambda v: v.memset(lnb2[:, :], 0.0))
    q.c("vector", lambda v: v.tensor_copy(
        lnb2[:, :].rearrange("p (c two) -> p c two", two=2)[:, :, 1:2],
        lnbfm[:, :].unsqueeze(2)))

    def _ureg(j):
        pd = psBig if j < 2 else psO
        return pd[0:2, (j % 2) * 512:(j % 2) * 512 + 512]

    def _ureg2(j):
        pd = psBig if j < 2 else psO
        return pd[1:2, (j % 2) * 512:(j % 2) * 512 + 512]

    def prep_scaled(Wd, outF, wt_sb, r1_sb, scale_fm, uscale=None, badd=None):
        nN = (outF + 511) // 512
        for j in range(nN):
            n0j, n1j = j * 512, min(outF, (j + 1) * 512)
            for kc in range(NCH):
                stgb, stgq = _stg()
                q.dma(stgq, lambda s, kc=kc, n0=n0j, n1=n1j, stgb=stgb: s.dma_start(
                    out=stgb[:, 0:n1 - n0],
                    in_=Wd.ap()[kc * 128:(kc + 1) * 128, n0:n1]))
                q.c("tensor", lambda t, kc=kc, n0=n0j, n1=n1j, j=j, stgb=stgb: t.matmul(
                    _ureg(j)[:, 0:n1 - n0],
                    lnb2[:, 2 * kc:2 * kc + 2], stgb[:, 0:n1 - n0],
                    start=(kc == 0), stop=(kc == NCH - 1)))
                q.c("vector", lambda v, kc=kc, n0=n0j, n1=n1j, stgb=stgb: v.tensor_scalar(
                    wt_sb[:, kc * outF + n0:kc * outF + n1], stgb[:, 0:n1 - n0],
                    scale_fm[:, kc:kc + 1], None, ALU.mult))
        for j in range(nN):
            n0, n1 = j * 512, min(outF, (j + 1) * 512)
            if badd is not None:
                q.dma("sync", lambda s, n0=n0, n1=n1: s.dma_start(
                    out=wst[0:2, 0:n1 - n0],
                    in_=badd.ap()[0:1, n0:n1].broadcast_to([2, n1 - n0])))
            if uscale is not None:
                q.c("vector", lambda v, j=j, n0=n0, n1=n1: v.tensor_scalar_mul(
                    r1_sb[0:2, n0:n1], _ureg(j)[:, 0:n1 - n0], uscale))
            elif badd is not None:
                q.c("vector", lambda v, j=j, n0=n0, n1=n1: v.tensor_add(
                    r1_sb[0:2, n0:n1], _ureg(j)[:, 0:n1 - n0], wst[0:2, 0:n1 - n0]))
            else:
                q.c("vector", lambda v, j=j, n0=n0, n1=n1: v.tensor_copy(
                    r1_sb[0:2, n0:n1], _ureg(j)[:, 0:n1 - n0]))
        for j in range(nN):
            n0, n1 = j * 512, min(outF, (j + 1) * 512)
            for kc in range(NCH):
                q.c("tensor", lambda t, kc=kc, n0=n0, n1=n1: t.matmul(
                    psB[0:1, 0:n1 - n0], onescb[:, 0:1],
                    wt_sb[:, kc * outF + n0:kc * outF + n1],
                    start=(kc == 0), stop=(kc == NCH - 1)))
            q.c("vector", lambda v, n0=n0, n1=n1: v.tensor_copy(
                r1_sb[0:1, n0:n1], psB[0:1, 0:n1 - n0]))

    prep_scaled(Wq, 512, wtq, r1q, lnwsfm, uscale=cfg.DKS)
    prep_scaled(Wk, 512, wtk, r1k, lnwfm)
    prep_scaled(Wv, 1024, wtv, r1v, lnwfm)
    prep_scaled(Wg, 1024, wtg, r1g, lnwfm)
    prep_scaled(gkw1, cfg.LR, wgk1, r1gk, lnwfm)
    prep_scaled(W1, cfg.FI, wt1, r1w1, lnwfm, badd=b1i)

    for kc in range(NCH):
        wcb = xf[:, 0:1024] if kc % 2 == 0 else xf[:, 1024:2048]
        wcq = "sync" if kc % 2 == 0 else "scalar"
        q.dma(wcq, lambda s, kc=kc, wcb=wcb: s.dma_start(
            out=wcb, in_=Wo.ap()[kc * 128:(kc + 1) * 128, :]))
        q.c("vector", lambda v, kc=kc, wcb=wcb: v.tensor_copy(
            wto[:, kc * 1024:(kc + 1) * 1024], wcb))
        q.dma("sync", lambda s, kc=kc: s.dma_start(
            out=wgw[:, kc * E:(kc + 1) * E], in_=gw.ap()[kc * 128:(kc + 1) * 128, :]))
    for kc in range(cfg.FICH):
        wcb = xf[:, 0:1024] if kc % 2 == 0 else xf[:, 1024:2048]
        wcq = "sync" if kc % 2 == 0 else "scalar"
        q.dma(wcq, lambda s, kc=kc, wcb=wcb: s.dma_start(
            out=wcb, in_=W2.ap()[kc * 128:(kc + 1) * 128, :]))
        q.c("vector", lambda v, kc=kc, wcb=wcb: v.tensor_copy(
            wt2[:, kc * 1024:(kc + 1) * 1024], wcb))
    q.dma("sync", lambda s: s.dma_start(out=wst[0:cfg.LR, 0:512], in_=gkw2.ap()))
    q.dma("sync", lambda s: s.dma_start(
        out=wst[cfg.LR:cfg.LR + 1, 0:512], in_=gk_b2.ap()))
    q.c("vector", lambda v: v.tensor_copy(
        wgk2[0:cfg.LR + 1, :], wst[0:cfg.LR + 1, 0:512]))
    q.dma("sync", lambda s: s.dma_start(
        out=w2bf[:, :], in_=b2i.ap().rearrange("o (c p) -> (o p) c", p=128),
        allow_slow_non_contiguous=True))

    q.dma("sync", lambda s: s.dma_start(
        out=wst[:, 0:E * NCH].rearrange("p (c e) -> p c e", c=NCH),
        in_=gw.ap().rearrange("(c p) e -> p c e", p=128)))
    for kc in range(NCH):
        q.c("tensor", lambda t, kc=kc: t.matmul(
            psA[0:1, 0:E], onescf[:, 0:1],
            wst[:, 0:E * NCH].rearrange("p (c e) -> p c e", c=NCH)[:, kc, :],
            start=(kc == 0), stop=(kc == NCH - 1)))
    q.c("vector", lambda v: v.tensor_copy(wst[0:1, 0:E], psA[0:1, 0:E]))
    q.dma("sync", lambda s: s.dma_start(out=srow.ap()[0:1, 0:E], in_=wst[0:1, 0:E]))
    q.dma("sync", lambda s: s.dma_start(
        out=csr[:, :], in_=srow.ap()[0:1, 0:E].broadcast_to([128, E])))

    free_from("wst")

    t1v = sc8[:, :].bitcast(BF16)
    gsc = sc8[:, 0:4 * TC]
    assert 4 * TC >= 1024
    aot = gka
    l2t = o2
    xf3 = xf[:, :].rearrange("p (c t) -> p c t", c=NCH)
    xm3 = xm[:, :].rearrange("p (c t) -> p c t", c=NCH)
    xa3 = xa[:, :].rearrange("p (c t) -> p c t", c=NCH)
    qf3 = qf[:, :].rearrange("p (c t) -> p c t", c=4)
    kf3 = kf[:, :].rearrange("p (c t) -> p c t", c=4)
    gka4 = gka[:, :].rearrange("p (c g u) -> p c g u", c=4, g=NGC)
    gsc4 = gsc.rearrange("p (c g u) -> p c g u", c=4, g=NGC)
    eq4 = eq[:, :].rearrange("p (c g u) -> p c g u", c=4, g=NGC)
    bmat3 = bmat[:, :].rearrange("p (c g) -> p c g", c=4)
    S3 = Ssb[:, :].rearrange("p (c v) -> p c v", c=4)
    Sbp3 = Sbp[:, :].rearrange("p (c v) -> p c v", c=4)
    Sbm3 = Sbm[:, :].rearrange("p (c v) -> p c v", c=4)
    o2f3 = o2f[:, :].rearrange("p (c t) -> p c t", c=NCH)
    t1v3 = t1v[:, 0:cfg.FICH * TC].rearrange("p (c t) -> p c t", c=cfg.FICH)

    def strow(r, s=0):
        return st8[r:r + 1, s * TC:(s + 1) * TC]

    chunk_marks = []
    for seq_i in range(cfg.SEQ):
        q.c("vector", lambda v: v.memset(Ssb[:, :], 0.0))
        q.c("vector", lambda v: v.memset(acar[:, :], 0.0))
        for chn in range(cfg.T // TC):
            t0 = seq_i * cfg.T + chn * TC
            _mk = {"cs": len(q.steps)}
            chunk_marks.append(_mk)
            agb = (chn % 2) * TPT * 16

            q.dma("sync", lambda s, t0=t0: s.dma_start(
                out=xf3, in_=xT.ap().rearrange("(c p) n -> p c n", p=128)[:, :, t0:t0 + TC]))

            q.c("vector", lambda v: v.tensor_mul(sc8[:, 0:NCH * TC], xf[:, :], xf[:, :]))
            for kc in range(NCH):
                q.c("tensor", lambda t, kc=kc: t.matmul(
                    psS[0:1, 0:TC], onescf[:, 0:1], xf3[:, kc, :],
                    start=(kc == 0), stop=(kc == NCH - 1)))
            for kc in range(NCH):
                q.c("tensor", lambda t, kc=kc: t.matmul(
                    psS[0:1, TC:2 * TC], onescf[:, 0:1],
                    sc8[:, 0:NCH * TC].rearrange("p (c t) -> p c t", c=NCH)[:, kc, :],
                    start=(kc == 0), stop=(kc == NCH - 1)))
            q.c("vector", lambda v: v.tensor_scalar_mul(
                strow(0, 3), psS[0:1, 0:TC], 1.0 / D))
            q.c("vector", lambda v: v.tensor_scalar_mul(
                strow(0, 1), psS[0:1, TC:2 * TC], 1.0 / D))
            q.c("vector", lambda v: v.tensor_mul(strow(0, 0), strow(0, 3), strow(0, 3)))
            q.c("vector", lambda v: v.tensor_sub(strow(0, 1), strow(0, 1), strow(0, 0)))
            q.c("vector", lambda v: v.tensor_scalar_add(strow(0, 0), strow(0, 1), EPS))
            q.c("scalar", lambda s: s.activation(strow(0, 2), strow(0, 0), AF.Sqrt))
            q.c("vector", lambda v: v.reciprocal(strow(0, 2), strow(0, 2)))
            q.c("vector", lambda v: v.reciprocal(strow(0, 0), strow(0, 0)))
            q.c("vector", lambda v: v.tensor_mul(strow(0, 1), strow(0, 1), strow(0, 0)))
            q.c("vector", lambda v: v.tensor_scalar_add(strow(0, 1), strow(0, 1), EPS))
            q.c("scalar", lambda s: s.activation(strow(0, 1), strow(0, 1), AF.Sqrt))
            q.c("vector", lambda v: v.reciprocal(strow(0, 1), strow(0, 1)))
            q.c("vector", lambda v: v.tensor_mul(strow(0, 1), strow(0, 1), strow(0, 2)))
            q.c("vector", lambda v: v.tensor_mul(strow(0, 0), strow(0, 1), strow(0, 3)))
            q.c("vector", lambda v: v.tensor_mul(strow(0, 3), strow(0, 3), strow(0, 2)))
            q.c("vector", lambda v: v.tensor_scalar_mul(strow(0, 0), strow(0, 0), -1.0))
            q.c("vector", lambda v: v.tensor_copy(r1p[0:1, :], strow(0, 0)))

            q.c("tensor", lambda t: t.matmul(
                psBig[:, 0:TC], onesr8[0:1, :], strow(0, 1), start=True, stop=True))
            q.c("vector", lambda v: v.tensor_tensor(
                xa3, xf3, psBig[:, 0:TC].unsqueeze(1).broadcast_to([128, NCH, TC]),
                ALU.mult))
            _mk["se"] = len(q.steps)

            for kc in range(NCH):
                q.c("tensor", lambda t, kc=kc: t.matmul(
                    psA[0:cfg.LR, 0:TC], wgk1[:, kc * cfg.LR:(kc + 1) * cfg.LR],
                    xa3[:, kc, :], start=(kc == 0), stop=False))
            q.c("tensor", lambda t: t.matmul(
                psA[0:cfg.LR, 0:TC], r1gk[:, :], r1p[:, :], start=False, stop=True))
            q.c("vector", lambda v: v.memset(pa[0:cfg.LR + 1, 0:TC], 1.0))
            q.c("vector", lambda v: v.tensor_copy(pa[0:cfg.LR, 0:TC], psA[0:cfg.LR, 0:TC]))
            for mc in range(4):
                q.c("tensor", lambda t, mc=mc: t.matmul(
                    psBig[:, mc * TC:(mc + 1) * TC], wgk2[:, mc * 128:(mc + 1) * 128],
                    pa[0:cfg.LR + 1, 0:TC], start=True, stop=True))
            q.c("scalar", lambda s: s.activation(gsc, psBig[:, 0:4 * TC], AF.Sigmoid))
            q.c("scalar", lambda s: s.activation(gka[:, :], gsc, AF.Ln))
            q.c("vector", lambda v: v.tensor_scalar_mul(gsc, gka[:, :], 1.0 / cfg.GATE_NORM))
            for fc in range(4):
                q.c("vector", lambda v, fc=fc: v.tensor_tensor_scan(
                    gka[:, :].rearrange("p (c t) -> p c t", c=4)[:, fc, :],
                    gsc.rearrange("p (c t) -> p c t", c=4)[:, fc, :],
                    zrow[:, 0:TC], acar[:, fc:fc + 1], ALU.add, ALU.add))
            q.c("vector", lambda v: v.tensor_copy(bmat3[:, :, 0:1], acar[:, :].unsqueeze(2)))
            for g in range(1, NGC):
                q.c("vector", lambda v, g=g: v.tensor_copy(
                    bmat3[:, :, g:g + 1],
                    gka[:, :].rearrange("p (c t) -> p c t", c=4)[:, :, g * GC - 1:g * GC]))
            q.c("vector", lambda v: v.tensor_copy(
                acar[:, :],
                gka[:, :].rearrange("p (c t) -> p c t", c=4)[:, :, TC - 1:TC].squeeze(2)))
            q.c("vector", lambda v: v.tensor_sub(
                gsc4, gka4, bmat3.unsqueeze(3).broadcast_to([128, 4, NGC, GC])))
            q.c("scalar", lambda s: s.activation(eq[:, :], gsc, AF.Exp))
            q.c("scalar", lambda s: s.activation(ek[:, :], gsc, AF.Exp, scale=-1.0))
            _mk["ge"] = len(q.steps)

            for dest, wt_sb, r1_sb, fac in ((qf, wtq, r1q, eq), (kf, wtk, r1k, ek)):
                for mc in range(4):
                    for kc in range(NCH):
                        q.c("tensor", lambda t, mc=mc, kc=kc, wt_sb=wt_sb: t.matmul(
                            psBig[:, mc * TC:(mc + 1) * TC],
                            wt_sb[:, kc * 512 + mc * 128:kc * 512 + (mc + 1) * 128],
                            xa3[:, kc, :], start=(kc == 0), stop=False))
                    q.c("tensor", lambda t, mc=mc, r1_sb=r1_sb: t.matmul(
                        psBig[:, mc * TC:(mc + 1) * TC],
                        r1_sb[:, mc * 128:(mc + 1) * 128], r1p[:, :],
                        start=False, stop=True))
                for mc in range(4):
                    q.c("vector", lambda v, mc=mc, dest=dest, fac=fac: v.tensor_tensor(
                        dest[:, mc * TC:(mc + 1) * TC], psBig[:, mc * TC:(mc + 1) * TC],
                        fac[:, mc * TC:(mc + 1) * TC], ALU.mult))

            for j in range(TPT):
                for kc in range(NCH):
                    q.c("tensor", lambda t, j=j, kc=kc: t.matmul(
                        psA[:, 0:E], xf3[:, kc, j * 128:(j + 1) * 128],
                        wgw[:, kc * E:(kc + 1) * E],
                        start=(kc == 0), stop=(kc == NCH - 1)))
                q.c("vector", lambda v, j=j, agb=agb: v.tensor_copy(
                    agw[:, agb + j * 16:agb + j * 16 + 8], psA[:, 0:E]))

            _mk["xe"] = len(q.steps)
            for j in range(TPT):
                ce, co = 2 * j, 2 * j + 1
                tcol = j * 128

                for half, wt_sb, r1_sb in ((0, wtv, r1v), (1, wtg, r1g)):
                    for nh in range(2):
                        for kc in range(NCH):
                            q.c("tensor", lambda t, kc=kc, nh=nh, tcol=tcol, wt_sb=wt_sb: t.matmul(
                                psO[:, nh * 512:(nh + 1) * 512],
                                xa3[:, kc, tcol:tcol + 128],
                                wt_sb[:, kc * 1024 + nh * 512:kc * 1024 + (nh + 1) * 512],
                                start=(kc == 0), stop=False))
                        q.c("tensor", lambda t, nh=nh, tcol=tcol, r1_sb=r1_sb: t.matmul(
                            psO[:, nh * 512:(nh + 1) * 512],
                            r1p[:, tcol:tcol + 128], r1_sb[:, nh * 512:(nh + 1) * 512],
                            start=False, stop=True))
                    if half == 0:
                        q.c("vector", lambda v: v.tensor_copy(vt[:, :], psO[:, :]))
                    else:
                        q.c("scalar", lambda s: s.activation(sg[:, :], psO[:, :], AF.Silu))

                q.c("vector", lambda v, ce=ce: v.tensor_copy(
                    ach[:, :], eq4[:, :, ce, GC - 1:GC].squeeze(2)))

                for fc in range(4):
                    q.c("tensor", lambda t, fc=fc, tcol=tcol: t.transpose(
                        psT[:, fc * 128:(fc + 1) * 128],
                        kf3[:, fc, tcol:tcol + 128], idb[:, :]))
                q.c("vector", lambda v: v.tensor_copy(ktm[:, 0:512], psT[:, 0:512]))

                q.c("vector", lambda v: v.tensor_copy(Sbp[:, :], Ssb[:, :]))
                for h in range(H):
                    rb = (h % 4) * 32
                    fch = h // 4
                    q.c("tensor", lambda t, h=h, rb=rb, fch=fch: t.matmul(
                        psA[rb:rb + 32, fch * DV:(fch + 1) * DV],
                        ktm[0:64, fch * 128 + rb:fch * 128 + rb + 32],
                        vt[0:64, h * DV:(h + 1) * DV],
                        start=True, stop=True, tile_position=(0, rb)))
                q.c("vector", lambda v: v.tensor_add(Ssb[:, :], Ssb[:, :], psA[:, 0:4 * DV]))
                q.c("vector", lambda v: v.tensor_tensor(
                    S3, S3, ach[:, :].unsqueeze(2).broadcast_to([128, 4, DV]), ALU.mult))
                q.c("vector", lambda v: v.tensor_copy(Sbm[:, :], Ssb[:, :]))
                for h in range(H):
                    rb = (h % 4) * 32
                    fch = h // 4
                    q.c("tensor", lambda t, h=h, rb=rb, fch=fch: t.matmul(
                        psA[rb:rb + 32, fch * DV:(fch + 1) * DV],
                        ktm[64:128, fch * 128 + rb:fch * 128 + rb + 32],
                        vt[64:128, h * DV:(h + 1) * DV],
                        start=True, stop=True, tile_position=(64, rb)))
                q.c("vector", lambda v, co=co: v.tensor_copy(
                    ach[:, :], eq4[:, :, co, GC - 1:GC].squeeze(2)))
                q.c("vector", lambda v: v.tensor_add(Ssb[:, :], Ssb[:, :], psA[:, 0:4 * DV]))
                q.c("vector", lambda v: v.tensor_tensor(
                    S3, S3, ach[:, :].unsqueeze(2).broadcast_to([128, 4, DV]), ALU.mult))

                for grp in range(2):
                    for hi in range(8):
                        h = grp * 8 + hi
                        rb = (h % 4) * 32
                        fch = h // 4
                        q.c("tensor", lambda t, hi=hi, rb=rb, fch=fch, ce=ce: t.matmul(
                            psBig[0:64, hi * 128:hi * 128 + 64],
                            kf3[rb:rb + 32, fch, ce * GC:(ce + 1) * GC],
                            qf3[rb:rb + 32, fch, ce * GC:(ce + 1) * GC],
                            start=True, stop=True, tile_position=(rb, 0)))
                        q.c("tensor", lambda t, hi=hi, rb=rb, fch=fch, co=co: t.matmul(
                            psBig[64:128, hi * 128 + 64:hi * 128 + 128],
                            kf3[rb:rb + 32, fch, co * GC:(co + 1) * GC],
                            qf3[rb:rb + 32, fch, co * GC:(co + 1) * GC],
                            start=True, stop=True, tile_position=(rb, 64)))
                    q.c("vector", lambda v: v.memset(
                        pa[64:128, :].rearrange("p (h t) -> p h t", h=8)[:, :, 0:64], 0.0))
                    q.c("vector", lambda v: v.memset(
                        pa[0:64, :].rearrange("p (h t) -> p h t", h=8)[:, :, 64:128], 0.0))
                    q.c("vector", lambda v: v.tensor_tensor(
                        pa[0:64, :].rearrange("p (h t) -> p h t", h=8)[:, :, 0:64],
                        psBig[0:64, :].rearrange("p (h t) -> p h t", h=8)[:, :, 0:64],
                        cmk[0:64, 0:64].unsqueeze(1).broadcast_to([64, 8, 64]), ALU.mult))
                    q.c("vector", lambda v: v.tensor_tensor(
                        pa[64:128, :].rearrange("p (h t) -> p h t", h=8)[:, :, 64:128],
                        psBig[64:128, :].rearrange("p (h t) -> p h t", h=8)[:, :, 64:128],
                        cmk[64:128, 0:64].unsqueeze(1).broadcast_to([64, 8, 64]),
                        ALU.mult))
                    for hi in range(8):
                        h = grp * 8 + hi
                        q.c("tensor", lambda t, h=h, hi=hi: t.matmul(
                            psO[:, h * DV:(h + 1) * DV],
                            pa[:, hi * 128:(hi + 1) * 128],
                            vt[:, h * DV:(h + 1) * DV], start=True, stop=True))

                for h in range(H):
                    rb = (h % 4) * 32
                    fch = h // 4
                    q.c("tensor", lambda t, h=h, rb=rb, fch=fch, ce=ce: t.matmul(
                        psBig[0:64, h * DV:(h + 1) * DV],
                        qf3[rb:rb + 32, fch, ce * GC:(ce + 1) * GC],
                        Sbp3[rb:rb + 32, fch, :],
                        start=True, stop=True, tile_position=(rb, 0)))
                    q.c("tensor", lambda t, h=h, rb=rb, fch=fch, co=co: t.matmul(
                        psBig[64:128, h * DV:(h + 1) * DV],
                        qf3[rb:rb + 32, fch, co * GC:(co + 1) * GC],
                        Sbm3[rb:rb + 32, fch, :],
                        start=True, stop=True, tile_position=(rb, 64)))
                q.c("vector", lambda v: v.tensor_copy(aot[:, 0:1024], psBig[:, :]))
                q.c("vector", lambda v: v.tensor_add(aot[:, 0:1024], aot[:, 0:1024], psO[:, :]))

                q.c("vector", lambda v: v.tensor_mul(
                    sc8[:, 1024:1024 + 1024], aot[:, 0:1024], aot[:, 0:1024]))
                q.c("vector", lambda v: v.tensor_reduce(
                    bmat[:, 0:H].unsqueeze(2),
                    sc8[:, 1024:1024 + 1024].rearrange("p (h v) -> p h v", h=H),
                    AX.X, ALU.add))
                q.c("vector", lambda v: v.tensor_scalar(
                    bmat[:, 0:H], bmat[:, 0:H], 1.0 / DV, EPS, ALU.mult, ALU.add))
                q.c("scalar", lambda s: s.activation(bmat[:, 0:H], bmat[:, 0:H], AF.Sqrt))
                q.c("vector", lambda v: v.reciprocal(bmat[:, 0:H], bmat[:, 0:H]))
                q.c("vector", lambda v: v.tensor_tensor(
                    sg[:, :].rearrange("p (h v) -> p h v", h=H),
                    sg[:, :].rearrange("p (h v) -> p h v", h=H),
                    rmsr[:, :].unsqueeze(1).broadcast_to([128, H, DV]), ALU.mult))
                q.c("vector", lambda v: v.tensor_tensor(
                    sg[:, :].rearrange("p (h v) -> p h v", h=H),
                    sg[:, :].rearrange("p (h v) -> p h v", h=H),
                    bmat[:, 0:H].unsqueeze(2).broadcast_to([128, H, DV]), ALU.mult))
                q.c("vector", lambda v: v.tensor_mul(o2[:, :], aot[:, 0:1024], sg[:, :]))
                for fc in range(NCH):
                    q.c("tensor", lambda t, fc=fc: t.transpose(
                        psT[:, fc * 128:(fc + 1) * 128], o2[:, fc * 128:(fc + 1) * 128],
                        idb[:, :]))
                q.c("vector", lambda v, tcol=tcol: v.tensor_copy(
                    o2f3[:, :, tcol:tcol + 128],
                    psT[:, :].rearrange("p (c t) -> p c t", c=NCH)[:, :, 0:128]))

            q.c("tensor", lambda t: t.matmul(
                psBig[:, 0:TC], onesr8[0:1, :], strow(0, 2), start=True, stop=True))
            q.c("vector", lambda v: v.tensor_tensor(
                xm3, xf3, psBig[:, 0:TC].unsqueeze(1).broadcast_to([128, NCH, TC]),
                ALU.mult))
            q.c("tensor", lambda t: t.matmul(
                psBig[:, 0:TC], onesr8[0:1, :], strow(0, 3), start=True, stop=True))
            q.c("vector", lambda v: v.tensor_tensor(
                xm3, xm3, psBig[:, 0:TC].unsqueeze(1).broadcast_to([128, NCH, TC]),
                ALU.subtract))
            for mc in range(NCH):
                pd = psA if mc % 2 == 0 else psB
                for kc in range(NCH):
                    q.c("tensor", lambda t, mc=mc, kc=kc, pd=pd: t.matmul(
                        pd[:, 0:TC],
                        wto[:, kc * 1024 + mc * 128:kc * 1024 + (mc + 1) * 128],
                        o2f3[:, kc, :], start=(kc == 0), stop=(kc == NCH - 1)))
                q.c("vector", lambda v, mc=mc, pd=pd: v.tensor_add(
                    xm3[:, mc, :], xm3[:, mc, :], pd[:, 0:TC]))

            q.c("vector", lambda v: v.tensor_mul(sc8[:, 0:NCH * TC], xm[:, :], xm[:, :]))
            for kc in range(NCH):
                q.c("tensor", lambda t, kc=kc: t.matmul(
                    psA[0:1, 0:TC], onescf[:, 0:1], xm3[:, kc, :],
                    start=(kc == 0), stop=(kc == NCH - 1)))
            for kc in range(NCH):
                q.c("tensor", lambda t, kc=kc: t.matmul(
                    psB[0:1, 0:TC], onescf[:, 0:1],
                    sc8[:, 0:NCH * TC].rearrange("p (c t) -> p c t", c=NCH)[:, kc, :],
                    start=(kc == 0), stop=(kc == NCH - 1)))
            q.c("vector", lambda v: v.tensor_scalar_mul(strow(0, 0), psA[0:1, 0:TC], 1.0 / D))
            q.c("vector", lambda v: v.tensor_scalar_mul(strow(0, 1), psB[0:1, 0:TC], 1.0 / D))
            q.c("vector", lambda v: v.tensor_mul(strow(0, 2), strow(0, 0), strow(0, 0)))
            q.c("vector", lambda v: v.tensor_sub(strow(0, 1), strow(0, 1), strow(0, 2)))
            q.c("vector", lambda v: v.tensor_scalar_add(strow(0, 3), strow(0, 1), EPS))
            q.c("scalar", lambda s: s.activation(strow(0, 1), strow(0, 3), AF.Sqrt))
            q.c("vector", lambda v: v.reciprocal(strow(0, 2), strow(0, 1)))
            q.c("vector", lambda v: v.tensor_mul(strow(0, 3), strow(0, 2), strow(0, 0)))
            q.c("vector", lambda v: v.tensor_scalar_mul(strow(0, 3), strow(0, 3), -1.0))
            q.c("vector", lambda v: v.tensor_copy(r2p[0:1, :], strow(0, 3)))
            q.c("tensor", lambda t: t.matmul(
                psBig[:, 0:TC], onesr8[0:1, :], strow(0, 2), start=True, stop=True))
            q.c("vector", lambda v: v.tensor_tensor(
                xa3, xm3, psBig[:, 0:TC].unsqueeze(1).broadcast_to([128, NCH, TC]),
                ALU.mult))

            for mc in range(cfg.FICH):
                pd = psA if mc % 2 == 0 else psB
                for kc in range(NCH):
                    q.c("tensor", lambda t, mc=mc, kc=kc, pd=pd: t.matmul(
                        pd[:, 0:TC],
                        wt1[:, kc * cfg.FI + mc * 128:kc * cfg.FI + (mc + 1) * 128],
                        xa3[:, kc, :], start=(kc == 0), stop=False))
                q.c("tensor", lambda t, mc=mc, pd=pd: t.matmul(
                    pd[:, 0:TC], r1w1[:, mc * 128:(mc + 1) * 128], r2p[:, :],
                    start=False, stop=True))
                q.c("scalar", lambda s, mc=mc, pd=pd: s.activation(
                    t1v3[:, mc, :], pd[:, 0:TC], AF.Gelu))
            for mc in range(NCH):
                pd = psA if mc % 2 == 0 else psB
                for kc in range(cfg.FICH):
                    q.c("tensor", lambda t, mc=mc, kc=kc, pd=pd: t.matmul(
                        pd[:, 0:TC],
                        wt2[:, kc * 1024 + mc * 128:kc * 1024 + (mc + 1) * 128],
                        t1v3[:, kc, :], start=(kc == 0),
                        stop=(kc == cfg.FICH - 1)))
                q.c("vector", lambda v, mc=mc, pd=pd: v.scalar_tensor_tensor(
                    xm3[:, mc, :], pd[:, 0:TC], w2bf[:, mc:mc + 1],
                    xm3[:, mc, :], ALU.add, ALU.add))

            _mk["ts"] = len(q.steps)
            _mk["tj"] = []
            att = sc8[:, 1024:2048]
            for j in range(TPT):
                _tj = {"a": len(q.steps)}
                _mk["tj"].append(_tj)
                tt = t0 + j * 128
                tcol = j * 128
                for kc in range(NCH):
                    q.c("tensor", lambda t, kc=kc, tcol=tcol: t.matmul(
                        psA[:, 0:E], xm3[:, kc, tcol:tcol + 128],
                        wgw[:, kc * E:(kc + 1) * E],
                        start=(kc == 0), stop=(kc == NCH - 1)))
                q.c("vector", lambda v, j=j, agb=agb: v.tensor_copy(
                    agw[:, agb + j * 16 + 8:agb + j * 16 + 16], psA[:, 0:E]))
                for fc in range(NCH):
                    q.c("tensor", lambda t, fc=fc, tcol=tcol: t.transpose(
                        psBig[:, fc * 128:(fc + 1) * 128],
                        xm3[:, fc, tcol:tcol + 128], idf[:, :]))
                q.c("vector", lambda v: v.tensor_copy(att, psBig[:, :]))
                _tj["m"] = len(q.steps)
                q.c("vector", lambda v: v.reduce_sum(str2[:, 0:1], att, AX.X))
                q.c("scalar", lambda s: s.activation(
                    psO[:, :], att, AF.Square,
                    accum_out=str2[:, 1:2]))
                q.c("vector", lambda v: v.tensor_scalar_mul(str2[:, 0:1], str2[:, 0:1], 1.0 / D))
                q.c("vector", lambda v: v.tensor_scalar_mul(str2[:, 1:2], str2[:, 1:2], 1.0 / D))
                q.c("vector", lambda v: v.tensor_mul(str2[:, 2:3], str2[:, 0:1], str2[:, 0:1]))
                q.c("vector", lambda v: v.tensor_sub(str2[:, 1:2], str2[:, 1:2], str2[:, 2:3]))
                q.c("vector", lambda v: v.tensor_scalar_add(str2[:, 1:2], str2[:, 1:2], EPS))
                q.c("scalar", lambda s: s.activation(str2[:, 2:3], str2[:, 1:2], AF.Sqrt))
                q.c("vector", lambda v: v.reciprocal(str2[:, 3:4], str2[:, 2:3]))
                q.c("vector", lambda v: v.tensor_mul(str2[:, 4:5], str2[:, 3:4], str2[:, 0:1]))
                q.c("vector", lambda v: v.tensor_scalar_mul(str2[:, 4:5], str2[:, 4:5], -1.0))
                q.dma("sync", lambda s, tt=tt: s.dma_start(
                    out=xt1[:, :], in_=x_tm.ap()[tt:tt + 128, :]))
                q.c("vector", lambda v: v.tensor_scalar(
                    att, att, str2[:, 3:4], str2[:, 4:5],
                    ALU.mult, ALU.add))
                q.c("vector", lambda v: v.tensor_add(xt1[:, :], xt1[:, :], att))
                q.dma("sync", lambda s, tt=tt: s.dma_start(
                    out=x1buf.ap()[tt:tt + 128, :], in_=xt1[:, :]))
                q.c("vector", lambda v: v.reduce_sum(str2[:, 5:6], xt1[:, :], AX.X))
                q.c("scalar", lambda s: s.activation(
                    psO[:, :], xt1[:, :], AF.Square,
                    accum_out=str2[:, 6:7]))
                q.c("vector", lambda v: v.tensor_scalar_mul(str2[:, 5:6], str2[:, 5:6], 1.0 / D))
                q.c("vector", lambda v: v.tensor_scalar_mul(str2[:, 6:7], str2[:, 6:7], 1.0 / D))
                q.c("vector", lambda v: v.tensor_mul(str2[:, 7:8], str2[:, 5:6], str2[:, 5:6]))
                q.c("vector", lambda v: v.tensor_sub(str2[:, 6:7], str2[:, 6:7], str2[:, 7:8]))
                q.c("vector", lambda v: v.tensor_scalar_add(str2[:, 6:7], str2[:, 6:7], EPS))
                q.c("scalar", lambda s: s.activation(str2[:, 7:8], str2[:, 6:7], AF.Sqrt))
                q.c("vector", lambda v: v.reciprocal(str2[:, 8:9], str2[:, 7:8]))
                q.c("vector", lambda v: v.tensor_mul(str2[:, 9:10], str2[:, 8:9], str2[:, 5:6]))
                q.c("vector", lambda v: v.tensor_scalar_mul(str2[:, 9:10], str2[:, 9:10], -1.0))
                q.c("vector", lambda v: v.tensor_scalar(
                    l2t[:, :], xt1[:, :], str2[:, 8:9], str2[:, 9:10],
                    ALU.mult, ALU.add))
                q.dma("sync", lambda s, tt=tt: s.dma_start(
                    out=ln2buf.ap()[tt:tt + 128, :], in_=l2t[:, :]))
                q.c("vector", lambda v, j=j, agb=agb: v.tensor_scalar(
                    tbm[:, 0:E], agw[:, agb + j * 16 + 8:agb + j * 16 + 16],
                    str2[:, 3:4], None, ALU.mult))
                q.c("vector", lambda v, j=j, agb=agb: v.tensor_add(
                    tbm[:, 0:E], tbm[:, 0:E], agw[:, agb + j * 16:agb + j * 16 + 8]))
                q.c("vector", lambda v: v.scalar_tensor_tensor(
                    tbm[:, 8:8 + E], csr[:, :], str2[:, 4:5], tbm[:, 0:E],
                    ALU.mult, ALU.add))
                q.c("vector", lambda v: v.tensor_scalar(
                    tbm[:, 8:8 + E], tbm[:, 8:8 + E], str2[:, 8:9], None, ALU.mult))
                q.c("vector", lambda v: v.scalar_tensor_tensor(
                    tbm[:, 0:E], csr[:, :], str2[:, 9:10], tbm[:, 8:8 + E],
                    ALU.mult, ALU.add))
                q.c("vector", lambda v: v.tensor_add(tbm[:, 0:E], tbm[:, 0:E], gbr[:, :]))
                q.c("tensor", lambda t: t.transpose(psA[0:8, 0:128], tbm[:, 0:E], idf[:, :]))
                q.c("vector", lambda v: v.tensor_copy(lgt[:, :], psA[0:8, 0:128]))
                q.dma("sync", lambda s, tt=tt: s.dma_start(
                    out=logitsT.ap()[:, tt:tt + 128], in_=lgt[:, :]))
                _tj["e"] = len(q.steps)
            _mk["ce"] = len(q.steps)

    _new = q.steps[:chunk_marks[0]["cs"]]
    _pend = None
    _prev = chunk_marks[0]["cs"]
    for _m in chunk_marks:
        _new += q.steps[_prev:_m["cs"]]
        _new += q.steps[_m["cs"]:_m["se"]]
        if _pend:
            _new += _pend[0]
        _new += q.steps[_m["se"]:_m["ge"]]
        if _pend:
            _new += _pend[1]
            _new += _pend[2]
        _new += q.steps[_m["ge"]:_m["xe"]]
        if _pend:
            _new += _pend[3]
        _new += q.steps[_m["xe"]:_m["ts"]]
        tj0, tj1 = _m["tj"]
        _pend = (q.steps[tj0["a"]:tj0["m"]], q.steps[tj0["m"]:tj0["e"]],
                 q.steps[tj1["a"]:tj1["m"]], q.steps[tj1["m"]:tj1["e"]])
        _prev = _m["ce"]
    for _p in _pend:
        _new += _p
    _new += q.steps[_prev:]
    assert len(_new) == len(q.steps), (len(_new), len(q.steps))
    q.steps = _new

    if milestones is not None:
        milestones['phaseA_end'] = len(q.steps)
    free_from("xf")

    slotc = sb("slotc", [128, NTILE], I32)
    gpc = sb("gpc", [128, NTILE], F32)
    w1s = sb("w1s", [128, NCH * cfg.F], BF16)
    w2s = sb("w2s", [128, cfg.FCH * D], BF16)
    lgs = sb("lgs", [128, E * PERP], F32)
    rt1 = sb("rt1", [128, E * PERP], F32)
    rt2 = sb("rt2", [128, E * PERP], F32)
    rt3 = sb("rt3", [128, PERP], F32)
    rt4 = sb("rt4", [128, PERP], F32)
    rt5 = sb("rt5", [128, PERP], F32)
    emx = sb("emx", [128, PERP], F32)
    gpv = sb("gpv", [128, PERP], F32)
    tot8 = sb("tot8", [128, E], F32)
    off8 = sb("off8", [128, E], F32)
    sltr = sb("sltr", [128, 128], F32)
    idx32 = sb("idx32", [128, 1], I32)
    idxf = sb("idxf", [128, 1], F32)
    cid = sb("cid", [128, 1], F32)
    l2s = sb("l2s", [128, D], BF16)

    lgs3 = lgs[:, :].rearrange("p (e t) -> p e t", e=E)
    rt13 = rt1[:, :].rearrange("p (e t) -> p e t", e=E)
    rt23 = rt2[:, :].rearrange("p (e t) -> p e t", e=E)

    q.dma("sync", lambda s: s.dma_start(out=sltr[:, :], in_=sltri.ap()))
    if milestones is not None:
        milestones['pre_ag1'] = len(q.steps)
    q.c("gpsimd", lambda g: g.collective_compute(
        "AllGather", ALU.bypass, replica_groups=[list(range(cfg.NCORES))],
        ins=[logitsT.ap().opt()], outs=[logits_all.ap().opt()]))
    if milestones is not None:
        milestones['post_ag1'] = len(q.steps)
    for c in range(cfg.NCORES):
        q.dma("sync", lambda s, c=c: s.dma_start(
            out=lgs3[c * 16:(c + 1) * 16, :, :],
            in_=logits_all.ap()[c * E:(c + 1) * E, :].rearrange(
                "e (pp t) -> pp e t", pp=16)))
    q.c("vector", lambda v: v.tensor_copy(emx[:, :], lgs3[:, 0, :]))
    for e in range(1, E):
        q.c("vector", lambda v, e=e: v.tensor_max(emx[:, :], emx[:, :], lgs3[:, e, :]))
    q.c("vector", lambda v: v.tensor_tensor(
        rt13, lgs3, emx[:, :].unsqueeze(1).broadcast_to([128, E, PERP]), ALU.subtract))
    q.c("scalar", lambda s: s.activation(rt2[:, :], rt1[:, :], AF.Exp))
    q.c("vector", lambda v: v.tensor_copy(gpv[:, :], rt23[:, 0, :]))
    for e in range(1, E):
        q.c("vector", lambda v, e=e: v.tensor_add(gpv[:, :], gpv[:, :], rt23[:, e, :]))
    q.c("vector", lambda v: v.reciprocal(gpv[:, :], gpv[:, :]))
    q.c("vector", lambda v: v.tensor_tensor(
        rt13, lgs3, emx[:, :].unsqueeze(1).broadcast_to([128, E, PERP]), ALU.is_ge))
    q.c("vector", lambda v: v.memset(rt3[:, :], 1e6))
    for e in range(E):
        q.c("vector", lambda v, e=e: v.tensor_scalar(
            rt4[:, :], rt13[:, e, :], -1e6, 1e6 + e, ALU.mult, ALU.add))
        q.c("vector", lambda v: v.tensor_tensor(rt3[:, :], rt3[:, :], rt4[:, :], ALU.min))
    q.c("vector", lambda v: v.tensor_copy(emx[:, :], rt3[:, :]))
    for e in range(E):
        q.c("vector", lambda v, e=e: v.tensor_single_scalar(
            rt13[:, e, :], emx[:, :], float(e), ALU.is_equal))
        q.c("vector", lambda v, e=e: v.tensor_tensor_scan(
            rt23[:, e, :], rt13[:, e, :], zrow[:, 0:PERP], 0.0, ALU.add, ALU.add))
    q.c("vector", lambda v: v.tensor_copy(tot8[:, :], rt23[:, :, PERP - 1]))
    q.c("tensor", lambda t: t.matmul(
        psA[0:8, 0:128], tot8[:, :], sltr[:, :], start=True, stop=True))
    q.c("vector", lambda v: v.tensor_copy(sltr[0:8, 0:128], psA[0:8, 0:128]))
    q.c("tensor", lambda t: t.transpose(psA[:, 0:8], sltr[0:8, 0:128], idf[0:8, 0:8]))
    q.c("vector", lambda v: v.tensor_copy(off8[:, :], psA[:, 0:8]))
    q.c("vector", lambda v: v.memset(rt3[:, :], 0.0))
    q.c("vector", lambda v: v.memset(rt4[:, :], 0.0))
    for e in range(E):
        q.c("vector", lambda v, e=e: v.scalar_tensor_tensor(
            rt3[:, :], rt13[:, e, :], off8[:, e:e + 1], rt3[:, :], ALU.mult, ALU.add))
        q.c("vector", lambda v, e=e: v.tensor_mul(rt5[:, :], rt13[:, e, :], rt23[:, e, :]))
        q.c("vector", lambda v: v.tensor_add(rt4[:, :], rt4[:, :], rt5[:, :]))
    q.c("vector", lambda v: v.tensor_add(rt3[:, :], rt3[:, :], rt4[:, :]))
    q.c("vector", lambda v: v.tensor_scalar_add(rt3[:, :], rt3[:, :], -1.0))
    q.c("vector", lambda v: v.tensor_single_scalar(
        rt4[:, :], rt3[:, :], float(cfg.CAP - 1), ALU.is_le))
    q.c("vector", lambda v: v.tensor_scalar(
        rt5[:, :], emx[:, :], float(cfg.CAP), None, ALU.mult))
    q.c("vector", lambda v: v.tensor_add(rt5[:, :], rt5[:, :], rt3[:, :]))
    q.c("vector", lambda v: v.tensor_mul(rt5[:, :], rt5[:, :], rt4[:, :]))
    q.c("vector", lambda v: v.tensor_scalar(
        rt4[:, :], rt4[:, :], -float(cfg.NSLOT), float(cfg.NSLOT), ALU.mult, ALU.add))
    q.c("vector", lambda v: v.tensor_add(rt5[:, :], rt5[:, :], rt4[:, :]))

    q.dma("sync", lambda s: s.dma_start(
        out=slotall.ap().rearrange("a w -> (a w)").rearrange("(p t) -> p t", p=128),
        in_=rt5[:, :]))
    q.dma("sync", lambda s: s.dma_start(
        out=gpall.ap().rearrange("a w -> (a w)").rearrange("(p t) -> p t", p=128),
        in_=gpv[:, :]))
    q.dma("sync", lambda s: s.dma_start(
        out=cid[:, :], in_=coreid.ap().broadcast_to([128, 1])))
    q.c("gpsimd", lambda g: g.iota(idxf[:, :], pattern=[[0, 1]], base=0,
                                   channel_multiplier=1,
                                   allow_small_or_imprecise_dtypes=True))
    q.c("vector", lambda v: v.tensor_single_scalar(cid[:, :], cid[:, :], float(NTILE), ALU.mult))
    q.c("vector", lambda v: v.tensor_add(idxf[:, :], idxf[:, :], cid[:, :]))
    q.c("vector", lambda v: v.tensor_copy(idx32[:, :], idxf[:, :]))
    q.dma("gpsimd", lambda g: g.indirect_dma_start(
        sltr[0:NTILE, :], None, slotall.ap(),
        IndirectOffsetOnAxis(idx32[0:NTILE, 0:1], 0)))
    if debug_out:
        sltrdbg = nc.dram_tensor("sltrdbg", [128, 128], F32,
                                 kind="ExternalOutput")
        idxdbg = nc.dram_tensor("idxdbg", [128, 2], F32, kind="ExternalOutput")
        q.dma("sync", lambda s: s.dma_start(out=sltrdbg.ap(), in_=sltr[:, :]))
        q.c("vector", lambda v: v.tensor_copy(rt3[:, 0:1], idx32[:, :]))
        q.c("vector", lambda v: v.tensor_copy(rt3[:, 1:2], idxf[:, :]))
        q.dma("sync", lambda s: s.dma_start(out=idxdbg.ap(), in_=rt3[:, 0:2]))
    q.c("tensor", lambda t: t.transpose(
        psA[:, 0:NTILE], sltr[0:NTILE, :], idf[0:NTILE, 0:NTILE]))
    q.c("vector", lambda v: v.tensor_copy(slotc[:, :], psA[:, 0:NTILE]))
    q.dma("gpsimd", lambda g: g.indirect_dma_start(
        sltr[0:NTILE, :], None, gpall.ap(),
        IndirectOffsetOnAxis(idx32[0:NTILE, 0:1], 0)))
    q.c("tensor", lambda t: t.transpose(
        psA[:, 0:NTILE], sltr[0:NTILE, :], idf[0:NTILE, 0:NTILE]))
    q.c("vector", lambda v: v.tensor_copy(gpc[:, :], psA[:, 0:NTILE]))

    for k in range(NTILE):
        q.dma("sync", lambda s, k=k: s.dma_start(
            out=l2s[:, :], in_=ln2buf.ap()[k * 128:(k + 1) * 128, :]))
        q.dma("gpsimd", lambda g, k=k: g.indirect_dma_start(
            slotbuf.ap(), IndirectOffsetOnAxis(slotc[:, k:k + 1], 0),
            l2s[:, :], None))

    if milestones is not None:
        milestones['pre_rs'] = len(q.steps)
    q.c("gpsimd", lambda g: g.collective_compute(
        "ReduceScatter", ALU.add, replica_groups=[list(range(cfg.NCORES))],
        ins=[slotbuf.ap()[0:cfg.NSLOT, :].opt()], outs=[rsout.ap().opt()]))
    if milestones is not None:
        milestones['post_rs'] = len(q.steps)

    free_from("lgs")

    onesb = sb("onesb", [1, 1024], BF16)
    q.c("vector", lambda v: v.memset(onesb[:, :], 1.0))
    xfm = sb("xfm", [128, NCH * cfg.CAP], BF16)
    xtile = sb("xtile", [128, D], BF16)
    estg = sb("estg", [128, 1024], F32)
    estg2 = sb("estg2", [128, 1024], F32)
    _esp = [0]
    def _estg():
        _esp[0] ^= 1
        return (estg, "sync") if _esp[0] else (estg2, "scalar")

    hid = sb("hid", [128, cfg.FCH * cfg.CAP], BF16)
    b1r = sb("b1r", [1, cfg.F], BF16)
    eb2r = sb("eb2r", [128, D], F32)
    oev = sb("oev", [128, D], F32)
    xfm3 = xfm[:, :].rearrange("p (c n) -> p c n", c=NCH)
    hid3 = hid[:, :].rearrange("p (c n) -> p c n", c=cfg.FCH)

    _ch = min(1024, cfg.F)
    for j in range(cfg.F // _ch):
        q.dma("sync", lambda s, j=j, _ch=_ch: s.dma_start(
            out=estg[0:1, 0:_ch], in_=eb1.ap()[0:1, j * _ch:(j + 1) * _ch]))
        q.c("vector", lambda v, j=j, _ch=_ch: v.tensor_copy(
            b1r[0:1, j * _ch:(j + 1) * _ch], estg[0:1, 0:_ch]))
    q.dma("sync", lambda s: s.dma_start(out=eb2r[:, :], in_=eb2.ap().broadcast_to([128, D])))

    for k in range(cfg.NXT):
        q.dma("sync", lambda s, k=k: s.dma_start(
            out=xtile[:, :], in_=rsout.ap()[k * 128:(k + 1) * 128, :]))
        for fc in range(NCH):
            q.c("tensor", lambda t, fc=fc: t.transpose(
                psT[:, fc * 128:(fc + 1) * 128], xtile[:, fc * 128:(fc + 1) * 128],
                idb[:, :]))
        q.c("vector", lambda v, k=k: v.tensor_copy(
            xfm3[:, :, k * 128:(k + 1) * 128],
            psT[:, :].rearrange("p (c n) -> p c n", c=NCH)[:, :, 0:128]))

    for kc in range(NCH):
        for j in range(cfg.F // _ch):
            eb_, eq_ = _estg()
            q.dma(eq_, lambda s, kc=kc, j=j, _ch=_ch, eb_=eb_: s.dma_start(
                out=eb_[:, 0:_ch],
                in_=ew1.ap()[kc * 128:(kc + 1) * 128, j * _ch:(j + 1) * _ch]))
            q.c("vector", lambda v, kc=kc, j=j, _ch=_ch, eb_=eb_: v.tensor_copy(
                w1s[:, kc * cfg.F + j * _ch:kc * cfg.F + (j + 1) * _ch],
                eb_[:, 0:_ch]))
    nsplits = [(0, min(512, cfg.CAP))]
    if cfg.CAP > 512:
        nsplits.append((512, cfg.CAP))
    for mc in range(cfg.FCH):
        for si, (n0, n1) in enumerate(nsplits):
            pd = psA if si == 0 else psB
            for kc in range(NCH):
                q.c("tensor", lambda t, mc=mc, kc=kc, n0=n0, n1=n1, pd=pd: t.matmul(
                    pd[:, 0:n1 - n0],
                    w1s[:, kc * cfg.F + mc * 128:kc * cfg.F + (mc + 1) * 128],
                    xfm3[:, kc, n0:n1], start=(kc == 0), stop=False))
            q.c("tensor", lambda t, mc=mc, n0=n0, n1=n1, pd=pd: t.matmul(
                pd[:, 0:n1 - n0], b1r[0:1, mc * 128:(mc + 1) * 128],
                onesb[0:1, n0:n1], start=False, stop=True))
            q.c("scalar", lambda s, mc=mc, n0=n0, n1=n1, pd=pd: s.activation(
                hid3[:, mc, n0:n1], pd[:, 0:n1 - n0], AF.Gelu_apprx_tanh))
    for kc in range(cfg.FCH):
        eb_, eq_ = _estg()
        q.dma(eq_, lambda s, kc=kc, eb_=eb_: s.dma_start(
            out=eb_[:, :], in_=ew2.ap()[kc * 128:(kc + 1) * 128, :]))
        q.c("vector", lambda v, kc=kc, eb_=eb_: v.tensor_copy(
            w2s[:, kc * D:(kc + 1) * D], eb_[:, :]))
    for k in range(cfg.NXT):
        for n0, n1 in ((0, 512), (512, 1024)):
            for kc in range(cfg.FCH):
                q.c("tensor", lambda t, k=k, kc=kc, n0=n0, n1=n1: t.matmul(
                    psO[:, n0:n1], hid3[:, kc, k * 128:(k + 1) * 128],
                    w2s[:, kc * D + n0:kc * D + n1],
                    start=(kc == 0), stop=(kc == cfg.FCH - 1)))
        q.c("vector", lambda v: v.tensor_add(oev[:, :], psO[:, :], eb2r[:, :]))
        q.dma("sync", lambda s, k=k: s.dma_start(
            out=ag2in.ap()[k * 128:(k + 1) * 128, :], in_=oev[:, :]))

    if milestones is not None:
        milestones['pre_ag2'] = len(q.steps)
    q.c("gpsimd", lambda g: g.collective_compute(
        "AllGather", ALU.bypass, replica_groups=[list(range(cfg.NCORES))],
        ins=[ag2in.ap().opt()], outs=[outbuf.ap()[0:cfg.NSLOT, :].opt()]))
    if milestones is not None:
        milestones['post_ag2'] = len(q.steps)

    free_from("w1s")
    NPC = 4
    gats = [sb("gat%d" % i, [128, D], F32) for i in range(NPC)]
    xcis = [sb("xci%d" % i, [128, D], F32) for i in range(NPC)]
    dm2s = [sb("dm2%d" % i, [128, D], F32) for i in range(NPC)]
    stcs = [sb("stc%d" % i, [128, 6], F32) for i in range(NPC)]
    for k in range(NTILE):
        gat = gats[k % NPC]
        xci = xcis[k % NPC]
        dm2 = dm2s[k % NPC]
        stc = stcs[k % NPC]
        q.dma("gpsimd", lambda g, k=k, gat=gat: g.indirect_dma_start(
            gat[:, :], None, outbuf.ap(), IndirectOffsetOnAxis(slotc[:, k:k + 1], 0)))
        q.dma("sync", lambda s, k=k, xci=xci: s.dma_start(
            out=xci[:, :], in_=x1buf.ap()[k * 128:(k + 1) * 128, :]))
        q.c("vector", lambda v, k=k, gat=gat: v.tensor_scalar_mul(
            gat[:, :], gat[:, :], gpc[:, k:k + 1]))
        q.c("vector", lambda v, gat=gat, stc=stc: v.reduce_sum(
            stc[:, 0:1], gat[:, :], AX.X))
        q.c("scalar", lambda s, gat=gat, dm2=dm2, stc=stc: s.activation(
            dm2[:, :], gat[:, :], AF.Square, accum_out=stc[:, 1:2]))
        q.c("vector", lambda v, stc=stc: v.tensor_scalar_mul(
            stc[:, 0:1], stc[:, 0:1], 1.0 / D))
        q.c("vector", lambda v, stc=stc: v.tensor_scalar_mul(
            stc[:, 1:2], stc[:, 1:2], 1.0 / D))
        q.c("vector", lambda v, stc=stc: v.tensor_mul(
            stc[:, 2:3], stc[:, 0:1], stc[:, 0:1]))
        q.c("vector", lambda v, stc=stc: v.tensor_sub(
            stc[:, 1:2], stc[:, 1:2], stc[:, 2:3]))
        q.c("vector", lambda v, stc=stc: v.tensor_scalar_add(
            stc[:, 1:2], stc[:, 1:2], EPS))
        q.c("scalar", lambda s, stc=stc: s.activation(
            stc[:, 2:3], stc[:, 1:2], AF.Sqrt))
        q.c("vector", lambda v, stc=stc: v.reciprocal(stc[:, 3:4], stc[:, 2:3]))
        q.c("vector", lambda v, stc=stc: v.tensor_mul(
            stc[:, 4:5], stc[:, 3:4], stc[:, 0:1]))
        q.c("vector", lambda v, stc=stc: v.tensor_scalar_mul(
            stc[:, 4:5], stc[:, 4:5], -1.0))
        q.c("vector", lambda v, gat=gat, stc=stc: v.tensor_scalar(
            gat[:, :], gat[:, :], stc[:, 3:4], stc[:, 4:5], ALU.mult, ALU.add))
        q.c("vector", lambda v, gat=gat, xci=xci: v.tensor_add(
            gat[:, :], gat[:, :], xci[:, :]))
        q.dma("sync", lambda s, k=k, gat=gat: s.dma_start(
            out=out.ap()[k * 128:(k + 1) * 128, :], in_=gat[:, :]))

    if milestones is not None:
        milestones['all'] = len(q.steps)
    if max_steps is not None:
        q.steps = q.steps[:max_steps]
    if phasec_probe is not None:
        pdbg = nc.dram_tensor("pdbg", [128, D], F32, kind="ExternalOutput")
        pdbg2 = nc.dram_tensor("pdbg2", [128, cfg.NTILE + 1], F32,
                               kind="ExternalOutput")
        gat = gats[0]
        xci = xcis[0]
        if phasec_probe == "iota":
            pidx = sb("pidx", [128, 2], I32)
            pidxf = pidx[:, 1:2].bitcast(F32)
            q.c("gpsimd", lambda g: g.iota(
                pidxf, pattern=[[0, 1]], base=0, channel_multiplier=1,
                allow_small_or_imprecise_dtypes=True))
            q.c("vector", lambda v: v.tensor_copy(pidx[:, 0:1], pidxf))
            q.dma("gpsimd", lambda g: g.indirect_dma_start(
                gat[:, :], None, outbuf.ap(),
                IndirectOffsetOnAxis(pidx[:, 0:1], 0)))
        else:
            q.dma("gpsimd", lambda g: g.indirect_dma_start(
                gat[:, :], None, outbuf.ap(),
                IndirectOffsetOnAxis(slotc[:, 0:1], 0)))
        q.dma("sync", lambda s: s.dma_start(out=pdbg.ap(), in_=gat[:, :]))
        q.c("vector", lambda v: v.tensor_copy(xci[:, 0:cfg.NTILE],
                                              slotc[:, :]))
        q.dma("sync", lambda s: s.dma_start(
            out=pdbg2.ap()[:, 0:cfg.NTILE], in_=xci[:, 0:cfg.NTILE]))

    if dump_src is not None:
        _dct = dict(logitsT=logitsT, logits_all=logits_all, slotall=slotall,
                    gpall=gpall, slotbuf=slotbuf, rsout=rsout, ag2in=ag2in,
                    outbuf=outbuf, x1buf=x1buf, ln2buf=ln2buf)
        for _nm, _shape, _dt in zip(dump_src, dump_shape, dump_dt):
            _src = _dct[_nm]
            dbgout = nc.dram_tensor("dbg_" + _nm, list(_shape), _dt,
                                    kind="ExternalOutput")
            for r0 in range(0, _shape[0], 512):
                r1 = min(_shape[0], r0 + 512)
                q.dma("sync", lambda s, r0=r0, r1=r1, dbgout=dbgout, _src=_src:
                      s.dma_start(out=dbgout.ap()[r0:r1, :],
                                  in_=_src.ap()[r0:r1, :]))
    if emit_mode == "probe":
        with nc.Block() as blk:
            probe_out["info"] = q.emit_probe(nc, blk)
            probe_out["seq"] = q
    elif emit_mode == "sched":
        import contextlib
        with contextlib.ExitStack() as st:
            sems = {}
            for d in schedule["dims"]:
                sems[d] = st.enter_context(nc.semaphore("sem_" + d))
            blk = st.enter_context(nc.Block())
            q.emit(nc, blk, sems, schedule=schedule)
    else:
        with nc.semaphore("s_y") as s_y, nc.semaphore("s_v") as s_v, \
             nc.semaphore("s_s") as s_s, \
             nc.semaphore("s_t") as s_t, nc.semaphore("s_g") as s_g, \
             nc.semaphore("s_yd") as s_yd, nc.semaphore("s_sd") as s_sd, \
             nc.semaphore("s_gd") as s_gd, nc.Block() as blk:
            q.emit(nc, blk, {"sync": s_y, "vector": s_v, "scalar": s_s,
                             "tensor": s_t,
                             "gpsimd": s_g, "sync_d": s_yd, "scalar_d": s_sd,
                             "gpsimd_d": s_gd})

    for g, _ in reversed(allocs):
        g.__exit__(None, None, None)
    return nc


def _consts(cfg):
    import ml_dtypes
    eye = np.eye(128, dtype=np.float32)
    g = cfg.GC
    triu = np.triu(np.ones((g, g), np.float32))
    cm = np.zeros((128, 64), np.float32)
    cm[0:g, 0:g] = triu
    cm[g:2 * g, 0:g] = triu
    return {
        "identf": eye,
        "identb": eye.astype(ml_dtypes.bfloat16),
        "cmask": cm.astype(ml_dtypes.bfloat16),
        "sltri": np.triu(np.ones((128, 128), np.float32), k=1),
    }


def make_in_maps(cfg, inputs):
    x = np.asarray(inputs["L0_attn_x"], np.float32)
    xr = np.ascontiguousarray(x.reshape(cfg.NCORES, cfg.NTOK, cfg.D))

    def f32(name):
        return np.ascontiguousarray(np.asarray(inputs[name], np.float32))

    common = {
        "Wq": f32("L0_attn_Wq"), "Wk": f32("L0_attn_Wk"),
        "Wv": f32("L0_attn_Wv"), "Wg": f32("L0_attn_Wg"),
        "Wo": f32("L0_attn_Wo"), "W1": f32("L0_attn_W1"), "W2": f32("L0_attn_W2"),
        "gkw1": f32("L0_attn_gk_w1"), "gkw2": f32("L0_attn_gk_w2"),
        "ln_w": f32("L0_attn_ln_w").reshape(1, -1),
        "ln_b": f32("L0_attn_ln_b").reshape(1, -1),
        "gk_b2": f32("L0_attn_gk_b2").reshape(1, -1),
        "b1i": f32("L0_attn_b1").reshape(1, -1),
        "b2i": f32("L0_attn_b2").reshape(1, -1),
        "rms_w": f32("L0_attn_rms_w").reshape(1, -1),
        "gw": f32("L0_ffn_gate_w"), "gb": f32("L0_ffn_gate_b").reshape(1, -1),
    }
    common.update(_consts(cfg))
    ew1 = f32("L0_ffn_w1"); eb1 = f32("L0_ffn_b1")
    ew2 = f32("L0_ffn_w2"); eb2 = f32("L0_ffn_b2")

    in_maps = []
    for c in range(cfg.NCORES):
        m = dict(common)
        m["x_tm"] = xr[c]
        m["xT"] = np.ascontiguousarray(xr[c].T)
        m["ew1"] = ew1[c]; m["eb1"] = eb1[c].reshape(1, -1)
        m["ew2"] = ew2[c]; m["eb2"] = eb2[c].reshape(1, -1)
        m["coreid"] = np.array([[c]], np.float32)
        in_maps.append(m)
    return in_maps


def build_scheduled(cfg, **kw):
    po = {}
    build_program(cfg, emit_mode="probe", probe_out=po, **kw)
    sched = po["seq"].compute_schedule(po["info"])
    return build_program(cfg, emit_mode="sched", schedule=sched, **kw)


def kernel(**inputs):
    cfg = Cfg()
    x = np.asarray(inputs["L0_attn_x"], np.float32)
    NB = x.shape[0]
    nc = build_scheduled(cfg)
    in_maps = make_in_maps(cfg, inputs)
    res = run_bass_kernel_spmd(nc, in_maps, core_ids=list(range(cfg.NCORES)))
    outs = np.stack([res.results[c]["out"] for c in range(cfg.NCORES)])
    return outs.reshape(NB, cfg.T, cfg.D)
